# version 19
# baseline (speedup 1.0000x reference)
import math
import numpy as np
import ml_dtypes

import concourse.bass as bass
import concourse.mybir as mybir
from concourse import dve_ops
from concourse.bass_utils import run_bass_kernel_spmd

AF = mybir.ActivationFunctionType
ALU = mybir.AluOpType
AX = mybir.AxisListType
F32 = mybir.dt.float32
BF16 = mybir.dt.bfloat16
U32 = mybir.dt.uint32

T = 4096
C = 1024
H = 16
HD = 64
NP = 8
ND = 8
NT = 32
NQ = 8
E = 8
CAP = int(1.2 * (8 * T) / E)
GS = 128
HS = 512
EPS = 1e-5

STAGES = "ABCD"
FUSED = True

_BF = ml_dtypes.bfloat16


_const_cache = {}


def _consts():
    if _const_cache:
        return _const_cache
    c = {}
    inv = 1.0 / 10000.0 ** (np.arange(0, HD, 2, dtype=np.float64) / HD)
    t = np.arange(T, dtype=np.float64)
    fr = inv[:, None] * t[None, :]
    cos64 = np.cos(np.concatenate([fr, fr], 0))
    sin64 = np.sin(np.concatenate([fr, fr], 0))
    sgn = np.where(np.arange(HD) < HD // 2, -1.0, 1.0)[:, None]
    c["ropecos"] = np.concatenate([cos64, cos64], 0).astype(_BF)
    c["ropesin"] = np.concatenate([sin64 * sgn, sin64 * sgn], 0).astype(_BF)
    p32 = np.zeros((128, 128), np.float32)
    for i in range(128):
        half = (i // HD) * HD
        src = half + ((i % HD) + HD // 2) % HD
        p32[src, i] = 1.0
    c["p32"] = p32.astype(_BF)
    c["identb"] = np.eye(128, dtype=np.float32).astype(_BF)
    c["identf"] = np.eye(128, dtype=np.float32)
    c["ltri128"] = np.tril(np.ones((128, 128), np.float32), -1).astype(_BF)
    c["ltri32"] = np.ascontiguousarray(
        np.tril(np.ones((32, 32), np.float32), -1))
    kr = np.arange(128)[:, None]
    q = np.arange(512)[None, :]
    mA = np.zeros((8, 128, 512), np.float32)
    for j in range(4):
        mA[j] = ((j * 128 + kr) > q)
    for j in range(4, 8):
        mA[j] = (((j - 4) * 128 + kr) <= q)
    c["maskA"] = mA.astype(_BF)
    mB = np.zeros((4, 128, 512), np.float32)
    for jd in range(4):
        mB[jd] = ((jd * 128 + kr) <= q).astype(np.float32)
    c["maskB"] = mB
    st = np.zeros((32, 128, 32), np.float32)
    for jp in range(32):
        st[jp, :, : jp + 1] = 1.0
    c["step32"] = st.astype(_BF)
    c["iota8"] = np.broadcast_to(
        np.arange(8, dtype=np.float32), (128, 8)).copy()
    c["iota16"] = np.broadcast_to(
        np.arange(16, dtype=np.float32), (128, 16)).copy()
    _const_cache.update(c)
    return c


def _prep_weights(inputs):
    w = {}
    tp = lambda a: np.ascontiguousarray(np.asarray(a).T)
    w["l0aqT"] = tp(inputs["L0_attn_Wq"])
    w["l0akT"] = tp(inputs["L0_attn_Wk"])
    w["l0avT"] = tp(inputs["L0_attn_Wv"])
    w["l0aoT"] = tp(inputs["L0_attn_Wo"])
    w["l0fqT"] = tp(np.asarray(inputs["L0_ffn_Wq"]).reshape(C, C))
    w["l0fkT"] = tp(np.asarray(inputs["L0_ffn_Wk"]).reshape(C, C))
    w["l0fvT"] = tp(np.asarray(inputs["L0_ffn_Wv"]).reshape(C, C))
    w["l0fbq"] = np.ascontiguousarray(
        np.asarray(inputs["L0_ffn_bq"]).reshape(C, 1))
    w["l0fbk"] = np.ascontiguousarray(
        np.asarray(inputs["L0_ffn_bk"]).reshape(C, 1))
    w["l0fbv"] = np.ascontiguousarray(
        np.asarray(inputs["L0_ffn_bv"]).reshape(C, 1))
    w["l0fWeT"] = np.ascontiguousarray(
        np.transpose(np.asarray(inputs["L0_ffn_We"]), (0, 2, 1))).astype(_BF)
    w["l0fbe"] = np.ascontiguousarray(np.asarray(inputs["L0_ffn_be"]))
    w["l0fswT"] = tp(np.asarray(inputs["L0_ffn_Wsw"]))
    w["l0fbsw"] = np.ascontiguousarray(
        np.asarray(inputs["L0_ffn_bsw"]).reshape(1, E))
    w["l0fg"] = np.ascontiguousarray(
        np.asarray(inputs["L0_ffn_gamma"]).reshape(C, 1))
    w["l0fb"] = np.ascontiguousarray(
        np.asarray(inputs["L0_ffn_beta"]).reshape(C, 1))
    w["l1qT"] = tp(inputs["L1_attn_q_w"])
    w["l1kT"] = tp(inputs["L1_attn_k_w"])
    w["l1vT"] = tp(inputs["L1_attn_v_w"])
    w["l1oT"] = tp(inputs["L1_attn_o_w"])
    w["l1qb"] = np.ascontiguousarray(
        np.asarray(inputs["L1_attn_q_b"]).reshape(C, 1))
    w["l1kb"] = np.ascontiguousarray(
        np.asarray(inputs["L1_attn_k_b"]).reshape(C, 1))
    w["l1vb"] = np.ascontiguousarray(
        np.asarray(inputs["L1_attn_v_b"]).reshape(C, 1))
    w["l1ob"] = np.ascontiguousarray(
        np.asarray(inputs["L1_attn_o_b"]).reshape(C, 1))
    w["l1wrT"] = tp(inputs["L1_ffn_w_router"])
    w["l1cent"] = np.ascontiguousarray(
        np.asarray(inputs["L1_ffn_head_centroids"]))
    w["l1temp"] = np.ascontiguousarray(
        np.asarray(inputs["L1_ffn_temperature"]).reshape(1, 1))
    w["l1IE"] = np.ascontiguousarray(np.asarray(inputs["L1_ffn_input_experts"]))
    w["l1OE"] = np.ascontiguousarray(
        np.asarray(inputs["L1_ffn_output_experts"]))
    return w


class QOp:

    __slots__ = ("meth", "args", "kwargs", "incs")

    def __init__(self, meth, args, kwargs):
        self.meth = meth
        self.args = args
        self.kwargs = kwargs
        self.incs = []

    def then_inc(self, sem_h, k):
        self.incs.append((sem_h, k))
        return self


class FEng:

    def __init__(self, q):
        self._q = q

    def __getattr__(self, name):
        def call(*a, **k):
            op = QOp(name, a, k)
            self._q.append(op)
            return op
        return call


class FakeBlock:

    ENGINES = ("tensor", "scalar", "vector", "gpsimd", "sync")

    def __init__(self):
        self.qs = {e: [] for e in self.ENGINES}
        self._f = {e: FEng(self.qs[e]) for e in self.ENGINES}

    def tensor(self, f):
        f(self._f["tensor"])

    def scalar(self, f):
        f(self._f["scalar"])

    def vector(self, f):
        f(self._f["vector"])

    def gpsimd(self, f):
        f(self._f["gpsimd"])

    def sync(self, f):
        f(self._f["sync"])

    def flush(self, block):
        for name in self.ENGINES:
            q = self.qs[name]

            def body(eng, q=q):
                for op in q:
                    ins = getattr(eng, op.meth)(*op.args, **op.kwargs)
                    for h, k in op.incs:
                        ins.then_inc(h, k)
            getattr(block, name)(body)


class Sem:
    def __init__(self, ctx, nc, name):
        self.h = ctx.enter_context(nc.semaphore(name))
        self.n = 0
        self.name = name

    def inc(self, instr, k=1):
        instr.then_inc(self.h, k)
        self.n += k
        return self.n


class Buf:

    __slots__ = ("w", "r")

    def __init__(self):
        self.w = {}
        self.r = {}

    def rd(self, eng, s):
        _wait(eng, s, self.w)

    def wr(self, eng, s):
        _wait(eng, s, self.w, self.r)

    def did_rd(self, sem_name, n):
        self.r[sem_name] = max(self.r.get(sem_name, 0), n)

    def did_wr(self, sem_name, n):
        self.w = {sem_name: n}
        self.r = {}


def _wait(eng, s, *dicts):
    agg = {}
    for d in dicts:
        for k, v in d.items():
            agg[k] = max(agg.get(k, 0), v)
    for k, v in agg.items():
        eng.wait_ge(s.sem(k).h, v)


class ColView:

    def __init__(self, t, off):
        self.t = t
        self.off = off

    def __getitem__(self, idx):
        p, f = idx
        return self.t[p, f.start + self.off:f.stop + self.off]


class S:
    def va_view(self, j, h):
        c0 = (j * 16 + h) * 65
        return self.va[:, c0:c0 + 65]

    def sem(self, name):
        if name not in self.sems:
            self.sems[name] = Sem(self._ctx, self.nc, name)
        return self.sems[name]


def emit_attn_norm_pre(s, block, vw):
    @block.vector
    def _(v):
        for bb in vw["bufs"]:
            bb.rd(v, s)
        s.b_rowa.wr(v, s)
        _i = emit_recip_row(s, v, s.row_a[:, 0:512], vw["denA"])
        n = s.dv.inc(_i)
        s.b_rowa.did_wr("dv", n)
        s.b_rowb.wr(v, s)
        _i = emit_recip_row(s, v, s.row_b[:, 0:512], vw["denB"])
        n = s.dv.inc(_i)
        s.b_rowb.did_wr("dv", n)
        for bb in vw["bufs"]:
            bb.did_rd("dv", n)

    @block.tensor
    def _(t):
        s.b_rowa.rd(t, s)
        s.b_rowb.rd(t, s)
        s.b_ps0.wr(t, s)
        _i = t.matmul(s.p_s0[0:64, 0:512], s.ones1_64f[:, :],
                      s.row_a[:, 0:512], start=True, stop=True)
        n = s.mm.inc(_i)
        s.b_ps0.did_wr("mm", n)
        s.b_ps1.wr(t, s)
        _i = t.matmul(s.p_s1[0:64, 0:512], s.ones1_64f[:, :],
                      s.row_b[:, 0:512], start=True, stop=True)
        n = s.mm.inc(_i)
        s.b_ps1.did_wr("mm", n)
        s.b_rowa.did_rd("mm", n)
        s.b_rowb.did_rd("mm", n)

    @block.vector
    def _(v):
        s.b_ps0.rd(v, s)
        s.b_aB.wr(v, s)
        _i = v.tensor_copy(s.a_aB[0:64, 0:512], s.p_s0[0:64, 0:512])
        n = s.dv.inc(_i)
        s.b_aB.did_wr("dv", n)
        s.b_ps0.did_rd("dv", n)
        s.b_ps1.rd(v, s)
        s.b_bB.wr(v, s)
        _i = v.tensor_copy(s.a_bB[0:64, 0:512], s.p_s1[0:64, 0:512])
        n = s.dv.inc(_i)
        s.b_bB.did_wr("dv", n)
        s.b_ps1.did_rd("dv", n)


def emit_recip_row(s, v, out_ap, in_ap):
    return v.reciprocal(out_ap, in_ap)



def _barrier(s, block):
    s.barn += 5
    for attr in ("tensor", "scalar", "vector", "gpsimd", "sync"):
        def body(eng, n=s.barn):
            eng.nop().then_inc(s.bar.h, 1)
            eng.wait_ge(s.bar.h, n)
        getattr(block, attr)(body)


def emit_ln_store(s, block, r_buf, r_t, out_f, out_b, qc,
                  gamma=None, beta=None, store=True, post_cb=None,
                  ch_of="cOF", ch_ob="cOB", mid_cb=None):
    for ec in range(ND):
        sl = ec % 2

        @block.scalar
        def _(sc, ec=ec, sl=sl):
            r_buf.rd(sc, s)
            s.b_rsq2[sl].wr(sc, s)
            _i = sc.activation(s.a_rsq2[:, sl * 512:(sl + 1) * 512],
                               r_t[:, ec * 512:(ec + 1) * 512], AF.Square)
            n = s.act.inc(_i)
            s.b_rsq2[sl].did_wr("act", n)
            r_buf.did_rd("act", n)

        @block.tensor
        def _(t, ec=ec, sl=sl):
            r_buf.rd(t, s)
            s.b_rsq2[sl].rd(t, s)
            if ec == 0:
                s.b_pg.wr(t, s)
            t.matmul(s.p_g[0:1, 0:512], s.ones128f[:, :],
                     r_t[:, ec * 512:(ec + 1) * 512],
                     start=(ec == 0), stop=(ec == ND - 1))
            _i = t.matmul(s.p_g[0:1, 512:1024], s.ones128b[:, :],
                          s.a_rsq2[:, sl * 512:(sl + 1) * 512],
                          start=(ec == 0), stop=(ec == ND - 1))
            n = s.mm.inc(_i)
            r_buf.did_rd("mm", n)
            s.b_rsq2[sl].did_rd("mm", n)
            if ec == ND - 1:
                s.b_pg.did_wr("mm", n)

    if mid_cb is not None:
        mid_cb()

    @block.vector
    def _(v):
        s.b_pg.rd(v, s)
        s.b_rowm.wr(v, s)
        _i = v.tensor_scalar_mul(s.row_m[:, 0:512], s.p_g[0:1, 0:512], 1.0 / C)
        n = s.dv.inc(_i)
        s.b_rowm.did_wr("dv", n)
        s.b_rowq.wr(v, s)
        _i = v.tensor_scalar_mul(s.row_q[:, 0:512], s.p_g[0:1, 512:1024],
                                 1.0 / C)
        n = s.dv.inc(_i)
        s.b_rowq.did_wr("dv", n)
        s.b_pg.did_rd("dv", n)

    @block.vector
    def _(v):
        s.b_rowm.rd(v, s)
        s.b_rowb.wr(v, s)
        _i = v.tensor_mul(s.row_b[:, 0:512], s.row_m[:, 0:512],
                          s.row_m[:, 0:512])
        n = s.dv.inc(_i)
        v.wait_ge(s.dv.h, n)
        s.b_rowq.rd(v, s)
        s.b_rowq.wr(v, s)
        _i = v.tensor_sub(s.row_q[:, 0:512], s.row_q[:, 0:512],
                          s.row_b[:, 0:512])
        n = s.dv.inc(_i)
        s.b_rowq.did_wr("dv", n)
        s.b_rowb.did_rd("dv", n)

    @block.scalar
    def _(sc):
        s.b_rowq.rd(sc, s)
        s.b_rowb.wr(sc, s)
        _i = sc.activation(s.row_b[:, 0:512], s.row_q[:, 0:512], AF.Sqrt,
                           bias=s.epsT[0:1, 0:1])
        n = s.act.inc(_i)
        s.b_rowq.did_rd("act", n)
        s.b_rowb.did_wr("act", n)

    @block.vector
    def _(v):
        s.b_rowb.rd(v, s)
        s.b_rowa.wr(v, s)
        _i = emit_recip_row(s, v, s.row_a[:, 0:512], s.row_b[:, 0:512])
        n = s.dv.inc(_i)
        s.b_rowb.did_rd("dv", n)
        s.b_rowa.did_wr("dv", n)
        v.wait_ge(s.dv.h, n)
        s.b_rowm.rd(v, s)
        s.b_rowb.wr(v, s)
        _i = v.tensor_mul(s.row_b[:, 0:512], s.row_m[:, 0:512],
                          s.row_a[:, 0:512])
        n = s.dv.inc(_i)
        v.wait_ge(s.dv.h, n)
        _i = v.tensor_scalar_mul(s.row_b[:, 0:512], s.row_b[:, 0:512], -1.0)
        n = s.dv.inc(_i)
        s.b_rowm.did_rd("dv", n)
        s.b_rowb.did_wr("dv", n)

    @block.tensor
    def _(t):
        s.b_rowa.rd(t, s)
        s.b_rowb.rd(t, s)
        s.b_pc.wr(t, s)
        t.matmul(s.p_c[:, 0:512], s.ones1_128f[:, :], s.row_a[:, 0:512],
                 start=True, stop=True)
        s.b_pd.wr(t, s)
        _i = t.matmul(s.p_d[:, 0:512], s.ones1_128f[:, :], s.row_b[:, 0:512],
                 start=True, stop=True)
        n = s.mm.inc(_i)
        s.b_rowa.did_rd("mm", n)
        s.b_rowb.did_rd("mm", n)
        s.b_pc.did_wr("mm", n)
        s.b_pd.did_wr("mm", n)

    @block.vector
    def _(v):
        s.b_pc.rd(v, s)
        s.b_pd.rd(v, s)
        s.b_aB.wr(v, s)
        _i = v.tensor_copy(s.a_aB[:, 0:512], s.p_c[:, 0:512])
        n = s.dv.inc(_i)
        s.b_aB.did_wr("dv", n)
        s.b_bB.wr(v, s)
        _i = v.tensor_copy(s.a_bB[:, 0:512], s.p_d[:, 0:512])
        n = s.dv.inc(_i)
        s.b_bB.did_wr("dv", n)
        s.b_pc.did_rd("dv", n)
        s.b_pd.did_rd("dv", n)

    for ec in range(ND):
        off = (ec % 2) * 512
        ob = s.b_o12[ec % 2]

        @block.vector
        def _(v, ec=ec, off=off, ob=ob):
            s.b_aB.rd(v, s)
            s.b_bB.rd(v, s)
            r_buf.rd(v, s)
            ob.wr(v, s)
            _i = v.tensor_mul(s.a_o1[:, off:off + 512],
                              r_t[:, ec * 512:(ec + 1) * 512],
                              s.a_aB[:, 0:512])
            n = s.dv.inc(_i)
            v.wait_ge(s.dv.h, n)
            _i = v.tensor_add(s.a_o1[:, off:off + 512],
                              s.a_o1[:, off:off + 512], s.a_bB[:, 0:512])
            n = s.dv.inc(_i)
            if gamma is not None:
                v.wait_ge(s.dv.h, n)
                _i = v.tensor_scalar(s.a_o1[:, off:off + 512],
                                     s.a_o1[:, off:off + 512],
                                     gamma[:, ec:ec + 1], beta[:, ec:ec + 1],
                                     ALU.mult, ALU.add)
                n = s.dv.inc(_i)
            s.b_aB.did_rd("dv", n)
            s.b_bB.did_rd("dv", n)
            r_buf.did_rd("dv", n)
            ob.did_wr("dv", n)

        if store:
            @block.sync
            def _(sy, ec=ec, off=off, ob=ob):
                ob.rd(sy, s)
                n = s.sem(ch_of).inc(sy.dma_start(
                    out=out_f[ec * 128:(ec + 1) * 128,
                              qc * 512:(qc + 1) * 512],
                    in_=s.a_o1[:, off:off + 512]), 16)
                ob.did_rd(ch_of, n)

            if out_b is not None:
                @block.gpsimd
                def _(g, ec=ec, off=off, ob=ob):
                    ob.rd(g, s)
                    n = s.sem(ch_ob).inc(g.dma_start(
                        out=out_b[ec * 128:(ec + 1) * 128,
                                  qc * 512:(qc + 1) * 512],
                        in_=s.a_o1[:, off:off + 512]), 16)
                    ob.did_rd(ch_ob, n)
        if post_cb is not None:
            post_cb(ec, off, ob)


def av_views(s, parity):
    if parity == 0:
        return dict(avA=s.p_g[0:65, 0:512], avB=s.p_g[0:65, 512:1024],
                    denA=s.p_g[64:65, 0:512], denB=s.p_g[64:65, 512:1024],
                    valA=s.p_g[0:64, 0:512], valB=s.p_g[0:64, 512:1024],
                    bufs=(s.b_pg,))
    return dict(avA=s.p_c[0:65, 0:512], avB=s.p_d[0:65, 0:512],
                denA=s.p_c[64:65, 0:512], denB=s.p_d[64:65, 0:512],
                valA=s.p_c[0:64, 0:512], valB=s.p_d[0:64, 0:512],
                bufs=(s.b_pc, s.b_pd))


def emit_attn_pair(s, block, js, qtA, qtB, qt_buf, kA_fn, kB_fn, ktbuf_fn,
                   vaA_fn, vaB_fn, vabuf_fn, premask_fn=None,
                   postmask_fn=None, parity=0):
    vw = av_views(s, parity)
    avA, avB = vw["avA"], vw["avB"]
    avbufs = vw["bufs"]
    def scores(idx, j):
        ps = s.p_s0 if idx % 2 == 0 else s.p_s1
        psb = s.b_ps0 if idx % 2 == 0 else s.b_ps1
        pb = idx % 2

        @block.tensor
        def _(t, idx=idx, j=j, ps=ps, psb=psb):
            qt_buf.rd(t, s)
            ktbuf_fn(j).rd(t, s)
            psb.wr(t, s)
            t.matmul(ps[:, 0:512], kA_fn(j), qtA, start=True, stop=True,
                     tile_position=(0, 0))
            _i = t.matmul(ps[:, 512:1024], kB_fn(j), qtB, start=True,
                          stop=True, tile_position=(64, 0))
            n = s.mm.inc(_i)
            psb.did_wr("mm", n)
            qt_buf.did_rd("mm", n)
            ktbuf_fn(j).did_rd("mm", n)

        if premask_fn is not None:
            premask_fn(j, ps, psb)

        @block.scalar
        def _(sc, idx=idx, ps=ps, psb=psb, pb=pb):
            psb.rd(sc, s)
            s.b_pbuf[pb].wr(sc, s)
            _i = sc.activation(s.pb_sb[:, pb * 1024:(pb + 1) * 1024],
                               ps[:, 0:1024], AF.Exp)
            n = s.act.inc(_i)
            psb.did_rd("act", n)
            s.b_pbuf[pb].did_wr("act", n)

        if postmask_fn is not None:
            postmask_fn(j, pb)

    def av(idx, j):
        pb = idx % 2
        first = (idx == 0)
        last = (idx == len(js) - 1)

        @block.tensor
        def _(t, idx=idx, j=j, pb=pb, first=first, last=last):
            s.b_pbuf[pb].rd(t, s)
            vabuf_fn(j).rd(t, s)
            if first:
                for bb in avbufs:
                    bb.wr(t, s)
            t.matmul(avA, vaA_fn(j),
                     s.pb_sb[:, pb * 1024:pb * 1024 + 512],
                     start=first, stop=last)
            _i = t.matmul(avB, vaB_fn(j),
                          s.pb_sb[:, pb * 1024 + 512:pb * 1024 + 1024],
                          start=first, stop=last)
            n = s.mm.inc(_i)
            s.b_pbuf[pb].did_rd("mm", n)
            vabuf_fn(j).did_rd("mm", n)
            if last:
                for bb in avbufs:
                    bb.did_wr("mm", n)

    for idx, j in enumerate(js):
        scores(idx, j)
        if idx >= 1:
            av(idx - 1, js[idx - 1])
    av(len(js) - 1, js[-1])
    return vw


def emit_stage_A(s, block):
    nc = s.nc
    SCALE = 1.0 / math.sqrt(HD)

    @block.gpsimd
    def _(g):
        for wi, hname in enumerate(["l0aqT", "l0akT", "l0avT", "l0aoT"]):
            for dc in range(ND):
                s.sem("cAw").inc(g.dma_start(
                    out=s.a_w[wi][:, dc * 1024:(dc + 1) * 1024],
                    in_=s.hbm[hname][dc * 128:(dc + 1) * 128, :]), 16)
        s.b_aw.did_wr("cAw", s.sem("cAw").n)

    @block.gpsimd
    def _(g):
        for j in range(8):
            s.sem("cAm").inc(g.dma_start(
                out=s.a_maskA[:, j * 512:(j + 1) * 512],
                in_=s.hbm["maskA"][j, :, :]), 16)
        s.b_maskA.did_wr("cAm", s.sem("cAm").n)

    @block.vector
    def _(v):
        for sl in range(3):
            for tt in range(4):
                view = s.a_vaw.ap().rearrange(
                    "p (s h x) -> p s h x", s=12, x=65)
                _i = v.memset(view[:, sl * 4 + tt, :, 64:65], 1.0)
        n = s.dv.inc(_i)
        for sl in range(3):
            s.b_vaw[sl].did_wr("dv", n)

    def load_qkv(b):
        c2 = b % 2
        c3 = b % 3
        xo = 0
        qo = 0

        @block.gpsimd
        def _(g, b=b, c2=c2, xo=xo):
            ch = "cAx%d" % c2
            s.b_xbf2[c2].wr(g, s)
            for dc in range(ND):
                s.sem(ch).inc(g.dma_start(
                    out=s.a_xbf[:, xo + dc * 512:xo + (dc + 1) * 512],
                    in_=s.hbm["x0T"][dc * 128:(dc + 1) * 128,
                                     b * 512:(b + 1) * 512]), 16)
            s.b_xbf2[c2].did_wr(ch, s.sem(ch).n)

        for ec in range(NP):
            for wi in (0, 1):
                psb = s.b_ps0 if (2 * ec + wi) % 2 == 0 else s.b_ps1
                ps = s.p_s0 if (2 * ec + wi) % 2 == 0 else s.p_s1

                @block.tensor
                def _(t, ec=ec, wi=wi, ps=ps, psb=psb, c2=c2, xo=xo):
                    s.b_xbf2[c2].rd(t, s)
                    s.b_aw.rd(t, s)
                    psb.wr(t, s)
                    for dc in range(ND):
                        _i = t.matmul(ps[:, 0:512],
                                 s.a_w[wi][:, dc * 1024 + ec * 128:
                                           dc * 1024 + (ec + 1) * 128],
                                 s.a_xbf[:, xo + dc * 512:xo + (dc + 1) * 512],
                                 start=(dc == 0), stop=(dc == ND - 1))
                    n = s.mm.inc(_i)
                    psb.did_wr("mm", n)
                    s.b_xbf2[c2].did_rd("mm", n)
                    s.b_aw.did_rd("mm", n)

                @block.vector
                def _(v, ec=ec, wi=wi, ps=ps, psb=psb, c2=c2, c3=c3, qo=qo):
                    psb.rd(v, s)
                    if wi == 0:
                        dst_buf = s.b_qt2[c2]
                        out = s.a_qt[:, qo + ec * 512:qo + (ec + 1) * 512]
                        scale = SCALE
                    else:
                        dst_buf = s.b_ktw[c3]
                        out = s.a_ktw[:, (c3 * NP + ec) * 512:
                                      (c3 * NP + ec + 1) * 512]
                        scale = 1.0
                    dst_buf.wr(v, s)
                    if scale == 1.0:
                        _i = v.tensor_copy(out, ps[:, 0:512])
                    else:
                        _i = v.tensor_scalar_mul(out, ps[:, 0:512], scale)
                    n = s.dv.inc(_i)
                    dst_buf.did_wr("dv", n)
                    psb.did_rd("dv", n)

        for tt in range(4):
            for eh in range(2):
                @block.tensor
                def _(t, tt=tt, eh=eh, c2=c2, xo=xo):
                    s.b_xbf2[c2].rd(t, s)
                    s.b_aw.rd(t, s)
                    s.b_pc.wr(t, s)
                    for dc in range(ND):
                        _i = t.matmul(s.p_c[:, 0:512],
                                 s.a_xbf[:, xo + dc * 512 + tt * 128:
                                         xo + dc * 512 + (tt + 1) * 128],
                                 s.a_w[2][:, dc * 1024 + eh * 512:
                                          dc * 1024 + (eh + 1) * 512],
                                 start=(dc == 0), stop=(dc == ND - 1))
                    n = s.mm.inc(_i)
                    s.b_pc.did_wr("mm", n)
                    s.b_xbf2[c2].did_rd("mm", n)

                @block.vector
                def _(v, tt=tt, eh=eh, c3=c3):
                    s.b_pc.rd(v, s)
                    s.b_vaw[c3].wr(v, s)
                    view = s.a_vaw.ap().rearrange(
                        "p (s h x) -> p s h x", s=12, x=65)
                    src = s.p_c.ap().rearrange("p (h x) -> p h x", x=64)
                    _i = v.tensor_copy(view[:, c3 * 4 + tt,
                                       eh * 8:(eh + 1) * 8, 0:64],
                                  src[:, 0:8, :])
                    n = s.dv.inc(_i)
                    s.b_vaw[c3].did_wr("dv", n)
                    s.b_pc.did_rd("dv", n)

    def attn(b):
        c2 = b % 2
        c3 = b % 3
        p3 = (b - 1) % 3
        qo = 0
        oo = 0
        first_j = 4 if b == 0 else 0
        pend = [None]
        for ec in range(NP):
            def slot_of(j, c3=c3, p3=p3):
                return p3 if j < 4 else c3

            def kbase_of(j, ec=ec):
                return (slot_of(j) * NP + ec) * 512 + (j % 4) * 128

            def vbase_of(j):
                return (slot_of(j) * 4 + (j % 4)) * 1040

            def postmask(j, pb):
                @block.vector
                def _(v, j=j, pb=pb):
                    s.b_pbuf[pb].rd(v, s)
                    s.b_maskA.rd(v, s)
                    s.b_pbuf[pb].wr(v, s)
                    m = s.a_maskA[:, j * 512:(j + 1) * 512]
                    _i = v.tensor_mul(s.pb_sb[:, pb * 1024:pb * 1024 + 512],
                                      s.pb_sb[:, pb * 1024:pb * 1024 + 512],
                                      m)
                    n = s.dv.inc(_i)
                    v.wait_ge(s.dv.h, n)
                    _i = v.tensor_mul(
                        s.pb_sb[:, pb * 1024 + 512:pb * 1024 + 1024],
                        s.pb_sb[:, pb * 1024 + 512:pb * 1024 + 1024], m)
                    n = s.dv.inc(_i)
                    s.b_pbuf[pb].did_wr("dv", n)
                    s.b_maskA.did_rd("dv", n)

            emit_attn_pair(
                s, block, list(range(first_j, 8)),
                qtA=s.a_qt[0:64, qo + ec * 512:qo + (ec + 1) * 512],
                qtB=s.a_qt[64:128, qo + ec * 512:qo + (ec + 1) * 512],
                qt_buf=s.b_qt2[c2],
                kA_fn=lambda j, ec=ec: s.a_ktw[0:64,
                                               kbase_of(j):kbase_of(j) + 128],
                kB_fn=lambda j, ec=ec: s.a_ktw[64:128,
                                               kbase_of(j):kbase_of(j) + 128],
                ktbuf_fn=lambda j: s.b_ktw[slot_of(j)],
                vaA_fn=lambda j, ec=ec: s.a_vaw[
                    :, vbase_of(j) + (2 * ec) * 65:
                    vbase_of(j) + (2 * ec) * 65 + 65],
                vaB_fn=lambda j, ec=ec: s.a_vaw[
                    :, vbase_of(j) + (2 * ec + 1) * 65:
                    vbase_of(j) + (2 * ec + 1) * 65 + 65],
                vabuf_fn=lambda j: s.b_vaw[slot_of(j)],
                postmask_fn=postmask, parity=ec % 2)
            vw = av_views(s, ec % 2)

            def fin(ec=ec, vw=vw, c2=c2, oo=oo):
                emit_attn_norm_pre(s, block, vw)

                @block.vector
                def _(v, ec=ec, vw=vw, c2=c2, oo=oo):
                    s.b_aB.rd(v, s)
                    s.b_bB.rd(v, s)
                    for bb in vw["bufs"]:
                        bb.rd(v, s)
                    s.b_ot2[c2].wr(v, s)
                    v.tensor_mul(
                        s.a_ot[0:64, oo + ec * 512:oo + (ec + 1) * 512],
                        vw["valA"], s.a_aB[0:64, 0:512])
                    _i = v.tensor_mul(
                        s.a_ot[64:128, oo + ec * 512:oo + (ec + 1) * 512],
                        vw["valB"], s.a_bB[0:64, 0:512])
                    n = s.dv.inc(_i)
                    s.b_ot2[c2].did_wr("dv", n)
                    for bb in vw["bufs"]:
                        bb.did_rd("dv", n)
                    s.b_aB.did_rd("dv", n)
                    s.b_bB.did_rd("dv", n)

            if pend[0] is not None:
                pend[0]()
            pend[0] = fin

        pend[0]()

    def out(b):
        c2 = b % 2
        oo = 0
        for ec in range(ND):
            psb = s.b_ps0 if ec % 2 == 0 else s.b_ps1
            ps = s.p_s0 if ec % 2 == 0 else s.p_s1

            @block.sync
            def _(sy, ec=ec, b=b):
                ch = "cAxf%d" % (ec % 2)
                s.b_xf[ec % 2].wr(sy, s)
                n = s.sem(ch).inc(sy.dma_start(
                    out=s.a_xf[:, (ec % 2) * 512:((ec % 2) + 1) * 512],
                    in_=s.hbm["x0T"][ec * 128:(ec + 1) * 128,
                                     b * 512:(b + 1) * 512]), 16)
                s.b_xf[ec % 2].did_wr(ch, n)

            @block.tensor
            def _(t, ec=ec, ps=ps, psb=psb, c2=c2, oo=oo):
                s.b_ot2[c2].rd(t, s)
                s.b_aw.rd(t, s)
                psb.wr(t, s)
                for dc in range(ND):
                    _i = t.matmul(ps[:, 0:512],
                             s.a_w[3][:, dc * 1024 + ec * 128:
                                     dc * 1024 + (ec + 1) * 128],
                             s.a_ot[:, oo + dc * 512:oo + (dc + 1) * 512],
                             start=(dc == 0), stop=(dc == ND - 1))
                n = s.mm.inc(_i)
                psb.did_wr("mm", n)
                s.b_ot2[c2].did_rd("mm", n)

            @block.vector
            def _(v, ec=ec, ps=ps, psb=psb):
                psb.rd(v, s)
                s.b_xf[ec % 2].rd(v, s)
                s.b_r.wr(v, s)
                _i = v.tensor_add(s.a_r[:, ec * 512:(ec + 1) * 512],
                             ps[:, 0:512],
                             s.a_xf[:, (ec % 2) * 512:((ec % 2) + 1) * 512])
                n = s.dv.inc(_i)
                psb.did_rd("dv", n)
                s.b_xf[ec % 2].did_rd("dv", n)
                s.b_r.did_wr("dv", n)

        emit_ln_store(s, block, s.b_r, s.a_r,
                      s.hbm["x1Tf"], s.hbm["x1Tb"], qc=b,
                      ch_of="cAof", ch_ob="cAob")

    load_qkv(0)
    for b in range(8):
        attn(b)
        if b + 1 < 8:
            load_qkv(b + 1)
        out(b)


def emit_B_setup(s, block):
    @block.gpsimd
    def _(g):
        ch = s.sem("cBs")
        for ec in range(ND):
            ch.inc(g.dma_start(out=s.bq_sb[:, ec:ec + 1],
                               in_=s.hbm["l0fbq"][ec * 128:(ec + 1) * 128, :]),
                   16)
            ch.inc(g.dma_start(out=s.bk_sb[:, ec:ec + 1],
                               in_=s.hbm["l0fbk"][ec * 128:(ec + 1) * 128, :]),
                   16)
            ch.inc(g.dma_start(out=s.bv_sb[:, ec:ec + 1],
                               in_=s.hbm["l0fbv"][ec * 128:(ec + 1) * 128, :]),
                   16)
            ch.inc(g.dma_start(out=s.gb_g[:, ec:ec + 1],
                               in_=s.hbm["l0fg"][ec * 128:(ec + 1) * 128, :]),
                   16)
            ch.inc(g.dma_start(out=s.gb_b[:, ec:ec + 1],
                               in_=s.hbm["l0fb"][ec * 128:(ec + 1) * 128, :]),
                   16)
            ch.inc(g.dma_start(out=s.sw_sb[:, ec * 8:(ec + 1) * 8],
                               in_=s.hbm["l0fswT"][ec * 128:(ec + 1) * 128, :]),
                   16)
        ch.inc(g.dma_start(out=s.bsw_sb[:, :], in_=s.hbm["l0fbsw"]), 16)
        ch.inc(g.dma_start(out=s.ltri_sb[:, :], in_=s.hbm["ltri128"]), 16)
        ch.inc(g.dma_start(out=s.ltri32_sb[:, :], in_=s.hbm["ltri32"]), 16)
        ch.inc(g.dma_start(out=s.identf_sb[:, :], in_=s.hbm["identf"]), 16)
        s.b_setup.did_wr("cBs", ch.n)

    @block.vector
    def _(v):
        s.b_setup.rd(v, s)
        _i = v.tensor_scalar_mul(s.bq_sb[:, 0:8], s.bq_sb[:, 0:8],
                                 1.0 / math.sqrt(C))
        n = s.dv.inc(_i)
        s.b_setup.did_rd("dv", n)
        s.b_setup.w["dv"] = n

    @block.tensor
    def _(t):
        s.b_setup.rd(t, s)
        s.b_pc.wr(t, s)
        _i = t.matmul(s.p_c[:, 0:8], s.ones1_128f[:, :], s.bsw_sb[:, :],
                      start=True, stop=True)
        n = s.mm.inc(_i)
        s.b_pc.did_wr("mm", n)
        s.b_setup.did_rd("mm", n)

    @block.vector
    def _(v):
        s.b_pc.rd(v, s)
        s.b_bswB.wr(v, s)
        _i = v.tensor_copy(s.bswB_sb[:, :], s.p_c[:, 0:8])
        n = s.dv.inc(_i)
        s.b_bswB.did_wr("dv", n)
        s.b_pc.did_rd("dv", n)


def emit_B_mst(s, block):
    @block.gpsimd
    def _(g):
        ch = s.sem("cBs2")
        s.b_setup.wr(g, s)
        ch.inc(g.dma_start(out=s.rankc_sb[:, :], in_=s.hbm["rankcol"]), 16)
        ch.inc(g.dma_start(out=s.rankm_sb[:, :], in_=s.hbm["rankmask"]), 16)
        ch.inc(g.dma_start(out=s.iota8_sb[:, :], in_=s.hbm["iota8"]), 16)
        for jd in range(4):
            ch.inc(g.dma_start(out=s.maskB_sb[:, jd * 512:(jd + 1) * 512],
                               in_=s.hbm["maskB"][jd, :, :]), 16)
        for jp in range(32):
            ch.inc(g.dma_start(out=s.step_sb[:, jp * 32:(jp + 1) * 32],
                               in_=s.hbm["step32"][jp, :, :]), 16)
        s.b_setup.did_wr("cBs2", ch.n)


def emit_B_loadw(s, block, hname, ch):
    @block.gpsimd
    def _(g):
        sem = s.sem(ch)
        s.b_w.wr(g, s)
        for dc in range(ND):
            sem.inc(g.dma_start(
                out=s.w_sb[:, dc * 1024:(dc + 1) * 1024],
                in_=s.hbm[hname][dc * 128:(dc + 1) * 128, :]), 16)
        s.b_w.did_wr(ch, sem.n)


def emit_B_loadx(s, block, qc, hname="x1Tb"):
    @block.gpsimd
    def _(g, qc=qc):
        ch = s.sem("cBx")
        s.b_xcb.wr(g, s)
        for dc in range(ND):
            ch.inc(g.dma_start(
                out=s.xcb[:, dc * 512:(dc + 1) * 512],
                in_=s.hbm[hname][dc * 128:(dc + 1) * 128,
                                 qc * 512:(qc + 1) * 512]), 16)
        s.b_xcb.did_wr("cBx", ch.n)


def emit_B_kv(s, block):
    @block.vector
    def _(v):
        view = s.va.ap().rearrange("p (n x) -> p n x", x=65)
        _i = v.memset(view[:, :, 64:65], 1.0)
        n = s.dv.inc(_i)
        s.b_va.did_wr("dv", n)

    emit_B_loadw(s, block, "l0fkT", "cBw")
    for qc in range(NQ):
        emit_B_loadx(s, block, qc)
        for ec in range(NP):
            psb = s.b_ps0 if ec % 2 == 0 else s.b_ps1
            ps = s.p_s0 if ec % 2 == 0 else s.p_s1

            @block.tensor
            def _(t, ec=ec, ps=ps, psb=psb):
                s.b_xcb.rd(t, s)
                s.b_w.rd(t, s)
                psb.wr(t, s)
                for dc in range(ND):
                    _i = t.matmul(ps[:, 0:512],
                                  s.w_sb[:, dc * 1024 + ec * 128:
                                         dc * 1024 + (ec + 1) * 128],
                                  s.xcb[:, dc * 512:(dc + 1) * 512],
                                  start=(dc == 0), stop=(dc == ND - 1))
                n = s.mm.inc(_i)
                psb.did_wr("mm", n)
                s.b_xcb.did_rd("mm", n)
                s.b_w.did_rd("mm", n)

            @block.vector
            def _(v, ec=ec, qc=qc, ps=ps, psb=psb):
                psb.rd(v, s)
                s.b_setup.rd(v, s)
                s.b_kt.wr(v, s)
                _i = v.tensor_scalar_add(
                    s.kt[:, ec * T + qc * 512:ec * T + (qc + 1) * 512],
                    ps[:, 0:512], s.bk_sb[:, ec:ec + 1])
                n = s.dv.inc(_i)
                s.b_kt.did_wr("dv", n)
                psb.did_rd("dv", n)
                s.b_setup.did_rd("dv", n)

    emit_B_loadw(s, block, "l0fvT", "cBw")
    for qc in range(NQ):
        emit_B_loadx(s, block, qc)
        for tt in range(4):
            for eh in range(2):
                @block.tensor
                def _(t, tt=tt, eh=eh):
                    s.b_xcb.rd(t, s)
                    s.b_w.rd(t, s)
                    s.b_pc.wr(t, s)
                    for dc in range(ND):
                        _i = t.matmul(
                            s.p_c[:, 0:512],
                            s.xcb[:, dc * 512 + tt * 128:
                                  dc * 512 + (tt + 1) * 128],
                            s.w_sb[:, dc * 1024 + eh * 512:
                                   dc * 1024 + (eh + 1) * 512],
                            start=(dc == 0), stop=(dc == ND - 1))
                    n = s.mm.inc(_i)
                    s.b_pc.did_wr("mm", n)
                    s.b_xcb.did_rd("mm", n)
                    s.b_w.did_rd("mm", n)

                @block.vector
                def _(v, tt=tt, eh=eh, qc=qc):
                    s.b_pc.rd(v, s)
                    s.b_va.wr(v, s)
                    view = s.va.ap().rearrange("p (n x) -> p n x", x=65)
                    src = s.p_c.ap().rearrange("p (h x) -> p h x", x=64)
                    til = qc * 4 + tt
                    _i = v.tensor_copy(
                        view[:, til * 16 + eh * 8:til * 16 + (eh + 1) * 8,
                             0:64], src[:, 0:8, :])
                    n = s.dv.inc(_i)
                    s.b_va.did_wr("dv", n)
                    s.b_pc.did_rd("dv", n)

    for h in range(H):
        @block.tensor
        def _(t, h=h):
            s.b_va.rd(t, s)
            s.b_setup.rd(t, s)
            if h == 0:
                s.b_pg.wr(t, s)
            for jp in range(NT):
                _i = t.matmul(
                    s.p_g[0:65, h * 32:(h + 1) * 32] if h < 8 else
                    s.p_g[0:65, 512 + (h - 8) * 32:512 + (h - 7) * 32],
                    s.va_view(jp, h),
                    s.step_sb[:, jp * 32:(jp + 1) * 32],
                    start=(jp == 0), stop=(jp == NT - 1))
            n = s.mm.inc(_i)
            s.b_va.did_rd("mm", n)
            if h == H - 1:
                s.b_pg.did_wr("mm", n)

    @block.vector
    def _(v):
        s.b_pg.rd(v, s)
        s.b_sfx.wr(v, s)
        v.tensor_copy(s.sfx[0:65, 0:256], s.p_g[0:65, 0:256])
        _i = v.tensor_copy(s.sfx[0:65, 256:512], s.p_g[0:65, 512:768])
        n = s.dv.inc(_i)
        s.b_sfx.did_wr("dv", n)
        s.b_pg.did_rd("dv", n)


def emit_B_attn_qchunk(s, block, qc):
    SC = 1.0 / math.sqrt(C)

    emit_B_loadx(s, block, qc)

    for ec in range(NP):
        psb = s.b_ps0 if ec % 2 == 0 else s.b_ps1
        ps = s.p_s0 if ec % 2 == 0 else s.p_s1

        @block.tensor
        def _(t, ec=ec, ps=ps, psb=psb):
            s.b_xcb.rd(t, s)
            s.b_w.rd(t, s)
            psb.wr(t, s)
            for dc in range(ND):
                _i = t.matmul(ps[:, 0:512],
                              s.w_sb[:, dc * 1024 + ec * 128:
                                     dc * 1024 + (ec + 1) * 128],
                              s.xcb[:, dc * 512:(dc + 1) * 512],
                              start=(dc == 0), stop=(dc == ND - 1))
            n = s.mm.inc(_i)
            psb.did_wr("mm", n)
            s.b_xcb.did_rd("mm", n)
            s.b_w.did_rd("mm", n)

        @block.vector
        def _(v, ec=ec, ps=ps, psb=psb):
            psb.rd(v, s)
            s.b_setup.rd(v, s)
            s.b_qt.wr(v, s)
            _i = v.tensor_scalar(s.qt[:, ec * 512:(ec + 1) * 512],
                                 ps[:, 0:512], SC, s.bq_sb[:, ec:ec + 1],
                                 ALU.mult, ALU.add)
            n = s.dv.inc(_i)
            s.b_qt.did_wr("dv", n)
            psb.did_rd("dv", n)
            s.b_setup.did_rd("dv", n)

    jmax = qc * 4 + 3
    pend = [None]
    for ec in range(NP):
        def premask(j, ps, psb, qc=qc):
            if j < qc * 4:
                return

            @block.vector
            def _(v, j=j, ps=ps, psb=psb, qc=qc):
                jd = j - qc * 4
                psb.rd(v, s)
                s.b_setup.rd(v, s)
                psb.wr(v, s)
                _i = v.tensor_mul(ps[:, 0:512], ps[:, 0:512],
                                  s.maskB_sb[:, jd * 512:(jd + 1) * 512])
                n = s.dv.inc(_i)
                v.wait_ge(s.dv.h, n)
                _i = v.tensor_mul(ps[:, 512:1024], ps[:, 512:1024],
                                  s.maskB_sb[:, jd * 512:(jd + 1) * 512])
                n = s.dv.inc(_i)
                psb.did_wr("dv", n)
                s.b_setup.did_rd("dv", n)

        emit_attn_pair(
            s, block, list(range(jmax + 1)),
            qtA=s.qt[0:64, ec * 512:(ec + 1) * 512],
            qtB=s.qt[64:128, ec * 512:(ec + 1) * 512],
            qt_buf=s.b_qt,
            kA_fn=lambda j, ec=ec: s.kt[0:64, ec * T + j * 128:
                                        ec * T + (j + 1) * 128],
            kB_fn=lambda j, ec=ec: s.kt[64:128, ec * T + j * 128:
                                        ec * T + (j + 1) * 128],
            ktbuf_fn=lambda j: s.b_kt,
            vaA_fn=lambda j, ec=ec: s.va_view(j, 2 * ec),
            vaB_fn=lambda j, ec=ec: s.va_view(j, 2 * ec + 1),
            vabuf_fn=lambda j: s.b_va,
            premask_fn=premask, parity=ec % 2)
        vw = av_views(s, ec % 2)

        def fin(ec=ec, vw=vw, qc=qc):
         if qc < NQ - 1:
             @block.vector
             def _(v, ec=ec, qc=qc, vw=vw):
                 s.b_sfx.rd(v, s)
                 for bb in vw["bufs"]:
                     bb.rd(v, s)
                     bb.wr(v, s)
                 j0 = qc * 4 + 4
                 _i = v.tensor_scalar_add(vw["avA"], vw["avA"],
                                          s.sfx[0:65, (2 * ec) * 32 + j0:
                                                (2 * ec) * 32 + j0 + 1])
                 n = s.dv.inc(_i)
                 _i = v.tensor_scalar_add(vw["avB"], vw["avB"],
                                          s.sfx[0:65, (2 * ec + 1) * 32 + j0:
                                                (2 * ec + 1) * 32 + j0 + 1])
                 n = s.dv.inc(_i)
                 for bb in vw["bufs"]:
                     bb.did_wr("dv", n)
                 s.b_sfx.did_rd("dv", n)

         emit_attn_norm_pre(s, block, vw)

         @block.vector
         def _(v, ec=ec, vw=vw):
             for bb in vw["bufs"]:
                 bb.rd(v, s)
             s.b_o1 = s.b_o12[0]
             s.b_o1.wr(v, s)
             s.b_aB.rd(v, s)
             _i = v.tensor_mul(s.a_o1[0:64, 0:512], vw["valA"],
                               s.a_aB[0:64, 0:512])
             n = s.dv.inc(_i)
             s.b_bB.rd(v, s)
             _i = v.tensor_mul(s.a_o1[64:128, 0:512], vw["valB"],
                               s.a_bB[0:64, 0:512])
             n = s.dv.inc(_i)
             for bb in vw["bufs"]:
                 bb.did_rd("dv", n)
             s.b_aB.did_rd("dv", n)
             s.b_bB.did_rd("dv", n)
             v.wait_ge(s.dv.h, n)
             s.b_setup.rd(v, s)
             _i = v.tensor_scalar_add(s.a_o1[:, 0:512], s.a_o1[:, 0:512],
                                      s.bv_sb[:, ec:ec + 1])
             n = s.dv.inc(_i)
             s.b_o1.did_wr("dv", n)
             s.b_setup.did_rd("dv", n)
             v.wait_ge(s.dv.h, n)
             s.b_xcb.wr(v, s)
             _i = v.tensor_copy(s.xcb[:, ec * 512:(ec + 1) * 512],
                                s.a_o1[:, 0:512])
             n = s.dv.inc(_i)
             s.b_xcb.did_wr("dv", n)

         @block.sync
         def _(sy, ec=ec, qc=qc):
             ch = s.sem("cBhf")
             s.b_o1.rd(sy, s)
             n = ch.inc(sy.dma_start(
                 out=s.hbm["hTf"][ec * 128:(ec + 1) * 128,
                                  qc * 512:(qc + 1) * 512],
                 in_=s.a_o1[:, 0:512]), 16)
             s.b_o1.did_rd("cBhf", n)

         @block.gpsimd
         def _(g, ec=ec, qc=qc):
             ch = s.sem("cBhb")
             s.b_o1.rd(g, s)
             n = ch.inc(g.dma_start(
                 out=s.hbm["hTb"][ec * 128:(ec + 1) * 128,
                                  qc * 512:(qc + 1) * 512],
                 in_=s.a_o1[:, 0:512]), 16)
             s.b_o1.did_rd("cBhb", n)

        if pend[0] is not None:
            pend[0]()
        pend[0] = fin

    pend[0]()

    for tt in range(4):
        j = qc * 4 + tt

        @block.tensor
        def _(t, tt=tt):
            s.b_xcb.rd(t, s)
            s.b_setup.rd(t, s)
            s.b_pc.wr(t, s)
            for dc in range(ND):
                _i = t.matmul(s.p_c[:, 0:8],
                              s.xcb[:, dc * 512 + tt * 128:
                                    dc * 512 + (tt + 1) * 128],
                              s.sw_sb[:, dc * 8:(dc + 1) * 8],
                              start=(dc == 0), stop=(dc == ND - 1))
            n = s.mm.inc(_i)
            s.b_pc.did_wr("mm", n)
            s.b_xcb.did_rd("mm", n)
            s.b_setup.did_rd("mm", n)

        @block.vector
        def _(v, j=j):
            s.b_pc.rd(v, s)
            s.b_bswB.rd(v, s)
            s.b_lg.wr(v, s)
            _i = v.tensor_add(s.lg_sb[:, 0:8], s.p_c[:, 0:8], s.bswB_sb[:, :])
            n = s.dv.inc(_i)
            s.b_lg.did_wr("dv", n)
            s.b_pc.did_rd("dv", n)
            v.wait_ge(s.dv.h, n)
            s.b_mx.wr(v, s)
            _i = v.max(s.mx_sb[:, 0:8], s.lg_sb[:, 0:8])
            n = s.dv.inc(_i)
            s.b_mx.did_wr("dv", n)
            v.wait_ge(s.dv.h, n)
            s.b_mi.wr(v, s)
            _i = v.max_index(s.mi_sb[:, 0:8], s.mx_sb[:, 0:8], s.lg_sb[:, 0:8])
            n = s.dv.inc(_i)
            s.b_mi.did_wr("dv", n)
            v.wait_ge(s.dv.h, n)
            s.b_ln.wr(v, s)
            _i = v.tensor_scalar_mul(s.ln_sb[:, 0:1], s.mx_sb[:, 0:1], -1.0)
            n = s.dv.inc(_i)
            s.b_ln.did_wr("dv", n)
            s.b_idxf.wr(v, s)
            _i = v.tensor_copy(s.idxf_sb[:, 0:1], s.mi_sb[:, 0:1])
            n = s.dv.inc(_i)
            s.b_idxf.did_wr("dv", n)

        @block.scalar
        def _(sc, j=j):
            s.b_lg.rd(sc, s)
            s.b_ln.rd(sc, s)
            s.b_pex.wr(sc, s)
            _i = sc.activation(s.pex_sb[:, 0:8], s.lg_sb[:, 0:8], AF.Exp,
                               bias=s.ln_sb[:, 0:1])
            n = s.act.inc(_i)
            s.b_pex.did_wr("act", n)
            s.b_lg.did_rd("act", n)
            s.b_ln.did_rd("act", n)

        @block.vector
        def _(v, j=j):
            s.b_pex.rd(v, s)
            s.b_zz.wr(v, s)
            _i = v.reduce_sum(s.zz_sb[:, 0:1], s.pex_sb[:, 0:8], axis=AX.X)
            n = s.dv.inc(_i)
            s.b_zz.did_wr("dv", n)
            s.b_pex.did_rd("dv", n)
            v.wait_ge(s.dv.h, n)
            s.b_rp.wr(v, s)
            _i = v.reciprocal(s.rp_sb[:, j:j + 1], s.zz_sb[:, 0:1])
            n = s.dv.inc(_i)
            s.b_rp.did_wr("dv", n)
            s.b_zz.did_rd("dv", n)
            s.b_idxf.rd(v, s)
            s.b_setup.rd(v, s)
            s.b_oh.wr(v, s)
            _i = v.tensor_single_scalar(s.oh_sb[:, j * 8:(j + 1) * 8],
                                        s.iota8_sb[:, :],
                                        s.idxf_sb[:, 0:1], ALU.is_equal)
            n = s.dv.inc(_i)
            s.b_oh.did_wr("dv", n)
            s.b_idxf.did_rd("dv", n)
            v.wait_ge(s.dv.h, n)
            s.b_ohb.wr(v, s)
            _i = v.tensor_copy(s.ohb_sb[:, j * 8:(j + 1) * 8],
                               s.oh_sb[:, j * 8:(j + 1) * 8])
            n = s.dv.inc(_i)
            s.b_ohb.did_wr("dv", n)
            s.b_oh.did_rd("dv", n)

        @block.tensor
        def _(t, j=j):
            s.b_oh.rd(t, s)
            s.b_pc.wr(t, s)
            _i = t.matmul(s.p_c[0:1, 0:8], s.ones128f[:, :],
                          s.oh_sb[:, j * 8:(j + 1) * 8],
                          start=True, stop=True)
            n = s.mm.inc(_i)
            s.b_oh.did_rd("mm", n)
            s.b_pc.did_wr("mm", n)

        @block.vector
        def _(v, j=j):
            s.b_pc.rd(v, s)
            s.b_tot.wr(v, s)
            _i = v.tensor_copy(s.tot_sb[0:1, j * 8:(j + 1) * 8],
                               s.p_c[0:1, 0:8])
            n = s.dv.inc(_i)
            s.b_tot.did_wr("dv", n)
            s.b_pc.did_rd("dv", n)


def emit_B_counts(s, block):
    @block.gpsimd
    def _(g):
        ch = s.sem("cBt")
        s.b_tot.rd(g, s)
        n = ch.inc(g.dma_start(out=s.hbm["scr_tot"], in_=s.tot_sb[:, :]), 16)
        g.wait_ge(ch.h, n)
        s.b_t32.wr(g, s)
        n = ch.inc(g.dma_start(
            out=s.t32_sb[:, :],
            in_=s.hbm["scr_tot"].rearrange("o (r c) -> (o r) c", c=8)), 16)
        s.b_t32.did_wr("cBt", n)

    @block.tensor
    def _(t):
        s.b_t32.rd(t, s)
        s.b_pc.wr(t, s)
        _i = t.matmul(s.p_c[0:1, 0:8], s.ones32f[:, :], s.t32_sb[:, :],
                      start=True, stop=True)
        n = s.mm.inc(_i)
        s.b_pc.did_wr("mm", n)
        s.b_t32.did_rd("mm", n)

    @block.vector
    def _(v):
        s.b_pc.rd(v, s)
        s.b_tall.wr(v, s)
        _i = v.tensor_copy(s.tall_sb[:, :], s.p_c[0:1, 0:8])
        n = s.dv.inc(_i)
        s.b_tall.did_wr("dv", n)
        s.b_pc.did_rd("dv", n)

    @block.tensor
    def _(t):
        s.b_tall.rd(t, s)
        s.b_setup.rd(t, s)
        s.b_pd.wr(t, s)
        _i = t.matmul(s.p_d[0:8, 0:8], s.rankc_sb[:, :], s.tall_sb[:, :],
                      start=True, stop=True)
        n = s.mm.inc(_i)
        s.b_pd.did_wr("mm", n)
        s.b_tall.did_rd("mm", n)

    @block.vector
    def _(v):
        s.b_pd.rd(v, s)
        s.b_slot.wr(v, s)
        _i = v.tensor_copy(s.slot_sb[:, :], s.p_d[0:8, 0:8])
        n = s.dv.inc(_i)
        s.b_slot.did_wr("dv", n)
        s.b_pd.did_rd("dv", n)

    @block.gpsimd
    def _(g):
        ch = s.sem("cBcc")
        s.b_slot.rd(g, s)
        n = ch.inc(g.dma_start(out=s.hbm["cc_in"], in_=s.slot_sb[:, :]), 16)
        g.wait_ge(ch.h, n)
        g.collective_compute(
            "AllReduce", mybir.AluOpType.add,
            replica_groups=[list(range(8))],
            ins=[s.hbm["cc_in"].opt()],
            outs=[s.hbm["cc_out"].opt()],
        ).then_inc(s.cc.h, 1)
        s.cc.n += 1
        g.wait_ge(s.cc.h, s.cc.n)
        s.b_gat.wr(g, s)
        n = ch.inc(g.dma_start(out=s.gat_sb[:, :], in_=s.hbm["cc_out"]), 16)
        s.b_gat.did_wr("cBcc", n)

    @block.tensor
    def _(t):
        s.b_gat.rd(t, s)
        s.b_setup.rd(t, s)
        s.b_pc.wr(t, s)
        _i = t.matmul(s.p_c[0:1, 0:8], s.rankm_sb[:, :], s.gat_sb[:, :],
                      start=True, stop=True)
        n = s.mm.inc(_i)
        s.b_pc.did_wr("mm", n)
        s.b_gat.did_rd("mm", n)

    @block.vector
    def _(v):
        s.b_pc.rd(v, s)
        s.b_coff.wr(v, s)
        _i = v.tensor_copy(s.coff_sb[:, :], s.p_c[0:1, 0:8])
        n = s.dv.inc(_i)
        s.b_coff.did_wr("dv", n)
        s.b_pc.did_rd("dv", n)

    @block.tensor
    def _(t):
        s.b_t32.rd(t, s)
        s.b_coff.rd(t, s)
        s.b_setup.rd(t, s)
        s.b_pd.wr(t, s)
        t.matmul(s.p_d[0:32, 0:8], s.ltri32_sb[:, :], s.t32_sb[:, :],
                 start=True, stop=False)
        _i = t.matmul(s.p_d[0:32, 0:8], s.ones1_32f[:, :], s.coff_sb[:, :],
                      start=False, stop=True)
        n = s.mm.inc(_i)
        s.b_pd.did_wr("mm", n)
        s.b_t32.did_rd("mm", n)
        s.b_coff.did_rd("mm", n)

    @block.vector
    def _(v):
        s.b_pd.rd(v, s)
        s.b_roff.wr(v, s)
        _i = v.tensor_copy(s.roff_sb[:, :], s.p_d[0:32, 0:8])
        n = s.dv.inc(_i)
        s.b_roff.did_wr("dv", n)
        s.b_pd.did_rd("dv", n)

    @block.gpsimd
    def _(g):
        ch = s.sem("cBt")
        s.b_roff.rd(g, s)
        n = ch.inc(g.dma_start(
            out=s.hbm["scr_row"].rearrange("o (r c) -> (o r) c", c=8),
            in_=s.roff_sb[:, :]), 16)
        g.wait_ge(ch.h, n)
        s.b_rrow.wr(g, s)
        n = ch.inc(g.dma_start(out=s.rrow_sb[:, :], in_=s.hbm["scr_row"]), 16)
        s.b_rrow.did_wr("cBt", n)


def emit_B_pos(s, block):
    @block.tensor
    def _(t):
        s.b_ohb.rd(t, s)
        s.b_setup.rd(t, s)
        s.b_pc.wr(t, s)
        _i = t.matmul(s.p_c[:, 0:256], s.ltri_sb[:, :], s.ohb_sb[:, 0:256],
                      start=True, stop=True)
        n = s.mm.inc(_i)
        s.b_pc.did_wr("mm", n)
        s.b_ohb.did_rd("mm", n)
        s.b_rrow.rd(t, s)
        s.b_pd.wr(t, s)
        _i = t.matmul(s.p_d[:, 0:256], s.ones1_128f[:, :],
                      s.rrow_sb[0:1, 0:256], start=True, stop=True)
        n = s.mm.inc(_i)
        s.b_pd.did_wr("mm", n)
        s.b_rrow.did_rd("mm", n)

    @block.vector
    def _(v):
        s.b_pd.rd(v, s)
        s.b_rb8.wr(v, s)
        _i = v.tensor_copy(s.rb8_sb[:, 0:256], s.p_d[:, 0:256])
        n = s.dv.inc(_i)
        s.b_rb8.did_wr("dv", n)
        s.b_pd.did_rd("dv", n)
        v.wait_ge(s.dv.h, n)
        s.b_pc.rd(v, s)
        s.b_pf8.wr(v, s)
        _i = v.tensor_add(s.pf8_sb[:, 0:256], s.p_c[:, 0:256],
                          s.rb8_sb[:, 0:256])
        n = s.dv.inc(_i)
        s.b_pf8.did_wr("dv", n)
        s.b_pc.did_rd("dv", n)
        s.b_rb8.did_rd("dv", n)
        v.wait_ge(s.dv.h, n)
        s.b_oh.rd(v, s)
        _i = v.tensor_mul(s.pf8_sb[:, 0:256], s.pf8_sb[:, 0:256],
                          s.oh_sb[:, 0:256])
        n = s.dv.inc(_i)
        s.b_pf8.did_wr("dv", n)
        s.b_oh.did_rd("dv", n)
        v.wait_ge(s.dv.h, n)
        s.b_pos.wr(v, s)
        _i = v.reduce_sum(
            s.pos_sb[:, 0:32],
            s.a_o1[:, 512:768].rearrange("p (t e) -> p t e", e=8),
            axis=AX.X)
        n = s.dv.inc(_i)
        s.b_pos.did_wr("dv", n)
        s.b_pf8.did_rd("dv", n)
        v.wait_ge(s.dv.h, n)
        s.b_kept.wr(v, s)
        _i = v.tensor_single_scalar(s.kept_sb[:, 0:32], s.pos_sb[:, 0:32],
                                    float(CAP), ALU.is_lt)
        n = s.dv.inc(_i)
        s.b_kept.did_wr("dv", n)
        s.b_pos.did_rd("dv", n)

    for j in range(NT):
        sl = j % 2

        @block.vector
        def _(v, j=j, sl=sl):
            s.b_oh.rd(v, s)
            s.b_kept.rd(v, s)
            s.b_ohkb2[sl].wr(v, s)
            _i = v.tensor_scalar_mul(s.ohkb_sb[:, sl * 8:(sl + 1) * 8],
                                     s.oh_sb[:, j * 8:(j + 1) * 8],
                                     s.kept_sb[:, j:j + 1])
            n = s.dv.inc(_i)
            s.b_ohkb2[sl].did_wr("dv", n)
            s.b_oh.did_rd("dv", n)
            s.b_kept.did_rd("dv", n)

        @block.tensor
        def _(t, j=j, sl=sl):
            s.b_ohkb2[sl].rd(t, s)
            s.b_setup.rd(t, s)
            pp = s.p_c if sl == 0 else s.p_d
            ppb = s.b_pc if sl == 0 else s.b_pd
            ppb.wr(t, s)
            _i = t.transpose(pp[0:8, 0:128],
                             s.ohkb_sb[:, sl * 8:(sl + 1) * 8],
                             s.identf_sb[:, :])
            n = s.mm.inc(_i)
            ppb.did_wr("mm", n)
            s.b_ohkb2[sl].did_rd("mm", n)

        @block.vector
        def _(v, j=j, sl=sl):
            pp = s.p_c if sl == 0 else s.p_d
            ppb = s.b_pc if sl == 0 else s.b_pd
            ppb.rd(v, s)
            s.b_ohkT.wr(v, s)
            _i = v.tensor_copy(s.ohkT_sb[0:8, j * 128:(j + 1) * 128],
                               pp[0:8, 0:128])
            n = s.dv.inc(_i)
            s.b_ohkT.did_wr("dv", n)
            ppb.did_rd("dv", n)


def eb_tc0_loads(s, block):
    @block.gpsimd
    def _(g):
        ch = s.sem("cBs")
        s.b_setup.wr(g, s)
        n = ch.inc(g.dma_start(out=s.be_sb[:, :], in_=s.hbm["l0fbe"]), 16)
        s.b_setup.did_wr("cBs", n)

    @block.scalar
    def _(sc):
        ch = s.sem("cBwes")
        for e_ in range(E):
            s.b_wes[e_].wr(sc, s)
            for dc in range(ND):
                ch.inc(sc.dma_start(
                    out=s.we_sb[:, (e_ * ND + dc) * 1024:
                                (e_ * ND + dc + 1) * 1024],
                    in_=s.hbm["l0fWeT"][e_, dc * 128:(dc + 1) * 128, :]),
                    16)
            s.b_wes[e_].did_wr("cBwes", ch.n)


def eb_mBprep(s, block, tc):
    for tt in range(4):
        j = tc * 4 + tt

        @block.vector
        def _(v, j=j, tt=tt):
            s.b_rp.rd(v, s)
            s.b_kept.rd(v, s)
            s.b_m1c.wr(v, s)
            _i = v.tensor_mul(s.m1c_sb[:, 0:1], s.rp_sb[:, j:j + 1],
                              s.kept_sb[:, j:j + 1])
            n = s.dv.inc(_i)
            s.b_m1c.did_wr("dv", n)
            v.wait_ge(s.dv.h, n)
            s.b_m2c.wr(v, s)
            _i = v.tensor_sub(s.m2c_sb[:, 0:1], s.rp_sb[:, j:j + 1],
                              s.m1c_sb[:, 0:1])
            n = s.dv.inc(_i)
            s.b_m2c.did_wr("dv", n)
            s.b_rp.did_rd("dv", n)
            s.b_kept.did_rd("dv", n)

        @block.tensor
        def _(t, tt=tt):
            s.b_m1c.rd(t, s)
            s.b_m2c.rd(t, s)
            s.b_setup.rd(t, s)
            if tt == 0:
                s.b_pc.wr(t, s)
                s.b_pd.wr(t, s)
            _i = t.matmul(s.p_c[0:1, tt * 128:(tt + 1) * 128],
                          s.m1c_sb[:, 0:1], s.identf_sb[:, :],
                          start=(tt == 0), stop=(tt == 3))
            n = s.mm.inc(_i)
            _i = t.matmul(s.p_d[0:1, tt * 128:(tt + 1) * 128],
                          s.m2c_sb[:, 0:1], s.identf_sb[:, :],
                          start=(tt == 0), stop=(tt == 3))
            n = s.mm.inc(_i)
            s.b_m1c.did_rd("mm", n)
            s.b_m2c.did_rd("mm", n)
            if tt == 3:
                s.b_pc.did_wr("mm", n)
                s.b_pd.did_wr("mm", n)

    @block.vector
    def _(v):
        s.b_pc.rd(v, s)
        s.b_m1r.wr(v, s)
        _i = v.tensor_copy(s.m1r_sb[:, :], s.p_c[0:1, 0:512])
        n = s.dv.inc(_i)
        s.b_m1r.did_wr("dv", n)
        s.b_pc.did_rd("dv", n)
        s.b_pd.rd(v, s)
        s.b_m2r.wr(v, s)
        _i = v.tensor_copy(s.m2r_sb[:, :], s.p_d[0:1, 0:512])
        n = s.dv.inc(_i)
        s.b_m2r.did_wr("dv", n)
        s.b_pd.did_rd("dv", n)

    @block.tensor
    def _(t):
        s.b_m1r.rd(t, s)
        s.b_m2r.rd(t, s)
        s.b_pc.wr(t, s)
        _i = t.matmul(s.p_c[:, 0:512], s.ones1_128f[:, :], s.m1r_sb[:, :],
                      start=True, stop=True)
        n = s.mm.inc(_i)
        s.b_pc.did_wr("mm", n)
        s.b_pd.wr(t, s)
        _i = t.matmul(s.p_d[:, 0:512], s.ones1_128f[:, :], s.m2r_sb[:, :],
                      start=True, stop=True)
        n = s.mm.inc(_i)
        s.b_pd.did_wr("mm", n)
        s.b_m1r.did_rd("mm", n)
        s.b_m2r.did_rd("mm", n)

    @block.vector
    def _(v):
        s.b_pc.rd(v, s)
        s.b_mB1.wr(v, s)
        _i = v.tensor_copy(s.mB1_sb[:, :], s.p_c[:, 0:512])
        n = s.dv.inc(_i)
        s.b_mB1.did_wr("dv", n)
        s.b_pc.did_rd("dv", n)
        s.b_pd.rd(v, s)
        s.b_mB2.wr(v, s)
        _i = v.tensor_copy(s.mB2_sb[:, :], s.p_d[:, 0:512])
        n = s.dv.inc(_i)
        s.b_mB2.did_wr("dv", n)
        s.b_pd.did_rd("dv", n)


def eb_prep_ok(s, block, tc):
    for e_ in range(E):
        @block.scalar
        def _(sc, e_=e_, tc=tc):
            ch = s.sem("cBok")
            s.b_ohkT.rd(sc, s)
            s.b_okr.wr(sc, s)
            n = ch.inc(sc.dma_start(
                out=s.okr_sb[0:1, 0:512],
                in_=s.ohkT_sb[e_:e_ + 1, tc * 512:(tc + 1) * 512]), 16)
            s.b_okr.did_wr("cBok", n)
            s.b_ohkT.did_rd("cBok", n)

        @block.tensor
        def _(t, e_=e_):
            s.b_okr.rd(t, s)
            s.b_pc.wr(t, s)
            _i = t.matmul(s.p_c[:, 0:512], s.ones1_128b[:, :],
                          s.okr_sb[0:1, 0:512],
                          start=True, stop=True)
            n = s.mm.inc(_i)
            s.b_pc.did_wr("mm", n)
            s.b_okr.did_rd("mm", n)

        @block.vector
        def _(v, e_=e_):
            s.b_pc.rd(v, s)
            s.b_okA.wr(v, s)
            _i = v.tensor_copy(s.okA_sb[:, e_ * 512:(e_ + 1) * 512],
                               s.p_c[:, 0:512])
            n = s.dv.inc(_i)
            s.b_okA.did_wr("dv", n)
            s.b_pc.did_rd("dv", n)

    @block.scalar
    def _(sc, tc=tc):
        ch = s.sem("cBht")
        s.b_htb.wr(sc, s)
        for dc in range(ND):
            ch.inc(sc.dma_start(
                out=s.htb_sb[:, dc * 512:(dc + 1) * 512],
                in_=s.hbm["hTb"][dc * 128:(dc + 1) * 128,
                                 tc * 512:(tc + 1) * 512]), 16)
        s.b_htb.did_wr("cBht", ch.n)


BANKS = None


def eb_banks(s):
    return [(s.p_s0, s.b_ps0, 0), (s.p_s0, s.b_ps0, 512),
            (s.p_s1, s.b_ps1, 0), (s.p_s1, s.b_ps1, 512)]


def eb_acc(s, block, tc, ph, es=None, bias=True):
    banks = eb_banks(s)
    for e_ in (es if es is not None else range(E)):
        for h in range(2):
            sl = (e_ * 2 + h) % 2

            if ph == 0:
                @block.vector
                def _(v, e_=e_, h=h, sl=sl):
                    s.b_htb.rd(v, s)
                    s.b_okA.rd(v, s)
                    s.b_hm2[sl].wr(v, s)
                    for i, dc in enumerate(range(h * 4, h * 4 + 4)):
                        _i = v.tensor_mul(
                            s.hm_sb2[sl][:, i * 512:(i + 1) * 512],
                            s.htb_sb[:, dc * 512:(dc + 1) * 512],
                            s.okA_sb[:, e_ * 512:(e_ + 1) * 512])
                    n = s.dv.inc(_i)
                    s.b_hm2[sl].did_wr("dv", n)
                    s.b_htb.did_rd("dv", n)
                    s.b_okA.did_rd("dv", n)
            else:
                @block.gpsimd
                def _(g, e_=e_, h=h, sl=sl):
                    s.b_htb.rd(g, s)
                    s.b_okA.rd(g, s)
                    s.b_hm2[sl].wr(g, s)
                    for i, dc in enumerate(range(h * 4, h * 4 + 4)):
                        _i = g.tensor_mul(
                            s.hm_sb2[sl][:, i * 512:(i + 1) * 512],
                            s.htb_sb[:, dc * 512:(dc + 1) * 512],
                            s.okA_sb[:, e_ * 512:(e_ + 1) * 512])
                    n = s.gp.inc(_i)
                    s.b_hm2[sl].did_wr("gp", n)
                    s.b_htb.did_rd("gp", n)
                    s.b_okA.did_rd("gp", n)

            @block.tensor
            def _(t, ph=ph, e_=e_, h=h, sl=sl):
                s.b_wes[e_].rd(t, s)
                s.b_hm2[sl].rd(t, s)
                if e_ == 0 and h == 0:
                    s.b_ps0.wr(t, s)
                    s.b_ps1.wr(t, s)
                for fc in range(ph * 4, ph * 4 + 4):
                    ps, psb, po = banks[fc - ph * 4]
                    for i, dc in enumerate(range(h * 4, h * 4 + 4)):
                        _i = t.matmul(
                            ps[:, po:po + 512],
                            s.we_sb[:, (e_ * ND + dc) * 1024 + fc * 128:
                                    (e_ * ND + dc) * 1024 +
                                    (fc + 1) * 128],
                            s.hm_sb2[sl][:, i * 512:(i + 1) * 512],
                            start=(e_ == 0 and h == 0 and i == 0),
                            stop=False)
                n = s.mm.inc(_i)
                s.b_wes[e_].did_rd("mm", n)
                s.b_hm2[sl].did_rd("mm", n)

    if not bias:
        return

    @block.tensor
    def _(t, ph=ph, tc=tc):
        s.b_setup.rd(t, s)
        s.b_ohkT.rd(t, s)
        for fc in range(ph * 4, ph * 4 + 4):
            ps, psb, po = banks[fc - ph * 4]
            _i = t.matmul(ps[:, po:po + 512],
                          s.be_sb[0:8, fc * 128:(fc + 1) * 128],
                          s.ohkT_sb[0:8, tc * 512:(tc + 1) * 512],
                          start=False, stop=True)
            n = s.mm.inc(_i)
            if fc == ph * 4 + 3:
                s.b_ps0.did_wr("mm", n)
                s.b_ps1.did_wr("mm", n)
        s.b_ohkT.did_rd("mm", n)


def eb_comb(s, block, tc, ph):
    banks = eb_banks(s)
    for fc in range(ph * 4, ph * 4 + 4):
        ps, psb, po = banks[fc - ph * 4]

        @block.sync
        def _(sy, fc=fc, tc=tc):
            ch = s.sem("cBhtf%d" % (fc % 2))
            s.b_htf[fc % 2].wr(sy, s)
            n = ch.inc(sy.dma_start(
                out=s.a_xf[:, (fc % 2) * 512:((fc % 2) + 1) * 512],
                in_=s.hbm["hTf"][fc * 128:(fc + 1) * 128,
                                 tc * 512:(tc + 1) * 512]), 16)
            s.b_htf[fc % 2].did_wr("cBhtf%d" % (fc % 2), n)
            ch = s.sem("cBx1f%d" % (fc % 2))
            s.b_x1f2[fc % 2].wr(sy, s)
            n = ch.inc(sy.dma_start(
                out=s.x1f_sb[:, (fc % 2) * 512:((fc % 2) + 1) * 512],
                in_=s.hbm["x1Tf"][fc * 128:(fc + 1) * 128,
                                  tc * 512:(tc + 1) * 512]), 16)
            s.b_x1f2[fc % 2].did_wr("cBx1f%d" % (fc % 2), n)

        @block.vector
        def _(v, fc=fc, ps=ps, psb=psb, po=po):
            psb.rd(v, s)
            s.b_mB1.rd(v, s)
            s.b_rb8.wr(v, s)
            _i = v.tensor_mul(s.a_o1[:, 0:512], ps[:, po:po + 512],
                              s.mB1_sb[:, :])
            n = s.dv.inc(_i)
            s.b_rb8.did_wr("dv", n)
            psb.did_rd("dv", n)
            s.b_mB1.did_rd("dv", n)
            s.b_htf[fc % 2].rd(v, s)
            s.b_mB2.rd(v, s)
            s.b_o2.wr(v, s)
            _i = v.tensor_mul(
                s.a_o1[:, 512:1024],
                s.a_xf[:, (fc % 2) * 512:((fc % 2) + 1) * 512],
                s.mB2_sb[:, :])
            n = s.dv.inc(_i)
            s.b_o2.did_wr("dv", n)
            s.b_htf[fc % 2].did_rd("dv", n)
            s.b_mB2.did_rd("dv", n)
            v.wait_ge(s.dv.h, n)
            s.b_rb8.rd(v, s)
            s.b_o2.rd(v, s)
            s.b_rb8.wr(v, s)
            _i = v.tensor_add(s.a_o1[:, 0:512], s.a_o1[:, 0:512],
                              s.a_o1[:, 512:1024])
            n = s.dv.inc(_i)
            s.b_rb8.did_wr("dv", n)
            s.b_o2.did_rd("dv", n)
            v.wait_ge(s.dv.h, n)
            s.b_x1f2[fc % 2].rd(v, s)
            s.b_r.wr(v, s)
            _i = v.tensor_add(
                s.a_r[:, fc * 512:(fc + 1) * 512], s.a_o1[:, 0:512],
                s.x1f_sb[:, (fc % 2) * 512:((fc % 2) + 1) * 512])
            n = s.dv.inc(_i)
            s.b_r.did_wr("dv", n)
            s.b_x1f2[fc % 2].did_rd("dv", n)
            s.b_rb8.did_rd("dv", n)


def eb_lns(s, block, tc, mid1=None, mid2=None):
    def post_cb(ec, off, ob):
        @block.sync
        def _(sy, ec=ec, tc=tc):
            ch = s.sem("cBx1f%d" % (ec % 2))
            s.b_x1f2[ec % 2].wr(sy, s)
            n = ch.inc(sy.dma_start(
                out=s.x1f_sb[:, (ec % 2) * 512:((ec % 2) + 1) * 512],
                in_=s.hbm["x1Tf"][ec * 128:(ec + 1) * 128,
                                  tc * 512:(tc + 1) * 512]), 16)
            s.b_x1f2[ec % 2].did_wr("cBx1f%d" % (ec % 2), n)

        @block.vector
        def _(v, ec=ec, off=off, ob=ob):
            ob.rd(v, s)
            s.b_x1f2[ec % 2].rd(v, s)
            s.b_r.wr(v, s)
            _i = v.tensor_add(s.a_r[:, ec * 512:(ec + 1) * 512],
                              s.a_o1[:, off:off + 512],
                              s.x1f_sb[:, (ec % 2) * 512:((ec % 2) + 1) * 512])
            n = s.dv.inc(_i)
            s.b_r.did_wr("dv", n)
            ob.did_rd("dv", n)
            s.b_x1f2[ec % 2].did_rd("dv", n)

    emit_ln_store(s, block, s.b_r, s.a_r, None, None, qc=tc,
                  gamma=s.gb_g, beta=s.gb_b, store=False, post_cb=post_cb,
                  mid_cb=mid1)

    emit_ln_store(s, block, s.b_r, s.a_r, s.hbm["x2Tf"], s.hbm["x2Tb"],
                  qc=tc, ch_of="cBof", ch_ob="cBob", mid_cb=mid2)



def emit_B_experts(s, block):
    eb_tc0_loads(s, block)
    eb_prep_ok(s, block, 0)
    eb_acc(s, block, 0, 0)
    for tc in range(NQ):
        eb_mBprep(s, block, tc)
        eb_comb(s, block, tc, 0)
        eb_acc(s, block, tc, 1)
        eb_comb(s, block, tc, 1)
        if tc + 1 < NQ:
            eb_prep_ok(s, block, tc + 1)
            mid1 = lambda tc=tc: eb_acc(s, block, tc + 1, 0,
                                        es=range(0, 4), bias=False)
            mid2 = lambda tc=tc: eb_acc(s, block, tc + 1, 0,
                                        es=range(4, 8), bias=True)
        else:
            mid1 = mid2 = None
        eb_lns(s, block, tc, mid1=mid1, mid2=mid2)


def emit_stage_B(s, block):
    emit_B_setup(s, block)
    emit_B_kv(s, block)
    emit_B_loadw(s, block, "l0fqT", "cBw")
    for qc in range(NQ):
        emit_B_attn_qchunk(s, block, qc)
    emit_B_counts(s, block)
    emit_B_pos(s, block)


def emit_C_setup(s, block):
    @block.gpsimd
    def _(g):
        ch = s.sem("cCs")
        for ec in range(ND):
            ch.inc(g.dma_start(out=s.cqb_sb[:, ec:ec + 1],
                               in_=s.hbm["l1qb"][ec * 128:(ec + 1) * 128, :]),
                   16)
            ch.inc(g.dma_start(out=s.ckb_sb[:, ec:ec + 1],
                               in_=s.hbm["l1kb"][ec * 128:(ec + 1) * 128, :]),
                   16)
            ch.inc(g.dma_start(out=s.cvb_sb[:, ec:ec + 1],
                               in_=s.hbm["l1vb"][ec * 128:(ec + 1) * 128, :]),
                   16)
            ch.inc(g.dma_start(out=s.cob_sb[:, ec:ec + 1],
                               in_=s.hbm["l1ob"][ec * 128:(ec + 1) * 128, :]),
                   16)
        ch.inc(g.dma_start(out=s.p32_sb[:, :], in_=s.hbm["p32"]), 16)
        s.b_csetup.did_wr("cCs", ch.n)

    @block.vector
    def _(v):
        s.b_csetup.rd(v, s)
        _i = v.tensor_scalar_mul(s.cqb_sb[:, 0:8], s.cqb_sb[:, 0:8],
                                 1.0 / math.sqrt(HD))
        n = s.dv.inc(_i)
        s.b_csetup.did_rd("dv", n)
        s.b_csetup.w["dv"] = n


def emit_C_rope_tile(s, block, ps, psb, bias_col, scale, qc, out_t, out_buf,
                     out_col):
    @block.vector
    def _(v):
        psb.rd(v, s)
        s.b_csetup.rd(v, s)
        s.b_craw.wr(v, s)
        _i = v.tensor_scalar(s.craw_sb[:, 0:512], ps[:, 0:512], scale,
                             bias_col, ALU.mult, ALU.add)
        n = s.dv.inc(_i)
        s.b_craw.did_wr("dv", n)
        psb.did_rd("dv", n)
        s.b_csetup.did_rd("dv", n)

    @block.vector
    def _(v):
        s.b_craw.rd(v, s)
        s.b_crawb.wr(v, s)
        _i = v.tensor_copy(s.crawb_sb[:, 0:512], s.craw_sb[:, 0:512])
        n = s.dv.inc(_i)
        s.b_crawb.did_wr("dv", n)

    @block.tensor
    def _(t):
        s.b_crawb.rd(t, s)
        s.b_csetup.rd(t, s)
        s.b_pd.wr(t, s)
        _i = t.matmul(s.p_d[:, 0:512], s.p32_sb[:, :], s.crawb_sb[:, 0:512],
                      start=True, stop=True)
        n = s.mm.inc(_i)
        s.b_pd.did_wr("mm", n)
        s.b_crawb.did_rd("mm", n)

    @block.vector
    def _(v):
        s.b_craw.rd(v, s)
        s.b_ccs.rd(v, s)
        s.b_craw.wr(v, s)
        _i = v.tensor_mul(s.craw_sb[:, 0:512], s.craw_sb[:, 0:512],
                          s.cosb_sb[:, 0:512])
        n = s.dv.inc(_i)
        s.b_craw.did_wr("dv", n)
        s.b_pd.rd(v, s)
        s.b_crawb.wr(v, s)
        _i = v.tensor_mul(s.crawb_sb[:, 0:512], s.p_d[:, 0:512],
                          s.sinb_sb[:, 0:512])
        n = s.dv.inc(_i)
        s.b_crawb.did_wr("dv", n)
        s.b_pd.did_rd("dv", n)
        s.b_ccs.did_rd("dv", n)
        v.wait_ge(s.dv.h, n)
        out_buf.wr(v, s)
        _i = v.tensor_add(out_t[:, out_col:out_col + 512],
                          s.craw_sb[:, 0:512], s.crawb_sb[:, 0:512])
        n = s.dv.inc(_i)
        out_buf.did_wr("dv", n)
        s.b_craw.did_rd("dv", n)
        s.b_crawb.did_rd("dv", n)


def emit_stage_C(s, block):
    emit_C_setup(s, block)

    @block.vector
    def _(v):
        view = s.va.ap().rearrange("p (n x) -> p n x", x=65)
        _i = v.memset(view[:, :, 64:65], 1.0)
        n = s.dv.inc(_i)
        s.b_va.did_wr("dv", n)

    def load_cs(qc):
        @block.gpsimd
        def _(g, qc=qc):
            ch = s.sem("cCcs")
            s.b_ccs.wr(g, s)
            ch.inc(g.dma_start(out=s.cosb_sb[:, :],
                               in_=s.hbm["ropecos"][:, qc * 512:
                                                    (qc + 1) * 512]), 16)
            ch.inc(g.dma_start(out=s.sinb_sb[:, :],
                               in_=s.hbm["ropesin"][:, qc * 512:
                                                    (qc + 1) * 512]), 16)
            s.b_ccs.did_wr("cCcs", ch.n)

    emit_B_loadw(s, block, "l1kT", "cCw")
    for qc in range(NQ):
        emit_B_loadx(s, block, qc, hname="x2Tb")
        load_cs(qc)
        for ec in range(NP):
            psb = s.b_ps0 if ec % 2 == 0 else s.b_ps1
            ps = s.p_s0 if ec % 2 == 0 else s.p_s1

            @block.tensor
            def _(t, ec=ec, ps=ps, psb=psb):
                s.b_xcb.rd(t, s)
                s.b_w.rd(t, s)
                psb.wr(t, s)
                for dc in range(ND):
                    _i = t.matmul(ps[:, 0:512],
                                  s.w_sb[:, dc * 1024 + ec * 128:
                                         dc * 1024 + (ec + 1) * 128],
                                  s.xcb[:, dc * 512:(dc + 1) * 512],
                                  start=(dc == 0), stop=(dc == ND - 1))
                n = s.mm.inc(_i)
                psb.did_wr("mm", n)
                s.b_xcb.did_rd("mm", n)
                s.b_w.did_rd("mm", n)

            emit_C_rope_tile(s, block, ps, psb, s.ckb_sb[:, ec:ec + 1], 1.0,
                             qc, s.kt, s.b_kt,
                             ec * T + qc * 512)

    emit_B_loadw(s, block, "l1vT", "cCw")
    for qc in range(NQ):
        emit_B_loadx(s, block, qc, hname="x2Tb")
        for tt in range(4):
            for eh in range(2):
                @block.tensor
                def _(t, tt=tt, eh=eh):
                    s.b_xcb.rd(t, s)
                    s.b_w.rd(t, s)
                    s.b_pc.wr(t, s)
                    for dc in range(ND):
                        _i = t.matmul(
                            s.p_c[:, 0:512],
                            s.xcb[:, dc * 512 + tt * 128:
                                  dc * 512 + (tt + 1) * 128],
                            s.w_sb[:, dc * 1024 + eh * 512:
                                   dc * 1024 + (eh + 1) * 512],
                            start=(dc == 0), stop=(dc == ND - 1))
                    n = s.mm.inc(_i)
                    s.b_pc.did_wr("mm", n)
                    s.b_xcb.did_rd("mm", n)
                    s.b_w.did_rd("mm", n)

                @block.vector
                def _(v, tt=tt, eh=eh, qc=qc):
                    s.b_pc.rd(v, s)
                    s.b_va.wr(v, s)
                    view = s.va.ap().rearrange("p (n x) -> p n x", x=65)
                    src = s.p_c.ap().rearrange("p (h x) -> p h x", x=64)
                    til = qc * 4 + tt
                    _i = v.tensor_copy(
                        view[:, til * 16 + eh * 8:til * 16 + (eh + 1) * 8,
                             0:64], src[:, 0:8, :])
                    n = s.dv.inc(_i)
                    s.b_va.did_wr("dv", n)
                    s.b_pc.did_rd("dv", n)

    def q_pack(qc):
        emit_B_loadw(s, block, "l1qT", "cCw")
        emit_B_loadx(s, block, qc, hname="x2Tb")
        load_cs(qc)
        for ec in range(NP):
            psb = s.b_ps0 if ec % 2 == 0 else s.b_ps1
            ps = s.p_s0 if ec % 2 == 0 else s.p_s1

            @block.tensor
            def _(t, ec=ec, ps=ps, psb=psb):
                s.b_xcb.rd(t, s)
                s.b_w.rd(t, s)
                psb.wr(t, s)
                for dc in range(ND):
                    _i = t.matmul(ps[:, 0:512],
                                  s.w_sb[:, dc * 1024 + ec * 128:
                                         dc * 1024 + (ec + 1) * 128],
                                  s.xcb[:, dc * 512:(dc + 1) * 512],
                                  start=(dc == 0), stop=(dc == ND - 1))
                n = s.mm.inc(_i)
                psb.did_wr("mm", n)
                s.b_xcb.did_rd("mm", n)
                s.b_w.did_rd("mm", n)

            emit_C_rope_tile(s, block, ps, psb, s.cqb_sb[:, ec:ec + 1],
                             1.0 / math.sqrt(HD), qc, s.qt, s.b_qt, ec * 512)

    q_pack(0)
    for qc in range(NQ):
        pend = [None]
        for ec in range(NP):
            emit_attn_pair(
                s, block, list(range(NT)),
                qtA=s.qt[0:64, ec * 512:(ec + 1) * 512],
                qtB=s.qt[64:128, ec * 512:(ec + 1) * 512],
                qt_buf=s.b_qt,
                kA_fn=lambda j, ec=ec: s.kt[0:64, ec * T + j * 128:
                                            ec * T + (j + 1) * 128],
                kB_fn=lambda j, ec=ec: s.kt[64:128, ec * T + j * 128:
                                            ec * T + (j + 1) * 128],
                ktbuf_fn=lambda j: s.b_kt,
                vaA_fn=lambda j, ec=ec: s.va_view(j, 2 * ec),
                vaB_fn=lambda j, ec=ec: s.va_view(j, 2 * ec + 1),
                vabuf_fn=lambda j: s.b_va, parity=ec % 2)
            vw = av_views(s, ec % 2)

            def fin(ec=ec, vw=vw):
                emit_attn_norm_pre(s, block, vw)

                @block.vector
                def _(v, ec=ec, vw=vw):
                    for bb in vw["bufs"]:
                        bb.rd(v, s)
                    s.b_ct1.wr(v, s)
                    s.b_aB.rd(v, s)
                    _i = v.tensor_mul(s.ct1_sb[0:64, 0:512], vw["valA"],
                                      s.a_aB[0:64, 0:512])
                    n = s.dv.inc(_i)
                    s.b_bB.rd(v, s)
                    _i = v.tensor_mul(s.ct1_sb[64:128, 0:512], vw["valB"],
                                      s.a_bB[0:64, 0:512])
                    n = s.dv.inc(_i)
                    for bb in vw["bufs"]:
                        bb.did_rd("dv", n)
                    s.b_aB.did_rd("dv", n)
                    s.b_bB.did_rd("dv", n)
                    s.b_ct1.did_wr("dv", n)
                    v.wait_ge(s.dv.h, n)
                    s.b_csetup.rd(v, s)
                    s.b_xcb.wr(v, s)
                    _i = v.tensor_scalar_add(
                        s.xcb[:, ec * 512:(ec + 1) * 512],
                        s.ct1_sb[:, 0:512], s.cvb_sb[:, ec:ec + 1])
                    n = s.dv.inc(_i)
                    s.b_xcb.did_wr("dv", n)
                    s.b_ct1.did_rd("dv", n)
                    s.b_csetup.did_rd("dv", n)

            if pend[0] is not None:
                pend[0]()
            pend[0] = fin

        pend[0]()

        emit_B_loadw(s, block, "l1oT", "cCw")
        for ec in range(ND):
            psb = s.b_ps0 if ec % 2 == 0 else s.b_ps1
            ps = s.p_s0 if ec % 2 == 0 else s.p_s1

            @block.sync
            def _(sy, ec=ec, qc=qc):
                ch = s.sem("cCxf0")
                s.b_xf[0].wr(sy, s)
                n = ch.inc(sy.dma_start(
                    out=s.a_xf[:, 0:512],
                    in_=s.hbm["x2Tf"][ec * 128:(ec + 1) * 128,
                                      qc * 512:(qc + 1) * 512]), 16)
                s.b_xf[0].did_wr("cCxf0", n)

            @block.tensor
            def _(t, ec=ec, ps=ps, psb=psb):
                s.b_xcb.rd(t, s)
                s.b_w.rd(t, s)
                psb.wr(t, s)
                for dc in range(ND):
                    _i = t.matmul(ps[:, 0:512],
                                  s.w_sb[:, dc * 1024 + ec * 128:
                                         dc * 1024 + (ec + 1) * 128],
                                  s.xcb[:, dc * 512:(dc + 1) * 512],
                                  start=(dc == 0), stop=(dc == ND - 1))
                n = s.mm.inc(_i)
                psb.did_wr("mm", n)
                s.b_xcb.did_rd("mm", n)
                s.b_w.did_rd("mm", n)

            @block.vector
            def _(v, ec=ec, ps=ps, psb=psb):
                psb.rd(v, s)
                s.b_xf[0].rd(v, s)
                s.b_csetup.rd(v, s)
                s.b_r.wr(v, s)
                _i = v.scalar_tensor_tensor(
                    s.a_r[:, ec * 512:(ec + 1) * 512], ps[:, 0:512],
                    s.cob_sb[:, ec:ec + 1],
                    s.a_xf[:, 0:512],
                    ALU.add, ALU.add)
                n = s.dv.inc(_i)
                psb.did_rd("dv", n)
                s.b_xf[0].did_rd("dv", n)
                s.b_csetup.did_rd("dv", n)
                s.b_r.did_wr("dv", n)

        emit_ln_store(s, block, s.b_r, s.a_r, s.hbm["x3Tf"], s.hbm["x3Tb"],
                      qc=qc, ch_of="cCof", ch_ob="cCob",
                      mid_cb=(lambda qc=qc: q_pack(qc + 1))
                      if qc + 1 < NQ else None)


def emit_stage_D(s, block):
    @block.gpsimd
    def _(g):
        ch = s.sem("cDx")
        s.b_x3b.wr(g, s)
        for dc in range(ND):
            for qcq in range(NQ):
                ch.inc(g.dma_start(
                    out=s.x3b_sb[:, dc * T + qcq * 512:dc * T +
                                 (qcq + 1) * 512],
                    in_=s.hbm["x3Tb"][dc * 128:(dc + 1) * 128,
                                      qcq * 512:(qcq + 1) * 512]), 16)
        s.b_x3b.did_wr("cDx", ch.n)

    @block.gpsimd
    def _(g):
        ch = s.sem("cDs")
        ch.inc(g.dma_start(out=s.cent_sb[:, :], in_=s.hbm["l1cent"]), 16)
        ch.inc(g.dma_start(out=s.temp_sb[:, :], in_=s.hbm["l1temp"]), 16)
        ch.inc(g.dma_start(out=s.iota16_sb[:, :], in_=s.hbm["iota16"]), 16)
        ch.inc(g.dma_start(out=s.didf_sb[:, :], in_=s.hbm["identf"]), 16)
        ch.inc(g.dma_start(out=s.didb_sb[:, :], in_=s.hbm["identb"]), 16)
        for dc in range(ND):
            ch.inc(g.dma_start(
                out=s.wr_sb[:, dc * 128:(dc + 1) * 128],
                in_=s.hbm["l1wrT"][dc * 128:(dc + 1) * 128, :]), 16)
        s.b_dsetup.did_wr("cDs", ch.n)

    @block.scalar
    def _(sc):
        s.b_dsetup.rd(sc, s)
        s.b_csq.wr(sc, s)
        _i = sc.activation(s.csq_sb[:, :], s.cent_sb[:, :], AF.Square)
        n = s.act.inc(_i)
        s.b_csq.did_wr("act", n)
        s.b_dsetup.did_rd("act", n)

    @block.vector
    def _(v):
        s.b_csq.rd(v, s)
        s.b_cn.wr(v, s)
        _i = v.reduce_sum(s.cw_sb[0:16, 0:1], s.csq_sb[:, :], axis=AX.X)
        n = s.dv.inc(_i)
        s.b_cn.did_wr("dv", n)
        s.b_csq.did_rd("dv", n)

    @block.scalar
    def _(sc):
        s.b_cn.rd(sc, s)
        s.b_cn.wr(sc, s)
        _i = sc.activation(s.cw_sb[0:16, 0:1], s.cw_sb[0:16, 0:1], AF.Sqrt)
        n = s.act.inc(_i)
        s.b_cn.did_wr("act", n)

    @block.vector
    def _(v):
        s.b_cn.rd(v, s)
        s.b_cn.wr(v, s)
        _i = v.tensor_scalar_max(s.cw_sb[0:16, 0:1], s.cw_sb[0:16, 0:1],
                                 1e-12)
        n = s.dv.inc(_i)
        v.wait_ge(s.dv.h, n)
        _i = v.reciprocal(s.cw_sb[0:16, 1:2], s.cw_sb[0:16, 0:1])
        n = s.dv.inc(_i)
        s.b_cn.did_wr("dv", n)
        v.wait_ge(s.dv.h, n)
        s.b_dsetup.rd(v, s)
        s.b_cnm.wr(v, s)
        _i = v.tensor_scalar_mul(s.cnm_sb[:, :], s.cent_sb[:, :],
                                 s.cw_sb[0:16, 1:2])
        n = s.dv.inc(_i)
        s.b_cnm.did_wr("dv", n)
        s.b_cn.did_rd("dv", n)
        s.b_dsetup.did_rd("dv", n)


    @block.tensor
    def _(t):
        s.b_cnm.rd(t, s)
        s.b_dsetup.rd(t, s)
        s.b_pc.wr(t, s)
        _i = t.transpose(s.p_c[0:128, 0:16], s.cnm_sb[:, :],
                         s.didf_sb[0:16, 0:16])
        n = s.mm.inc(_i)
        s.b_pc.did_wr("mm", n)
        s.b_cnm.did_rd("mm", n)

    @block.vector
    def _(v):
        s.b_pc.rd(v, s)
        s.b_cnT.wr(v, s)
        _i = v.tensor_copy(s.cnT_sb[:, :], s.p_c[0:128, 0:16])
        n = s.dv.inc(_i)
        s.b_cnT.did_wr("dv", n)
        s.b_pc.did_rd("dv", n)

    @block.scalar
    def _(sc):
        s.b_dsetup.rd(sc, s)
        s.b_stemp.wr(sc, s)
        _i = sc.activation(s.stemp_sb[:, :], s.temp_sb[:, :], AF.Exp)
        n = s.act.inc(_i)
        s.b_stemp.did_wr("act", n)

    @block.vector
    def _(v):
        s.b_stemp.rd(v, s)
        s.b_stemp.wr(v, s)
        _i = v.tensor_scalar_mul(s.stemp_sb[:, :], s.stemp_sb[:, :],
                                 1.0 / math.sqrt(GS))
        n = s.dv.inc(_i)
        s.b_stemp.did_wr("dv", n)

    for tc in range(NQ):
        @block.tensor
        def _(t, tc=tc):
            s.b_x3b.rd(t, s)
            s.b_dsetup.rd(t, s)
            s.b_pc.wr(t, s)
            for dc in range(ND):
                _i = t.matmul(s.p_c[:, 0:512],
                              s.wr_sb[:, dc * 128:(dc + 1) * 128],
                              s.x3b_sb[:, dc * T + tc * 512:dc * T +
                                       (tc + 1) * 512],
                              start=(dc == 0), stop=(dc == ND - 1))
            n = s.mm.inc(_i)
            s.b_pc.did_wr("mm", n)
            s.b_x3b.did_rd("mm", n)
            s.b_dsetup.did_rd("mm", n)

        @block.vector
        def _(v, tc=tc):
            s.b_pc.rd(v, s)
            s.b_zf.wr(v, s)
            _i = v.tensor_copy(s.zf_sb[:, 0:512], s.p_c[:, 0:512])
            n = s.dv.inc(_i)
            s.b_zf.did_wr("dv", n)
            s.b_pc.did_rd("dv", n)
            v.wait_ge(s.dv.h, n)
            s.b_zb.wr(v, s)
            _i = v.tensor_copy(s.zb_sb[:, 0:512], s.zf_sb[:, 0:512])
            n = s.dv.inc(_i)
            s.b_zb.did_wr("dv", n)
            v.wait_ge(s.dv.h, n)
            s.b_zsq.wr(v, s)
            _i = v.tensor_mul(s.zsq_sb[:, 0:512], s.zb_sb[:, 0:512],
                              s.zb_sb[:, 0:512])
            n = s.dv.inc(_i)
            s.b_zsq.did_wr("dv", n)

        @block.tensor
        def _(t, tc=tc):
            s.b_zsq.rd(t, s)
            s.b_pd.wr(t, s)
            _i = t.matmul(s.p_d[0:1, 0:512], s.ones128b[:, :],
                          s.zsq_sb[:, 0:512], start=True, stop=True)
            n = s.mm.inc(_i)
            s.b_pd.did_wr("mm", n)
            s.b_zsq.did_rd("mm", n)

        @block.scalar
        def _(sc, tc=tc):
            s.b_pd.rd(sc, s)
            s.b_rowm.wr(sc, s)
            _i = sc.activation(s.row_m[:, 0:512], s.p_d[0:1, 0:512], AF.Sqrt)
            n = s.act.inc(_i)
            s.b_rowm.did_wr("act", n)
            s.b_pd.did_rd("act", n)

        @block.vector
        def _(v, tc=tc):
            s.b_rowm.rd(v, s)
            s.b_rowm.wr(v, s)
            _i = v.tensor_scalar_max(s.row_m[:, 0:512], s.row_m[:, 0:512],
                                     1e-12)
            n = s.dv.inc(_i)
            v.wait_ge(s.dv.h, n)
            s.b_rowa.wr(v, s)
            _i = emit_recip_row(s, v, s.row_a[:, 0:512], s.row_m[:, 0:512])
            n = s.dv.inc(_i)
            s.b_rowa.did_wr("dv", n)
            s.b_rowm.did_rd("dv", n)
            v.wait_ge(s.dv.h, n)
            s.b_stemp.rd(v, s)
            _i = v.tensor_scalar_mul(s.row_a[:, 0:512], s.row_a[:, 0:512],
                                     s.stemp_sb[0:1, 0:1])
            n = s.dv.inc(_i)
            s.b_rowa.did_wr("dv", n)
            s.b_stemp.did_rd("dv", n)

        @block.tensor
        def _(t, tc=tc):
            s.b_rowa.rd(t, s)
            s.b_pd.wr(t, s)
            _i = t.matmul(s.p_d[:, 0:512], s.ones1_128f[:, :],
                          s.row_a[:, 0:512], start=True, stop=True)
            n = s.mm.inc(_i)
            s.b_pd.did_wr("mm", n)
            s.b_rowa.did_rd("mm", n)

        @block.vector
        def _(v, tc=tc):
            s.b_pd.rd(v, s)
            s.b_zf.rd(v, s)
            s.b_znb.wr(v, s)
            _i = v.tensor_mul(s.znb_sb[:, 0:512], s.zf_sb[:, 0:512],
                              s.p_d[:, 0:512])
            n = s.dv.inc(_i)
            s.b_znb.did_wr("dv", n)
            s.b_pd.did_rd("dv", n)
            s.b_zf.did_rd("dv", n)

        for tt in range(4):
            j = tc * 4 + tt

            @block.tensor
            def _(t, tt=tt):
                s.b_znb.rd(t, s)
                s.b_cnT.rd(t, s)
                s.b_pc.wr(t, s)
                _i = t.matmul(s.p_c[:, 0:16],
                              s.znb_sb[:, tt * 128:(tt + 1) * 128],
                              s.cnT_sb[:, :], start=True, stop=True)
                n = s.mm.inc(_i)
                s.b_pc.did_wr("mm", n)
                s.b_znb.did_rd("mm", n)
                s.b_cnT.did_rd("mm", n)

            @block.vector
            def _(v, j=j):
                s.b_pc.rd(v, s)
                s.b_lg16.wr(v, s)
                _i = v.tensor_copy(s.lg16_sb[:, :], s.p_c[:, 0:16])
                n = s.dv.inc(_i)
                s.b_lg16.did_wr("dv", n)
                s.b_pc.did_rd("dv", n)
                v.wait_ge(s.dv.h, n)
                s.b_mx16.wr(v, s)
                _i = v.max(s.mx16_sb[:, :], s.lg16_sb[:, :])
                n = s.dv.inc(_i)
                s.b_mx16.did_wr("dv", n)
                v.wait_ge(s.dv.h, n)
                s.b_ln.wr(v, s)
                _i = v.tensor_scalar_mul(s.ln_sb[:, 0:1], s.mx16_sb[:, 0:1],
                                         -1.0)
                n = s.dv.inc(_i)
                s.b_ln.did_wr("dv", n)

            @block.scalar
            def _(sc, j=j):
                s.b_lg16.rd(sc, s)
                s.b_ln.rd(sc, s)
                s.b_pex16.wr(sc, s)
                _i = sc.activation(s.pex16_sb[:, :], s.lg16_sb[:, :], AF.Exp,
                                   bias=s.ln_sb[:, 0:1])
                n = s.act.inc(_i)
                s.b_pex16.did_wr("act", n)
                s.b_lg16.did_rd("act", n)
                s.b_ln.did_rd("act", n)

            @block.vector
            def _(v, j=j):
                s.b_pex16.rd(v, s)
                s.b_zz.wr(v, s)
                _i = v.reduce_sum(s.zz_sb[:, 0:1], s.pex16_sb[:, :],
                                  axis=AX.X)
                n = s.dv.inc(_i)
                s.b_zz.did_wr("dv", n)
                v.wait_ge(s.dv.h, n)
                _i = v.reciprocal(s.zz_sb[:, 1:2], s.zz_sb[:, 0:1])
                n = s.dv.inc(_i)
                s.b_zz.did_wr("dv", n)
                v.wait_ge(s.dv.h, n)
                s.b_pr16.wr(v, s)
                _i = v.tensor_scalar_mul(s.pr16_sb[:, :], s.pex16_sb[:, :],
                                         s.zz_sb[:, 1:2])
                n = s.dv.inc(_i)
                s.b_pr16.did_wr("dv", n)
                s.b_pex16.did_rd("dv", n)
                s.b_zz.did_rd("dv", n)
                v.wait_ge(s.dv.h, n)
                s.b_mx16.rd(v, s)
                s.b_lg16.rd(v, s)
                s.b_mi16.wr(v, s)
                _i = v.max_index(s.mi16_sb[:, :], s.mx16_sb[:, :],
                                 s.lg16_sb[:, :])
                n = s.dv.inc(_i)
                s.b_mi16.did_wr("dv", n)
                s.b_mx16.did_rd("dv", n)
                s.b_lg16.did_rd("dv", n)
                v.wait_ge(s.dv.h, n)
                s.b_i0.wr(v, s)
                _i = v.tensor_copy(s.i0_sb[:, 0:1], s.mi16_sb[:, 0:1])
                n = s.dv.inc(_i)
                _i = v.tensor_copy(s.i0_sb[:, 1:2], s.mi16_sb[:, 1:2])
                n = s.dv.inc(_i)
                s.b_i0.did_wr("dv", n)
                s.b_mi16.did_rd("dv", n)
                v.wait_ge(s.dv.h, n)
                s.b_dsetup.rd(v, s)
                s.b_eq.wr(v, s)
                _i = v.tensor_single_scalar(s.eq_sb[:, 0:16],
                                            s.iota16_sb[:, :],
                                            s.i0_sb[:, 0:1], ALU.is_equal)
                n = s.dv.inc(_i)
                _i = v.tensor_single_scalar(s.eq_sb[:, 16:32],
                                            s.iota16_sb[:, :],
                                            s.i0_sb[:, 1:2], ALU.is_equal)
                n = s.dv.inc(_i)
                s.b_eq.did_wr("dv", n)
                s.b_i0.did_rd("dv", n)
                v.wait_ge(s.dv.h, n)
                _i = v.tensor_add(s.eq_sb[:, 0:16], s.eq_sb[:, 0:16],
                                  s.eq_sb[:, 16:32])
                n = s.dv.inc(_i)
                s.b_eq.did_wr("dv", n)
                v.wait_ge(s.dv.h, n)
                s.b_pr16.rd(v, s)
                s.b_gt.wr(v, s)
                _i = v.tensor_mul(s.gt_sb[:, 0:16], s.pr16_sb[:, :],
                                  s.eq_sb[:, 0:16])
                n = s.dv.inc(_i)
                s.b_gt.did_wr("dv", n)
                s.b_pr16.did_rd("dv", n)
                s.b_eq.did_rd("dv", n)

            @block.tensor
            def _(t, j=j):
                s.b_gt.rd(t, s)
                s.b_dsetup.rd(t, s)
                s.b_pd.wr(t, s)
                _i = t.transpose(s.p_d[0:16, 0:128], s.gt_sb[:, 0:16],
                                 s.didf_sb[:, :])
                n = s.mm.inc(_i)
                s.b_pd.did_wr("mm", n)
                s.b_gt.did_rd("mm", n)

            @block.vector
            def _(v, j=j):
                s.b_pd.rd(v, s)
                s.b_gT.wr(v, s)
                _i = v.tensor_copy(s.gT_sb[0:16, j * 128:(j + 1) * 128],
                                   s.p_d[0:16, 0:128])
                n = s.dv.inc(_i)
                s.b_gT.did_wr("dv", n)
                s.b_pd.did_rd("dv", n)

    for nh in range(H):
        islot = nh % 2

        @block.gpsimd
        def _(g, nh=nh, islot=islot):
            ch = s.sem("cDie%d" % islot)
            s.b_ie[islot].wr(g, s)
            for dc in range(ND):
                ch.inc(g.dma_start(
                    out=s.ie_sb[islot][:, dc * 512:(dc + 1) * 512],
                    in_=s.hbm["l1IE"][nh, dc * 128:(dc + 1) * 128, :]), 16)
            s.b_ie[islot].did_wr("cDie%d" % islot, ch.n)
            s.b_oe[islot].wr(g, s)
            for hc in range(4):
                ch.inc(g.dma_start(
                    out=s.oe_sb[islot][:, hc * 64:(hc + 1) * 64],
                    in_=s.hbm["l1OE"][nh, hc * 128:(hc + 1) * 128, :]), 16)
            s.b_oe[islot].did_wr("cDie%d" % islot, ch.n)

        for tc in range(NQ):
            @block.gpsimd
            def _(g, nh=nh, tc=tc):
                ch = s.sem("cDgr")
                s.b_gT.rd(g, s)
                s.b_gr.wr(g, s)
                n = ch.inc(g.dma_start(
                    out=s.gr_sb[0:1, 0:512],
                    in_=s.gT_sb[nh:nh + 1, tc * 512:(tc + 1) * 512]), 16)
                s.b_gr.did_wr("cDgr", n)
                s.b_gT.did_rd("cDgr", n)

            for hc in range(4):
                ps = s.p_s0 if hc % 2 == 0 else s.p_s1
                psb = s.b_ps0 if hc % 2 == 0 else s.b_ps1

                @block.tensor
                def _(t, nh=nh, tc=tc, hc=hc, ps=ps, psb=psb, islot=islot):
                    s.b_ie[islot].rd(t, s)
                    s.b_x3b.rd(t, s)
                    psb.wr(t, s)
                    for dc in range(ND):
                        _i = t.matmul(
                            ps[:, 0:512],
                            s.ie_sb[islot][:, dc * 512 + hc * 128:
                                           dc * 512 + (hc + 1) * 128],
                            s.x3b_sb[:, dc * T + tc * 512:dc * T +
                                     (tc + 1) * 512],
                            start=(dc == 0), stop=(dc == ND - 1))
                    n = s.mm.inc(_i)
                    psb.did_wr("mm", n)
                    s.b_ie[islot].did_rd("mm", n)
                    s.b_x3b.did_rd("mm", n)

                @block.scalar
                def _(sc, hc=hc, ps=ps, psb=psb):
                    psb.rd(sc, s)
                    s.b_hb.wr(sc, s)
                    _i = sc.activation(s.hb_sb[:, hc * 512:(hc + 1) * 512],
                                       ps[:, 0:512], AF.Gelu)
                    n = s.act.inc(_i)
                    s.b_hb.did_wr("act", n)
                    psb.did_rd("act", n)

            @block.tensor
            def _(t, nh=nh, tc=tc, islot=islot):
                s.b_hb.rd(t, s)
                s.b_oe[islot].rd(t, s)
                s.b_pg.wr(t, s)
                for hc in range(4):
                    _i = t.matmul(s.p_g[0:64, 0:512],
                                  s.oe_sb[islot][:, hc * 64:(hc + 1) * 64],
                                  s.hb_sb[:, hc * 512:(hc + 1) * 512],
                                  start=(hc == 0), stop=(hc == 3))
                n = s.mm.inc(_i)
                s.b_pg.did_wr("mm", n)
                s.b_hb.did_rd("mm", n)
                s.b_oe[islot].did_rd("mm", n)
                s.b_gr.rd(t, s)
                s.b_pc.wr(t, s)
                _i = t.matmul(s.p_c[0:64, 0:512], s.ones1_64f[:, :],
                              s.gr_sb[0:1, 0:512], start=True, stop=True)
                n = s.mm.inc(_i)
                s.b_pc.did_wr("mm", n)
                s.b_gr.did_rd("mm", n)

            @block.vector
            def _(v, nh=nh, tc=tc):
                s.b_pc.rd(v, s)
                s.b_gB.wr(v, s)
                _i = v.tensor_copy(s.gB_sb[0:64, 0:512], s.p_c[0:64, 0:512])
                n = s.dv.inc(_i)
                s.b_gB.did_wr("dv", n)
                s.b_pc.did_rd("dv", n)
                v.wait_ge(s.dv.h, n)
                s.b_pg.rd(v, s)
                s.b_ct1.wr(v, s)
                _i = v.tensor_mul(s.ct1_sb[0:64, 0:512], s.p_g[0:64, 0:512],
                                  s.gB_sb[0:64, 0:512])
                n = s.dv.inc(_i)
                s.b_ct1.did_wr("dv", n)
                s.b_pg.did_rd("dv", n)
                s.b_gB.did_rd("dv", n)

            @block.sync
            def _(sy, nh=nh, tc=tc):
                ch = s.sem("cDy4")
                s.b_ct1.rd(sy, s)
                n = ch.inc(sy.dma_start(
                    out=s.hbm["scr_y4"][nh * 64:(nh + 1) * 64,
                                        tc * 512:(tc + 1) * 512],
                    in_=s.ct1_sb[0:64, 0:512]), 16)
                s.b_ct1.did_rd("cDy4", n)

    for tc in range(NQ):
        @block.gpsimd
        def _(g, tc=tc):
            ch = s.sem("cDy4r")
            s.b_r.wr(g, s)
            for ec in range(ND):
                ch.inc(g.dma_start(
                    out=s.a_r[:, ec * 512:(ec + 1) * 512],
                    in_=s.hbm["scr_y4"][ec * 128:(ec + 1) * 128,
                                        tc * 512:(tc + 1) * 512]), 16)
            s.b_r.did_wr("cDy4r", ch.n)

        for ec in range(ND):
            @block.sync
            def _(sy, ec=ec, tc=tc):
                ch = s.sem("cDxf%d" % (ec % 2))
                s.b_xf[ec % 2].wr(sy, s)
                n = ch.inc(sy.dma_start(
                    out=s.a_xf[:, (ec % 2) * 512:((ec % 2) + 1) * 512],
                    in_=s.hbm["x3Tf"][ec * 128:(ec + 1) * 128,
                                      tc * 512:(tc + 1) * 512]), 16)
                s.b_xf[ec % 2].did_wr("cDxf%d" % (ec % 2), n)

            @block.vector
            def _(v, ec=ec):
                s.b_xf[ec % 2].rd(v, s)
                s.b_r.rd(v, s)
                s.b_r.wr(v, s)
                _i = v.tensor_add(s.a_r[:, ec * 512:(ec + 1) * 512],
                                  s.a_r[:, ec * 512:(ec + 1) * 512],
                                  s.a_xf[:, (ec % 2) * 512:
                                         ((ec % 2) + 1) * 512])
                n = s.dv.inc(_i)
                s.b_r.did_wr("dv", n)
                s.b_xf[ec % 2].did_rd("dv", n)

        emit_ln_store(s, block, s.b_r, s.a_r, s.hbm["xoutT"], None,
                      qc=tc, ch_of="cDof")


def build_program(stages):
    from contextlib import ExitStack
    cst = _consts()
    nc = bass.Bass("TRN2", target_bir_lowering=False)
    s = S()
    s.nc = nc
    s.cst = cst
    s.hbm = {}
    s.innames = set()

    def din(name, shape, dt=F32):
        s.hbm[name] = nc.dram_tensor(name, list(shape), dt,
                                     kind="ExternalInput").ap()
        s.innames.add(name)

    def dout(name, shape, dt=F32):
        s.hbm[name] = nc.dram_tensor(name, list(shape), dt,
                                     kind="ExternalOutput").ap()

    def dscr(name, shape, dt, produced, consumed):
        if produced in stages and consumed in stages:
            s.hbm[name] = nc.dram_tensor("scr_" + name, list(shape), dt).ap()
        elif produced in stages:
            dout(name, shape, dt)
        elif consumed in stages:
            din(name, shape, dt)

    if "A" in stages:
        din("x0T", (C, T))
    dscr("x1Tf", (C, T), F32, "A", "B")
    dscr("x1Tb", (C, T), BF16, "A", "B")
    dscr("hTf", (C, T), F32, "B", "B")
    dscr("hTb", (C, T), BF16, "B", "B")
    dscr("x2Tf", (C, T), F32, "B", "C")
    dscr("x2Tb", (C, T), BF16, "B", "C")
    dscr("x3Tf", (C, T), F32, "C", "D")
    dscr("x3Tb", (C, T), BF16, "C", "D")
    if "D" in stages:
        dout("xoutT", (C, T))
    if "D" in stages:
        s.hbm["scr_y4"] = nc.dram_tensor("scr_y4", [C, T], F32).ap()
    if "B" in stages:
        s.hbm["cc_in"] = nc.dram_tensor("cc_in", [8, 8], F32).ap()
        s.hbm["cc_out"] = nc.dram_tensor("cc_out", [8, 8], F32).ap()
        s.hbm["scr_tot"] = nc.dram_tensor("scr_tot", [1, 256], F32).ap()
        s.hbm["scr_row"] = nc.dram_tensor("scr_row", [1, 256], F32).ap()

    for name, arr in cst.items():
        dt = BF16 if arr.dtype == _BF else F32
        din(name, arr.shape, dt)
    din("rankcol", (1, 8))
    din("rankmask", (8, 1))

    if "A" in stages:
        for n in ["l0aqT", "l0akT", "l0avT", "l0aoT"]:
            din(n, (C, C))
    if "B" in stages:
        for n in ["l0fqT", "l0fkT", "l0fvT"]:
            din(n, (C, C))
        din("l0fbq", (C, 1)); din("l0fbk", (C, 1)); din("l0fbv", (C, 1))
        din("l0fWeT", (E, C, C), BF16); din("l0fbe", (E, C))
        din("l0fswT", (C, E)); din("l0fbsw", (1, E))
        din("l0fg", (C, 1)); din("l0fb", (C, 1))
    if "C" in stages:
        for n in ["l1qT", "l1kT", "l1vT", "l1oT"]:
            din(n, (C, C))
        for n in ["l1qb", "l1kb", "l1ob"]:
            din(n, (C, 1))
        din("l1vb", (C, 1))
    if "D" in stages:
        din("l1wrT", (C, GS)); din("l1cent", (H, GS)); din("l1temp", (1, 1))
        din("l1IE", (H, C, HS)); din("l1OE", (H, HS, HD))

    with ExitStack() as ctx:
        e = ctx.enter_context
        s.p_s0 = e(nc.psum_tensor("p_s0", [128, 1024], F32))
        s.p_s1 = e(nc.psum_tensor("p_s1", [128, 1024], F32))
        s.p_g = e(nc.psum_tensor("p_g", [128, 1024], F32))
        s.p_c = e(nc.psum_tensor("p_c", [128, 512], F32))
        s.p_d = e(nc.psum_tensor("p_d", [128, 512], F32))

        s.sems = {}
        s._ctx = ctx
        s.mm = s.sem("mm")
        s.act = s.sem("act")
        s.dv = s.sem("dv")
        s.gp = s.sem("gp")
        s.cc = s.sem("cc")
        s.bar = s.sem("bar")
        s.barn = 0

        s.ones1_64f = e(nc.sbuf_tensor("ones1_64f", [1, 64], F32))
        s.ones1_128f = e(nc.sbuf_tensor("ones1_128f", [1, 128], F32))
        s.ones1_128b = e(nc.sbuf_tensor("ones1_128b", [1, 128], BF16))
        s.ones128b = e(nc.sbuf_tensor("ones128b", [128, 1], BF16))
        s.ones128f = e(nc.sbuf_tensor("ones128f", [128, 1], F32))
        s.row_a = e(nc.sbuf_tensor("row_a", [1, 512], F32))
        s.row_b = e(nc.sbuf_tensor("row_b", [1, 512], F32))
        s.row_m = e(nc.sbuf_tensor("row_m", [1, 512], F32))
        s.row_q = e(nc.sbuf_tensor("row_q", [1, 512], F32))
        s.epsT = e(nc.sbuf_tensor("epsT", [1, 1], F32))
        s.row_s = s.row_q
        s.a_rsq2 = e(nc.sbuf_tensor("a_rsq2", [128, 1024], BF16))
        s.a_aB = e(nc.sbuf_tensor("a_aB", [128, 512], F32))
        s.a_bB = e(nc.sbuf_tensor("a_bB", [128, 512], F32))
        s.a_o1 = e(nc.sbuf_tensor("a_o1", [128, 1024], F32))

        for nm in ("b_ps0 b_ps1 b_pc b_pd b_pg b_rowa b_rowb "
                   "b_rowm b_rowq b_rowv b_rowsd b_aB b_bB b_o1 "
                   "b_rdb b_ones").split():
            setattr(s, nm, Buf())
        s.b_rsq2 = [Buf(), Buf()]
        s.b_o12 = [Buf(), Buf()]

        block = FakeBlock()
        if True:
            @block.vector
            def _(v):
                v.memset(s.ones1_64f[:, :], 1.0)
                v.memset(s.ones1_128f[:, :], 1.0)
                v.memset(s.ones1_128b[:, :], 1.0)
                v.memset(s.ones128b[:, :], 1.0)
                v.memset(s.ones128f[:, :], 1.0)
                _i = v.memset(s.epsT[:, :], EPS)
                n = s.dv.inc(_i)
                s.b_ones.did_wr("dv", n)

            @block.tensor
            def _(t):
                t.wait_ge(s.dv.h, s.dv.n)

            @block.scalar
            def _(sc):
                sc.wait_ge(s.dv.h, s.dv.n)

            if "A" in stages:
                with ExitStack() as actx:
                    ea = actx.enter_context
                    s.a_w = [ea(nc.sbuf_tensor(f"a_w{i}", [128, ND * 1024],
                                               BF16)) for i in range(4)]
                    s.a_xbf = ea(nc.sbuf_tensor("a_xbf", [128, ND * 512],
                                                BF16))
                    s.a_qt = ea(nc.sbuf_tensor("a_qt", [128, NP * 512], BF16))
                    s.a_ktw = ea(nc.sbuf_tensor("a_ktw",
                                                [128, 3 * NP * 512], BF16))
                    s.a_vaw = ea(nc.sbuf_tensor("a_vaw",
                                                [128, 3 * 4 * 1040], BF16))
                    s.pb_sb = ea(nc.sbuf_tensor("a_pb", [128, 2 * 1024], BF16))
                    s.a_ot = ea(nc.sbuf_tensor("a_ot", [128, ND * 512], BF16))
                    s.a_xf = ea(nc.sbuf_tensor("a_xf", [128, 2 * 512], F32))
                    s.a_r = ea(nc.sbuf_tensor("a_r", [128, ND * 512], F32))
                    s.a_maskA = ea(nc.sbuf_tensor("a_maskA", [128, 8 * 512],
                                                  BF16))
                    for nm in ("b_aw b_r b_maskA").split():
                        setattr(s, nm, Buf())
                    _bx, _bq, _bo = Buf(), Buf(), Buf()
                    s.b_xbf2 = [_bx, _bx]
                    s.b_qt2 = [_bq, _bq]
                    s.b_ot2 = [_bo, _bo]
                    s.b_ktw = [Buf(), Buf(), Buf()]
                    s.b_vaw = [Buf(), Buf(), Buf()]
                    s.b_pbuf = [Buf(), Buf()]
                    s.b_xf = [Buf(), Buf()]
                    s.b_pb_half = [[0, 0], [0, 0]]

                    emit_stage_A(s, block)
                _barrier(s, block)

            if "B" in stages:
                s.fence = {k: sem.n for k, sem in s.sems.items()}

                def fbuf():
                    b = Buf()
                    b.w = dict(s.fence)
                    return b

                from contextlib import ExitStack as _ES
                with _ES() as bctx:
                    eb = bctx.enter_context
                    s.bq_sb = eb(nc.sbuf_tensor("bbq", [128, 8], F32))
                    s.bk_sb = eb(nc.sbuf_tensor("bbk", [128, 8], F32))
                    s.bv_sb = eb(nc.sbuf_tensor("bbv", [128, 8], F32))
                    s.gb_g = eb(nc.sbuf_tensor("bgg", [128, 8], F32))
                    s.gb_b = eb(nc.sbuf_tensor("bgb", [128, 8], F32))
                    s.sw_sb = eb(nc.sbuf_tensor("bsw", [128, 64], BF16))
                    s.bsw_sb = eb(nc.sbuf_tensor("bbsw", [1, 8], F32))
                    s.bswB_sb = eb(nc.sbuf_tensor("bbswB", [128, 8], F32))
                    s.ltri_sb = eb(nc.sbuf_tensor("bltri", [128, 128], BF16))
                    s.ltri32_sb = eb(nc.sbuf_tensor("bltri32", [32, 32], F32))
                    s.identf_sb = eb(nc.sbuf_tensor("bidf", [128, 128], F32))
                    s.rp_sb = eb(nc.sbuf_tensor("brp", [128, 32], F32))
                    s.kept_sb = eb(nc.sbuf_tensor("bkept", [128, 32], F32))
                    s.ohkT_sb = eb(nc.sbuf_tensor("bohkT", [8, T], BF16))
                    s.m1c_sb = eb(nc.sbuf_tensor("bm1c", [128, 1], F32))
                    s.m2c_sb = eb(nc.sbuf_tensor("bm2c", [128, 1], F32))
                    for nm in ("b_setup b_w b_bswB b_kt b_va b_xcb b_qt "
                               "b_sfx b_oh b_ohb b_rp b_kept b_ohkT b_ohkb "
                               "b_tot b_t32 b_tall b_slot b_gat b_coff "
                               "b_roff b_rrow b_lg b_mx b_pex b_mi b_ln "
                               "b_idxf b_zz b_pos b_rb8 b_pf8 b_m1c b_m2c "
                               "b_m1r b_m2r b_mB1 b_mB2 b_okr b_htb "
                               "b_o2 b_r").split():
                        setattr(s, nm, fbuf())
                    s.b_pbuf = [fbuf(), fbuf()]
                    s.b_htf = [fbuf(), fbuf()]
                    s.b_ohkb2 = [fbuf(), fbuf()]
                    s.rb8_sb = ColView(s.a_o1, 0)
                    s.b_rb8 = s.b_o12[0]
                    s.pf8_sb = ColView(s.a_o1, 512)
                    s.b_pf8 = s.b_o12[1]

                    @block.vector
                    def _(v):
                        _wait(v, s, s.fence)
                        _i = v.memset(s.row_m[0:1, 0:1], 0.0)
                        n = s.dv.inc(_i)

                    emit_B_setup(s, block)

                    with _ES() as b2ctx:
                        e2 = b2ctx.enter_context
                        s.kt = e2(nc.sbuf_tensor("bkt", [128, NP * T], BF16))
                        s.w_sb = e2(nc.sbuf_tensor("bw", [128, ND * 1024],
                                                   BF16))
                        s.maskB_sb = e2(nc.sbuf_tensor("bmaskB", [128, 2048],
                                                       BF16))
                        s.step_sb = e2(nc.sbuf_tensor("bstep", [128, 1024],
                                                      BF16))
                        s.rankc_sb = e2(nc.sbuf_tensor("brkc", [1, 8], F32))
                        s.rankm_sb = e2(nc.sbuf_tensor("brkm", [8, 1], F32))
                        s.iota8_sb = e2(nc.sbuf_tensor("bio8", [128, 8], F32))
                        s.oh_sb = e2(nc.sbuf_tensor("boh", [128, 256], F32))
                        s.ohb_sb = e2(nc.sbuf_tensor("bohb", [128, 256], BF16))
                        s.ohkb_sb = e2(nc.sbuf_tensor("bohkb", [128, 16], F32))
                        s.tot_sb = e2(nc.sbuf_tensor("btot", [1, 256], F32))
                        s.t32_sb = e2(nc.sbuf_tensor("bt32", [32, 8], F32))
                        s.tall_sb = e2(nc.sbuf_tensor("btall", [1, 8], F32))
                        s.slot_sb = e2(nc.sbuf_tensor("bslot", [8, 8], F32))
                        s.gat_sb = e2(nc.sbuf_tensor("bgat", [8, 8], F32))
                        s.coff_sb = e2(nc.sbuf_tensor("bcoff", [1, 8], F32))
                        s.roff_sb = e2(nc.sbuf_tensor("broff", [32, 8], F32))
                        s.rrow_sb = e2(nc.sbuf_tensor("brrow", [1, 256], F32))
                        s.lg_sb = e2(nc.sbuf_tensor("blg", [128, 8], F32))
                        s.mx_sb = e2(nc.sbuf_tensor("bmx", [128, 8], F32))
                        s.pex_sb = e2(nc.sbuf_tensor("bpex", [128, 8], F32))
                        s.mi_sb = e2(nc.sbuf_tensor("bmi", [128, 8], U32))
                        s.ln_sb = e2(nc.sbuf_tensor("bln", [128, 1], F32))
                        s.idxf_sb = e2(nc.sbuf_tensor("bidxf", [128, 1], F32))
                        s.zz_sb = e2(nc.sbuf_tensor("bzz", [128, 1], F32))
                        s.pos_sb = e2(nc.sbuf_tensor("bpos", [128, 32], F32))
                        s.ones1_32f = e2(nc.sbuf_tensor("bones1_32", [1, 32], F32))
                        s.ones32f = e2(nc.sbuf_tensor("bones32", [32, 1], F32))

                        s.va = e2(nc.sbuf_tensor("bva", [128, NT * 16 * 65],
                                                 BF16))
                        s.qt = e2(nc.sbuf_tensor("bqt", [128, NP * 512], BF16))
                        s.xcb = e2(nc.sbuf_tensor("bxcb", [128, ND * 512],
                                                  BF16))
                        s.pb_sb = e2(nc.sbuf_tensor("bpb", [128, 2048], BF16))
                        s.sfx = e2(nc.sbuf_tensor("bsfx", [128, 512], F32))
                        @block.vector
                        def _(v):
                            v.memset(s.ones1_32f[:, :], 1.0)
                            _i = v.memset(s.ones32f[:, :], 1.0)
                            n = s.dv.inc(_i)
                            s.b_setup.did_wr("dv", n)

                        emit_B_mst(s, block)
                        emit_B_kv(s, block)
                        for qc in range(NQ):
                            emit_B_attn_qchunk(s, block, qc)
                        emit_B_counts(s, block)
                        emit_B_pos(s, block)

                    s.fence = {k: sem.n for k, sem in s.sems.items()}
                    with _ES() as b6ctx:
                        e6 = b6ctx.enter_context
                        s.we_sb = e6(nc.sbuf_tensor("bwe",
                                                    [128, E * ND * 1024],
                                                    BF16))
                        s.hm_sb2 = [e6(nc.sbuf_tensor("bhm%d" % i,
                                                      [128, 4 * 512], BF16))
                                    for i in range(2)]
                        s.okA_sb = e6(nc.sbuf_tensor("bokA", [128, E * 512],
                                                     BF16))
                        s.x1f_sb = e6(nc.sbuf_tensor("bx1f", [128, 2 * 512],
                                                     F32))
                        s.htb_sb = e6(nc.sbuf_tensor("bhtb", [128, ND * 512],
                                                     BF16))
                        s.okr_sb = e6(nc.sbuf_tensor("bokr", [1, 512], BF16))
                        s.be_sb = e6(nc.sbuf_tensor("bbe", [8, 1024], BF16))

                        s.a_r = e6(nc.sbuf_tensor("br", [128, ND * 512], F32))
                        s.a_xf = e6(nc.sbuf_tensor("bxf", [128, 2 * 512],
                                                   F32))
                        for nm in ("b_okA b_htb b_okr b_r").split():
                            setattr(s, nm, fbuf())
                        s.m1r_sb = s.row_a
                        s.b_m1r = s.b_rowa
                        s.m2r_sb = s.row_b
                        s.b_m2r = s.b_rowb
                        s.b_hm2 = [fbuf(), fbuf()]
                        s.b_o2 = s.b_o12[1]
                        s.b_rb8 = s.b_o12[0]
                        s.b_wes = [fbuf() for _ in range(E)]
                        s.b_x1f2 = [fbuf(), fbuf()]
                        s.b_htf = [fbuf(), fbuf()]
                        s.mB1_sb = s.a_aB
                        s.b_mB1 = s.b_aB
                        s.mB2_sb = s.a_bB
                        s.b_mB2 = s.b_bB
                        emit_B_experts(s, block)
                _barrier(s, block)

            if "C" in stages:
                s.fence = {k: sem.n for k, sem in s.sems.items()}

                def fbuf():
                    b = Buf()
                    b.w = dict(s.fence)
                    return b

                from contextlib import ExitStack as _ES
                with _ES() as cctx:
                    ec_ = cctx.enter_context
                    s.kt = ec_(nc.sbuf_tensor("ckt", [128, NP * T], BF16))
                    s.va = ec_(nc.sbuf_tensor("cva", [128, NT * 16 * 65],
                                              BF16))
                    s.qt = ec_(nc.sbuf_tensor("cqt", [128, NP * 512], BF16))
                    s.xcb = ec_(nc.sbuf_tensor("cxcb", [128, ND * 512], BF16))
                    s.pb_sb = ec_(nc.sbuf_tensor("cpb", [128, 2048], BF16))
                    s.w_sb = ec_(nc.sbuf_tensor("cw", [128, ND * 1024], BF16))
                    s.cosb_sb = ec_(nc.sbuf_tensor("ccos", [128, 512], BF16))
                    s.sinb_sb = ec_(nc.sbuf_tensor("csin", [128, 512], BF16))
                    s.crawb_sb = ec_(nc.sbuf_tensor("ccrawb", [128, 512],
                                                    BF16))
                    s.cqb_sb = ec_(nc.sbuf_tensor("ccqb", [128, 8], F32))
                    s.ckb_sb = ec_(nc.sbuf_tensor("cckb", [128, 8], F32))
                    s.cvb_sb = ec_(nc.sbuf_tensor("ccvb", [128, 8], F32))
                    s.cob_sb = ec_(nc.sbuf_tensor("ccob", [128, 8], F32))
                    s.p32_sb = ec_(nc.sbuf_tensor("cp32", [128, 128], BF16))
                    s.a_r = ec_(nc.sbuf_tensor("cr", [128, ND * 512], F32))
                    s.a_xf = ec_(nc.sbuf_tensor("cxf", [128, 512], F32))
                    s.c_rope = ec_(nc.sbuf_tensor("crope", [128, 512],
                                                  F32))
                    s.b_rope = fbuf()
                    s.ct1_sb = ColView(s.c_rope, 0)
                    s.craw_sb = ColView(s.c_rope, 0)
                    for nm in ("b_csetup b_ccs b_crawb "
                               "b_kt b_va b_qt b_xcb b_w b_r").split():
                        setattr(s, nm, fbuf())
                    s.b_ct1 = s.b_rope
                    s.b_craw = s.b_rope
                    s.b_pbuf = [fbuf(), fbuf()]
                    s.b_xf = [fbuf(), fbuf()]
                    emit_stage_C(s, block)
                _barrier(s, block)

            if "D" in stages:
                s.fence = {k: sem.n for k, sem in s.sems.items()}

                def fbuf():
                    b = Buf()
                    b.w = dict(s.fence)
                    return b

                from contextlib import ExitStack as _ES
                with _ES() as dctx:
                    ed = dctx.enter_context
                    s.x3b_sb = ed(nc.sbuf_tensor("dx3b", [128, ND * T], BF16))
                    s.wr_sb = ed(nc.sbuf_tensor("dwr", [128, ND * 128], BF16))
                    s.cent_sb = ed(nc.sbuf_tensor("dcent", [16, 128], F32))
                    s.csq_sb = ed(nc.sbuf_tensor("dcsq", [16, 128], F32))
                    s.cw_sb = ed(nc.sbuf_tensor("dcw", [16, 2], F32))
                    s.cnm_sb = ed(nc.sbuf_tensor("dcnm", [16, 128], F32))
                    s.cnT_sb = ed(nc.sbuf_tensor("dcnT", [128, 16], BF16))
                    s.temp_sb = ed(nc.sbuf_tensor("dtemp", [1, 1], F32))
                    s.stemp_sb = ed(nc.sbuf_tensor("dstemp", [1, 1], F32))
                    s.iota16_sb = ed(nc.sbuf_tensor("dio16", [128, 16], F32))
                    s.didf_sb = ed(nc.sbuf_tensor("didf", [128, 128], F32))
                    s.didb_sb = ed(nc.sbuf_tensor("didb", [128, 128], BF16))
                    s.zf_sb = ed(nc.sbuf_tensor("dzf", [128, 512], F32))
                    s.zb_sb = ed(nc.sbuf_tensor("dzb", [128, 512], BF16))
                    s.zsq_sb = ed(nc.sbuf_tensor("dzsq", [128, 512], BF16))
                    s.znb_sb = ed(nc.sbuf_tensor("dznb", [128, 512], BF16))
                    s.lg16_sb = ed(nc.sbuf_tensor("dlg16", [128, 16], F32))
                    s.mx16_sb = ed(nc.sbuf_tensor("dmx16", [128, 8], F32))
                    s.mi16_sb = ed(nc.sbuf_tensor("dmi16", [128, 8], U32))
                    s.pex16_sb = ed(nc.sbuf_tensor("dpex16", [128, 16], F32))
                    s.pr16_sb = ed(nc.sbuf_tensor("dpr16", [128, 16], F32))
                    s.i0_sb = ed(nc.sbuf_tensor("di0", [128, 2], F32))
                    s.eq_sb = ed(nc.sbuf_tensor("deq", [128, 32], F32))
                    s.gt_sb = ed(nc.sbuf_tensor("dgt", [128, 16], F32))
                    s.gT_sb = ed(nc.sbuf_tensor("dgT", [16, T], F32))
                    s.gr_sb = ed(nc.sbuf_tensor("dgr", [1, 512], F32))
                    s.gB_sb = ed(nc.sbuf_tensor("dgB", [128, 512], F32))
                    s.ie_sb = [ed(nc.sbuf_tensor("die%d" % i,
                                                 [128, ND * 512], BF16))
                               for i in range(2)]
                    s.oe_sb = [ed(nc.sbuf_tensor("doe%d" % i,
                                                 [128, 4 * 64], BF16))
                               for i in range(2)]
                    s.hb_sb = ed(nc.sbuf_tensor("dhb", [128, 4 * 512], BF16))
                    s.ct1_sb = ed(nc.sbuf_tensor("dct1", [128, 512], F32))
                    s.ln_sb = ed(nc.sbuf_tensor("dln", [128, 1], F32))
                    s.zz_sb = ed(nc.sbuf_tensor("dzz", [128, 2], F32))
                    s.a_r = ed(nc.sbuf_tensor("dr", [128, ND * 512], F32))
                    s.a_xf = ed(nc.sbuf_tensor("dxf", [128, 2 * 512], F32))
                    for nm in ("b_x3b b_dsetup b_csq b_cn b_cnm b_cnT "
                               "b_stemp b_zf b_zb b_zsq b_znb b_lg16 b_mx16 "
                               "b_mi16 b_pex16 b_pr16 b_i0 b_eq b_gt b_gT "
                               "b_gr b_gB b_hb b_ct1 b_r b_ln "
                               "b_zz").split():
                        setattr(s, nm, fbuf())
                    s.b_ie = [fbuf(), fbuf()]
                    s.b_oe = [fbuf(), fbuf()]
                    s.b_xf = [fbuf(), fbuf()]
                    emit_stage_D(s, block)
                _barrier(s, block)

        with nc.Block() as rblock:
            block.flush(rblock)

    return nc, s


def _core_inputs(inputs, w, core, stages):
    cst = _consts()
    d = {}
    for name, arr in cst.items():
        d[name] = np.asarray(arr)
    d["rankcol"] = (np.arange(8) == core).astype(np.float32).reshape(1, 8)
    d["rankmask"] = (np.arange(8) < core).astype(np.float32).reshape(8, 1)
    if "A" in stages:
        x = np.asarray(inputs["L0_attn_x"][core])
        d["x0T"] = np.ascontiguousarray(x.T)
        for n in ["l0aqT", "l0akT", "l0avT", "l0aoT"]:
            d[n] = w[n]
    if "B" in stages:
        for n in ["l0fqT", "l0fkT", "l0fvT", "l0fbq", "l0fbk", "l0fbv",
                  "l0fWeT", "l0fbe", "l0fswT", "l0fbsw", "l0fg", "l0fb"]:
            d[n] = w[n]
    if "C" in stages:
        for n in ["l1qT", "l1kT", "l1vT", "l1oT", "l1qb", "l1kb", "l1vb",
                  "l1ob"]:
            d[n] = w[n]
    if "D" in stages:
        for n in ["l1wrT", "l1cent", "l1temp", "l1IE", "l1OE"]:
            d[n] = w[n]
    return d


def kernel(**inputs):
    w = _prep_weights(inputs)
    stage_sets = [STAGES] if FUSED else list(STAGES)

    carry = [dict() for _ in range(8)]
    out = np.zeros((8, T, C), np.float32)
    for stages in stage_sets:
        nc, s = build_program(stages)
        in_maps = []
        for core in range(8):
            m = _core_inputs(inputs, w, core, stages)
            m.update(carry[core])
            m = {k: v for k, v in m.items() if k in s.innames}
            missing = s.innames - set(m)
            assert not missing, f"missing inputs {missing}"
            in_maps.append(m)
        res = run_bass_kernel_spmd(nc, in_maps, core_ids=list(range(8)))
        for core in range(8):
            carry[core].update(res.results[core])
        del nc, s
    for core in range(8):
        if "xoutT" in carry[core]:
            out[core] = np.asarray(carry[core]["xoutT"], np.float32).T
    return out


# version 20
# speedup vs baseline: 1.1599x; 1.1599x over previous
import math
import numpy as np
import ml_dtypes

import concourse.bass as bass
import concourse.mybir as mybir
from concourse import dve_ops
from concourse.bass_utils import run_bass_kernel_spmd

AF = mybir.ActivationFunctionType
ALU = mybir.AluOpType
AX = mybir.AxisListType
F32 = mybir.dt.float32
BF16 = mybir.dt.bfloat16
U32 = mybir.dt.uint32

T = 4096
C = 1024
H = 16
HD = 64
NP = 8
ND = 8
NT = 32
NQ = 8
E = 8
CAP = int(1.2 * (8 * T) / E)
GS = 128
HS = 512
EPS = 1e-5

STAGES = "ABCD"
FUSED = True

_BF = ml_dtypes.bfloat16


_const_cache = {}


def _consts():
    if _const_cache:
        return _const_cache
    c = {}
    inv = 1.0 / 10000.0 ** (np.arange(0, HD, 2, dtype=np.float64) / HD)
    t = np.arange(T, dtype=np.float64)
    fr = inv[:, None] * t[None, :]
    cos64 = np.cos(np.concatenate([fr, fr], 0))
    sin64 = np.sin(np.concatenate([fr, fr], 0))
    sgn = np.where(np.arange(HD) < HD // 2, -1.0, 1.0)[:, None]
    c["ropecos"] = np.concatenate([cos64, cos64], 0).astype(_BF)
    c["ropesin"] = np.concatenate([sin64 * sgn, sin64 * sgn], 0).astype(_BF)
    p32 = np.zeros((128, 128), np.float32)
    for i in range(128):
        half = (i // HD) * HD
        src = half + ((i % HD) + HD // 2) % HD
        p32[src, i] = 1.0
    c["p32"] = p32.astype(_BF)
    c["identb"] = np.eye(128, dtype=np.float32).astype(_BF)
    c["identf"] = np.eye(128, dtype=np.float32)
    c["ltri128"] = np.tril(np.ones((128, 128), np.float32), -1).astype(_BF)
    c["ltri32"] = np.ascontiguousarray(
        np.tril(np.ones((32, 32), np.float32), -1))
    kr = np.arange(128)[:, None]
    q = np.arange(512)[None, :]
    mA = np.zeros((8, 128, 512), np.float32)
    for j in range(4):
        mA[j] = ((j * 128 + kr) > q)
    for j in range(4, 8):
        mA[j] = (((j - 4) * 128 + kr) <= q)
    c["maskA"] = mA.astype(_BF)
    mB = np.zeros((4, 128, 512), np.float32)
    for jd in range(4):
        mB[jd] = ((jd * 128 + kr) <= q).astype(np.float32)
    c["maskB"] = mB
    st = np.zeros((32, 128, 32), np.float32)
    for jp in range(32):
        st[jp, :, : jp + 1] = 1.0
    c["step32"] = st.astype(_BF)
    c["iota8"] = np.broadcast_to(
        np.arange(8, dtype=np.float32), (128, 8)).copy()
    c["iota16"] = np.broadcast_to(
        np.arange(16, dtype=np.float32), (128, 16)).copy()
    _const_cache.update(c)
    return c


def _prep_weights(inputs):
    w = {}
    tp = lambda a: np.ascontiguousarray(np.asarray(a).T)
    w["l0aqT"] = tp(inputs["L0_attn_Wq"])
    w["l0akT"] = tp(inputs["L0_attn_Wk"])
    w["l0avT"] = tp(inputs["L0_attn_Wv"])
    w["l0aoT"] = tp(inputs["L0_attn_Wo"])
    w["l0fqT"] = tp(np.asarray(inputs["L0_ffn_Wq"]).reshape(C, C))
    w["l0fkT"] = tp(np.asarray(inputs["L0_ffn_Wk"]).reshape(C, C))
    w["l0fvT"] = tp(np.asarray(inputs["L0_ffn_Wv"]).reshape(C, C))
    w["l0fbq"] = np.ascontiguousarray(
        np.asarray(inputs["L0_ffn_bq"]).reshape(C, 1))
    w["l0fbk"] = np.ascontiguousarray(
        np.asarray(inputs["L0_ffn_bk"]).reshape(C, 1))
    w["l0fbv"] = np.ascontiguousarray(
        np.asarray(inputs["L0_ffn_bv"]).reshape(C, 1))
    w["l0fWeT"] = np.ascontiguousarray(
        np.transpose(np.asarray(inputs["L0_ffn_We"]), (0, 2, 1))).astype(_BF)
    w["l0fbe"] = np.ascontiguousarray(np.asarray(inputs["L0_ffn_be"]))
    w["l0fswT"] = tp(np.asarray(inputs["L0_ffn_Wsw"]))
    w["l0fbsw"] = np.ascontiguousarray(
        np.asarray(inputs["L0_ffn_bsw"]).reshape(1, E))
    w["l0fg"] = np.ascontiguousarray(
        np.asarray(inputs["L0_ffn_gamma"]).reshape(C, 1))
    w["l0fb"] = np.ascontiguousarray(
        np.asarray(inputs["L0_ffn_beta"]).reshape(C, 1))
    w["l1qT"] = tp(inputs["L1_attn_q_w"])
    w["l1kT"] = tp(inputs["L1_attn_k_w"])
    w["l1vT"] = tp(inputs["L1_attn_v_w"])
    w["l1oT"] = tp(inputs["L1_attn_o_w"])
    w["l1qb"] = np.ascontiguousarray(
        np.asarray(inputs["L1_attn_q_b"]).reshape(C, 1))
    w["l1kb"] = np.ascontiguousarray(
        np.asarray(inputs["L1_attn_k_b"]).reshape(C, 1))
    w["l1vb"] = np.ascontiguousarray(
        np.asarray(inputs["L1_attn_v_b"]).reshape(C, 1))
    w["l1ob"] = np.ascontiguousarray(
        np.asarray(inputs["L1_attn_o_b"]).reshape(C, 1))
    w["l1wrT"] = tp(inputs["L1_ffn_w_router"])
    w["l1cent"] = np.ascontiguousarray(
        np.asarray(inputs["L1_ffn_head_centroids"]))
    w["l1temp"] = np.ascontiguousarray(
        np.asarray(inputs["L1_ffn_temperature"]).reshape(1, 1))
    w["l1IE"] = np.ascontiguousarray(
        np.asarray(inputs["L1_ffn_input_experts"])).astype(_BF)
    w["l1OE"] = np.ascontiguousarray(
        np.asarray(inputs["L1_ffn_output_experts"]).astype(_BF))
    return w


class QOp:

    __slots__ = ("meth", "args", "kwargs", "incs")

    def __init__(self, meth, args, kwargs):
        self.meth = meth
        self.args = args
        self.kwargs = kwargs
        self.incs = []

    def then_inc(self, sem_h, k):
        self.incs.append((sem_h, k))
        return self


class FEng:

    def __init__(self, q):
        self._q = q

    def __getattr__(self, name):
        def call(*a, **k):
            op = QOp(name, a, k)
            self._q.append(op)
            return op
        return call


class FakeBlock:

    ENGINES = ("tensor", "scalar", "vector", "gpsimd", "sync")

    def __init__(self):
        self.qs = {e: [] for e in self.ENGINES}
        self._f = {e: FEng(self.qs[e]) for e in self.ENGINES}

    def tensor(self, f):
        f(self._f["tensor"])

    def scalar(self, f):
        f(self._f["scalar"])

    def vector(self, f):
        f(self._f["vector"])

    def gpsimd(self, f):
        f(self._f["gpsimd"])

    def sync(self, f):
        f(self._f["sync"])

    def flush(self, block):
        for name in self.ENGINES:
            q = self.qs[name]

            def body(eng, q=q):
                for op in q:
                    ins = getattr(eng, op.meth)(*op.args, **op.kwargs)
                    for h, k in op.incs:
                        ins.then_inc(h, k)
            getattr(block, name)(body)


class Sem:
    def __init__(self, ctx, nc, name):
        self.h = ctx.enter_context(nc.semaphore(name))
        self.n = 0
        self.name = name

    def inc(self, instr, k=1):
        instr.then_inc(self.h, k)
        self.n += k
        return self.n


class Buf:

    __slots__ = ("w", "r")

    def __init__(self):
        self.w = {}
        self.r = {}

    def rd(self, eng, s):
        _wait(eng, s, self.w)

    def wr(self, eng, s):
        _wait(eng, s, self.w, self.r)

    def did_rd(self, sem_name, n):
        self.r[sem_name] = max(self.r.get(sem_name, 0), n)

    def did_wr(self, sem_name, n):
        self.w = {sem_name: n}
        self.r = {}


def _wait(eng, s, *dicts):
    agg = {}
    for d in dicts:
        for k, v in d.items():
            agg[k] = max(agg.get(k, 0), v)
    for k, v in agg.items():
        eng.wait_ge(s.sem(k).h, v)


class ColView:

    def __init__(self, t, off):
        self.t = t
        self.off = off

    def __getitem__(self, idx):
        p, f = idx
        return self.t[p, f.start + self.off:f.stop + self.off]


class S:
    def va_view(self, j, h):
        c0 = (j * 16 + h) * 65
        return self.va[:, c0:c0 + 65]

    def sem(self, name):
        if name not in self.sems:
            self.sems[name] = Sem(self._ctx, self.nc, name)
        return self.sems[name]


def emit_attn_norm_pre(s, block, vw):
    @block.vector
    def _(v):
        for bb in vw["bufs"]:
            bb.rd(v, s)
        s.b_rowa.wr(v, s)
        _i = emit_recip_row(s, v, s.row_a[:, 0:512], vw["denA"])
        n = s.dv.inc(_i)
        s.b_rowa.did_wr("dv", n)
        s.b_rowb.wr(v, s)
        _i = emit_recip_row(s, v, s.row_b[:, 0:512], vw["denB"])
        n = s.dv.inc(_i)
        s.b_rowb.did_wr("dv", n)
        for bb in vw["bufs"]:
            bb.did_rd("dv", n)

    @block.tensor
    def _(t):
        s.b_rowa.rd(t, s)
        s.b_rowb.rd(t, s)
        s.b_ps0.wr(t, s)
        _i = t.matmul(s.p_s0[0:64, 0:512], s.ones1_64f[:, :],
                      s.row_a[:, 0:512], start=True, stop=True)
        n = s.mm.inc(_i)
        s.b_ps0.did_wr("mm", n)
        s.b_ps1.wr(t, s)
        _i = t.matmul(s.p_s1[0:64, 0:512], s.ones1_64f[:, :],
                      s.row_b[:, 0:512], start=True, stop=True)
        n = s.mm.inc(_i)
        s.b_ps1.did_wr("mm", n)
        s.b_rowa.did_rd("mm", n)
        s.b_rowb.did_rd("mm", n)

    @block.vector
    def _(v):
        s.b_ps0.rd(v, s)
        s.b_aB.wr(v, s)
        _i = v.tensor_copy(s.a_aB[0:64, 0:512], s.p_s0[0:64, 0:512])
        n = s.dv.inc(_i)
        s.b_aB.did_wr("dv", n)
        s.b_ps0.did_rd("dv", n)
        s.b_ps1.rd(v, s)
        s.b_bB.wr(v, s)
        _i = v.tensor_copy(s.a_bB[0:64, 0:512], s.p_s1[0:64, 0:512])
        n = s.dv.inc(_i)
        s.b_bB.did_wr("dv", n)
        s.b_ps1.did_rd("dv", n)


def emit_recip_row(s, v, out_ap, in_ap):
    return v.reciprocal(out_ap, in_ap)



def _barrier(s, block):
    s.barn += 5
    for attr in ("tensor", "scalar", "vector", "gpsimd", "sync"):
        def body(eng, n=s.barn):
            eng.nop().then_inc(s.bar.h, 1)
            eng.wait_ge(s.bar.h, n)
        getattr(block, attr)(body)


def emit_ln_store(s, block, r_buf, r_t, out_f, out_b, qc,
                  gamma=None, beta=None, store=True, post_cb=None,
                  ch_of="cOF", ch_ob="cOB", mid_cb=None):
    for ec in range(ND):
        sl = ec % 2

        @block.scalar
        def _(sc, ec=ec, sl=sl):
            r_buf.rd(sc, s)
            s.b_rsq2[sl].wr(sc, s)
            _i = sc.activation(s.a_rsq2[:, sl * 512:(sl + 1) * 512],
                               r_t[:, ec * 512:(ec + 1) * 512], AF.Square)
            n = s.act.inc(_i)
            s.b_rsq2[sl].did_wr("act", n)
            r_buf.did_rd("act", n)

        @block.tensor
        def _(t, ec=ec, sl=sl):
            r_buf.rd(t, s)
            s.b_rsq2[sl].rd(t, s)
            if ec == 0:
                s.b_pg.wr(t, s)
            t.matmul(s.p_g[0:1, 0:512], s.ones128f[:, :],
                     r_t[:, ec * 512:(ec + 1) * 512],
                     start=(ec == 0), stop=(ec == ND - 1))
            _i = t.matmul(s.p_g[0:1, 512:1024], s.ones128b[:, :],
                          s.a_rsq2[:, sl * 512:(sl + 1) * 512],
                          start=(ec == 0), stop=(ec == ND - 1))
            n = s.mm.inc(_i)
            r_buf.did_rd("mm", n)
            s.b_rsq2[sl].did_rd("mm", n)
            if ec == ND - 1:
                s.b_pg.did_wr("mm", n)

    if mid_cb is not None:
        mid_cb()

    @block.vector
    def _(v):
        s.b_pg.rd(v, s)
        s.b_rowm.wr(v, s)
        _i = v.tensor_scalar_mul(s.row_m[:, 0:512], s.p_g[0:1, 0:512], 1.0 / C)
        n = s.dv.inc(_i)
        s.b_rowm.did_wr("dv", n)
        s.b_rowq.wr(v, s)
        _i = v.tensor_scalar_mul(s.row_q[:, 0:512], s.p_g[0:1, 512:1024],
                                 1.0 / C)
        n = s.dv.inc(_i)
        s.b_rowq.did_wr("dv", n)
        s.b_pg.did_rd("dv", n)

    @block.vector
    def _(v):
        s.b_rowm.rd(v, s)
        s.b_rowb.wr(v, s)
        _i = v.tensor_mul(s.row_b[:, 0:512], s.row_m[:, 0:512],
                          s.row_m[:, 0:512])
        n = s.dv.inc(_i)
        v.wait_ge(s.dv.h, n)
        s.b_rowq.rd(v, s)
        s.b_rowq.wr(v, s)
        _i = v.tensor_sub(s.row_q[:, 0:512], s.row_q[:, 0:512],
                          s.row_b[:, 0:512])
        n = s.dv.inc(_i)
        s.b_rowq.did_wr("dv", n)
        s.b_rowb.did_rd("dv", n)

    @block.scalar
    def _(sc):
        s.b_rowq.rd(sc, s)
        s.b_rowb.wr(sc, s)
        _i = sc.activation(s.row_b[:, 0:512], s.row_q[:, 0:512], AF.Sqrt,
                           bias=s.epsT[0:1, 0:1])
        n = s.act.inc(_i)
        s.b_rowq.did_rd("act", n)
        s.b_rowb.did_wr("act", n)

    @block.vector
    def _(v):
        s.b_rowb.rd(v, s)
        s.b_rowa.wr(v, s)
        _i = emit_recip_row(s, v, s.row_a[:, 0:512], s.row_b[:, 0:512])
        n = s.dv.inc(_i)
        s.b_rowb.did_rd("dv", n)
        s.b_rowa.did_wr("dv", n)
        v.wait_ge(s.dv.h, n)
        s.b_rowm.rd(v, s)
        s.b_rowb.wr(v, s)
        _i = v.tensor_mul(s.row_b[:, 0:512], s.row_m[:, 0:512],
                          s.row_a[:, 0:512])
        n = s.dv.inc(_i)
        v.wait_ge(s.dv.h, n)
        _i = v.tensor_scalar_mul(s.row_b[:, 0:512], s.row_b[:, 0:512], -1.0)
        n = s.dv.inc(_i)
        s.b_rowm.did_rd("dv", n)
        s.b_rowb.did_wr("dv", n)

    @block.tensor
    def _(t):
        s.b_rowa.rd(t, s)
        s.b_rowb.rd(t, s)
        s.b_pc.wr(t, s)
        t.matmul(s.p_c[:, 0:512], s.ones1_128f[:, :], s.row_a[:, 0:512],
                 start=True, stop=True)
        s.b_pd.wr(t, s)
        _i = t.matmul(s.p_d[:, 0:512], s.ones1_128f[:, :], s.row_b[:, 0:512],
                 start=True, stop=True)
        n = s.mm.inc(_i)
        s.b_rowa.did_rd("mm", n)
        s.b_rowb.did_rd("mm", n)
        s.b_pc.did_wr("mm", n)
        s.b_pd.did_wr("mm", n)

    @block.vector
    def _(v):
        s.b_pc.rd(v, s)
        s.b_pd.rd(v, s)
        s.b_aB.wr(v, s)
        _i = v.tensor_copy(s.a_aB[:, 0:512], s.p_c[:, 0:512])
        n = s.dv.inc(_i)
        s.b_aB.did_wr("dv", n)
        s.b_bB.wr(v, s)
        _i = v.tensor_copy(s.a_bB[:, 0:512], s.p_d[:, 0:512])
        n = s.dv.inc(_i)
        s.b_bB.did_wr("dv", n)
        s.b_pc.did_rd("dv", n)
        s.b_pd.did_rd("dv", n)

    for ec in range(ND):
        off = (ec % 2) * 512
        ob = s.b_o12[ec % 2]

        @block.vector
        def _(v, ec=ec, off=off, ob=ob):
            s.b_aB.rd(v, s)
            s.b_bB.rd(v, s)
            r_buf.rd(v, s)
            ob.wr(v, s)
            _i = v.tensor_mul(s.a_o1[:, off:off + 512],
                              r_t[:, ec * 512:(ec + 1) * 512],
                              s.a_aB[:, 0:512])
            n = s.dv.inc(_i)
            v.wait_ge(s.dv.h, n)
            _i = v.tensor_add(s.a_o1[:, off:off + 512],
                              s.a_o1[:, off:off + 512], s.a_bB[:, 0:512])
            n = s.dv.inc(_i)
            if gamma is not None:
                v.wait_ge(s.dv.h, n)
                _i = v.tensor_scalar(s.a_o1[:, off:off + 512],
                                     s.a_o1[:, off:off + 512],
                                     gamma[:, ec:ec + 1], beta[:, ec:ec + 1],
                                     ALU.mult, ALU.add)
                n = s.dv.inc(_i)
            s.b_aB.did_rd("dv", n)
            s.b_bB.did_rd("dv", n)
            r_buf.did_rd("dv", n)
            ob.did_wr("dv", n)

        if store:
            @block.sync
            def _(sy, ec=ec, off=off, ob=ob):
                ob.rd(sy, s)
                n = s.sem(ch_of).inc(sy.dma_start(
                    out=out_f[ec * 128:(ec + 1) * 128,
                              qc * 512:(qc + 1) * 512],
                    in_=s.a_o1[:, off:off + 512]), 16)
                ob.did_rd(ch_of, n)

            if out_b is not None:
                @block.gpsimd
                def _(g, ec=ec, off=off, ob=ob):
                    ob.rd(g, s)
                    n = s.sem(ch_ob).inc(g.dma_start(
                        out=out_b[ec * 128:(ec + 1) * 128,
                                  qc * 512:(qc + 1) * 512],
                        in_=s.a_o1[:, off:off + 512]), 16)
                    ob.did_rd(ch_ob, n)
        if post_cb is not None:
            post_cb(ec, off, ob)


def av_views(s, parity):
    if parity == 0:
        return dict(avA=s.p_g[0:65, 0:512], avB=s.p_g[0:65, 512:1024],
                    denA=s.p_g[64:65, 0:512], denB=s.p_g[64:65, 512:1024],
                    valA=s.p_g[0:64, 0:512], valB=s.p_g[0:64, 512:1024],
                    bufs=(s.b_pg,))
    return dict(avA=s.p_c[0:65, 0:512], avB=s.p_d[0:65, 0:512],
                denA=s.p_c[64:65, 0:512], denB=s.p_d[64:65, 0:512],
                valA=s.p_c[0:64, 0:512], valB=s.p_d[0:64, 0:512],
                bufs=(s.b_pc, s.b_pd))


def emit_attn_pair(s, block, js, qtA, qtB, qt_buf, kA_fn, kB_fn, ktbuf_fn,
                   vaA_fn, vaB_fn, vabuf_fn, premask_fn=None,
                   postmask_fn=None, parity=0):
    vw = av_views(s, parity)
    avA, avB = vw["avA"], vw["avB"]
    avbufs = vw["bufs"]
    def scores(idx, j):
        ps = s.p_s0 if idx % 2 == 0 else s.p_s1
        psb = s.b_ps0 if idx % 2 == 0 else s.b_ps1
        pb = idx % 2

        @block.tensor
        def _(t, idx=idx, j=j, ps=ps, psb=psb):
            qt_buf.rd(t, s)
            ktbuf_fn(j).rd(t, s)
            psb.wr(t, s)
            t.matmul(ps[:, 0:512], kA_fn(j), qtA, start=True, stop=True,
                     tile_position=(0, 0))
            _i = t.matmul(ps[:, 512:1024], kB_fn(j), qtB, start=True,
                          stop=True, tile_position=(64, 0))
            n = s.mm.inc(_i)
            psb.did_wr("mm", n)
            qt_buf.did_rd("mm", n)
            ktbuf_fn(j).did_rd("mm", n)

        if premask_fn is not None:
            premask_fn(j, ps, psb)

        @block.scalar
        def _(sc, idx=idx, ps=ps, psb=psb, pb=pb):
            psb.rd(sc, s)
            s.b_pbuf[pb].wr(sc, s)
            _i = sc.activation(s.pb_sb[:, pb * 1024:(pb + 1) * 1024],
                               ps[:, 0:1024], AF.Exp)
            n = s.act.inc(_i)
            psb.did_rd("act", n)
            s.b_pbuf[pb].did_wr("act", n)

        if postmask_fn is not None:
            postmask_fn(j, pb)

    def av(idx, j):
        pb = idx % 2
        first = (idx == 0)
        last = (idx == len(js) - 1)

        @block.tensor
        def _(t, idx=idx, j=j, pb=pb, first=first, last=last):
            s.b_pbuf[pb].rd(t, s)
            vabuf_fn(j).rd(t, s)
            if first:
                for bb in avbufs:
                    bb.wr(t, s)
            t.matmul(avA, vaA_fn(j),
                     s.pb_sb[:, pb * 1024:pb * 1024 + 512],
                     start=first, stop=last)
            _i = t.matmul(avB, vaB_fn(j),
                          s.pb_sb[:, pb * 1024 + 512:pb * 1024 + 1024],
                          start=first, stop=last)
            n = s.mm.inc(_i)
            s.b_pbuf[pb].did_rd("mm", n)
            vabuf_fn(j).did_rd("mm", n)
            if last:
                for bb in avbufs:
                    bb.did_wr("mm", n)

    for idx, j in enumerate(js):
        scores(idx, j)
        if idx >= 1:
            av(idx - 1, js[idx - 1])
    av(len(js) - 1, js[-1])
    return vw


def emit_stage_A(s, block):
    nc = s.nc
    SCALE = 1.0 / math.sqrt(HD)

    @block.gpsimd
    def _(g):
        for wi, hname in enumerate(["l0aqT", "l0akT", "l0avT", "l0aoT"]):
            for dc in range(ND):
                s.sem("cAw").inc(g.dma_start(
                    out=s.a_w[wi][:, dc * 1024:(dc + 1) * 1024],
                    in_=s.hbm[hname][dc * 128:(dc + 1) * 128, :]), 16)
        s.b_aw.did_wr("cAw", s.sem("cAw").n)

    @block.gpsimd
    def _(g):
        for j in range(8):
            s.sem("cAm").inc(g.dma_start(
                out=s.a_maskA[:, j * 512:(j + 1) * 512],
                in_=s.hbm["maskA"][j, :, :]), 16)
        s.b_maskA.did_wr("cAm", s.sem("cAm").n)

    @block.vector
    def _(v):
        for sl in range(3):
            for tt in range(4):
                view = s.a_vaw.ap().rearrange(
                    "p (s h x) -> p s h x", s=12, x=65)
                _i = v.memset(view[:, sl * 4 + tt, :, 64:65], 1.0)
        n = s.dv.inc(_i)
        for sl in range(3):
            s.b_vaw[sl].did_wr("dv", n)

    def load_qkv(b):
        c2 = b % 2
        c3 = b % 3
        xo = 0
        qo = 0

        @block.gpsimd
        def _(g, b=b, c2=c2, xo=xo):
            ch = "cAx%d" % c2
            s.b_xbf2[c2].wr(g, s)
            for dc in range(ND):
                s.sem(ch).inc(g.dma_start(
                    out=s.a_xbf[:, xo + dc * 512:xo + (dc + 1) * 512],
                    in_=s.hbm["x0T"][dc * 128:(dc + 1) * 128,
                                     b * 512:(b + 1) * 512]), 16)
            s.b_xbf2[c2].did_wr(ch, s.sem(ch).n)

        for ec in range(NP):
            for wi in (0, 1):
                psb = s.b_ps0 if (2 * ec + wi) % 2 == 0 else s.b_ps1
                ps = s.p_s0 if (2 * ec + wi) % 2 == 0 else s.p_s1

                @block.tensor
                def _(t, ec=ec, wi=wi, ps=ps, psb=psb, c2=c2, xo=xo):
                    s.b_xbf2[c2].rd(t, s)
                    s.b_aw.rd(t, s)
                    psb.wr(t, s)
                    for dc in range(ND):
                        _i = t.matmul(ps[:, 0:512],
                                 s.a_w[wi][:, dc * 1024 + ec * 128:
                                           dc * 1024 + (ec + 1) * 128],
                                 s.a_xbf[:, xo + dc * 512:xo + (dc + 1) * 512],
                                 start=(dc == 0), stop=(dc == ND - 1))
                    n = s.mm.inc(_i)
                    psb.did_wr("mm", n)
                    s.b_xbf2[c2].did_rd("mm", n)
                    s.b_aw.did_rd("mm", n)

                @block.vector
                def _(v, ec=ec, wi=wi, ps=ps, psb=psb, c2=c2, c3=c3, qo=qo):
                    psb.rd(v, s)
                    if wi == 0:
                        dst_buf = s.b_qt2[c2]
                        out = s.a_qt[:, qo + ec * 512:qo + (ec + 1) * 512]
                        scale = SCALE
                    else:
                        dst_buf = s.b_ktw[c3]
                        out = s.a_ktw[:, (c3 * NP + ec) * 512:
                                      (c3 * NP + ec + 1) * 512]
                        scale = 1.0
                    dst_buf.wr(v, s)
                    if scale == 1.0:
                        _i = v.tensor_copy(out, ps[:, 0:512])
                    else:
                        _i = v.tensor_scalar_mul(out, ps[:, 0:512], scale)
                    n = s.dv.inc(_i)
                    dst_buf.did_wr("dv", n)
                    psb.did_rd("dv", n)

        for tt in range(4):
            for eh in range(2):
                @block.tensor
                def _(t, tt=tt, eh=eh, c2=c2, xo=xo):
                    s.b_xbf2[c2].rd(t, s)
                    s.b_aw.rd(t, s)
                    s.b_pc.wr(t, s)
                    for dc in range(ND):
                        _i = t.matmul(s.p_c[:, 0:512],
                                 s.a_xbf[:, xo + dc * 512 + tt * 128:
                                         xo + dc * 512 + (tt + 1) * 128],
                                 s.a_w[2][:, dc * 1024 + eh * 512:
                                          dc * 1024 + (eh + 1) * 512],
                                 start=(dc == 0), stop=(dc == ND - 1))
                    n = s.mm.inc(_i)
                    s.b_pc.did_wr("mm", n)
                    s.b_xbf2[c2].did_rd("mm", n)

                @block.vector
                def _(v, tt=tt, eh=eh, c3=c3):
                    s.b_pc.rd(v, s)
                    s.b_vaw[c3].wr(v, s)
                    view = s.a_vaw.ap().rearrange(
                        "p (s h x) -> p s h x", s=12, x=65)
                    src = s.p_c.ap().rearrange("p (h x) -> p h x", x=64)
                    _i = v.tensor_copy(view[:, c3 * 4 + tt,
                                       eh * 8:(eh + 1) * 8, 0:64],
                                  src[:, 0:8, :])
                    n = s.dv.inc(_i)
                    s.b_vaw[c3].did_wr("dv", n)
                    s.b_pc.did_rd("dv", n)

    def attn(b):
        c2 = b % 2
        c3 = b % 3
        p3 = (b - 1) % 3
        qo = 0
        oo = 0
        first_j = 4 if b == 0 else 0
        pend = [None]
        for ec in range(NP):
            def slot_of(j, c3=c3, p3=p3):
                return p3 if j < 4 else c3

            def kbase_of(j, ec=ec):
                return (slot_of(j) * NP + ec) * 512 + (j % 4) * 128

            def vbase_of(j):
                return (slot_of(j) * 4 + (j % 4)) * 1040

            def postmask(j, pb):
                @block.vector
                def _(v, j=j, pb=pb):
                    s.b_pbuf[pb].rd(v, s)
                    s.b_maskA.rd(v, s)
                    s.b_pbuf[pb].wr(v, s)
                    m = s.a_maskA[:, j * 512:(j + 1) * 512]
                    _i = v.tensor_mul(s.pb_sb[:, pb * 1024:pb * 1024 + 512],
                                      s.pb_sb[:, pb * 1024:pb * 1024 + 512],
                                      m)
                    n = s.dv.inc(_i)
                    v.wait_ge(s.dv.h, n)
                    _i = v.tensor_mul(
                        s.pb_sb[:, pb * 1024 + 512:pb * 1024 + 1024],
                        s.pb_sb[:, pb * 1024 + 512:pb * 1024 + 1024], m)
                    n = s.dv.inc(_i)
                    s.b_pbuf[pb].did_wr("dv", n)
                    s.b_maskA.did_rd("dv", n)

            emit_attn_pair(
                s, block, list(range(first_j, 8)),
                qtA=s.a_qt[0:64, qo + ec * 512:qo + (ec + 1) * 512],
                qtB=s.a_qt[64:128, qo + ec * 512:qo + (ec + 1) * 512],
                qt_buf=s.b_qt2[c2],
                kA_fn=lambda j, ec=ec: s.a_ktw[0:64,
                                               kbase_of(j):kbase_of(j) + 128],
                kB_fn=lambda j, ec=ec: s.a_ktw[64:128,
                                               kbase_of(j):kbase_of(j) + 128],
                ktbuf_fn=lambda j: s.b_ktw[slot_of(j)],
                vaA_fn=lambda j, ec=ec: s.a_vaw[
                    :, vbase_of(j) + (2 * ec) * 65:
                    vbase_of(j) + (2 * ec) * 65 + 65],
                vaB_fn=lambda j, ec=ec: s.a_vaw[
                    :, vbase_of(j) + (2 * ec + 1) * 65:
                    vbase_of(j) + (2 * ec + 1) * 65 + 65],
                vabuf_fn=lambda j: s.b_vaw[slot_of(j)],
                postmask_fn=postmask, parity=ec % 2)
            vw = av_views(s, ec % 2)

            def fin(ec=ec, vw=vw, c2=c2, oo=oo):
                emit_attn_norm_pre(s, block, vw)

                @block.vector
                def _(v, ec=ec, vw=vw, c2=c2, oo=oo):
                    s.b_aB.rd(v, s)
                    s.b_bB.rd(v, s)
                    for bb in vw["bufs"]:
                        bb.rd(v, s)
                    s.b_ot2[c2].wr(v, s)
                    v.tensor_mul(
                        s.a_ot[0:64, oo + ec * 512:oo + (ec + 1) * 512],
                        vw["valA"], s.a_aB[0:64, 0:512])
                    _i = v.tensor_mul(
                        s.a_ot[64:128, oo + ec * 512:oo + (ec + 1) * 512],
                        vw["valB"], s.a_bB[0:64, 0:512])
                    n = s.dv.inc(_i)
                    s.b_ot2[c2].did_wr("dv", n)
                    for bb in vw["bufs"]:
                        bb.did_rd("dv", n)
                    s.b_aB.did_rd("dv", n)
                    s.b_bB.did_rd("dv", n)

            if pend[0] is not None:
                pend[0]()
            pend[0] = fin

        pend[0]()

    def out(b):
        c2 = b % 2
        oo = 0
        for ec in range(ND):
            psb = s.b_ps0 if ec % 2 == 0 else s.b_ps1
            ps = s.p_s0 if ec % 2 == 0 else s.p_s1

            @block.sync
            def _(sy, ec=ec, b=b):
                ch = "cAxf%d" % (ec % 2)
                s.b_xf[ec % 2].wr(sy, s)
                n = s.sem(ch).inc(sy.dma_start(
                    out=s.a_xf[:, (ec % 2) * 512:((ec % 2) + 1) * 512],
                    in_=s.hbm["x0T"][ec * 128:(ec + 1) * 128,
                                     b * 512:(b + 1) * 512]), 16)
                s.b_xf[ec % 2].did_wr(ch, n)

            @block.tensor
            def _(t, ec=ec, ps=ps, psb=psb, c2=c2, oo=oo):
                s.b_ot2[c2].rd(t, s)
                s.b_aw.rd(t, s)
                psb.wr(t, s)
                for dc in range(ND):
                    _i = t.matmul(ps[:, 0:512],
                             s.a_w[3][:, dc * 1024 + ec * 128:
                                     dc * 1024 + (ec + 1) * 128],
                             s.a_ot[:, oo + dc * 512:oo + (dc + 1) * 512],
                             start=(dc == 0), stop=(dc == ND - 1))
                n = s.mm.inc(_i)
                psb.did_wr("mm", n)
                s.b_ot2[c2].did_rd("mm", n)

            @block.vector
            def _(v, ec=ec, ps=ps, psb=psb):
                psb.rd(v, s)
                s.b_xf[ec % 2].rd(v, s)
                s.b_r.wr(v, s)
                _i = v.tensor_add(s.a_r[:, ec * 512:(ec + 1) * 512],
                             ps[:, 0:512],
                             s.a_xf[:, (ec % 2) * 512:((ec % 2) + 1) * 512])
                n = s.dv.inc(_i)
                psb.did_rd("dv", n)
                s.b_xf[ec % 2].did_rd("dv", n)
                s.b_r.did_wr("dv", n)

        emit_ln_store(s, block, s.b_r, s.a_r,
                      s.hbm["x1Tf"], s.hbm["x1Tb"], qc=b,
                      ch_of="cAof", ch_ob="cAob")

    load_qkv(0)
    for b in range(8):
        attn(b)
        if b + 1 < 8:
            load_qkv(b + 1)
        out(b)


def emit_B_setup(s, block):
    @block.gpsimd
    def _(g):
        ch = s.sem("cBs")
        for ec in range(ND):
            ch.inc(g.dma_start(out=s.bq_sb[:, ec:ec + 1],
                               in_=s.hbm["l0fbq"][ec * 128:(ec + 1) * 128, :]),
                   16)
            ch.inc(g.dma_start(out=s.bk_sb[:, ec:ec + 1],
                               in_=s.hbm["l0fbk"][ec * 128:(ec + 1) * 128, :]),
                   16)
            ch.inc(g.dma_start(out=s.bv_sb[:, ec:ec + 1],
                               in_=s.hbm["l0fbv"][ec * 128:(ec + 1) * 128, :]),
                   16)
            ch.inc(g.dma_start(out=s.gb_g[:, ec:ec + 1],
                               in_=s.hbm["l0fg"][ec * 128:(ec + 1) * 128, :]),
                   16)
            ch.inc(g.dma_start(out=s.gb_b[:, ec:ec + 1],
                               in_=s.hbm["l0fb"][ec * 128:(ec + 1) * 128, :]),
                   16)
            ch.inc(g.dma_start(out=s.sw_sb[:, ec * 8:(ec + 1) * 8],
                               in_=s.hbm["l0fswT"][ec * 128:(ec + 1) * 128, :]),
                   16)
        ch.inc(g.dma_start(out=s.bsw_sb[:, :], in_=s.hbm["l0fbsw"]), 16)
        ch.inc(g.dma_start(out=s.ltri_sb[:, :], in_=s.hbm["ltri128"]), 16)
        ch.inc(g.dma_start(out=s.ltri32_sb[:, :], in_=s.hbm["ltri32"]), 16)
        ch.inc(g.dma_start(out=s.identf_sb[:, :], in_=s.hbm["identf"]), 16)
        s.b_setup.did_wr("cBs", ch.n)

    @block.vector
    def _(v):
        s.b_setup.rd(v, s)
        _i = v.tensor_scalar_mul(s.bq_sb[:, 0:8], s.bq_sb[:, 0:8],
                                 1.0 / math.sqrt(C))
        n = s.dv.inc(_i)
        s.b_setup.did_rd("dv", n)
        s.b_setup.w["dv"] = n

    @block.tensor
    def _(t):
        s.b_setup.rd(t, s)
        s.b_pc.wr(t, s)
        _i = t.matmul(s.p_c[:, 0:8], s.ones1_128f[:, :], s.bsw_sb[:, :],
                      start=True, stop=True)
        n = s.mm.inc(_i)
        s.b_pc.did_wr("mm", n)
        s.b_setup.did_rd("mm", n)

    @block.vector
    def _(v):
        s.b_pc.rd(v, s)
        s.b_bswB.wr(v, s)
        _i = v.tensor_copy(s.bswB_sb[:, :], s.p_c[:, 0:8])
        n = s.dv.inc(_i)
        s.b_bswB.did_wr("dv", n)
        s.b_pc.did_rd("dv", n)


def emit_B_mst(s, block):
    @block.gpsimd
    def _(g):
        ch = s.sem("cBs2")
        s.b_setup.wr(g, s)
        ch.inc(g.dma_start(out=s.rankc_sb[:, :], in_=s.hbm["rankcol"]), 16)
        ch.inc(g.dma_start(out=s.rankm_sb[:, :], in_=s.hbm["rankmask"]), 16)
        ch.inc(g.dma_start(out=s.iota8_sb[:, :], in_=s.hbm["iota8"]), 16)
        for jd in range(4):
            ch.inc(g.dma_start(out=s.maskB_sb[:, jd * 512:(jd + 1) * 512],
                               in_=s.hbm["maskB"][jd, :, :]), 16)
        for jp in range(32):
            ch.inc(g.dma_start(out=s.step_sb[:, jp * 32:(jp + 1) * 32],
                               in_=s.hbm["step32"][jp, :, :]), 16)
        s.b_setup.did_wr("cBs2", ch.n)


def emit_B_loadw(s, block, hname, ch):
    @block.gpsimd
    def _(g):
        sem = s.sem(ch)
        s.b_w.wr(g, s)
        for dc in range(ND):
            sem.inc(g.dma_start(
                out=s.w_sb[:, dc * 1024:(dc + 1) * 1024],
                in_=s.hbm[hname][dc * 128:(dc + 1) * 128, :]), 16)
        s.b_w.did_wr(ch, sem.n)


def emit_B_loadx(s, block, qc, hname="x1Tb"):
    @block.gpsimd
    def _(g, qc=qc):
        ch = s.sem("cBx")
        s.b_xcb.wr(g, s)
        for dc in range(ND):
            ch.inc(g.dma_start(
                out=s.xcb[:, dc * 512:(dc + 1) * 512],
                in_=s.hbm[hname][dc * 128:(dc + 1) * 128,
                                 qc * 512:(qc + 1) * 512]), 16)
        s.b_xcb.did_wr("cBx", ch.n)


def emit_B_kv(s, block):
    @block.vector
    def _(v):
        view = s.va.ap().rearrange("p (n x) -> p n x", x=65)
        _i = v.memset(view[:, :, 64:65], 1.0)
        n = s.dv.inc(_i)
        s.b_va.did_wr("dv", n)

    emit_B_loadw(s, block, "l0fkT", "cBw")
    for qc in range(NQ):
        emit_B_loadx(s, block, qc)
        for ec in range(NP):
            psb = s.b_ps0 if ec % 2 == 0 else s.b_ps1
            ps = s.p_s0 if ec % 2 == 0 else s.p_s1

            @block.tensor
            def _(t, ec=ec, ps=ps, psb=psb):
                s.b_xcb.rd(t, s)
                s.b_w.rd(t, s)
                psb.wr(t, s)
                for dc in range(ND):
                    _i = t.matmul(ps[:, 0:512],
                                  s.w_sb[:, dc * 1024 + ec * 128:
                                         dc * 1024 + (ec + 1) * 128],
                                  s.xcb[:, dc * 512:(dc + 1) * 512],
                                  start=(dc == 0), stop=(dc == ND - 1))
                n = s.mm.inc(_i)
                psb.did_wr("mm", n)
                s.b_xcb.did_rd("mm", n)
                s.b_w.did_rd("mm", n)

            @block.vector
            def _(v, ec=ec, qc=qc, ps=ps, psb=psb):
                psb.rd(v, s)
                s.b_setup.rd(v, s)
                s.b_kt.wr(v, s)
                _i = v.tensor_scalar_add(
                    s.kt[:, ec * T + qc * 512:ec * T + (qc + 1) * 512],
                    ps[:, 0:512], s.bk_sb[:, ec:ec + 1])
                n = s.dv.inc(_i)
                s.b_kt.did_wr("dv", n)
                psb.did_rd("dv", n)
                s.b_setup.did_rd("dv", n)

    emit_B_loadw(s, block, "l0fvT", "cBw")
    for qc in range(NQ):
        emit_B_loadx(s, block, qc)
        for tt in range(4):
            for eh in range(2):
                @block.tensor
                def _(t, tt=tt, eh=eh):
                    s.b_xcb.rd(t, s)
                    s.b_w.rd(t, s)
                    s.b_pc.wr(t, s)
                    for dc in range(ND):
                        _i = t.matmul(
                            s.p_c[:, 0:512],
                            s.xcb[:, dc * 512 + tt * 128:
                                  dc * 512 + (tt + 1) * 128],
                            s.w_sb[:, dc * 1024 + eh * 512:
                                   dc * 1024 + (eh + 1) * 512],
                            start=(dc == 0), stop=(dc == ND - 1))
                    n = s.mm.inc(_i)
                    s.b_pc.did_wr("mm", n)
                    s.b_xcb.did_rd("mm", n)
                    s.b_w.did_rd("mm", n)

                @block.vector
                def _(v, tt=tt, eh=eh, qc=qc):
                    s.b_pc.rd(v, s)
                    s.b_va.wr(v, s)
                    view = s.va.ap().rearrange("p (n x) -> p n x", x=65)
                    src = s.p_c.ap().rearrange("p (h x) -> p h x", x=64)
                    til = qc * 4 + tt
                    _i = v.tensor_copy(
                        view[:, til * 16 + eh * 8:til * 16 + (eh + 1) * 8,
                             0:64], src[:, 0:8, :])
                    n = s.dv.inc(_i)
                    s.b_va.did_wr("dv", n)
                    s.b_pc.did_rd("dv", n)

    for h in range(H):
        @block.tensor
        def _(t, h=h):
            s.b_va.rd(t, s)
            s.b_setup.rd(t, s)
            if h == 0:
                s.b_pg.wr(t, s)
            for jp in range(NT):
                _i = t.matmul(
                    s.p_g[0:65, h * 32:(h + 1) * 32] if h < 8 else
                    s.p_g[0:65, 512 + (h - 8) * 32:512 + (h - 7) * 32],
                    s.va_view(jp, h),
                    s.step_sb[:, jp * 32:(jp + 1) * 32],
                    start=(jp == 0), stop=(jp == NT - 1))
            n = s.mm.inc(_i)
            s.b_va.did_rd("mm", n)
            if h == H - 1:
                s.b_pg.did_wr("mm", n)

    @block.vector
    def _(v):
        s.b_pg.rd(v, s)
        s.b_sfx.wr(v, s)
        v.tensor_copy(s.sfx[0:65, 0:256], s.p_g[0:65, 0:256])
        _i = v.tensor_copy(s.sfx[0:65, 256:512], s.p_g[0:65, 512:768])
        n = s.dv.inc(_i)
        s.b_sfx.did_wr("dv", n)
        s.b_pg.did_rd("dv", n)


def emit_B_attn_qchunk(s, block, qc):
    SC = 1.0 / math.sqrt(C)
    emit_B_loadw(s, block, "l0fqT", "cBw")

    emit_B_loadx(s, block, qc)

    for ec in range(NP):
        psb = s.b_ps0 if ec % 2 == 0 else s.b_ps1
        ps = s.p_s0 if ec % 2 == 0 else s.p_s1

        @block.tensor
        def _(t, ec=ec, ps=ps, psb=psb):
            s.b_xcb.rd(t, s)
            s.b_w.rd(t, s)
            psb.wr(t, s)
            for dc in range(ND):
                _i = t.matmul(ps[:, 0:512],
                              s.w_sb[:, dc * 1024 + ec * 128:
                                     dc * 1024 + (ec + 1) * 128],
                              s.xcb[:, dc * 512:(dc + 1) * 512],
                              start=(dc == 0), stop=(dc == ND - 1))
            n = s.mm.inc(_i)
            psb.did_wr("mm", n)
            s.b_xcb.did_rd("mm", n)
            s.b_w.did_rd("mm", n)

        @block.vector
        def _(v, ec=ec, ps=ps, psb=psb):
            psb.rd(v, s)
            s.b_setup.rd(v, s)
            s.b_qt.wr(v, s)
            _i = v.tensor_scalar(s.qt[:, ec * 512:(ec + 1) * 512],
                                 ps[:, 0:512], SC, s.bq_sb[:, ec:ec + 1],
                                 ALU.mult, ALU.add)
            n = s.dv.inc(_i)
            s.b_qt.did_wr("dv", n)
            psb.did_rd("dv", n)
            s.b_setup.did_rd("dv", n)

    jmax = qc * 4 + 3
    pend = [None]
    for ec in range(NP):
        def premask(j, ps, psb, qc=qc):
            if j < qc * 4:
                return

            @block.vector
            def _(v, j=j, ps=ps, psb=psb, qc=qc):
                jd = j - qc * 4
                psb.rd(v, s)
                s.b_setup.rd(v, s)
                psb.wr(v, s)
                _i = v.tensor_mul(ps[:, 0:512], ps[:, 0:512],
                                  s.maskB_sb[:, jd * 512:(jd + 1) * 512])
                n = s.dv.inc(_i)
                v.wait_ge(s.dv.h, n)
                _i = v.tensor_mul(ps[:, 512:1024], ps[:, 512:1024],
                                  s.maskB_sb[:, jd * 512:(jd + 1) * 512])
                n = s.dv.inc(_i)
                psb.did_wr("dv", n)
                s.b_setup.did_rd("dv", n)

        emit_attn_pair(
            s, block, list(range(jmax + 1)),
            qtA=s.qt[0:64, ec * 512:(ec + 1) * 512],
            qtB=s.qt[64:128, ec * 512:(ec + 1) * 512],
            qt_buf=s.b_qt,
            kA_fn=lambda j, ec=ec: s.kt[0:64, ec * T + j * 128:
                                        ec * T + (j + 1) * 128],
            kB_fn=lambda j, ec=ec: s.kt[64:128, ec * T + j * 128:
                                        ec * T + (j + 1) * 128],
            ktbuf_fn=lambda j: s.b_kt,
            vaA_fn=lambda j, ec=ec: s.va_view(j, 2 * ec),
            vaB_fn=lambda j, ec=ec: s.va_view(j, 2 * ec + 1),
            vabuf_fn=lambda j: s.b_va,
            premask_fn=premask, parity=ec % 2)
        vw = av_views(s, ec % 2)

        def fin(ec=ec, vw=vw, qc=qc):
         if qc < NQ - 1:
             @block.vector
             def _(v, ec=ec, qc=qc, vw=vw):
                 s.b_sfx.rd(v, s)
                 for bb in vw["bufs"]:
                     bb.rd(v, s)
                     bb.wr(v, s)
                 j0 = qc * 4 + 4
                 _i = v.tensor_scalar_add(vw["avA"], vw["avA"],
                                          s.sfx[0:65, (2 * ec) * 32 + j0:
                                                (2 * ec) * 32 + j0 + 1])
                 n = s.dv.inc(_i)
                 _i = v.tensor_scalar_add(vw["avB"], vw["avB"],
                                          s.sfx[0:65, (2 * ec + 1) * 32 + j0:
                                                (2 * ec + 1) * 32 + j0 + 1])
                 n = s.dv.inc(_i)
                 for bb in vw["bufs"]:
                     bb.did_wr("dv", n)
                 s.b_sfx.did_rd("dv", n)

         emit_attn_norm_pre(s, block, vw)

         @block.vector
         def _(v, ec=ec, vw=vw):
             for bb in vw["bufs"]:
                 bb.rd(v, s)
             s.b_o1 = s.b_o12[0]
             s.b_o1.wr(v, s)
             s.b_aB.rd(v, s)
             _i = v.tensor_mul(s.a_o1[0:64, 0:512], vw["valA"],
                               s.a_aB[0:64, 0:512])
             n = s.dv.inc(_i)
             s.b_bB.rd(v, s)
             _i = v.tensor_mul(s.a_o1[64:128, 0:512], vw["valB"],
                               s.a_bB[0:64, 0:512])
             n = s.dv.inc(_i)
             for bb in vw["bufs"]:
                 bb.did_rd("dv", n)
             s.b_aB.did_rd("dv", n)
             s.b_bB.did_rd("dv", n)
             v.wait_ge(s.dv.h, n)
             s.b_setup.rd(v, s)
             _i = v.tensor_scalar_add(s.a_o1[:, 0:512], s.a_o1[:, 0:512],
                                      s.bv_sb[:, ec:ec + 1])
             n = s.dv.inc(_i)
             s.b_o1.did_wr("dv", n)
             s.b_setup.did_rd("dv", n)
             v.wait_ge(s.dv.h, n)
             s.b_xcb.wr(v, s)
             _i = v.tensor_copy(s.xcb[:, ec * 512:(ec + 1) * 512],
                                s.a_o1[:, 0:512])
             n = s.dv.inc(_i)
             s.b_xcb.did_wr("dv", n)

         @block.sync
         def _(sy, ec=ec, qc=qc):
             ch = s.sem("cBhf")
             s.b_o1.rd(sy, s)
             n = ch.inc(sy.dma_start(
                 out=s.hbm["hTf"][ec * 128:(ec + 1) * 128,
                                  qc * 512:(qc + 1) * 512],
                 in_=s.a_o1[:, 0:512]), 16)
             s.b_o1.did_rd("cBhf", n)

         @block.gpsimd
         def _(g, ec=ec, qc=qc):
             ch = s.sem("cBhb")
             s.b_o1.rd(g, s)
             n = ch.inc(g.dma_start(
                 out=s.hbm["hTb"][ec * 128:(ec + 1) * 128,
                                  qc * 512:(qc + 1) * 512],
                 in_=s.a_o1[:, 0:512]), 16)
             s.b_o1.did_rd("cBhb", n)

        if pend[0] is not None:
            pend[0]()
        pend[0] = fin

    pend[0]()

    for tt in range(4):
        j = qc * 4 + tt

        @block.tensor
        def _(t, tt=tt):
            s.b_xcb.rd(t, s)
            s.b_setup.rd(t, s)
            s.b_pc.wr(t, s)
            for dc in range(ND):
                _i = t.matmul(s.p_c[:, 0:8],
                              s.xcb[:, dc * 512 + tt * 128:
                                    dc * 512 + (tt + 1) * 128],
                              s.sw_sb[:, dc * 8:(dc + 1) * 8],
                              start=(dc == 0), stop=(dc == ND - 1))
            n = s.mm.inc(_i)
            s.b_pc.did_wr("mm", n)
            s.b_xcb.did_rd("mm", n)
            s.b_setup.did_rd("mm", n)

        @block.vector
        def _(v, j=j):
            s.b_pc.rd(v, s)
            s.b_bswB.rd(v, s)
            s.b_lg.wr(v, s)
            _i = v.tensor_add(s.lg_sb[:, 0:8], s.p_c[:, 0:8], s.bswB_sb[:, :])
            n = s.dv.inc(_i)
            s.b_lg.did_wr("dv", n)
            s.b_pc.did_rd("dv", n)
            v.wait_ge(s.dv.h, n)
            s.b_mx.wr(v, s)
            _i = v.max(s.mx_sb[:, 0:8], s.lg_sb[:, 0:8])
            n = s.dv.inc(_i)
            s.b_mx.did_wr("dv", n)
            v.wait_ge(s.dv.h, n)
            s.b_mi.wr(v, s)
            _i = v.max_index(s.mi_sb[:, 0:8], s.mx_sb[:, 0:8], s.lg_sb[:, 0:8])
            n = s.dv.inc(_i)
            s.b_mi.did_wr("dv", n)
            v.wait_ge(s.dv.h, n)
            s.b_ln.wr(v, s)
            _i = v.tensor_scalar_mul(s.ln_sb[:, 0:1], s.mx_sb[:, 0:1], -1.0)
            n = s.dv.inc(_i)
            s.b_ln.did_wr("dv", n)
            s.b_idxf.wr(v, s)
            _i = v.tensor_copy(s.idxf_sb[:, 0:1], s.mi_sb[:, 0:1])
            n = s.dv.inc(_i)
            s.b_idxf.did_wr("dv", n)

        @block.scalar
        def _(sc, j=j):
            s.b_lg.rd(sc, s)
            s.b_ln.rd(sc, s)
            s.b_pex.wr(sc, s)
            _i = sc.activation(s.pex_sb[:, 0:8], s.lg_sb[:, 0:8], AF.Exp,
                               bias=s.ln_sb[:, 0:1])
            n = s.act.inc(_i)
            s.b_pex.did_wr("act", n)
            s.b_lg.did_rd("act", n)
            s.b_ln.did_rd("act", n)

        @block.vector
        def _(v, j=j):
            s.b_pex.rd(v, s)
            s.b_zz.wr(v, s)
            _i = v.reduce_sum(s.zz_sb[:, 0:1], s.pex_sb[:, 0:8], axis=AX.X)
            n = s.dv.inc(_i)
            s.b_zz.did_wr("dv", n)
            s.b_pex.did_rd("dv", n)
            v.wait_ge(s.dv.h, n)
            s.b_rp.wr(v, s)
            _i = v.reciprocal(s.rp_sb[:, j:j + 1], s.zz_sb[:, 0:1])
            n = s.dv.inc(_i)
            s.b_rp.did_wr("dv", n)
            s.b_zz.did_rd("dv", n)
            s.b_idxf.rd(v, s)
            s.b_setup.rd(v, s)
            s.b_oh.wr(v, s)
            _i = v.tensor_single_scalar(s.oh_sb[:, j * 8:(j + 1) * 8],
                                        s.iota8_sb[:, :],
                                        s.idxf_sb[:, 0:1], ALU.is_equal)
            n = s.dv.inc(_i)
            s.b_oh.did_wr("dv", n)
            s.b_idxf.did_rd("dv", n)
            v.wait_ge(s.dv.h, n)
            s.b_ohb.wr(v, s)
            _i = v.tensor_copy(s.ohb_sb[:, j * 8:(j + 1) * 8],
                               s.oh_sb[:, j * 8:(j + 1) * 8])
            n = s.dv.inc(_i)
            s.b_ohb.did_wr("dv", n)
            s.b_oh.did_rd("dv", n)

        @block.tensor
        def _(t, j=j):
            s.b_oh.rd(t, s)
            s.b_pc.wr(t, s)
            _i = t.matmul(s.p_c[0:1, 0:8], s.ones128f[:, :],
                          s.oh_sb[:, j * 8:(j + 1) * 8],
                          start=True, stop=True)
            n = s.mm.inc(_i)
            s.b_oh.did_rd("mm", n)
            s.b_pc.did_wr("mm", n)

        @block.vector
        def _(v, j=j):
            s.b_pc.rd(v, s)
            s.b_tot.wr(v, s)
            _i = v.tensor_copy(s.tot_sb[0:1, j * 8:(j + 1) * 8],
                               s.p_c[0:1, 0:8])
            n = s.dv.inc(_i)
            s.b_tot.did_wr("dv", n)
            s.b_pc.did_rd("dv", n)


def emit_B_counts(s, block):
    @block.gpsimd
    def _(g):
        ch = s.sem("cBt")
        s.b_tot.rd(g, s)
        n = ch.inc(g.dma_start(out=s.hbm["scr_tot"], in_=s.tot_sb[:, :]), 16)
        g.wait_ge(ch.h, n)
        s.b_t32.wr(g, s)
        n = ch.inc(g.dma_start(
            out=s.t32_sb[:, :],
            in_=s.hbm["scr_tot"].rearrange("o (r c) -> (o r) c", c=8)), 16)
        s.b_t32.did_wr("cBt", n)

    @block.tensor
    def _(t):
        s.b_t32.rd(t, s)
        s.b_pc.wr(t, s)
        _i = t.matmul(s.p_c[0:1, 0:8], s.ones32f[:, :], s.t32_sb[:, :],
                      start=True, stop=True)
        n = s.mm.inc(_i)
        s.b_pc.did_wr("mm", n)
        s.b_t32.did_rd("mm", n)

    @block.vector
    def _(v):
        s.b_pc.rd(v, s)
        s.b_tall.wr(v, s)
        _i = v.tensor_copy(s.tall_sb[:, :], s.p_c[0:1, 0:8])
        n = s.dv.inc(_i)
        s.b_tall.did_wr("dv", n)
        s.b_pc.did_rd("dv", n)

    @block.tensor
    def _(t):
        s.b_tall.rd(t, s)
        s.b_setup.rd(t, s)
        s.b_pd.wr(t, s)
        _i = t.matmul(s.p_d[0:8, 0:8], s.rankc_sb[:, :], s.tall_sb[:, :],
                      start=True, stop=True)
        n = s.mm.inc(_i)
        s.b_pd.did_wr("mm", n)
        s.b_tall.did_rd("mm", n)

    @block.vector
    def _(v):
        s.b_pd.rd(v, s)
        s.b_slot.wr(v, s)
        _i = v.tensor_copy(s.slot_sb[:, :], s.p_d[0:8, 0:8])
        n = s.dv.inc(_i)
        s.b_slot.did_wr("dv", n)
        s.b_pd.did_rd("dv", n)

    @block.gpsimd
    def _(g):
        ch = s.sem("cBcc")
        s.b_slot.rd(g, s)
        n = ch.inc(g.dma_start(out=s.hbm["cc_in"], in_=s.slot_sb[:, :]), 16)
        g.wait_ge(ch.h, n)
        g.collective_compute(
            "AllReduce", mybir.AluOpType.add,
            replica_groups=[list(range(8))],
            ins=[s.hbm["cc_in"].opt()],
            outs=[s.hbm["cc_out"].opt()],
        ).then_inc(s.cc.h, 1)
        s.cc.n += 1
        g.wait_ge(s.cc.h, s.cc.n)
        s.b_gat.wr(g, s)
        n = ch.inc(g.dma_start(out=s.gat_sb[:, :], in_=s.hbm["cc_out"]), 16)
        s.b_gat.did_wr("cBcc", n)

    @block.tensor
    def _(t):
        s.b_gat.rd(t, s)
        s.b_setup.rd(t, s)
        s.b_pc.wr(t, s)
        _i = t.matmul(s.p_c[0:1, 0:8], s.rankm_sb[:, :], s.gat_sb[:, :],
                      start=True, stop=True)
        n = s.mm.inc(_i)
        s.b_pc.did_wr("mm", n)
        s.b_gat.did_rd("mm", n)

    @block.vector
    def _(v):
        s.b_pc.rd(v, s)
        s.b_coff.wr(v, s)
        _i = v.tensor_copy(s.coff_sb[:, :], s.p_c[0:1, 0:8])
        n = s.dv.inc(_i)
        s.b_coff.did_wr("dv", n)
        s.b_pc.did_rd("dv", n)

    @block.tensor
    def _(t):
        s.b_t32.rd(t, s)
        s.b_coff.rd(t, s)
        s.b_setup.rd(t, s)
        s.b_pd.wr(t, s)
        t.matmul(s.p_d[0:32, 0:8], s.ltri32_sb[:, :], s.t32_sb[:, :],
                 start=True, stop=False)
        _i = t.matmul(s.p_d[0:32, 0:8], s.ones1_32f[:, :], s.coff_sb[:, :],
                      start=False, stop=True)
        n = s.mm.inc(_i)
        s.b_pd.did_wr("mm", n)
        s.b_t32.did_rd("mm", n)
        s.b_coff.did_rd("mm", n)

    @block.vector
    def _(v):
        s.b_pd.rd(v, s)
        s.b_roff.wr(v, s)
        _i = v.tensor_copy(s.roff_sb[:, :], s.p_d[0:32, 0:8])
        n = s.dv.inc(_i)
        s.b_roff.did_wr("dv", n)
        s.b_pd.did_rd("dv", n)

    @block.gpsimd
    def _(g):
        ch = s.sem("cBt")
        s.b_roff.rd(g, s)
        n = ch.inc(g.dma_start(
            out=s.hbm["scr_row"].rearrange("o (r c) -> (o r) c", c=8),
            in_=s.roff_sb[:, :]), 16)
        g.wait_ge(ch.h, n)
        s.b_rrow.wr(g, s)
        n = ch.inc(g.dma_start(out=s.rrow_sb[:, :], in_=s.hbm["scr_row"]), 16)
        s.b_rrow.did_wr("cBt", n)


def emit_B_pos(s, block):
    @block.tensor
    def _(t):
        s.b_ohb.rd(t, s)
        s.b_setup.rd(t, s)
        s.b_pc.wr(t, s)
        _i = t.matmul(s.p_c[:, 0:256], s.ltri_sb[:, :], s.ohb_sb[:, 0:256],
                      start=True, stop=True)
        n = s.mm.inc(_i)
        s.b_pc.did_wr("mm", n)
        s.b_ohb.did_rd("mm", n)
        s.b_rrow.rd(t, s)
        s.b_pd.wr(t, s)
        _i = t.matmul(s.p_d[:, 0:256], s.ones1_128f[:, :],
                      s.rrow_sb[0:1, 0:256], start=True, stop=True)
        n = s.mm.inc(_i)
        s.b_pd.did_wr("mm", n)
        s.b_rrow.did_rd("mm", n)

    @block.vector
    def _(v):
        s.b_pd.rd(v, s)
        s.b_rb8.wr(v, s)
        _i = v.tensor_copy(s.rb8_sb[:, 0:256], s.p_d[:, 0:256])
        n = s.dv.inc(_i)
        s.b_rb8.did_wr("dv", n)
        s.b_pd.did_rd("dv", n)
        v.wait_ge(s.dv.h, n)
        s.b_pc.rd(v, s)
        s.b_pf8.wr(v, s)
        _i = v.tensor_add(s.pf8_sb[:, 0:256], s.p_c[:, 0:256],
                          s.rb8_sb[:, 0:256])
        n = s.dv.inc(_i)
        s.b_pf8.did_wr("dv", n)
        s.b_pc.did_rd("dv", n)
        s.b_rb8.did_rd("dv", n)
        v.wait_ge(s.dv.h, n)
        s.b_oh.rd(v, s)
        _i = v.tensor_mul(s.pf8_sb[:, 0:256], s.pf8_sb[:, 0:256],
                          s.oh_sb[:, 0:256])
        n = s.dv.inc(_i)
        s.b_pf8.did_wr("dv", n)
        s.b_oh.did_rd("dv", n)
        v.wait_ge(s.dv.h, n)
        s.b_pos.wr(v, s)
        _i = v.reduce_sum(
            s.pos_sb[:, 0:32],
            s.a_o1[:, 512:768].rearrange("p (t e) -> p t e", e=8),
            axis=AX.X)
        n = s.dv.inc(_i)
        s.b_pos.did_wr("dv", n)
        s.b_pf8.did_rd("dv", n)
        v.wait_ge(s.dv.h, n)
        s.b_kept.wr(v, s)
        _i = v.tensor_single_scalar(s.kept_sb[:, 0:32], s.pos_sb[:, 0:32],
                                    float(CAP), ALU.is_lt)
        n = s.dv.inc(_i)
        s.b_kept.did_wr("dv", n)
        s.b_pos.did_rd("dv", n)

    for j in range(NT):
        sl = j % 2

        @block.vector
        def _(v, j=j, sl=sl):
            s.b_oh.rd(v, s)
            s.b_kept.rd(v, s)
            s.b_ohkb2[sl].wr(v, s)
            _i = v.tensor_scalar_mul(s.ohkb_sb[:, sl * 8:(sl + 1) * 8],
                                     s.oh_sb[:, j * 8:(j + 1) * 8],
                                     s.kept_sb[:, j:j + 1])
            n = s.dv.inc(_i)
            s.b_ohkb2[sl].did_wr("dv", n)
            s.b_oh.did_rd("dv", n)
            s.b_kept.did_rd("dv", n)

        @block.tensor
        def _(t, j=j, sl=sl):
            s.b_ohkb2[sl].rd(t, s)
            s.b_setup.rd(t, s)
            pp = s.p_c if sl == 0 else s.p_d
            ppb = s.b_pc if sl == 0 else s.b_pd
            ppb.wr(t, s)
            _i = t.transpose(pp[0:8, 0:128],
                             s.ohkb_sb[:, sl * 8:(sl + 1) * 8],
                             s.identf_sb[:, :])
            n = s.mm.inc(_i)
            ppb.did_wr("mm", n)
            s.b_ohkb2[sl].did_rd("mm", n)

        @block.vector
        def _(v, j=j, sl=sl):
            pp = s.p_c if sl == 0 else s.p_d
            ppb = s.b_pc if sl == 0 else s.b_pd
            ppb.rd(v, s)
            s.b_ohkT.wr(v, s)
            _i = v.tensor_copy(s.ohkT_sb[0:8, j * 128:(j + 1) * 128],
                               pp[0:8, 0:128])
            n = s.dv.inc(_i)
            s.b_ohkT.did_wr("dv", n)
            ppb.did_rd("dv", n)


def eb_tc0_loads(s, block):
    @block.gpsimd
    def _(g):
        ch = s.sem("cBs")
        s.b_setup.wr(g, s)
        n = ch.inc(g.dma_start(out=s.be_sb[:, :], in_=s.hbm["l0fbe"]), 16)
        s.b_setup.did_wr("cBs", n)

    @block.scalar
    def _(sc):
        ch = s.sem("cBwes")
        for e_ in range(E):
            s.b_wes[e_].wr(sc, s)
            for dc in range(ND):
                ch.inc(sc.dma_start(
                    out=s.we_sb[:, (e_ * ND + dc) * 1024:
                                (e_ * ND + dc + 1) * 1024],
                    in_=s.hbm["l0fWeT"][e_, dc * 128:(dc + 1) * 128, :]),
                    16)
            s.b_wes[e_].did_wr("cBwes", ch.n)


def eb_mBprep(s, block, tc):
    for tt in range(4):
        j = tc * 4 + tt

        @block.vector
        def _(v, j=j, tt=tt):
            s.b_rp.rd(v, s)
            s.b_kept.rd(v, s)
            s.b_m1c.wr(v, s)
            _i = v.tensor_mul(s.m1c_sb[:, 0:1], s.rp_sb[:, j:j + 1],
                              s.kept_sb[:, j:j + 1])
            n = s.dv.inc(_i)
            s.b_m1c.did_wr("dv", n)
            v.wait_ge(s.dv.h, n)
            s.b_m2c.wr(v, s)
            _i = v.tensor_sub(s.m2c_sb[:, 0:1], s.rp_sb[:, j:j + 1],
                              s.m1c_sb[:, 0:1])
            n = s.dv.inc(_i)
            s.b_m2c.did_wr("dv", n)
            s.b_rp.did_rd("dv", n)
            s.b_kept.did_rd("dv", n)

        @block.tensor
        def _(t, tt=tt):
            s.b_m1c.rd(t, s)
            s.b_m2c.rd(t, s)
            s.b_setup.rd(t, s)
            if tt == 0:
                s.b_pc.wr(t, s)
                s.b_pd.wr(t, s)
            _i = t.matmul(s.p_c[0:1, tt * 128:(tt + 1) * 128],
                          s.m1c_sb[:, 0:1], s.identf_sb[:, :],
                          start=(tt == 0), stop=(tt == 3))
            n = s.mm.inc(_i)
            _i = t.matmul(s.p_d[0:1, tt * 128:(tt + 1) * 128],
                          s.m2c_sb[:, 0:1], s.identf_sb[:, :],
                          start=(tt == 0), stop=(tt == 3))
            n = s.mm.inc(_i)
            s.b_m1c.did_rd("mm", n)
            s.b_m2c.did_rd("mm", n)
            if tt == 3:
                s.b_pc.did_wr("mm", n)
                s.b_pd.did_wr("mm", n)

    @block.vector
    def _(v):
        s.b_pc.rd(v, s)
        s.b_m1r.wr(v, s)
        _i = v.tensor_copy(s.m1r_sb[:, :], s.p_c[0:1, 0:512])
        n = s.dv.inc(_i)
        s.b_m1r.did_wr("dv", n)
        s.b_pc.did_rd("dv", n)
        s.b_pd.rd(v, s)
        s.b_m2r.wr(v, s)
        _i = v.tensor_copy(s.m2r_sb[:, :], s.p_d[0:1, 0:512])
        n = s.dv.inc(_i)
        s.b_m2r.did_wr("dv", n)
        s.b_pd.did_rd("dv", n)

    @block.tensor
    def _(t):
        s.b_m1r.rd(t, s)
        s.b_m2r.rd(t, s)
        s.b_pc.wr(t, s)
        _i = t.matmul(s.p_c[:, 0:512], s.ones1_128f[:, :], s.m1r_sb[:, :],
                      start=True, stop=True)
        n = s.mm.inc(_i)
        s.b_pc.did_wr("mm", n)
        s.b_pd.wr(t, s)
        _i = t.matmul(s.p_d[:, 0:512], s.ones1_128f[:, :], s.m2r_sb[:, :],
                      start=True, stop=True)
        n = s.mm.inc(_i)
        s.b_pd.did_wr("mm", n)
        s.b_m1r.did_rd("mm", n)
        s.b_m2r.did_rd("mm", n)

    @block.vector
    def _(v):
        s.b_pc.rd(v, s)
        s.b_mB1.wr(v, s)
        _i = v.tensor_copy(s.mB1_sb[:, :], s.p_c[:, 0:512])
        n = s.dv.inc(_i)
        s.b_mB1.did_wr("dv", n)
        s.b_pc.did_rd("dv", n)
        s.b_pd.rd(v, s)
        s.b_mB2.wr(v, s)
        _i = v.tensor_copy(s.mB2_sb[:, :], s.p_d[:, 0:512])
        n = s.dv.inc(_i)
        s.b_mB2.did_wr("dv", n)
        s.b_pd.did_rd("dv", n)


def eb_prep_ok(s, block, tc):
    for e_ in range(E):
        @block.scalar
        def _(sc, e_=e_, tc=tc):
            ch = s.sem("cBok")
            s.b_ohkT.rd(sc, s)
            s.b_okr.wr(sc, s)
            n = ch.inc(sc.dma_start(
                out=s.okr_sb[0:1, 0:512],
                in_=s.ohkT_sb[e_:e_ + 1, tc * 512:(tc + 1) * 512]), 16)
            s.b_okr.did_wr("cBok", n)
            s.b_ohkT.did_rd("cBok", n)

        @block.tensor
        def _(t, e_=e_):
            s.b_okr.rd(t, s)
            s.b_pc.wr(t, s)
            _i = t.matmul(s.p_c[:, 0:512], s.ones1_128b[:, :],
                          s.okr_sb[0:1, 0:512],
                          start=True, stop=True)
            n = s.mm.inc(_i)
            s.b_pc.did_wr("mm", n)
            s.b_okr.did_rd("mm", n)

        @block.vector
        def _(v, e_=e_):
            s.b_pc.rd(v, s)
            s.b_okA.wr(v, s)
            _i = v.tensor_copy(s.okA_sb[:, e_ * 512:(e_ + 1) * 512],
                               s.p_c[:, 0:512])
            n = s.dv.inc(_i)
            s.b_okA.did_wr("dv", n)
            s.b_pc.did_rd("dv", n)

    @block.scalar
    def _(sc, tc=tc):
        ch = s.sem("cBht")
        s.b_htb.wr(sc, s)
        for dc in range(ND):
            ch.inc(sc.dma_start(
                out=s.htb_sb[:, dc * 512:(dc + 1) * 512],
                in_=s.hbm["hTb"][dc * 128:(dc + 1) * 128,
                                 tc * 512:(tc + 1) * 512]), 16)
        s.b_htb.did_wr("cBht", ch.n)


BANKS = None


def eb_banks(s):
    return [(s.p_s0, s.b_ps0, 0), (s.p_s0, s.b_ps0, 512),
            (s.p_s1, s.b_ps1, 0), (s.p_s1, s.b_ps1, 512)]


def eb_acc(s, block, tc, ph, es=None, bias=True):
    banks = eb_banks(s)
    for e_ in (es if es is not None else range(E)):
        for h in range(2):
            sl = (e_ * 2 + h) % 2

            if ph == 0:
                @block.vector
                def _(v, e_=e_, h=h, sl=sl):
                    s.b_htb.rd(v, s)
                    s.b_okA.rd(v, s)
                    s.b_hm2[sl].wr(v, s)
                    for i, dc in enumerate(range(h * 4, h * 4 + 4)):
                        _i = v.tensor_mul(
                            s.hm_sb2[sl][:, i * 512:(i + 1) * 512],
                            s.htb_sb[:, dc * 512:(dc + 1) * 512],
                            s.okA_sb[:, e_ * 512:(e_ + 1) * 512])
                    n = s.dv.inc(_i)
                    s.b_hm2[sl].did_wr("dv", n)
                    s.b_htb.did_rd("dv", n)
                    s.b_okA.did_rd("dv", n)
            else:
                @block.gpsimd
                def _(g, e_=e_, h=h, sl=sl):
                    s.b_htb.rd(g, s)
                    s.b_okA.rd(g, s)
                    s.b_hm2[sl].wr(g, s)
                    for i, dc in enumerate(range(h * 4, h * 4 + 4)):
                        _i = g.tensor_mul(
                            s.hm_sb2[sl][:, i * 512:(i + 1) * 512],
                            s.htb_sb[:, dc * 512:(dc + 1) * 512],
                            s.okA_sb[:, e_ * 512:(e_ + 1) * 512])
                    n = s.gp.inc(_i)
                    s.b_hm2[sl].did_wr("gp", n)
                    s.b_htb.did_rd("gp", n)
                    s.b_okA.did_rd("gp", n)

            @block.tensor
            def _(t, ph=ph, e_=e_, h=h, sl=sl):
                s.b_wes[e_].rd(t, s)
                s.b_hm2[sl].rd(t, s)
                if e_ == 0 and h == 0:
                    s.b_ps0.wr(t, s)
                    s.b_ps1.wr(t, s)
                for fc in range(ph * 4, ph * 4 + 4):
                    ps, psb, po = banks[fc - ph * 4]
                    for i, dc in enumerate(range(h * 4, h * 4 + 4)):
                        _i = t.matmul(
                            ps[:, po:po + 512],
                            s.we_sb[:, (e_ * ND + dc) * 1024 + fc * 128:
                                    (e_ * ND + dc) * 1024 +
                                    (fc + 1) * 128],
                            s.hm_sb2[sl][:, i * 512:(i + 1) * 512],
                            start=(e_ == 0 and h == 0 and i == 0),
                            stop=False)
                n = s.mm.inc(_i)
                s.b_wes[e_].did_rd("mm", n)
                s.b_hm2[sl].did_rd("mm", n)

    if not bias:
        return

    @block.tensor
    def _(t, ph=ph, tc=tc):
        s.b_setup.rd(t, s)
        s.b_ohkT.rd(t, s)
        for fc in range(ph * 4, ph * 4 + 4):
            ps, psb, po = banks[fc - ph * 4]
            _i = t.matmul(ps[:, po:po + 512],
                          s.be_sb[0:8, fc * 128:(fc + 1) * 128],
                          s.ohkT_sb[0:8, tc * 512:(tc + 1) * 512],
                          start=False, stop=True)
            n = s.mm.inc(_i)
            if fc == ph * 4 + 3:
                s.b_ps0.did_wr("mm", n)
                s.b_ps1.did_wr("mm", n)
        s.b_ohkT.did_rd("mm", n)


def eb_comb(s, block, tc, ph):
    banks = eb_banks(s)
    for fc in range(ph * 4, ph * 4 + 4):
        ps, psb, po = banks[fc - ph * 4]

        @block.sync
        def _(sy, fc=fc, tc=tc):
            ch = s.sem("cBhtf%d" % (fc % 2))
            s.b_htf[fc % 2].wr(sy, s)
            n = ch.inc(sy.dma_start(
                out=s.a_xf[:, (fc % 2) * 512:((fc % 2) + 1) * 512],
                in_=s.hbm["hTf"][fc * 128:(fc + 1) * 128,
                                 tc * 512:(tc + 1) * 512]), 16)
            s.b_htf[fc % 2].did_wr("cBhtf%d" % (fc % 2), n)
            ch = s.sem("cBx1f%d" % (fc % 2))
            s.b_x1f2[fc % 2].wr(sy, s)
            n = ch.inc(sy.dma_start(
                out=s.x1f_sb[:, (fc % 2) * 512:((fc % 2) + 1) * 512],
                in_=s.hbm["x1Tf"][fc * 128:(fc + 1) * 128,
                                  tc * 512:(tc + 1) * 512]), 16)
            s.b_x1f2[fc % 2].did_wr("cBx1f%d" % (fc % 2), n)

        @block.vector
        def _(v, fc=fc, ps=ps, psb=psb, po=po):
            psb.rd(v, s)
            s.b_mB1.rd(v, s)
            s.b_rb8.wr(v, s)
            _i = v.tensor_mul(s.a_o1[:, 0:512], ps[:, po:po + 512],
                              s.mB1_sb[:, :])
            n = s.dv.inc(_i)
            s.b_rb8.did_wr("dv", n)
            psb.did_rd("dv", n)
            s.b_mB1.did_rd("dv", n)
            s.b_htf[fc % 2].rd(v, s)
            s.b_mB2.rd(v, s)
            s.b_o2.wr(v, s)
            _i = v.tensor_mul(
                s.a_o1[:, 512:1024],
                s.a_xf[:, (fc % 2) * 512:((fc % 2) + 1) * 512],
                s.mB2_sb[:, :])
            n = s.dv.inc(_i)
            s.b_o2.did_wr("dv", n)
            s.b_htf[fc % 2].did_rd("dv", n)
            s.b_mB2.did_rd("dv", n)
            v.wait_ge(s.dv.h, n)
            s.b_rb8.rd(v, s)
            s.b_o2.rd(v, s)
            s.b_rb8.wr(v, s)
            _i = v.tensor_add(s.a_o1[:, 0:512], s.a_o1[:, 0:512],
                              s.a_o1[:, 512:1024])
            n = s.dv.inc(_i)
            s.b_rb8.did_wr("dv", n)
            s.b_o2.did_rd("dv", n)
            v.wait_ge(s.dv.h, n)
            s.b_x1f2[fc % 2].rd(v, s)
            s.b_r.wr(v, s)
            _i = v.tensor_add(
                s.a_r[:, fc * 512:(fc + 1) * 512], s.a_o1[:, 0:512],
                s.x1f_sb[:, (fc % 2) * 512:((fc % 2) + 1) * 512])
            n = s.dv.inc(_i)
            s.b_r.did_wr("dv", n)
            s.b_x1f2[fc % 2].did_rd("dv", n)
            s.b_rb8.did_rd("dv", n)


def eb_lns(s, block, tc, mid1=None, mid2=None):
    def post_cb(ec, off, ob):
        @block.sync
        def _(sy, ec=ec, tc=tc):
            ch = s.sem("cBx1f%d" % (ec % 2))
            s.b_x1f2[ec % 2].wr(sy, s)
            n = ch.inc(sy.dma_start(
                out=s.x1f_sb[:, (ec % 2) * 512:((ec % 2) + 1) * 512],
                in_=s.hbm["x1Tf"][ec * 128:(ec + 1) * 128,
                                  tc * 512:(tc + 1) * 512]), 16)
            s.b_x1f2[ec % 2].did_wr("cBx1f%d" % (ec % 2), n)

        @block.vector
        def _(v, ec=ec, off=off, ob=ob):
            ob.rd(v, s)
            s.b_x1f2[ec % 2].rd(v, s)
            s.b_r.wr(v, s)
            _i = v.tensor_add(s.a_r[:, ec * 512:(ec + 1) * 512],
                              s.a_o1[:, off:off + 512],
                              s.x1f_sb[:, (ec % 2) * 512:((ec % 2) + 1) * 512])
            n = s.dv.inc(_i)
            s.b_r.did_wr("dv", n)
            ob.did_rd("dv", n)
            s.b_x1f2[ec % 2].did_rd("dv", n)

    emit_ln_store(s, block, s.b_r, s.a_r, None, None, qc=tc,
                  gamma=s.gb_g, beta=s.gb_b, store=False, post_cb=post_cb,
                  mid_cb=mid1)

    emit_ln_store(s, block, s.b_r, s.a_r, s.hbm["x2Tf"], s.hbm["x2Tb"],
                  qc=tc, ch_of="cBof", ch_ob="cBob", mid_cb=mid2)



def emit_B_experts(s, block):
    eb_tc0_loads(s, block)
    eb_prep_ok(s, block, 0)
    eb_acc(s, block, 0, 0)
    for tc in range(NQ):
        eb_mBprep(s, block, tc)
        eb_comb(s, block, tc, 0)
        eb_acc(s, block, tc, 1)
        eb_comb(s, block, tc, 1)
        if tc + 1 < NQ:
            eb_prep_ok(s, block, tc + 1)
            mid1 = lambda tc=tc: eb_acc(s, block, tc + 1, 0,
                                        es=range(0, 4), bias=False)
            mid2 = lambda tc=tc: eb_acc(s, block, tc + 1, 0,
                                        es=range(4, 8), bias=True)
        else:
            mid1 = mid2 = None
        eb_lns(s, block, tc, mid1=mid1, mid2=mid2)


def emit_stage_B(s, block):
    emit_B_setup(s, block)
    emit_B_kv(s, block)
    for qc in range(NQ):
        emit_B_attn_qchunk(s, block, qc)
    emit_B_counts(s, block)
    emit_B_pos(s, block)


def emit_C_setup(s, block):
    @block.gpsimd
    def _(g):
        ch = s.sem("cCs")
        for ec in range(ND):
            ch.inc(g.dma_start(out=s.cqb_sb[:, ec:ec + 1],
                               in_=s.hbm["l1qb"][ec * 128:(ec + 1) * 128, :]),
                   16)
            ch.inc(g.dma_start(out=s.ckb_sb[:, ec:ec + 1],
                               in_=s.hbm["l1kb"][ec * 128:(ec + 1) * 128, :]),
                   16)
            ch.inc(g.dma_start(out=s.cvb_sb[:, ec:ec + 1],
                               in_=s.hbm["l1vb"][ec * 128:(ec + 1) * 128, :]),
                   16)
            ch.inc(g.dma_start(out=s.cob_sb[:, ec:ec + 1],
                               in_=s.hbm["l1ob"][ec * 128:(ec + 1) * 128, :]),
                   16)
        ch.inc(g.dma_start(out=s.p32_sb[:, :], in_=s.hbm["p32"]), 16)
        s.b_csetup.did_wr("cCs", ch.n)

    @block.vector
    def _(v):
        s.b_csetup.rd(v, s)
        _i = v.tensor_scalar_mul(s.cqb_sb[:, 0:8], s.cqb_sb[:, 0:8],
                                 1.0 / math.sqrt(HD))
        n = s.dv.inc(_i)
        s.b_csetup.did_rd("dv", n)
        s.b_csetup.w["dv"] = n


def emit_C_rope_tile(s, block, ps, psb, bias_col, scale, qc, out_t, out_buf,
                     out_col):
    @block.vector
    def _(v):
        psb.rd(v, s)
        s.b_csetup.rd(v, s)
        s.b_craw.wr(v, s)
        _i = v.tensor_scalar(s.craw_sb[:, 0:512], ps[:, 0:512], scale,
                             bias_col, ALU.mult, ALU.add)
        n = s.dv.inc(_i)
        s.b_craw.did_wr("dv", n)
        psb.did_rd("dv", n)
        s.b_csetup.did_rd("dv", n)

    @block.vector
    def _(v):
        s.b_craw.rd(v, s)
        s.b_crawb.wr(v, s)
        _i = v.tensor_copy(s.crawb_sb[:, 0:512], s.craw_sb[:, 0:512])
        n = s.dv.inc(_i)
        s.b_crawb.did_wr("dv", n)

    @block.tensor
    def _(t):
        s.b_crawb.rd(t, s)
        s.b_csetup.rd(t, s)
        s.b_pd.wr(t, s)
        _i = t.matmul(s.p_d[:, 0:512], s.p32_sb[:, :], s.crawb_sb[:, 0:512],
                      start=True, stop=True)
        n = s.mm.inc(_i)
        s.b_pd.did_wr("mm", n)
        s.b_crawb.did_rd("mm", n)

    @block.vector
    def _(v):
        s.b_craw.rd(v, s)
        s.b_ccs.rd(v, s)
        s.b_craw.wr(v, s)
        _i = v.tensor_mul(s.craw_sb[:, 0:512], s.craw_sb[:, 0:512],
                          s.cosb_sb[:, 0:512])
        n = s.dv.inc(_i)
        s.b_craw.did_wr("dv", n)
        s.b_pd.rd(v, s)
        s.b_crawb.wr(v, s)
        _i = v.tensor_mul(s.crawb_sb[:, 0:512], s.p_d[:, 0:512],
                          s.sinb_sb[:, 0:512])
        n = s.dv.inc(_i)
        s.b_crawb.did_wr("dv", n)
        s.b_pd.did_rd("dv", n)
        s.b_ccs.did_rd("dv", n)
        v.wait_ge(s.dv.h, n)
        out_buf.wr(v, s)
        _i = v.tensor_add(out_t[:, out_col:out_col + 512],
                          s.craw_sb[:, 0:512], s.crawb_sb[:, 0:512])
        n = s.dv.inc(_i)
        out_buf.did_wr("dv", n)
        s.b_craw.did_rd("dv", n)
        s.b_crawb.did_rd("dv", n)


def emit_stage_C(s, block):
    emit_C_setup(s, block)

    @block.vector
    def _(v):
        view = s.va.ap().rearrange("p (n x) -> p n x", x=65)
        _i = v.memset(view[:, :, 64:65], 1.0)
        n = s.dv.inc(_i)
        s.b_va.did_wr("dv", n)

    def load_cs(qc):
        @block.gpsimd
        def _(g, qc=qc):
            ch = s.sem("cCcs")
            s.b_ccs.wr(g, s)
            ch.inc(g.dma_start(out=s.cosb_sb[:, :],
                               in_=s.hbm["ropecos"][:, qc * 512:
                                                    (qc + 1) * 512]), 16)
            ch.inc(g.dma_start(out=s.sinb_sb[:, :],
                               in_=s.hbm["ropesin"][:, qc * 512:
                                                    (qc + 1) * 512]), 16)
            s.b_ccs.did_wr("cCcs", ch.n)

    emit_B_loadw(s, block, "l1kT", "cCw")
    for qc in range(NQ):
        emit_B_loadx(s, block, qc, hname="x2Tb")
        load_cs(qc)
        for ec in range(NP):
            psb = s.b_ps0 if ec % 2 == 0 else s.b_ps1
            ps = s.p_s0 if ec % 2 == 0 else s.p_s1

            @block.tensor
            def _(t, ec=ec, ps=ps, psb=psb):
                s.b_xcb.rd(t, s)
                s.b_w.rd(t, s)
                psb.wr(t, s)
                for dc in range(ND):
                    _i = t.matmul(ps[:, 0:512],
                                  s.w_sb[:, dc * 1024 + ec * 128:
                                         dc * 1024 + (ec + 1) * 128],
                                  s.xcb[:, dc * 512:(dc + 1) * 512],
                                  start=(dc == 0), stop=(dc == ND - 1))
                n = s.mm.inc(_i)
                psb.did_wr("mm", n)
                s.b_xcb.did_rd("mm", n)
                s.b_w.did_rd("mm", n)

            emit_C_rope_tile(s, block, ps, psb, s.ckb_sb[:, ec:ec + 1], 1.0,
                             qc, s.kt, s.b_kt,
                             ec * T + qc * 512)

    emit_B_loadw(s, block, "l1vT", "cCw")
    for qc in range(NQ):
        emit_B_loadx(s, block, qc, hname="x2Tb")
        for tt in range(4):
            for eh in range(2):
                @block.tensor
                def _(t, tt=tt, eh=eh):
                    s.b_xcb.rd(t, s)
                    s.b_w.rd(t, s)
                    s.b_pc.wr(t, s)
                    for dc in range(ND):
                        _i = t.matmul(
                            s.p_c[:, 0:512],
                            s.xcb[:, dc * 512 + tt * 128:
                                  dc * 512 + (tt + 1) * 128],
                            s.w_sb[:, dc * 1024 + eh * 512:
                                   dc * 1024 + (eh + 1) * 512],
                            start=(dc == 0), stop=(dc == ND - 1))
                    n = s.mm.inc(_i)
                    s.b_pc.did_wr("mm", n)
                    s.b_xcb.did_rd("mm", n)
                    s.b_w.did_rd("mm", n)

                @block.vector
                def _(v, tt=tt, eh=eh, qc=qc):
                    s.b_pc.rd(v, s)
                    s.b_va.wr(v, s)
                    view = s.va.ap().rearrange("p (n x) -> p n x", x=65)
                    src = s.p_c.ap().rearrange("p (h x) -> p h x", x=64)
                    til = qc * 4 + tt
                    _i = v.tensor_copy(
                        view[:, til * 16 + eh * 8:til * 16 + (eh + 1) * 8,
                             0:64], src[:, 0:8, :])
                    n = s.dv.inc(_i)
                    s.b_va.did_wr("dv", n)
                    s.b_pc.did_rd("dv", n)

    def q_pack(qc):
        emit_B_loadw(s, block, "l1qT", "cCw")
        emit_B_loadx(s, block, qc, hname="x2Tb")
        load_cs(qc)
        for ec in range(NP):
            psb = s.b_ps0 if ec % 2 == 0 else s.b_ps1
            ps = s.p_s0 if ec % 2 == 0 else s.p_s1

            @block.tensor
            def _(t, ec=ec, ps=ps, psb=psb):
                s.b_xcb.rd(t, s)
                s.b_w.rd(t, s)
                psb.wr(t, s)
                for dc in range(ND):
                    _i = t.matmul(ps[:, 0:512],
                                  s.w_sb[:, dc * 1024 + ec * 128:
                                         dc * 1024 + (ec + 1) * 128],
                                  s.xcb[:, dc * 512:(dc + 1) * 512],
                                  start=(dc == 0), stop=(dc == ND - 1))
                n = s.mm.inc(_i)
                psb.did_wr("mm", n)
                s.b_xcb.did_rd("mm", n)
                s.b_w.did_rd("mm", n)

            emit_C_rope_tile(s, block, ps, psb, s.cqb_sb[:, ec:ec + 1],
                             1.0 / math.sqrt(HD), qc, s.qt, s.b_qt, ec * 512)

    q_pack(0)
    for qc in range(NQ):
        pend = [None]
        for ec in range(NP):
            emit_attn_pair(
                s, block, list(range(NT)),
                qtA=s.qt[0:64, ec * 512:(ec + 1) * 512],
                qtB=s.qt[64:128, ec * 512:(ec + 1) * 512],
                qt_buf=s.b_qt,
                kA_fn=lambda j, ec=ec: s.kt[0:64, ec * T + j * 128:
                                            ec * T + (j + 1) * 128],
                kB_fn=lambda j, ec=ec: s.kt[64:128, ec * T + j * 128:
                                            ec * T + (j + 1) * 128],
                ktbuf_fn=lambda j: s.b_kt,
                vaA_fn=lambda j, ec=ec: s.va_view(j, 2 * ec),
                vaB_fn=lambda j, ec=ec: s.va_view(j, 2 * ec + 1),
                vabuf_fn=lambda j: s.b_va, parity=ec % 2)
            vw = av_views(s, ec % 2)

            def fin(ec=ec, vw=vw):
                emit_attn_norm_pre(s, block, vw)

                @block.vector
                def _(v, ec=ec, vw=vw):
                    for bb in vw["bufs"]:
                        bb.rd(v, s)
                    s.b_ct1.wr(v, s)
                    s.b_aB.rd(v, s)
                    _i = v.tensor_mul(s.ct1_sb[0:64, 0:512], vw["valA"],
                                      s.a_aB[0:64, 0:512])
                    n = s.dv.inc(_i)
                    s.b_bB.rd(v, s)
                    _i = v.tensor_mul(s.ct1_sb[64:128, 0:512], vw["valB"],
                                      s.a_bB[0:64, 0:512])
                    n = s.dv.inc(_i)
                    for bb in vw["bufs"]:
                        bb.did_rd("dv", n)
                    s.b_aB.did_rd("dv", n)
                    s.b_bB.did_rd("dv", n)
                    s.b_ct1.did_wr("dv", n)
                    v.wait_ge(s.dv.h, n)
                    s.b_csetup.rd(v, s)
                    s.b_xcb.wr(v, s)
                    _i = v.tensor_scalar_add(
                        s.xcb[:, ec * 512:(ec + 1) * 512],
                        s.ct1_sb[:, 0:512], s.cvb_sb[:, ec:ec + 1])
                    n = s.dv.inc(_i)
                    s.b_xcb.did_wr("dv", n)
                    s.b_ct1.did_rd("dv", n)
                    s.b_csetup.did_rd("dv", n)

            if pend[0] is not None:
                pend[0]()
            pend[0] = fin

        pend[0]()

        emit_B_loadw(s, block, "l1oT", "cCw")
        for ec in range(ND):
            psb = s.b_ps0 if ec % 2 == 0 else s.b_ps1
            ps = s.p_s0 if ec % 2 == 0 else s.p_s1

            @block.sync
            def _(sy, ec=ec, qc=qc):
                ch = s.sem("cCxf0")
                s.b_xf[0].wr(sy, s)
                n = ch.inc(sy.dma_start(
                    out=s.a_xf[:, 0:512],
                    in_=s.hbm["x2Tf"][ec * 128:(ec + 1) * 128,
                                      qc * 512:(qc + 1) * 512]), 16)
                s.b_xf[0].did_wr("cCxf0", n)

            @block.tensor
            def _(t, ec=ec, ps=ps, psb=psb):
                s.b_xcb.rd(t, s)
                s.b_w.rd(t, s)
                psb.wr(t, s)
                for dc in range(ND):
                    _i = t.matmul(ps[:, 0:512],
                                  s.w_sb[:, dc * 1024 + ec * 128:
                                         dc * 1024 + (ec + 1) * 128],
                                  s.xcb[:, dc * 512:(dc + 1) * 512],
                                  start=(dc == 0), stop=(dc == ND - 1))
                n = s.mm.inc(_i)
                psb.did_wr("mm", n)
                s.b_xcb.did_rd("mm", n)
                s.b_w.did_rd("mm", n)

            @block.vector
            def _(v, ec=ec, ps=ps, psb=psb):
                psb.rd(v, s)
                s.b_xf[0].rd(v, s)
                s.b_csetup.rd(v, s)
                s.b_r.wr(v, s)
                _i = v.scalar_tensor_tensor(
                    s.a_r[:, ec * 512:(ec + 1) * 512], ps[:, 0:512],
                    s.cob_sb[:, ec:ec + 1],
                    s.a_xf[:, 0:512],
                    ALU.add, ALU.add)
                n = s.dv.inc(_i)
                psb.did_rd("dv", n)
                s.b_xf[0].did_rd("dv", n)
                s.b_csetup.did_rd("dv", n)
                s.b_r.did_wr("dv", n)

        emit_ln_store(s, block, s.b_r, s.a_r, s.hbm["x3Tf"], s.hbm["x3Tb"],
                      qc=qc, ch_of="cCof", ch_ob="cCob",
                      mid_cb=(lambda qc=qc: q_pack(qc + 1))
                      if qc + 1 < NQ else None)


def emit_stage_D(s, block):
    @block.gpsimd
    def _(g):
        ch = s.sem("cDx")
        s.b_x3b.wr(g, s)
        for dc in range(ND):
            for qcq in range(NQ):
                ch.inc(g.dma_start(
                    out=s.x3b_sb[:, dc * T + qcq * 512:dc * T +
                                 (qcq + 1) * 512],
                    in_=s.hbm["x3Tb"][dc * 128:(dc + 1) * 128,
                                      qcq * 512:(qcq + 1) * 512]), 16)
        s.b_x3b.did_wr("cDx", ch.n)

    @block.gpsimd
    def _(g):
        ch = s.sem("cDs")
        ch.inc(g.dma_start(out=s.cent_sb[:, :], in_=s.hbm["l1cent"]), 16)
        ch.inc(g.dma_start(out=s.temp_sb[:, :], in_=s.hbm["l1temp"]), 16)
        ch.inc(g.dma_start(out=s.iota16_sb[:, :], in_=s.hbm["iota16"]), 16)
        ch.inc(g.dma_start(out=s.didf_sb[:, :], in_=s.hbm["identf"]), 16)
        ch.inc(g.dma_start(out=s.didb_sb[:, :], in_=s.hbm["identb"]), 16)
        for dc in range(ND):
            ch.inc(g.dma_start(
                out=s.wr_sb[:, dc * 128:(dc + 1) * 128],
                in_=s.hbm["l1wrT"][dc * 128:(dc + 1) * 128, :]), 16)
        s.b_dsetup.did_wr("cDs", ch.n)

    @block.scalar
    def _(sc):
        s.b_dsetup.rd(sc, s)
        s.b_csq.wr(sc, s)
        _i = sc.activation(s.csq_sb[:, :], s.cent_sb[:, :], AF.Square)
        n = s.act.inc(_i)
        s.b_csq.did_wr("act", n)
        s.b_dsetup.did_rd("act", n)

    @block.vector
    def _(v):
        s.b_csq.rd(v, s)
        s.b_cn.wr(v, s)
        _i = v.reduce_sum(s.cw_sb[0:16, 0:1], s.csq_sb[:, :], axis=AX.X)
        n = s.dv.inc(_i)
        s.b_cn.did_wr("dv", n)
        s.b_csq.did_rd("dv", n)

    @block.scalar
    def _(sc):
        s.b_cn.rd(sc, s)
        s.b_cn.wr(sc, s)
        _i = sc.activation(s.cw_sb[0:16, 0:1], s.cw_sb[0:16, 0:1], AF.Sqrt)
        n = s.act.inc(_i)
        s.b_cn.did_wr("act", n)

    @block.vector
    def _(v):
        s.b_cn.rd(v, s)
        s.b_cn.wr(v, s)
        _i = v.tensor_scalar_max(s.cw_sb[0:16, 0:1], s.cw_sb[0:16, 0:1],
                                 1e-12)
        n = s.dv.inc(_i)
        v.wait_ge(s.dv.h, n)
        _i = v.reciprocal(s.cw_sb[0:16, 1:2], s.cw_sb[0:16, 0:1])
        n = s.dv.inc(_i)
        s.b_cn.did_wr("dv", n)
        v.wait_ge(s.dv.h, n)
        s.b_dsetup.rd(v, s)
        s.b_cnm.wr(v, s)
        _i = v.tensor_scalar_mul(s.cnm_sb[:, :], s.cent_sb[:, :],
                                 s.cw_sb[0:16, 1:2])
        n = s.dv.inc(_i)
        s.b_cnm.did_wr("dv", n)
        s.b_cn.did_rd("dv", n)
        s.b_dsetup.did_rd("dv", n)


    @block.tensor
    def _(t):
        s.b_cnm.rd(t, s)
        s.b_dsetup.rd(t, s)
        s.b_pc.wr(t, s)
        _i = t.transpose(s.p_c[0:128, 0:16], s.cnm_sb[:, :],
                         s.didf_sb[0:16, 0:16])
        n = s.mm.inc(_i)
        s.b_pc.did_wr("mm", n)
        s.b_cnm.did_rd("mm", n)

    @block.vector
    def _(v):
        s.b_pc.rd(v, s)
        s.b_cnT.wr(v, s)
        _i = v.tensor_copy(s.cnT_sb[:, :], s.p_c[0:128, 0:16])
        n = s.dv.inc(_i)
        s.b_cnT.did_wr("dv", n)
        s.b_pc.did_rd("dv", n)

    @block.scalar
    def _(sc):
        s.b_dsetup.rd(sc, s)
        s.b_stemp.wr(sc, s)
        _i = sc.activation(s.stemp_sb[:, :], s.temp_sb[:, :], AF.Exp)
        n = s.act.inc(_i)
        s.b_stemp.did_wr("act", n)

    @block.vector
    def _(v):
        s.b_stemp.rd(v, s)
        s.b_stemp.wr(v, s)
        _i = v.tensor_scalar_mul(s.stemp_sb[:, :], s.stemp_sb[:, :],
                                 1.0 / math.sqrt(GS))
        n = s.dv.inc(_i)
        s.b_stemp.did_wr("dv", n)

    for tc in range(NQ):
        @block.tensor
        def _(t, tc=tc):
            s.b_x3b.rd(t, s)
            s.b_dsetup.rd(t, s)
            s.b_pc.wr(t, s)
            for dc in range(ND):
                _i = t.matmul(s.p_c[:, 0:512],
                              s.wr_sb[:, dc * 128:(dc + 1) * 128],
                              s.x3b_sb[:, dc * T + tc * 512:dc * T +
                                       (tc + 1) * 512],
                              start=(dc == 0), stop=(dc == ND - 1))
            n = s.mm.inc(_i)
            s.b_pc.did_wr("mm", n)
            s.b_x3b.did_rd("mm", n)
            s.b_dsetup.did_rd("mm", n)

        @block.vector
        def _(v, tc=tc):
            s.b_pc.rd(v, s)
            s.b_zf.wr(v, s)
            _i = v.tensor_copy(s.zf_sb[:, 0:512], s.p_c[:, 0:512])
            n = s.dv.inc(_i)
            s.b_zf.did_wr("dv", n)
            s.b_pc.did_rd("dv", n)
            v.wait_ge(s.dv.h, n)
            s.b_zb.wr(v, s)
            _i = v.tensor_copy(s.zb_sb[:, 0:512], s.zf_sb[:, 0:512])
            n = s.dv.inc(_i)
            s.b_zb.did_wr("dv", n)
            v.wait_ge(s.dv.h, n)
            s.b_zsq.wr(v, s)
            _i = v.tensor_mul(s.zsq_sb[:, 0:512], s.zb_sb[:, 0:512],
                              s.zb_sb[:, 0:512])
            n = s.dv.inc(_i)
            s.b_zsq.did_wr("dv", n)

        @block.tensor
        def _(t, tc=tc):
            s.b_zsq.rd(t, s)
            s.b_pd.wr(t, s)
            _i = t.matmul(s.p_d[0:1, 0:512], s.ones128b[:, :],
                          s.zsq_sb[:, 0:512], start=True, stop=True)
            n = s.mm.inc(_i)
            s.b_pd.did_wr("mm", n)
            s.b_zsq.did_rd("mm", n)

        @block.scalar
        def _(sc, tc=tc):
            s.b_pd.rd(sc, s)
            s.b_rowm.wr(sc, s)
            _i = sc.activation(s.row_m[:, 0:512], s.p_d[0:1, 0:512], AF.Sqrt)
            n = s.act.inc(_i)
            s.b_rowm.did_wr("act", n)
            s.b_pd.did_rd("act", n)

        @block.vector
        def _(v, tc=tc):
            s.b_rowm.rd(v, s)
            s.b_rowm.wr(v, s)
            _i = v.tensor_scalar_max(s.row_m[:, 0:512], s.row_m[:, 0:512],
                                     1e-12)
            n = s.dv.inc(_i)
            v.wait_ge(s.dv.h, n)
            s.b_rowa.wr(v, s)
            _i = emit_recip_row(s, v, s.row_a[:, 0:512], s.row_m[:, 0:512])
            n = s.dv.inc(_i)
            s.b_rowa.did_wr("dv", n)
            s.b_rowm.did_rd("dv", n)
            v.wait_ge(s.dv.h, n)
            s.b_stemp.rd(v, s)
            _i = v.tensor_scalar_mul(s.row_a[:, 0:512], s.row_a[:, 0:512],
                                     s.stemp_sb[0:1, 0:1])
            n = s.dv.inc(_i)
            s.b_rowa.did_wr("dv", n)
            s.b_stemp.did_rd("dv", n)

        @block.tensor
        def _(t, tc=tc):
            s.b_rowa.rd(t, s)
            s.b_pd.wr(t, s)
            _i = t.matmul(s.p_d[:, 0:512], s.ones1_128f[:, :],
                          s.row_a[:, 0:512], start=True, stop=True)
            n = s.mm.inc(_i)
            s.b_pd.did_wr("mm", n)
            s.b_rowa.did_rd("mm", n)

        @block.vector
        def _(v, tc=tc):
            s.b_pd.rd(v, s)
            s.b_zf.rd(v, s)
            s.b_znb.wr(v, s)
            _i = v.tensor_mul(s.znb_sb[:, 0:512], s.zf_sb[:, 0:512],
                              s.p_d[:, 0:512])
            n = s.dv.inc(_i)
            s.b_znb.did_wr("dv", n)
            s.b_pd.did_rd("dv", n)
            s.b_zf.did_rd("dv", n)

        for tt in range(4):
            j = tc * 4 + tt

            @block.tensor
            def _(t, tt=tt):
                s.b_znb.rd(t, s)
                s.b_cnT.rd(t, s)
                s.b_pc.wr(t, s)
                _i = t.matmul(s.p_c[:, 0:16],
                              s.znb_sb[:, tt * 128:(tt + 1) * 128],
                              s.cnT_sb[:, :], start=True, stop=True)
                n = s.mm.inc(_i)
                s.b_pc.did_wr("mm", n)
                s.b_znb.did_rd("mm", n)
                s.b_cnT.did_rd("mm", n)

            @block.vector
            def _(v, j=j):
                s.b_pc.rd(v, s)
                s.b_lg16.wr(v, s)
                _i = v.tensor_copy(s.lg16_sb[:, :], s.p_c[:, 0:16])
                n = s.dv.inc(_i)
                s.b_lg16.did_wr("dv", n)
                s.b_pc.did_rd("dv", n)
                v.wait_ge(s.dv.h, n)
                s.b_mx16.wr(v, s)
                _i = v.max(s.mx16_sb[:, :], s.lg16_sb[:, :])
                n = s.dv.inc(_i)
                s.b_mx16.did_wr("dv", n)
                v.wait_ge(s.dv.h, n)
                s.b_ln.wr(v, s)
                _i = v.tensor_scalar_mul(s.ln_sb[:, 0:1], s.mx16_sb[:, 0:1],
                                         -1.0)
                n = s.dv.inc(_i)
                s.b_ln.did_wr("dv", n)

            @block.scalar
            def _(sc, j=j):
                s.b_lg16.rd(sc, s)
                s.b_ln.rd(sc, s)
                s.b_pex16.wr(sc, s)
                _i = sc.activation(s.pex16_sb[:, :], s.lg16_sb[:, :], AF.Exp,
                                   bias=s.ln_sb[:, 0:1])
                n = s.act.inc(_i)
                s.b_pex16.did_wr("act", n)
                s.b_lg16.did_rd("act", n)
                s.b_ln.did_rd("act", n)

            @block.vector
            def _(v, j=j):
                s.b_pex16.rd(v, s)
                s.b_zz.wr(v, s)
                _i = v.reduce_sum(s.zz_sb[:, 0:1], s.pex16_sb[:, :],
                                  axis=AX.X)
                n = s.dv.inc(_i)
                s.b_zz.did_wr("dv", n)
                v.wait_ge(s.dv.h, n)
                _i = v.reciprocal(s.zz_sb[:, 1:2], s.zz_sb[:, 0:1])
                n = s.dv.inc(_i)
                s.b_zz.did_wr("dv", n)
                v.wait_ge(s.dv.h, n)
                s.b_pr16.wr(v, s)
                _i = v.tensor_scalar_mul(s.pr16_sb[:, :], s.pex16_sb[:, :],
                                         s.zz_sb[:, 1:2])
                n = s.dv.inc(_i)
                s.b_pr16.did_wr("dv", n)
                s.b_pex16.did_rd("dv", n)
                s.b_zz.did_rd("dv", n)
                v.wait_ge(s.dv.h, n)
                s.b_mx16.rd(v, s)
                s.b_lg16.rd(v, s)
                s.b_mi16.wr(v, s)
                _i = v.max_index(s.mi16_sb[:, :], s.mx16_sb[:, :],
                                 s.lg16_sb[:, :])
                n = s.dv.inc(_i)
                s.b_mi16.did_wr("dv", n)
                s.b_mx16.did_rd("dv", n)
                s.b_lg16.did_rd("dv", n)
                v.wait_ge(s.dv.h, n)
                s.b_i0.wr(v, s)
                _i = v.tensor_copy(s.i0_sb[:, 0:1], s.mi16_sb[:, 0:1])
                n = s.dv.inc(_i)
                _i = v.tensor_copy(s.i0_sb[:, 1:2], s.mi16_sb[:, 1:2])
                n = s.dv.inc(_i)
                s.b_i0.did_wr("dv", n)
                s.b_mi16.did_rd("dv", n)
                v.wait_ge(s.dv.h, n)
                s.b_dsetup.rd(v, s)
                s.b_eq.wr(v, s)
                _i = v.tensor_single_scalar(s.eq_sb[:, 0:16],
                                            s.iota16_sb[:, :],
                                            s.i0_sb[:, 0:1], ALU.is_equal)
                n = s.dv.inc(_i)
                _i = v.tensor_single_scalar(s.eq_sb[:, 16:32],
                                            s.iota16_sb[:, :],
                                            s.i0_sb[:, 1:2], ALU.is_equal)
                n = s.dv.inc(_i)
                s.b_eq.did_wr("dv", n)
                s.b_i0.did_rd("dv", n)
                v.wait_ge(s.dv.h, n)
                _i = v.tensor_add(s.eq_sb[:, 0:16], s.eq_sb[:, 0:16],
                                  s.eq_sb[:, 16:32])
                n = s.dv.inc(_i)
                s.b_eq.did_wr("dv", n)
                v.wait_ge(s.dv.h, n)
                s.b_pr16.rd(v, s)
                s.b_gt.wr(v, s)
                _i = v.tensor_mul(s.gt_sb[:, 0:16], s.pr16_sb[:, :],
                                  s.eq_sb[:, 0:16])
                n = s.dv.inc(_i)
                s.b_gt.did_wr("dv", n)
                s.b_pr16.did_rd("dv", n)
                s.b_eq.did_rd("dv", n)

            @block.tensor
            def _(t, j=j):
                s.b_gt.rd(t, s)
                s.b_dsetup.rd(t, s)
                s.b_pd.wr(t, s)
                _i = t.transpose(s.p_d[0:16, 0:128], s.gt_sb[:, 0:16],
                                 s.didf_sb[:, :])
                n = s.mm.inc(_i)
                s.b_pd.did_wr("mm", n)
                s.b_gt.did_rd("mm", n)

            @block.vector
            def _(v, j=j):
                s.b_pd.rd(v, s)
                s.b_gT.wr(v, s)
                _i = v.tensor_copy(s.gT_sb[0:16, j * 128:(j + 1) * 128],
                                   s.p_d[0:16, 0:128])
                n = s.dv.inc(_i)
                s.b_gT.did_wr("dv", n)
                s.b_pd.did_rd("dv", n)

    for nh in range(H):
        islot = nh % 2

        @block.scalar
        def _(sc, nh=nh, islot=islot):
            ch = s.sem("cDie%d" % islot)
            s.b_ie[islot].wr(sc, s)
            for dc in range(ND):
                ch.inc(sc.dma_start(
                    out=s.ie_sb[islot][:, dc * 512:(dc + 1) * 512],
                    in_=s.hbm["l1IE"][nh, dc * 128:(dc + 1) * 128, :]), 16)
            s.b_ie[islot].did_wr("cDie%d" % islot, ch.n)
            s.b_oe[islot].wr(sc, s)
            for hc in range(4):
                ch.inc(sc.dma_start(
                    out=s.oe_sb[islot][:, hc * 64:(hc + 1) * 64],
                    in_=s.hbm["l1OE"][nh, hc * 128:(hc + 1) * 128, :]), 16)
            s.b_oe[islot].did_wr("cDie%d" % islot, ch.n)

        for tc in range(NQ):
            @block.gpsimd
            def _(g, nh=nh, tc=tc):
                ch = s.sem("cDgr")
                s.b_gT.rd(g, s)
                s.b_gr.wr(g, s)
                n = ch.inc(g.dma_start(
                    out=s.gr_sb[0:1, 0:512],
                    in_=s.gT_sb[nh:nh + 1, tc * 512:(tc + 1) * 512]), 16)
                s.b_gr.did_wr("cDgr", n)
                s.b_gT.did_rd("cDgr", n)

            for hc in range(4):
                ps = s.p_s0 if hc % 2 == 0 else s.p_s1
                psb = s.b_ps0 if hc % 2 == 0 else s.b_ps1

                @block.tensor
                def _(t, nh=nh, tc=tc, hc=hc, ps=ps, psb=psb, islot=islot):
                    s.b_ie[islot].rd(t, s)
                    s.b_x3b.rd(t, s)
                    psb.wr(t, s)
                    for dc in range(ND):
                        _i = t.matmul(
                            ps[:, 0:512],
                            s.ie_sb[islot][:, dc * 512 + hc * 128:
                                           dc * 512 + (hc + 1) * 128],
                            s.x3b_sb[:, dc * T + tc * 512:dc * T +
                                     (tc + 1) * 512],
                            start=(dc == 0), stop=(dc == ND - 1))
                    n = s.mm.inc(_i)
                    psb.did_wr("mm", n)
                    s.b_ie[islot].did_rd("mm", n)
                    s.b_x3b.did_rd("mm", n)

                @block.scalar
                def _(sc, hc=hc, ps=ps, psb=psb):
                    psb.rd(sc, s)
                    s.b_hb.wr(sc, s)
                    _i = sc.activation(s.hb_sb[:, hc * 512:(hc + 1) * 512],
                                       ps[:, 0:512], AF.Gelu)
                    n = s.act.inc(_i)
                    s.b_hb.did_wr("act", n)
                    psb.did_rd("act", n)

            @block.tensor
            def _(t, nh=nh, tc=tc, islot=islot):
                s.b_hb.rd(t, s)
                s.b_oe[islot].rd(t, s)
                s.b_pg.wr(t, s)
                for hc in range(4):
                    _i = t.matmul(s.p_g[0:64, 0:512],
                                  s.oe_sb[islot][:, hc * 64:(hc + 1) * 64],
                                  s.hb_sb[:, hc * 512:(hc + 1) * 512],
                                  start=(hc == 0), stop=(hc == 3))
                n = s.mm.inc(_i)
                s.b_pg.did_wr("mm", n)
                s.b_hb.did_rd("mm", n)
                s.b_oe[islot].did_rd("mm", n)
                s.b_gr.rd(t, s)
                s.b_pc.wr(t, s)
                _i = t.matmul(s.p_c[0:64, 0:512], s.ones1_64f[:, :],
                              s.gr_sb[0:1, 0:512], start=True, stop=True)
                n = s.mm.inc(_i)
                s.b_pc.did_wr("mm", n)
                s.b_gr.did_rd("mm", n)

            @block.vector
            def _(v, nh=nh, tc=tc):
                s.b_pc.rd(v, s)
                s.b_gB.wr(v, s)
                _i = v.tensor_copy(s.gB_sb[0:64, 0:512], s.p_c[0:64, 0:512])
                n = s.dv.inc(_i)
                s.b_gB.did_wr("dv", n)
                s.b_pc.did_rd("dv", n)
                v.wait_ge(s.dv.h, n)
                s.b_pg.rd(v, s)
                s.b_ct1.wr(v, s)
                _i = v.tensor_mul(s.ct1_sb[0:64, 0:512], s.p_g[0:64, 0:512],
                                  s.gB_sb[0:64, 0:512])
                n = s.dv.inc(_i)
                s.b_ct1.did_wr("dv", n)
                s.b_pg.did_rd("dv", n)
                s.b_gB.did_rd("dv", n)

            @block.sync
            def _(sy, nh=nh, tc=tc):
                ch = s.sem("cDy4")
                s.b_ct1.rd(sy, s)
                n = ch.inc(sy.dma_start(
                    out=s.hbm["scr_y4"][nh * 64:(nh + 1) * 64,
                                        tc * 512:(tc + 1) * 512],
                    in_=s.ct1_sb[0:64, 0:512]), 16)
                s.b_ct1.did_rd("cDy4", n)

    for tc in range(NQ):
        @block.gpsimd
        def _(g, tc=tc):
            ch = s.sem("cDy4r")
            s.b_r.wr(g, s)
            for ec in range(ND):
                ch.inc(g.dma_start(
                    out=s.a_r[:, ec * 512:(ec + 1) * 512],
                    in_=s.hbm["scr_y4"][ec * 128:(ec + 1) * 128,
                                        tc * 512:(tc + 1) * 512]), 16)
            s.b_r.did_wr("cDy4r", ch.n)

        for ec in range(ND):
            @block.sync
            def _(sy, ec=ec, tc=tc):
                ch = s.sem("cDxf%d" % (ec % 2))
                s.b_xf[ec % 2].wr(sy, s)
                n = ch.inc(sy.dma_start(
                    out=s.a_xf[:, (ec % 2) * 512:((ec % 2) + 1) * 512],
                    in_=s.hbm["x3Tf"][ec * 128:(ec + 1) * 128,
                                      tc * 512:(tc + 1) * 512]), 16)
                s.b_xf[ec % 2].did_wr("cDxf%d" % (ec % 2), n)

            @block.vector
            def _(v, ec=ec):
                s.b_xf[ec % 2].rd(v, s)
                s.b_r.rd(v, s)
                s.b_r.wr(v, s)
                _i = v.tensor_add(s.a_r[:, ec * 512:(ec + 1) * 512],
                                  s.a_r[:, ec * 512:(ec + 1) * 512],
                                  s.a_xf[:, (ec % 2) * 512:
                                         ((ec % 2) + 1) * 512])
                n = s.dv.inc(_i)
                s.b_r.did_wr("dv", n)
                s.b_xf[ec % 2].did_rd("dv", n)

        emit_ln_store(s, block, s.b_r, s.a_r, s.hbm["xoutT"], None,
                      qc=tc, ch_of="cDof")


def build_program(stages):
    from contextlib import ExitStack
    cst = _consts()
    nc = bass.Bass("TRN2", target_bir_lowering=False)
    s = S()
    s.nc = nc
    s.cst = cst
    s.hbm = {}
    s.innames = set()

    def din(name, shape, dt=F32):
        s.hbm[name] = nc.dram_tensor(name, list(shape), dt,
                                     kind="ExternalInput").ap()
        s.innames.add(name)

    def dout(name, shape, dt=F32):
        s.hbm[name] = nc.dram_tensor(name, list(shape), dt,
                                     kind="ExternalOutput").ap()

    def dscr(name, shape, dt, produced, consumed):
        if produced in stages and consumed in stages:
            s.hbm[name] = nc.dram_tensor("scr_" + name, list(shape), dt).ap()
        elif produced in stages:
            dout(name, shape, dt)
        elif consumed in stages:
            din(name, shape, dt)

    if "A" in stages:
        din("x0T", (C, T))
    dscr("x1Tf", (C, T), F32, "A", "B")
    dscr("x1Tb", (C, T), BF16, "A", "B")
    dscr("hTf", (C, T), F32, "B", "B")
    dscr("hTb", (C, T), BF16, "B", "B")
    dscr("x2Tf", (C, T), F32, "B", "C")
    dscr("x2Tb", (C, T), BF16, "B", "C")
    dscr("x3Tf", (C, T), F32, "C", "D")
    dscr("x3Tb", (C, T), BF16, "C", "D")
    if "D" in stages:
        dout("xoutT", (C, T))
    if "D" in stages:
        s.hbm["scr_y4"] = nc.dram_tensor("scr_y4", [C, T], F32).ap()
    if "B" in stages:
        s.hbm["cc_in"] = nc.dram_tensor("cc_in", [8, 8], F32).ap()
        s.hbm["cc_out"] = nc.dram_tensor("cc_out", [8, 8], F32).ap()
        s.hbm["scr_tot"] = nc.dram_tensor("scr_tot", [1, 256], F32).ap()
        s.hbm["scr_row"] = nc.dram_tensor("scr_row", [1, 256], F32).ap()

    for name, arr in cst.items():
        dt = BF16 if arr.dtype == _BF else F32
        din(name, arr.shape, dt)
    din("rankcol", (1, 8))
    din("rankmask", (8, 1))

    if "A" in stages:
        for n in ["l0aqT", "l0akT", "l0avT", "l0aoT"]:
            din(n, (C, C))
    if "B" in stages:
        for n in ["l0fqT", "l0fkT", "l0fvT"]:
            din(n, (C, C))
        din("l0fbq", (C, 1)); din("l0fbk", (C, 1)); din("l0fbv", (C, 1))
        din("l0fWeT", (E, C, C), BF16); din("l0fbe", (E, C))
        din("l0fswT", (C, E)); din("l0fbsw", (1, E))
        din("l0fg", (C, 1)); din("l0fb", (C, 1))
    if "C" in stages:
        for n in ["l1qT", "l1kT", "l1vT", "l1oT"]:
            din(n, (C, C))
        for n in ["l1qb", "l1kb", "l1ob"]:
            din(n, (C, 1))
        din("l1vb", (C, 1))
    if "D" in stages:
        din("l1wrT", (C, GS)); din("l1cent", (H, GS)); din("l1temp", (1, 1))
        din("l1IE", (H, C, HS), BF16); din("l1OE", (H, HS, HD), BF16)

    with ExitStack() as ctx:
        e = ctx.enter_context
        s.p_s0 = e(nc.psum_tensor("p_s0", [128, 1024], F32))
        s.p_s1 = e(nc.psum_tensor("p_s1", [128, 1024], F32))
        s.p_g = e(nc.psum_tensor("p_g", [128, 1024], F32))
        s.p_c = e(nc.psum_tensor("p_c", [128, 512], F32))
        s.p_d = e(nc.psum_tensor("p_d", [128, 512], F32))

        s.sems = {}
        s._ctx = ctx
        s.mm = s.sem("mm")
        s.act = s.sem("act")
        s.dv = s.sem("dv")
        s.gp = s.sem("gp")
        s.cc = s.sem("cc")
        s.bar = s.sem("bar")
        s.barn = 0

        s.ones1_64f = e(nc.sbuf_tensor("ones1_64f", [1, 64], F32))
        s.ones1_128f = e(nc.sbuf_tensor("ones1_128f", [1, 128], F32))
        s.ones1_128b = e(nc.sbuf_tensor("ones1_128b", [1, 128], BF16))
        s.ones128b = e(nc.sbuf_tensor("ones128b", [128, 1], BF16))
        s.ones128f = e(nc.sbuf_tensor("ones128f", [128, 1], F32))
        s.row_a = e(nc.sbuf_tensor("row_a", [1, 512], F32))
        s.row_b = e(nc.sbuf_tensor("row_b", [1, 512], F32))
        s.row_m = e(nc.sbuf_tensor("row_m", [1, 512], F32))
        s.row_q = e(nc.sbuf_tensor("row_q", [1, 512], F32))
        s.epsT = e(nc.sbuf_tensor("epsT", [1, 1], F32))
        s.row_s = s.row_q
        s.a_rsq2 = e(nc.sbuf_tensor("a_rsq2", [128, 1024], BF16))
        s.a_aB = e(nc.sbuf_tensor("a_aB", [128, 512], F32))
        s.a_bB = e(nc.sbuf_tensor("a_bB", [128, 512], F32))
        s.a_o1 = e(nc.sbuf_tensor("a_o1", [128, 1024], F32))

        for nm in ("b_ps0 b_ps1 b_pc b_pd b_pg b_rowa b_rowb "
                   "b_rowm b_rowq b_rowv b_rowsd b_aB b_bB b_o1 "
                   "b_rdb b_ones").split():
            setattr(s, nm, Buf())
        s.b_rsq2 = [Buf(), Buf()]
        s.b_o12 = [Buf(), Buf()]

        block = FakeBlock()
        if True:
            @block.vector
            def _(v):
                v.memset(s.ones1_64f[:, :], 1.0)
                v.memset(s.ones1_128f[:, :], 1.0)
                v.memset(s.ones1_128b[:, :], 1.0)
                v.memset(s.ones128b[:, :], 1.0)
                v.memset(s.ones128f[:, :], 1.0)
                _i = v.memset(s.epsT[:, :], EPS)
                n = s.dv.inc(_i)
                s.b_ones.did_wr("dv", n)

            @block.tensor
            def _(t):
                t.wait_ge(s.dv.h, s.dv.n)

            @block.scalar
            def _(sc):
                sc.wait_ge(s.dv.h, s.dv.n)

            if "A" in stages:
                with ExitStack() as actx:
                    ea = actx.enter_context
                    s.a_w = [ea(nc.sbuf_tensor(f"a_w{i}", [128, ND * 1024],
                                               BF16)) for i in range(4)]
                    s.a_xbf = ea(nc.sbuf_tensor("a_xbf", [128, ND * 512],
                                                BF16))
                    s.a_qt = ea(nc.sbuf_tensor("a_qt", [128, NP * 512], BF16))
                    s.a_ktw = ea(nc.sbuf_tensor("a_ktw",
                                                [128, 3 * NP * 512], BF16))
                    s.a_vaw = ea(nc.sbuf_tensor("a_vaw",
                                                [128, 3 * 4 * 1040], BF16))
                    s.pb_sb = ea(nc.sbuf_tensor("a_pb", [128, 2 * 1024], BF16))
                    s.a_ot = ea(nc.sbuf_tensor("a_ot", [128, ND * 512], BF16))
                    s.a_xf = ea(nc.sbuf_tensor("a_xf", [128, 2 * 512], F32))
                    s.a_r = ea(nc.sbuf_tensor("a_r", [128, ND * 512], F32))
                    s.a_maskA = ea(nc.sbuf_tensor("a_maskA", [128, 8 * 512],
                                                  BF16))
                    for nm in ("b_aw b_r b_maskA").split():
                        setattr(s, nm, Buf())
                    _bx, _bq, _bo = Buf(), Buf(), Buf()
                    s.b_xbf2 = [_bx, _bx]
                    s.b_qt2 = [_bq, _bq]
                    s.b_ot2 = [_bo, _bo]
                    s.b_ktw = [Buf(), Buf(), Buf()]
                    s.b_vaw = [Buf(), Buf(), Buf()]
                    s.b_pbuf = [Buf(), Buf()]
                    s.b_xf = [Buf(), Buf()]
                    s.b_pb_half = [[0, 0], [0, 0]]

                    emit_stage_A(s, block)
                _barrier(s, block)

            if "B" in stages:
                s.fence = {k: sem.n for k, sem in s.sems.items()}

                def fbuf():
                    b = Buf()
                    b.w = dict(s.fence)
                    return b

                from contextlib import ExitStack as _ES
                with _ES() as bctx:
                    eb = bctx.enter_context
                    s.bq_sb = eb(nc.sbuf_tensor("bbq", [128, 8], F32))
                    s.bk_sb = eb(nc.sbuf_tensor("bbk", [128, 8], F32))
                    s.bv_sb = eb(nc.sbuf_tensor("bbv", [128, 8], F32))
                    s.gb_g = eb(nc.sbuf_tensor("bgg", [128, 8], F32))
                    s.gb_b = eb(nc.sbuf_tensor("bgb", [128, 8], F32))
                    s.sw_sb = eb(nc.sbuf_tensor("bsw", [128, 64], BF16))
                    s.bsw_sb = eb(nc.sbuf_tensor("bbsw", [1, 8], F32))
                    s.bswB_sb = eb(nc.sbuf_tensor("bbswB", [128, 8], F32))
                    s.ltri_sb = eb(nc.sbuf_tensor("bltri", [128, 128], BF16))
                    s.ltri32_sb = eb(nc.sbuf_tensor("bltri32", [32, 32], F32))
                    s.identf_sb = eb(nc.sbuf_tensor("bidf", [128, 128], F32))
                    s.rp_sb = eb(nc.sbuf_tensor("brp", [128, 32], F32))
                    s.kept_sb = eb(nc.sbuf_tensor("bkept", [128, 32], F32))
                    s.ohkT_sb = eb(nc.sbuf_tensor("bohkT", [8, T], BF16))
                    s.m1c_sb = eb(nc.sbuf_tensor("bm1c", [128, 1], F32))
                    s.m2c_sb = eb(nc.sbuf_tensor("bm2c", [128, 1], F32))
                    for nm in ("b_setup b_w b_bswB b_kt b_va b_xcb b_qt "
                               "b_sfx b_oh b_ohb b_rp b_kept b_ohkT b_ohkb "
                               "b_tot b_t32 b_tall b_slot b_gat b_coff "
                               "b_roff b_rrow b_lg b_mx b_pex b_mi b_ln "
                               "b_idxf b_zz b_pos b_rb8 b_pf8 b_m1c b_m2c "
                               "b_m1r b_m2r b_mB1 b_mB2 b_okr b_htb "
                               "b_o2 b_r").split():
                        setattr(s, nm, fbuf())
                    s.b_pbuf = [fbuf(), fbuf()]
                    s.b_htf = [fbuf(), fbuf()]
                    s.b_ohkb2 = [fbuf(), fbuf()]
                    s.rb8_sb = ColView(s.a_o1, 0)
                    s.b_rb8 = s.b_o12[0]
                    s.pf8_sb = ColView(s.a_o1, 512)
                    s.b_pf8 = s.b_o12[1]

                    @block.vector
                    def _(v):
                        _wait(v, s, s.fence)
                        _i = v.memset(s.row_m[0:1, 0:1], 0.0)
                        n = s.dv.inc(_i)

                    emit_B_setup(s, block)

                    with _ES() as b2ctx:
                        e2 = b2ctx.enter_context
                        s.kt = e2(nc.sbuf_tensor("bkt", [128, NP * T], BF16))
                        s.w_sb = e2(nc.sbuf_tensor("bw", [128, ND * 1024],
                                                   BF16))
                        s.maskB_sb = e2(nc.sbuf_tensor("bmaskB", [128, 2048],
                                                       BF16))
                        s.step_sb = e2(nc.sbuf_tensor("bstep", [128, 1024],
                                                      BF16))
                        s.rankc_sb = e2(nc.sbuf_tensor("brkc", [1, 8], F32))
                        s.rankm_sb = e2(nc.sbuf_tensor("brkm", [8, 1], F32))
                        s.iota8_sb = e2(nc.sbuf_tensor("bio8", [128, 8], F32))
                        s.oh_sb = e2(nc.sbuf_tensor("boh", [128, 256], F32))
                        s.ohb_sb = e2(nc.sbuf_tensor("bohb", [128, 256], BF16))
                        s.ohkb_sb = e2(nc.sbuf_tensor("bohkb", [128, 16], F32))
                        s.tot_sb = e2(nc.sbuf_tensor("btot", [1, 256], F32))
                        s.t32_sb = e2(nc.sbuf_tensor("bt32", [32, 8], F32))
                        s.tall_sb = e2(nc.sbuf_tensor("btall", [1, 8], F32))
                        s.slot_sb = e2(nc.sbuf_tensor("bslot", [8, 8], F32))
                        s.gat_sb = e2(nc.sbuf_tensor("bgat", [8, 8], F32))
                        s.coff_sb = e2(nc.sbuf_tensor("bcoff", [1, 8], F32))
                        s.roff_sb = e2(nc.sbuf_tensor("broff", [32, 8], F32))
                        s.rrow_sb = e2(nc.sbuf_tensor("brrow", [1, 256], F32))
                        s.lg_sb = e2(nc.sbuf_tensor("blg", [128, 8], F32))
                        s.mx_sb = e2(nc.sbuf_tensor("bmx", [128, 8], F32))
                        s.pex_sb = e2(nc.sbuf_tensor("bpex", [128, 8], F32))
                        s.mi_sb = e2(nc.sbuf_tensor("bmi", [128, 8], U32))
                        s.ln_sb = e2(nc.sbuf_tensor("bln", [128, 1], F32))
                        s.idxf_sb = e2(nc.sbuf_tensor("bidxf", [128, 1], F32))
                        s.zz_sb = e2(nc.sbuf_tensor("bzz", [128, 1], F32))
                        s.pos_sb = e2(nc.sbuf_tensor("bpos", [128, 32], F32))
                        s.ones1_32f = e2(nc.sbuf_tensor("bones1_32", [1, 32], F32))
                        s.ones32f = e2(nc.sbuf_tensor("bones32", [32, 1], F32))

                        s.va = e2(nc.sbuf_tensor("bva", [128, NT * 16 * 65],
                                                 BF16))
                        s.qt = e2(nc.sbuf_tensor("bqt", [128, NP * 512], BF16))
                        s.xcb = e2(nc.sbuf_tensor("bxcb", [128, ND * 512],
                                                  BF16))
                        s.pb_sb = e2(nc.sbuf_tensor("bpb", [128, 2048], BF16))
                        s.sfx = e2(nc.sbuf_tensor("bsfx", [128, 512], F32))
                        @block.vector
                        def _(v):
                            v.memset(s.ones1_32f[:, :], 1.0)
                            _i = v.memset(s.ones32f[:, :], 1.0)
                            n = s.dv.inc(_i)
                            s.b_setup.did_wr("dv", n)

                        emit_B_mst(s, block)
                        emit_B_kv(s, block)
                        for qc in range(NQ):
                            emit_B_attn_qchunk(s, block, qc)
                        emit_B_counts(s, block)
                        emit_B_pos(s, block)

                    s.fence = {k: sem.n for k, sem in s.sems.items()}
                    with _ES() as b6ctx:
                        e6 = b6ctx.enter_context
                        s.we_sb = e6(nc.sbuf_tensor("bwe",
                                                    [128, E * ND * 1024],
                                                    BF16))
                        s.hm_sb2 = [e6(nc.sbuf_tensor("bhm%d" % i,
                                                      [128, 4 * 512], BF16))
                                    for i in range(2)]
                        s.okA_sb = e6(nc.sbuf_tensor("bokA", [128, E * 512],
                                                     BF16))
                        s.x1f_sb = e6(nc.sbuf_tensor("bx1f", [128, 2 * 512],
                                                     F32))
                        s.htb_sb = e6(nc.sbuf_tensor("bhtb", [128, ND * 512],
                                                     BF16))
                        s.okr_sb = e6(nc.sbuf_tensor("bokr", [1, 512], BF16))
                        s.be_sb = e6(nc.sbuf_tensor("bbe", [8, 1024], BF16))

                        s.a_r = e6(nc.sbuf_tensor("br", [128, ND * 512], F32))
                        s.a_xf = e6(nc.sbuf_tensor("bxf", [128, 2 * 512],
                                                   F32))
                        for nm in ("b_okA b_htb b_okr b_r").split():
                            setattr(s, nm, fbuf())
                        s.m1r_sb = s.row_a
                        s.b_m1r = s.b_rowa
                        s.m2r_sb = s.row_b
                        s.b_m2r = s.b_rowb
                        s.b_hm2 = [fbuf(), fbuf()]
                        s.b_o2 = s.b_o12[1]
                        s.b_rb8 = s.b_o12[0]
                        s.b_wes = [fbuf() for _ in range(E)]
                        s.b_x1f2 = [fbuf(), fbuf()]
                        s.b_htf = [fbuf(), fbuf()]
                        s.mB1_sb = s.a_aB
                        s.b_mB1 = s.b_aB
                        s.mB2_sb = s.a_bB
                        s.b_mB2 = s.b_bB
                        emit_B_experts(s, block)
                _barrier(s, block)

            if "C" in stages:
                s.fence = {k: sem.n for k, sem in s.sems.items()}

                def fbuf():
                    b = Buf()
                    b.w = dict(s.fence)
                    return b

                from contextlib import ExitStack as _ES
                with _ES() as cctx:
                    ec_ = cctx.enter_context
                    s.kt = ec_(nc.sbuf_tensor("ckt", [128, NP * T], BF16))
                    s.va = ec_(nc.sbuf_tensor("cva", [128, NT * 16 * 65],
                                              BF16))
                    s.qt = ec_(nc.sbuf_tensor("cqt", [128, NP * 512], BF16))
                    s.xcb = ec_(nc.sbuf_tensor("cxcb", [128, ND * 512], BF16))
                    s.pb_sb = ec_(nc.sbuf_tensor("cpb", [128, 2048], BF16))
                    s.w_sb = ec_(nc.sbuf_tensor("cw", [128, ND * 1024], BF16))
                    s.cosb_sb = ec_(nc.sbuf_tensor("ccos", [128, 512], BF16))
                    s.sinb_sb = ec_(nc.sbuf_tensor("csin", [128, 512], BF16))
                    s.crawb_sb = ec_(nc.sbuf_tensor("ccrawb", [128, 512],
                                                    BF16))
                    s.cqb_sb = ec_(nc.sbuf_tensor("ccqb", [128, 8], F32))
                    s.ckb_sb = ec_(nc.sbuf_tensor("cckb", [128, 8], F32))
                    s.cvb_sb = ec_(nc.sbuf_tensor("ccvb", [128, 8], F32))
                    s.cob_sb = ec_(nc.sbuf_tensor("ccob", [128, 8], F32))
                    s.p32_sb = ec_(nc.sbuf_tensor("cp32", [128, 128], BF16))
                    s.a_r = ec_(nc.sbuf_tensor("cr", [128, ND * 512], F32))
                    s.a_xf = ec_(nc.sbuf_tensor("cxf", [128, 512], F32))
                    s.c_rope = ec_(nc.sbuf_tensor("crope", [128, 512],
                                                  F32))
                    s.b_rope = fbuf()
                    s.ct1_sb = ColView(s.c_rope, 0)
                    s.craw_sb = ColView(s.c_rope, 0)
                    for nm in ("b_csetup b_ccs b_crawb "
                               "b_kt b_va b_qt b_xcb b_w b_r").split():
                        setattr(s, nm, fbuf())
                    s.b_ct1 = s.b_rope
                    s.b_craw = s.b_rope
                    s.b_pbuf = [fbuf(), fbuf()]
                    s.b_xf = [fbuf(), fbuf()]
                    emit_stage_C(s, block)
                _barrier(s, block)

            if "D" in stages:
                s.fence = {k: sem.n for k, sem in s.sems.items()}

                def fbuf():
                    b = Buf()
                    b.w = dict(s.fence)
                    return b

                from contextlib import ExitStack as _ES
                with _ES() as dctx:
                    ed = dctx.enter_context
                    s.x3b_sb = ed(nc.sbuf_tensor("dx3b", [128, ND * T], BF16))
                    s.wr_sb = ed(nc.sbuf_tensor("dwr", [128, ND * 128], BF16))
                    s.cent_sb = ed(nc.sbuf_tensor("dcent", [16, 128], F32))
                    s.csq_sb = ed(nc.sbuf_tensor("dcsq", [16, 128], F32))
                    s.cw_sb = ed(nc.sbuf_tensor("dcw", [16, 2], F32))
                    s.cnm_sb = ed(nc.sbuf_tensor("dcnm", [16, 128], F32))
                    s.cnT_sb = ed(nc.sbuf_tensor("dcnT", [128, 16], BF16))
                    s.temp_sb = ed(nc.sbuf_tensor("dtemp", [1, 1], F32))
                    s.stemp_sb = ed(nc.sbuf_tensor("dstemp", [1, 1], F32))
                    s.iota16_sb = ed(nc.sbuf_tensor("dio16", [128, 16], F32))
                    s.didf_sb = ed(nc.sbuf_tensor("didf", [128, 128], F32))
                    s.didb_sb = ed(nc.sbuf_tensor("didb", [128, 128], BF16))
                    s.zf_sb = ed(nc.sbuf_tensor("dzf", [128, 512], F32))
                    s.zb_sb = ed(nc.sbuf_tensor("dzb", [128, 512], BF16))
                    s.zsq_sb = ed(nc.sbuf_tensor("dzsq", [128, 512], BF16))
                    s.znb_sb = ed(nc.sbuf_tensor("dznb", [128, 512], BF16))
                    s.lg16_sb = ed(nc.sbuf_tensor("dlg16", [128, 16], F32))
                    s.mx16_sb = ed(nc.sbuf_tensor("dmx16", [128, 8], F32))
                    s.mi16_sb = ed(nc.sbuf_tensor("dmi16", [128, 8], U32))
                    s.pex16_sb = ed(nc.sbuf_tensor("dpex16", [128, 16], F32))
                    s.pr16_sb = ed(nc.sbuf_tensor("dpr16", [128, 16], F32))
                    s.i0_sb = ed(nc.sbuf_tensor("di0", [128, 2], F32))
                    s.eq_sb = ed(nc.sbuf_tensor("deq", [128, 32], F32))
                    s.gt_sb = ed(nc.sbuf_tensor("dgt", [128, 16], F32))
                    s.gT_sb = ed(nc.sbuf_tensor("dgT", [16, T], F32))
                    s.gr_sb = ed(nc.sbuf_tensor("dgr", [1, 512], F32))
                    s.gB_sb = ed(nc.sbuf_tensor("dgB", [128, 512], F32))
                    s.ie_sb = [ed(nc.sbuf_tensor("die%d" % i,
                                                 [128, ND * 512], BF16))
                               for i in range(2)]
                    s.oe_sb = [ed(nc.sbuf_tensor("doe%d" % i,
                                                 [128, 4 * 64], BF16))
                               for i in range(2)]
                    s.hb_sb = ed(nc.sbuf_tensor("dhb", [128, 4 * 512], BF16))
                    s.ct1_sb = ed(nc.sbuf_tensor("dct1", [128, 512], F32))
                    s.ln_sb = ed(nc.sbuf_tensor("dln", [128, 1], F32))
                    s.zz_sb = ed(nc.sbuf_tensor("dzz", [128, 2], F32))
                    s.a_r = ed(nc.sbuf_tensor("dr", [128, ND * 512], F32))
                    s.a_xf = ed(nc.sbuf_tensor("dxf", [128, 2 * 512], F32))
                    for nm in ("b_x3b b_dsetup b_csq b_cn b_cnm b_cnT "
                               "b_stemp b_zf b_zb b_zsq b_znb b_lg16 b_mx16 "
                               "b_mi16 b_pex16 b_pr16 b_i0 b_eq b_gt b_gT "
                               "b_gr b_gB b_hb b_ct1 b_r b_ln "
                               "b_zz").split():
                        setattr(s, nm, fbuf())
                    s.b_ie = [fbuf(), fbuf()]
                    s.b_oe = [fbuf(), fbuf()]
                    s.b_xf = [fbuf(), fbuf()]
                    emit_stage_D(s, block)
                _barrier(s, block)

        with nc.Block() as rblock:
            block.flush(rblock)

    return nc, s


def _core_inputs(inputs, w, core, stages):
    cst = _consts()
    d = {}
    for name, arr in cst.items():
        d[name] = np.asarray(arr)
    d["rankcol"] = (np.arange(8) == core).astype(np.float32).reshape(1, 8)
    d["rankmask"] = (np.arange(8) < core).astype(np.float32).reshape(8, 1)
    if "A" in stages:
        x = np.asarray(inputs["L0_attn_x"][core])
        d["x0T"] = np.ascontiguousarray(x.T)
        for n in ["l0aqT", "l0akT", "l0avT", "l0aoT"]:
            d[n] = w[n]
    if "B" in stages:
        for n in ["l0fqT", "l0fkT", "l0fvT", "l0fbq", "l0fbk", "l0fbv",
                  "l0fWeT", "l0fbe", "l0fswT", "l0fbsw", "l0fg", "l0fb"]:
            d[n] = w[n]
    if "C" in stages:
        for n in ["l1qT", "l1kT", "l1vT", "l1oT", "l1qb", "l1kb", "l1vb",
                  "l1ob"]:
            d[n] = w[n]
    if "D" in stages:
        for n in ["l1wrT", "l1cent", "l1temp", "l1IE", "l1OE"]:
            d[n] = w[n]
    return d


def kernel(**inputs):
    w = _prep_weights(inputs)
    stage_sets = [STAGES] if FUSED else list(STAGES)

    carry = [dict() for _ in range(8)]
    out = np.zeros((8, T, C), np.float32)
    for stages in stage_sets:
        nc, s = build_program(stages)
        in_maps = []
        for core in range(8):
            m = _core_inputs(inputs, w, core, stages)
            m.update(carry[core])
            m = {k: v for k, v in m.items() if k in s.innames}
            missing = s.innames - set(m)
            assert not missing, f"missing inputs {missing}"
            in_maps.append(m)
        res = run_bass_kernel_spmd(nc, in_maps, core_ids=list(range(8)))
        for core in range(8):
            carry[core].update(res.results[core])
        del nc, s
    for core in range(8):
        if "xoutT" in carry[core]:
            out[core] = np.asarray(carry[core]["xoutT"], np.float32).T
    return out


# version 21
# speedup vs baseline: 1.1602x; 1.0002x over previous
import math
import numpy as np
import ml_dtypes

import concourse.bass as bass
import concourse.mybir as mybir
from concourse import dve_ops
from concourse.bass_utils import run_bass_kernel_spmd

AF = mybir.ActivationFunctionType
ALU = mybir.AluOpType
AX = mybir.AxisListType
F32 = mybir.dt.float32
BF16 = mybir.dt.bfloat16
U32 = mybir.dt.uint32

T = 4096
C = 1024
H = 16
HD = 64
NP = 8
ND = 8
NT = 32
NQ = 8
E = 8
CAP = int(1.2 * (8 * T) / E)
GS = 128
HS = 512
EPS = 1e-5

STAGES = "ABCD"
FUSED = True

_BF = ml_dtypes.bfloat16


_const_cache = {}


def _consts():
    if _const_cache:
        return _const_cache
    c = {}
    inv = 1.0 / 10000.0 ** (np.arange(0, HD, 2, dtype=np.float64) / HD)
    t = np.arange(T, dtype=np.float64)
    fr = inv[:, None] * t[None, :]
    cos64 = np.cos(np.concatenate([fr, fr], 0))
    sin64 = np.sin(np.concatenate([fr, fr], 0))
    sgn = np.where(np.arange(HD) < HD // 2, -1.0, 1.0)[:, None]
    c["ropecos"] = np.concatenate([cos64, cos64], 0).astype(_BF)
    c["ropesin"] = np.concatenate([sin64 * sgn, sin64 * sgn], 0).astype(_BF)
    p32 = np.zeros((128, 128), np.float32)
    for i in range(128):
        half = (i // HD) * HD
        src = half + ((i % HD) + HD // 2) % HD
        p32[src, i] = 1.0
    c["p32"] = p32.astype(_BF)
    c["identb"] = np.eye(128, dtype=np.float32).astype(_BF)
    c["identf"] = np.eye(128, dtype=np.float32)
    c["ltri128"] = np.tril(np.ones((128, 128), np.float32), -1).astype(_BF)
    c["ltri32"] = np.ascontiguousarray(
        np.tril(np.ones((32, 32), np.float32), -1))
    kr = np.arange(128)[:, None]
    q = np.arange(512)[None, :]
    mA = np.zeros((8, 128, 512), np.float32)
    for j in range(4):
        mA[j] = ((j * 128 + kr) > q)
    for j in range(4, 8):
        mA[j] = (((j - 4) * 128 + kr) <= q)
    c["maskA"] = mA.astype(_BF)
    mB = np.zeros((4, 128, 512), np.float32)
    for jd in range(4):
        mB[jd] = ((jd * 128 + kr) <= q).astype(np.float32)
    c["maskB"] = mB
    st = np.zeros((32, 128, 32), np.float32)
    for jp in range(32):
        st[jp, :, : jp + 1] = 1.0
    c["step32"] = st.astype(_BF)
    c["iota8"] = np.broadcast_to(
        np.arange(8, dtype=np.float32), (128, 8)).copy()
    c["iota16"] = np.broadcast_to(
        np.arange(16, dtype=np.float32), (128, 16)).copy()
    _const_cache.update(c)
    return c


def _prep_weights(inputs):
    w = {}
    tp = lambda a: np.ascontiguousarray(np.asarray(a).T)
    w["l0aqT"] = tp(inputs["L0_attn_Wq"])
    w["l0akT"] = tp(inputs["L0_attn_Wk"])
    w["l0avT"] = tp(inputs["L0_attn_Wv"])
    w["l0aoT"] = tp(inputs["L0_attn_Wo"])
    w["l0fqT"] = tp(np.asarray(inputs["L0_ffn_Wq"]).reshape(C, C))
    w["l0fkT"] = tp(np.asarray(inputs["L0_ffn_Wk"]).reshape(C, C))
    w["l0fvT"] = tp(np.asarray(inputs["L0_ffn_Wv"]).reshape(C, C))
    w["l0fbq"] = np.ascontiguousarray(
        np.asarray(inputs["L0_ffn_bq"]).reshape(C, 1))
    w["l0fbk"] = np.ascontiguousarray(
        np.asarray(inputs["L0_ffn_bk"]).reshape(C, 1))
    w["l0fbv"] = np.ascontiguousarray(
        np.asarray(inputs["L0_ffn_bv"]).reshape(C, 1))
    w["l0fWeT"] = np.ascontiguousarray(
        np.transpose(np.asarray(inputs["L0_ffn_We"]), (0, 2, 1))).astype(_BF)
    w["l0fbe"] = np.ascontiguousarray(np.asarray(inputs["L0_ffn_be"]))
    w["l0fswT"] = tp(np.asarray(inputs["L0_ffn_Wsw"]))
    w["l0fbsw"] = np.ascontiguousarray(
        np.asarray(inputs["L0_ffn_bsw"]).reshape(1, E))
    w["l0fg"] = np.ascontiguousarray(
        np.asarray(inputs["L0_ffn_gamma"]).reshape(C, 1))
    w["l0fb"] = np.ascontiguousarray(
        np.asarray(inputs["L0_ffn_beta"]).reshape(C, 1))
    w["l1qT"] = tp(inputs["L1_attn_q_w"])
    w["l1kT"] = tp(inputs["L1_attn_k_w"])
    w["l1vT"] = tp(inputs["L1_attn_v_w"])
    w["l1oT"] = tp(inputs["L1_attn_o_w"])
    w["l1qb"] = np.ascontiguousarray(
        np.asarray(inputs["L1_attn_q_b"]).reshape(C, 1))
    w["l1kb"] = np.ascontiguousarray(
        np.asarray(inputs["L1_attn_k_b"]).reshape(C, 1))
    w["l1vb"] = np.ascontiguousarray(
        np.asarray(inputs["L1_attn_v_b"]).reshape(C, 1))
    w["l1ob"] = np.ascontiguousarray(
        np.asarray(inputs["L1_attn_o_b"]).reshape(C, 1))
    w["l1wrT"] = tp(inputs["L1_ffn_w_router"])
    w["l1cent"] = np.ascontiguousarray(
        np.asarray(inputs["L1_ffn_head_centroids"]))
    w["l1temp"] = np.ascontiguousarray(
        np.asarray(inputs["L1_ffn_temperature"]).reshape(1, 1))
    w["l1IE"] = np.ascontiguousarray(np.asarray(inputs["L1_ffn_input_experts"]))
    w["l1OE"] = np.ascontiguousarray(
        np.asarray(inputs["L1_ffn_output_experts"]))
    return w


class QOp:

    __slots__ = ("meth", "args", "kwargs", "incs")

    def __init__(self, meth, args, kwargs):
        self.meth = meth
        self.args = args
        self.kwargs = kwargs
        self.incs = []

    def then_inc(self, sem_h, k):
        self.incs.append((sem_h, k))
        return self


class FEng:

    def __init__(self, q):
        self._q = q

    def __getattr__(self, name):
        def call(*a, **k):
            op = QOp(name, a, k)
            self._q.append(op)
            return op
        return call


class FakeBlock:

    ENGINES = ("tensor", "scalar", "vector", "gpsimd", "sync")

    def __init__(self):
        self.qs = {e: [] for e in self.ENGINES}
        self._f = {e: FEng(self.qs[e]) for e in self.ENGINES}

    def tensor(self, f):
        f(self._f["tensor"])

    def scalar(self, f):
        f(self._f["scalar"])

    def vector(self, f):
        f(self._f["vector"])

    def gpsimd(self, f):
        f(self._f["gpsimd"])

    def sync(self, f):
        f(self._f["sync"])

    def flush(self, block):
        for name in self.ENGINES:
            q = self.qs[name]

            def body(eng, q=q):
                for op in q:
                    ins = getattr(eng, op.meth)(*op.args, **op.kwargs)
                    for h, k in op.incs:
                        ins.then_inc(h, k)
            getattr(block, name)(body)


class Sem:
    def __init__(self, ctx, nc, name):
        self.h = ctx.enter_context(nc.semaphore(name))
        self.n = 0
        self.name = name

    def inc(self, instr, k=1):
        instr.then_inc(self.h, k)
        self.n += k
        return self.n


class Buf:

    __slots__ = ("w", "r")

    def __init__(self):
        self.w = {}
        self.r = {}

    def rd(self, eng, s):
        _wait(eng, s, self.w)

    def wr(self, eng, s):
        _wait(eng, s, self.w, self.r)

    def did_rd(self, sem_name, n):
        self.r[sem_name] = max(self.r.get(sem_name, 0), n)

    def did_wr(self, sem_name, n):
        self.w = {sem_name: n}
        self.r = {}


def _wait(eng, s, *dicts):
    agg = {}
    for d in dicts:
        for k, v in d.items():
            agg[k] = max(agg.get(k, 0), v)
    for k, v in agg.items():
        eng.wait_ge(s.sem(k).h, v)


class ColView:

    def __init__(self, t, off):
        self.t = t
        self.off = off

    def __getitem__(self, idx):
        p, f = idx
        return self.t[p, f.start + self.off:f.stop + self.off]


class S:
    def va_view(self, j, h):
        c0 = (j * 16 + h) * 65
        return self.va[:, c0:c0 + 65]

    def sem(self, name):
        if name not in self.sems:
            self.sems[name] = Sem(self._ctx, self.nc, name)
        return self.sems[name]


def emit_attn_norm_pre(s, block, vw):
    @block.vector
    def _(v):
        for bb in vw["bufs"]:
            bb.rd(v, s)
        s.b_rowa.wr(v, s)
        _i = emit_recip_row(s, v, s.row_a[:, 0:512], vw["denA"])
        n = s.dv.inc(_i)
        s.b_rowa.did_wr("dv", n)
        s.b_rowb.wr(v, s)
        _i = emit_recip_row(s, v, s.row_b[:, 0:512], vw["denB"])
        n = s.dv.inc(_i)
        s.b_rowb.did_wr("dv", n)
        for bb in vw["bufs"]:
            bb.did_rd("dv", n)

    @block.tensor
    def _(t):
        s.b_rowa.rd(t, s)
        s.b_rowb.rd(t, s)
        s.b_ps0.wr(t, s)
        _i = t.matmul(s.p_s0[0:64, 0:512], s.ones1_64f[:, :],
                      s.row_a[:, 0:512], start=True, stop=True)
        n = s.mm.inc(_i)
        s.b_ps0.did_wr("mm", n)
        s.b_ps1.wr(t, s)
        _i = t.matmul(s.p_s1[0:64, 0:512], s.ones1_64f[:, :],
                      s.row_b[:, 0:512], start=True, stop=True)
        n = s.mm.inc(_i)
        s.b_ps1.did_wr("mm", n)
        s.b_rowa.did_rd("mm", n)
        s.b_rowb.did_rd("mm", n)

    @block.vector
    def _(v):
        s.b_ps0.rd(v, s)
        s.b_aB.wr(v, s)
        _i = v.tensor_copy(s.a_aB[0:64, 0:512], s.p_s0[0:64, 0:512])
        n = s.dv.inc(_i)
        s.b_aB.did_wr("dv", n)
        s.b_ps0.did_rd("dv", n)
        s.b_ps1.rd(v, s)
        s.b_bB.wr(v, s)
        _i = v.tensor_copy(s.a_bB[0:64, 0:512], s.p_s1[0:64, 0:512])
        n = s.dv.inc(_i)
        s.b_bB.did_wr("dv", n)
        s.b_ps1.did_rd("dv", n)


def emit_recip_row(s, v, out_ap, in_ap):
    return v.reciprocal(out_ap, in_ap)



def _barrier(s, block):
    s.barn += 5
    for attr in ("tensor", "scalar", "vector", "gpsimd", "sync"):
        def body(eng, n=s.barn):
            eng.nop().then_inc(s.bar.h, 1)
            eng.wait_ge(s.bar.h, n)
        getattr(block, attr)(body)


def emit_ln_store(s, block, r_buf, r_t, out_f, out_b, qc,
                  gamma=None, beta=None, store=True, post_cb=None,
                  ch_of="cOF", ch_ob="cOB", mid_cb=None):
    for ec in range(ND):
        sl = ec % 2

        @block.scalar
        def _(sc, ec=ec, sl=sl):
            r_buf.rd(sc, s)
            s.b_rsq2[sl].wr(sc, s)
            _i = sc.activation(s.a_rsq2[:, sl * 512:(sl + 1) * 512],
                               r_t[:, ec * 512:(ec + 1) * 512], AF.Square)
            n = s.act.inc(_i)
            s.b_rsq2[sl].did_wr("act", n)
            r_buf.did_rd("act", n)

        @block.tensor
        def _(t, ec=ec, sl=sl):
            r_buf.rd(t, s)
            s.b_rsq2[sl].rd(t, s)
            if ec == 0:
                s.b_pg.wr(t, s)
            t.matmul(s.p_g[0:1, 0:512], s.ones128f[:, :],
                     r_t[:, ec * 512:(ec + 1) * 512],
                     start=(ec == 0), stop=(ec == ND - 1))
            _i = t.matmul(s.p_g[0:1, 512:1024], s.ones128b[:, :],
                          s.a_rsq2[:, sl * 512:(sl + 1) * 512],
                          start=(ec == 0), stop=(ec == ND - 1))
            n = s.mm.inc(_i)
            r_buf.did_rd("mm", n)
            s.b_rsq2[sl].did_rd("mm", n)
            if ec == ND - 1:
                s.b_pg.did_wr("mm", n)

    if mid_cb is not None:
        mid_cb()

    @block.vector
    def _(v):
        s.b_pg.rd(v, s)
        s.b_rowm.wr(v, s)
        _i = v.tensor_scalar_mul(s.row_m[:, 0:512], s.p_g[0:1, 0:512], 1.0 / C)
        n = s.dv.inc(_i)
        s.b_rowm.did_wr("dv", n)
        s.b_rowq.wr(v, s)
        _i = v.tensor_scalar_mul(s.row_q[:, 0:512], s.p_g[0:1, 512:1024],
                                 1.0 / C)
        n = s.dv.inc(_i)
        s.b_rowq.did_wr("dv", n)
        s.b_pg.did_rd("dv", n)

    @block.vector
    def _(v):
        s.b_rowm.rd(v, s)
        s.b_rowb.wr(v, s)
        _i = v.tensor_mul(s.row_b[:, 0:512], s.row_m[:, 0:512],
                          s.row_m[:, 0:512])
        n = s.dv.inc(_i)
        v.wait_ge(s.dv.h, n)
        s.b_rowq.rd(v, s)
        s.b_rowq.wr(v, s)
        _i = v.tensor_sub(s.row_q[:, 0:512], s.row_q[:, 0:512],
                          s.row_b[:, 0:512])
        n = s.dv.inc(_i)
        s.b_rowq.did_wr("dv", n)
        s.b_rowb.did_rd("dv", n)

    @block.scalar
    def _(sc):
        s.b_rowq.rd(sc, s)
        s.b_rowb.wr(sc, s)
        _i = sc.activation(s.row_b[:, 0:512], s.row_q[:, 0:512], AF.Sqrt,
                           bias=s.epsT[0:1, 0:1])
        n = s.act.inc(_i)
        s.b_rowq.did_rd("act", n)
        s.b_rowb.did_wr("act", n)

    @block.vector
    def _(v):
        s.b_rowb.rd(v, s)
        s.b_rowa.wr(v, s)
        _i = emit_recip_row(s, v, s.row_a[:, 0:512], s.row_b[:, 0:512])
        n = s.dv.inc(_i)
        s.b_rowb.did_rd("dv", n)
        s.b_rowa.did_wr("dv", n)
        v.wait_ge(s.dv.h, n)
        s.b_rowm.rd(v, s)
        s.b_rowb.wr(v, s)
        _i = v.tensor_mul(s.row_b[:, 0:512], s.row_m[:, 0:512],
                          s.row_a[:, 0:512])
        n = s.dv.inc(_i)
        v.wait_ge(s.dv.h, n)
        _i = v.tensor_scalar_mul(s.row_b[:, 0:512], s.row_b[:, 0:512], -1.0)
        n = s.dv.inc(_i)
        s.b_rowm.did_rd("dv", n)
        s.b_rowb.did_wr("dv", n)

    @block.tensor
    def _(t):
        s.b_rowa.rd(t, s)
        s.b_rowb.rd(t, s)
        s.b_pc.wr(t, s)
        t.matmul(s.p_c[:, 0:512], s.ones1_128f[:, :], s.row_a[:, 0:512],
                 start=True, stop=True)
        s.b_pd.wr(t, s)
        _i = t.matmul(s.p_d[:, 0:512], s.ones1_128f[:, :], s.row_b[:, 0:512],
                 start=True, stop=True)
        n = s.mm.inc(_i)
        s.b_rowa.did_rd("mm", n)
        s.b_rowb.did_rd("mm", n)
        s.b_pc.did_wr("mm", n)
        s.b_pd.did_wr("mm", n)

    @block.vector
    def _(v):
        s.b_pc.rd(v, s)
        s.b_pd.rd(v, s)
        s.b_aB.wr(v, s)
        _i = v.tensor_copy(s.a_aB[:, 0:512], s.p_c[:, 0:512])
        n = s.dv.inc(_i)
        s.b_aB.did_wr("dv", n)
        s.b_bB.wr(v, s)
        _i = v.tensor_copy(s.a_bB[:, 0:512], s.p_d[:, 0:512])
        n = s.dv.inc(_i)
        s.b_bB.did_wr("dv", n)
        s.b_pc.did_rd("dv", n)
        s.b_pd.did_rd("dv", n)

    for ec in range(ND):
        off = (ec % 2) * 512
        ob = s.b_o12[ec % 2]

        @block.vector
        def _(v, ec=ec, off=off, ob=ob):
            s.b_aB.rd(v, s)
            s.b_bB.rd(v, s)
            r_buf.rd(v, s)
            ob.wr(v, s)
            _i = v.tensor_mul(s.a_o1[:, off:off + 512],
                              r_t[:, ec * 512:(ec + 1) * 512],
                              s.a_aB[:, 0:512])
            n = s.dv.inc(_i)
            v.wait_ge(s.dv.h, n)
            _i = v.tensor_add(s.a_o1[:, off:off + 512],
                              s.a_o1[:, off:off + 512], s.a_bB[:, 0:512])
            n = s.dv.inc(_i)
            if gamma is not None:
                v.wait_ge(s.dv.h, n)
                _i = v.tensor_scalar(s.a_o1[:, off:off + 512],
                                     s.a_o1[:, off:off + 512],
                                     gamma[:, ec:ec + 1], beta[:, ec:ec + 1],
                                     ALU.mult, ALU.add)
                n = s.dv.inc(_i)
            s.b_aB.did_rd("dv", n)
            s.b_bB.did_rd("dv", n)
            r_buf.did_rd("dv", n)
            ob.did_wr("dv", n)

        if store:
            @block.sync
            def _(sy, ec=ec, off=off, ob=ob):
                ob.rd(sy, s)
                n = s.sem(ch_of).inc(sy.dma_start(
                    out=out_f[ec * 128:(ec + 1) * 128,
                              qc * 512:(qc + 1) * 512],
                    in_=s.a_o1[:, off:off + 512]), 16)
                ob.did_rd(ch_of, n)

            if out_b is not None:
                @block.gpsimd
                def _(g, ec=ec, off=off, ob=ob):
                    ob.rd(g, s)
                    n = s.sem(ch_ob).inc(g.dma_start(
                        out=out_b[ec * 128:(ec + 1) * 128,
                                  qc * 512:(qc + 1) * 512],
                        in_=s.a_o1[:, off:off + 512]), 16)
                    ob.did_rd(ch_ob, n)
        if post_cb is not None:
            post_cb(ec, off, ob)


def av_views(s, parity):
    if parity == 0:
        return dict(avA=s.p_g[0:65, 0:512], avB=s.p_g[0:65, 512:1024],
                    denA=s.p_g[64:65, 0:512], denB=s.p_g[64:65, 512:1024],
                    valA=s.p_g[0:64, 0:512], valB=s.p_g[0:64, 512:1024],
                    bufs=(s.b_pg,))
    return dict(avA=s.p_c[0:65, 0:512], avB=s.p_d[0:65, 0:512],
                denA=s.p_c[64:65, 0:512], denB=s.p_d[64:65, 0:512],
                valA=s.p_c[0:64, 0:512], valB=s.p_d[0:64, 0:512],
                bufs=(s.b_pc, s.b_pd))


def emit_attn_pair(s, block, js, qtA, qtB, qt_buf, kA_fn, kB_fn, ktbuf_fn,
                   vaA_fn, vaB_fn, vabuf_fn, premask_fn=None,
                   postmask_fn=None, parity=0):
    vw = av_views(s, parity)
    avA, avB = vw["avA"], vw["avB"]
    avbufs = vw["bufs"]
    def scores(idx, j):
        ps = s.p_s0 if idx % 2 == 0 else s.p_s1
        psb = s.b_ps0 if idx % 2 == 0 else s.b_ps1
        pb = idx % 2

        @block.tensor
        def _(t, idx=idx, j=j, ps=ps, psb=psb):
            qt_buf.rd(t, s)
            ktbuf_fn(j).rd(t, s)
            psb.wr(t, s)
            t.matmul(ps[:, 0:512], kA_fn(j), qtA, start=True, stop=True,
                     tile_position=(0, 0))
            _i = t.matmul(ps[:, 512:1024], kB_fn(j), qtB, start=True,
                          stop=True, tile_position=(64, 0))
            n = s.mm.inc(_i)
            psb.did_wr("mm", n)
            qt_buf.did_rd("mm", n)
            ktbuf_fn(j).did_rd("mm", n)

        if premask_fn is not None:
            premask_fn(j, ps, psb)

        @block.scalar
        def _(sc, idx=idx, ps=ps, psb=psb, pb=pb):
            psb.rd(sc, s)
            s.b_pbuf[pb].wr(sc, s)
            _i = sc.activation(s.pb_sb[:, pb * 1024:(pb + 1) * 1024],
                               ps[:, 0:1024], AF.Exp)
            n = s.act.inc(_i)
            psb.did_rd("act", n)
            s.b_pbuf[pb].did_wr("act", n)

        if postmask_fn is not None:
            postmask_fn(j, pb)

    def av(idx, j):
        pb = idx % 2
        first = (idx == 0)
        last = (idx == len(js) - 1)

        @block.tensor
        def _(t, idx=idx, j=j, pb=pb, first=first, last=last):
            s.b_pbuf[pb].rd(t, s)
            vabuf_fn(j).rd(t, s)
            if first:
                for bb in avbufs:
                    bb.wr(t, s)
            t.matmul(avA, vaA_fn(j),
                     s.pb_sb[:, pb * 1024:pb * 1024 + 512],
                     start=first, stop=last)
            _i = t.matmul(avB, vaB_fn(j),
                          s.pb_sb[:, pb * 1024 + 512:pb * 1024 + 1024],
                          start=first, stop=last)
            n = s.mm.inc(_i)
            s.b_pbuf[pb].did_rd("mm", n)
            vabuf_fn(j).did_rd("mm", n)
            if last:
                for bb in avbufs:
                    bb.did_wr("mm", n)

    for idx, j in enumerate(js):
        scores(idx, j)
        if idx >= 1:
            av(idx - 1, js[idx - 1])
    av(len(js) - 1, js[-1])
    return vw


def emit_stage_A(s, block):
    nc = s.nc
    SCALE = 1.0 / math.sqrt(HD)

    @block.gpsimd
    def _(g):
        for wi, hname in enumerate(["l0aqT", "l0akT", "l0avT", "l0aoT"]):
            for dc in range(ND):
                s.sem("cAw").inc(g.dma_start(
                    out=s.a_w[wi][:, dc * 1024:(dc + 1) * 1024],
                    in_=s.hbm[hname][dc * 128:(dc + 1) * 128, :]), 16)
        s.b_aw.did_wr("cAw", s.sem("cAw").n)

    @block.gpsimd
    def _(g):
        for j in range(8):
            s.sem("cAm").inc(g.dma_start(
                out=s.a_maskA[:, j * 512:(j + 1) * 512],
                in_=s.hbm["maskA"][j, :, :]), 16)
        s.b_maskA.did_wr("cAm", s.sem("cAm").n)

    @block.vector
    def _(v):
        for sl in range(3):
            for tt in range(4):
                view = s.a_vaw.ap().rearrange(
                    "p (s h x) -> p s h x", s=12, x=65)
                _i = v.memset(view[:, sl * 4 + tt, :, 64:65], 1.0)
        n = s.dv.inc(_i)
        for sl in range(3):
            s.b_vaw[sl].did_wr("dv", n)

    def load_qkv(b):
        c2 = b % 2
        c3 = b % 3
        xo = 0
        qo = 0

        @block.gpsimd
        def _(g, b=b, c2=c2, xo=xo):
            ch = "cAx%d" % c2
            s.b_xbf2[c2].wr(g, s)
            for dc in range(ND):
                s.sem(ch).inc(g.dma_start(
                    out=s.a_xbf[:, xo + dc * 512:xo + (dc + 1) * 512],
                    in_=s.hbm["x0T"][dc * 128:(dc + 1) * 128,
                                     b * 512:(b + 1) * 512]), 16)
            s.b_xbf2[c2].did_wr(ch, s.sem(ch).n)

        for ec in range(NP):
            for wi in (0, 1):
                psb = s.b_ps0 if (2 * ec + wi) % 2 == 0 else s.b_ps1
                ps = s.p_s0 if (2 * ec + wi) % 2 == 0 else s.p_s1

                @block.tensor
                def _(t, ec=ec, wi=wi, ps=ps, psb=psb, c2=c2, xo=xo):
                    s.b_xbf2[c2].rd(t, s)
                    s.b_aw.rd(t, s)
                    psb.wr(t, s)
                    for dc in range(ND):
                        _i = t.matmul(ps[:, 0:512],
                                 s.a_w[wi][:, dc * 1024 + ec * 128:
                                           dc * 1024 + (ec + 1) * 128],
                                 s.a_xbf[:, xo + dc * 512:xo + (dc + 1) * 512],
                                 start=(dc == 0), stop=(dc == ND - 1))
                    n = s.mm.inc(_i)
                    psb.did_wr("mm", n)
                    s.b_xbf2[c2].did_rd("mm", n)
                    s.b_aw.did_rd("mm", n)

                @block.vector
                def _(v, ec=ec, wi=wi, ps=ps, psb=psb, c2=c2, c3=c3, qo=qo):
                    psb.rd(v, s)
                    if wi == 0:
                        dst_buf = s.b_qt2[c2]
                        out = s.a_qt[:, qo + ec * 512:qo + (ec + 1) * 512]
                        scale = SCALE
                    else:
                        dst_buf = s.b_ktw[c3]
                        out = s.a_ktw[:, (c3 * NP + ec) * 512:
                                      (c3 * NP + ec + 1) * 512]
                        scale = 1.0
                    dst_buf.wr(v, s)
                    if scale == 1.0:
                        _i = v.tensor_copy(out, ps[:, 0:512])
                    else:
                        _i = v.tensor_scalar_mul(out, ps[:, 0:512], scale)
                    n = s.dv.inc(_i)
                    dst_buf.did_wr("dv", n)
                    psb.did_rd("dv", n)

        for tt in range(4):
            for eh in range(2):
                @block.tensor
                def _(t, tt=tt, eh=eh, c2=c2, xo=xo):
                    s.b_xbf2[c2].rd(t, s)
                    s.b_aw.rd(t, s)
                    s.b_pc.wr(t, s)
                    for dc in range(ND):
                        _i = t.matmul(s.p_c[:, 0:512],
                                 s.a_xbf[:, xo + dc * 512 + tt * 128:
                                         xo + dc * 512 + (tt + 1) * 128],
                                 s.a_w[2][:, dc * 1024 + eh * 512:
                                          dc * 1024 + (eh + 1) * 512],
                                 start=(dc == 0), stop=(dc == ND - 1))
                    n = s.mm.inc(_i)
                    s.b_pc.did_wr("mm", n)
                    s.b_xbf2[c2].did_rd("mm", n)

                @block.vector
                def _(v, tt=tt, eh=eh, c3=c3):
                    s.b_pc.rd(v, s)
                    s.b_vaw[c3].wr(v, s)
                    view = s.a_vaw.ap().rearrange(
                        "p (s h x) -> p s h x", s=12, x=65)
                    src = s.p_c.ap().rearrange("p (h x) -> p h x", x=64)
                    _i = v.tensor_copy(view[:, c3 * 4 + tt,
                                       eh * 8:(eh + 1) * 8, 0:64],
                                  src[:, 0:8, :])
                    n = s.dv.inc(_i)
                    s.b_vaw[c3].did_wr("dv", n)
                    s.b_pc.did_rd("dv", n)

    def attn(b):
        c2 = b % 2
        c3 = b % 3
        p3 = (b - 1) % 3
        qo = 0
        oo = 0
        first_j = 4 if b == 0 else 0
        pend = [None]
        for ec in range(NP):
            def slot_of(j, c3=c3, p3=p3):
                return p3 if j < 4 else c3

            def kbase_of(j, ec=ec):
                return (slot_of(j) * NP + ec) * 512 + (j % 4) * 128

            def vbase_of(j):
                return (slot_of(j) * 4 + (j % 4)) * 1040

            def postmask(j, pb):
                @block.vector
                def _(v, j=j, pb=pb):
                    s.b_pbuf[pb].rd(v, s)
                    s.b_maskA.rd(v, s)
                    s.b_pbuf[pb].wr(v, s)
                    m = s.a_maskA[:, j * 512:(j + 1) * 512]
                    _i = v.tensor_mul(s.pb_sb[:, pb * 1024:pb * 1024 + 512],
                                      s.pb_sb[:, pb * 1024:pb * 1024 + 512],
                                      m)
                    n = s.dv.inc(_i)
                    v.wait_ge(s.dv.h, n)
                    _i = v.tensor_mul(
                        s.pb_sb[:, pb * 1024 + 512:pb * 1024 + 1024],
                        s.pb_sb[:, pb * 1024 + 512:pb * 1024 + 1024], m)
                    n = s.dv.inc(_i)
                    s.b_pbuf[pb].did_wr("dv", n)
                    s.b_maskA.did_rd("dv", n)

            emit_attn_pair(
                s, block, list(range(first_j, 8)),
                qtA=s.a_qt[0:64, qo + ec * 512:qo + (ec + 1) * 512],
                qtB=s.a_qt[64:128, qo + ec * 512:qo + (ec + 1) * 512],
                qt_buf=s.b_qt2[c2],
                kA_fn=lambda j, ec=ec: s.a_ktw[0:64,
                                               kbase_of(j):kbase_of(j) + 128],
                kB_fn=lambda j, ec=ec: s.a_ktw[64:128,
                                               kbase_of(j):kbase_of(j) + 128],
                ktbuf_fn=lambda j: s.b_ktw[slot_of(j)],
                vaA_fn=lambda j, ec=ec: s.a_vaw[
                    :, vbase_of(j) + (2 * ec) * 65:
                    vbase_of(j) + (2 * ec) * 65 + 65],
                vaB_fn=lambda j, ec=ec: s.a_vaw[
                    :, vbase_of(j) + (2 * ec + 1) * 65:
                    vbase_of(j) + (2 * ec + 1) * 65 + 65],
                vabuf_fn=lambda j: s.b_vaw[slot_of(j)],
                postmask_fn=postmask, parity=ec % 2)
            vw = av_views(s, ec % 2)

            def fin(ec=ec, vw=vw, c2=c2, oo=oo):
                emit_attn_norm_pre(s, block, vw)

                @block.vector
                def _(v, ec=ec, vw=vw, c2=c2, oo=oo):
                    s.b_aB.rd(v, s)
                    s.b_bB.rd(v, s)
                    for bb in vw["bufs"]:
                        bb.rd(v, s)
                    s.b_ot2[c2].wr(v, s)
                    v.tensor_mul(
                        s.a_ot[0:64, oo + ec * 512:oo + (ec + 1) * 512],
                        vw["valA"], s.a_aB[0:64, 0:512])
                    _i = v.tensor_mul(
                        s.a_ot[64:128, oo + ec * 512:oo + (ec + 1) * 512],
                        vw["valB"], s.a_bB[0:64, 0:512])
                    n = s.dv.inc(_i)
                    s.b_ot2[c2].did_wr("dv", n)
                    for bb in vw["bufs"]:
                        bb.did_rd("dv", n)
                    s.b_aB.did_rd("dv", n)
                    s.b_bB.did_rd("dv", n)

            if pend[0] is not None:
                pend[0]()
            pend[0] = fin

        pend[0]()

    def out(b):
        c2 = b % 2
        oo = 0
        for ec in range(ND):
            psb = s.b_ps0 if ec % 2 == 0 else s.b_ps1
            ps = s.p_s0 if ec % 2 == 0 else s.p_s1

            @block.sync
            def _(sy, ec=ec, b=b):
                ch = "cAxf%d" % (ec % 2)
                s.b_xf[ec % 2].wr(sy, s)
                n = s.sem(ch).inc(sy.dma_start(
                    out=s.a_xf[:, (ec % 2) * 512:((ec % 2) + 1) * 512],
                    in_=s.hbm["x0T"][ec * 128:(ec + 1) * 128,
                                     b * 512:(b + 1) * 512]), 16)
                s.b_xf[ec % 2].did_wr(ch, n)

            @block.tensor
            def _(t, ec=ec, ps=ps, psb=psb, c2=c2, oo=oo):
                s.b_ot2[c2].rd(t, s)
                s.b_aw.rd(t, s)
                psb.wr(t, s)
                for dc in range(ND):
                    _i = t.matmul(ps[:, 0:512],
                             s.a_w[3][:, dc * 1024 + ec * 128:
                                     dc * 1024 + (ec + 1) * 128],
                             s.a_ot[:, oo + dc * 512:oo + (dc + 1) * 512],
                             start=(dc == 0), stop=(dc == ND - 1))
                n = s.mm.inc(_i)
                psb.did_wr("mm", n)
                s.b_ot2[c2].did_rd("mm", n)

            @block.vector
            def _(v, ec=ec, ps=ps, psb=psb):
                psb.rd(v, s)
                s.b_xf[ec % 2].rd(v, s)
                s.b_r.wr(v, s)
                _i = v.tensor_add(s.a_r[:, ec * 512:(ec + 1) * 512],
                             ps[:, 0:512],
                             s.a_xf[:, (ec % 2) * 512:((ec % 2) + 1) * 512])
                n = s.dv.inc(_i)
                psb.did_rd("dv", n)
                s.b_xf[ec % 2].did_rd("dv", n)
                s.b_r.did_wr("dv", n)

        emit_ln_store(s, block, s.b_r, s.a_r,
                      s.hbm["x1Tf"], s.hbm["x1Tb"], qc=b,
                      ch_of="cAof", ch_ob="cAob")

    load_qkv(0)
    for b in range(8):
        attn(b)
        if b + 1 < 8:
            load_qkv(b + 1)
        out(b)


def emit_B_setup(s, block):
    @block.gpsimd
    def _(g):
        ch = s.sem("cBs")
        for ec in range(ND):
            ch.inc(g.dma_start(out=s.bq_sb[:, ec:ec + 1],
                               in_=s.hbm["l0fbq"][ec * 128:(ec + 1) * 128, :]),
                   16)
            ch.inc(g.dma_start(out=s.bk_sb[:, ec:ec + 1],
                               in_=s.hbm["l0fbk"][ec * 128:(ec + 1) * 128, :]),
                   16)
            ch.inc(g.dma_start(out=s.bv_sb[:, ec:ec + 1],
                               in_=s.hbm["l0fbv"][ec * 128:(ec + 1) * 128, :]),
                   16)
            ch.inc(g.dma_start(out=s.gb_g[:, ec:ec + 1],
                               in_=s.hbm["l0fg"][ec * 128:(ec + 1) * 128, :]),
                   16)
            ch.inc(g.dma_start(out=s.gb_b[:, ec:ec + 1],
                               in_=s.hbm["l0fb"][ec * 128:(ec + 1) * 128, :]),
                   16)
            ch.inc(g.dma_start(out=s.sw_sb[:, ec * 8:(ec + 1) * 8],
                               in_=s.hbm["l0fswT"][ec * 128:(ec + 1) * 128, :]),
                   16)
        ch.inc(g.dma_start(out=s.bsw_sb[:, :], in_=s.hbm["l0fbsw"]), 16)
        ch.inc(g.dma_start(out=s.ltri_sb[:, :], in_=s.hbm["ltri128"]), 16)
        ch.inc(g.dma_start(out=s.ltri32_sb[:, :], in_=s.hbm["ltri32"]), 16)
        ch.inc(g.dma_start(out=s.identf_sb[:, :], in_=s.hbm["identf"]), 16)
        s.b_setup.did_wr("cBs", ch.n)

    @block.vector
    def _(v):
        s.b_setup.rd(v, s)
        _i = v.tensor_scalar_mul(s.bq_sb[:, 0:8], s.bq_sb[:, 0:8],
                                 1.0 / math.sqrt(C))
        n = s.dv.inc(_i)
        s.b_setup.did_rd("dv", n)
        s.b_setup.w["dv"] = n

    @block.tensor
    def _(t):
        s.b_setup.rd(t, s)
        s.b_pc.wr(t, s)
        _i = t.matmul(s.p_c[:, 0:8], s.ones1_128f[:, :], s.bsw_sb[:, :],
                      start=True, stop=True)
        n = s.mm.inc(_i)
        s.b_pc.did_wr("mm", n)
        s.b_setup.did_rd("mm", n)

    @block.vector
    def _(v):
        s.b_pc.rd(v, s)
        s.b_bswB.wr(v, s)
        _i = v.tensor_copy(s.bswB_sb[:, :], s.p_c[:, 0:8])
        n = s.dv.inc(_i)
        s.b_bswB.did_wr("dv", n)
        s.b_pc.did_rd("dv", n)


def emit_B_mst(s, block):
    @block.gpsimd
    def _(g):
        ch = s.sem("cBs2")
        s.b_setup.wr(g, s)
        ch.inc(g.dma_start(out=s.rankc_sb[:, :], in_=s.hbm["rankcol"]), 16)
        ch.inc(g.dma_start(out=s.rankm_sb[:, :], in_=s.hbm["rankmask"]), 16)
        ch.inc(g.dma_start(out=s.iota8_sb[:, :], in_=s.hbm["iota8"]), 16)
        for jd in range(4):
            ch.inc(g.dma_start(out=s.maskB_sb[:, jd * 512:(jd + 1) * 512],
                               in_=s.hbm["maskB"][jd, :, :]), 16)
        for jp in range(32):
            ch.inc(g.dma_start(out=s.step_sb[:, jp * 32:(jp + 1) * 32],
                               in_=s.hbm["step32"][jp, :, :]), 16)
        s.b_setup.did_wr("cBs2", ch.n)


def emit_B_loadw(s, block, hname, ch):
    @block.gpsimd
    def _(g):
        sem = s.sem(ch)
        s.b_w.wr(g, s)
        for dc in range(ND):
            sem.inc(g.dma_start(
                out=s.w_sb[:, dc * 1024:(dc + 1) * 1024],
                in_=s.hbm[hname][dc * 128:(dc + 1) * 128, :]), 16)
        s.b_w.did_wr(ch, sem.n)


def emit_B_loadx(s, block, qc, hname="x1Tb"):
    @block.gpsimd
    def _(g, qc=qc):
        ch = s.sem("cBx")
        s.b_xcb.wr(g, s)
        for dc in range(ND):
            ch.inc(g.dma_start(
                out=s.xcb[:, dc * 512:(dc + 1) * 512],
                in_=s.hbm[hname][dc * 128:(dc + 1) * 128,
                                 qc * 512:(qc + 1) * 512]), 16)
        s.b_xcb.did_wr("cBx", ch.n)


def emit_B_kv(s, block):
    @block.vector
    def _(v):
        view = s.va.ap().rearrange("p (n x) -> p n x", x=65)
        _i = v.memset(view[:, :, 64:65], 1.0)
        n = s.dv.inc(_i)
        s.b_va.did_wr("dv", n)

    emit_B_loadw(s, block, "l0fkT", "cBw")
    for qc in range(NQ):
        emit_B_loadx(s, block, qc)
        for ec in range(NP):
            psb = s.b_ps0 if ec % 2 == 0 else s.b_ps1
            ps = s.p_s0 if ec % 2 == 0 else s.p_s1

            @block.tensor
            def _(t, ec=ec, ps=ps, psb=psb):
                s.b_xcb.rd(t, s)
                s.b_w.rd(t, s)
                psb.wr(t, s)
                for dc in range(ND):
                    _i = t.matmul(ps[:, 0:512],
                                  s.w_sb[:, dc * 1024 + ec * 128:
                                         dc * 1024 + (ec + 1) * 128],
                                  s.xcb[:, dc * 512:(dc + 1) * 512],
                                  start=(dc == 0), stop=(dc == ND - 1))
                n = s.mm.inc(_i)
                psb.did_wr("mm", n)
                s.b_xcb.did_rd("mm", n)
                s.b_w.did_rd("mm", n)

            @block.vector
            def _(v, ec=ec, qc=qc, ps=ps, psb=psb):
                psb.rd(v, s)
                s.b_setup.rd(v, s)
                s.b_kt.wr(v, s)
                _i = v.tensor_scalar_add(
                    s.kt[:, ec * T + qc * 512:ec * T + (qc + 1) * 512],
                    ps[:, 0:512], s.bk_sb[:, ec:ec + 1])
                n = s.dv.inc(_i)
                s.b_kt.did_wr("dv", n)
                psb.did_rd("dv", n)
                s.b_setup.did_rd("dv", n)

    emit_B_loadw(s, block, "l0fvT", "cBw")
    for qc in range(NQ):
        emit_B_loadx(s, block, qc)
        for tt in range(4):
            for eh in range(2):
                @block.tensor
                def _(t, tt=tt, eh=eh):
                    s.b_xcb.rd(t, s)
                    s.b_w.rd(t, s)
                    s.b_pc.wr(t, s)
                    for dc in range(ND):
                        _i = t.matmul(
                            s.p_c[:, 0:512],
                            s.xcb[:, dc * 512 + tt * 128:
                                  dc * 512 + (tt + 1) * 128],
                            s.w_sb[:, dc * 1024 + eh * 512:
                                   dc * 1024 + (eh + 1) * 512],
                            start=(dc == 0), stop=(dc == ND - 1))
                    n = s.mm.inc(_i)
                    s.b_pc.did_wr("mm", n)
                    s.b_xcb.did_rd("mm", n)
                    s.b_w.did_rd("mm", n)

                @block.vector
                def _(v, tt=tt, eh=eh, qc=qc):
                    s.b_pc.rd(v, s)
                    s.b_va.wr(v, s)
                    view = s.va.ap().rearrange("p (n x) -> p n x", x=65)
                    src = s.p_c.ap().rearrange("p (h x) -> p h x", x=64)
                    til = qc * 4 + tt
                    _i = v.tensor_copy(
                        view[:, til * 16 + eh * 8:til * 16 + (eh + 1) * 8,
                             0:64], src[:, 0:8, :])
                    n = s.dv.inc(_i)
                    s.b_va.did_wr("dv", n)
                    s.b_pc.did_rd("dv", n)

    for h in range(H):
        @block.tensor
        def _(t, h=h):
            s.b_va.rd(t, s)
            s.b_setup.rd(t, s)
            if h == 0:
                s.b_pg.wr(t, s)
            for jp in range(NT):
                _i = t.matmul(
                    s.p_g[0:65, h * 32:(h + 1) * 32] if h < 8 else
                    s.p_g[0:65, 512 + (h - 8) * 32:512 + (h - 7) * 32],
                    s.va_view(jp, h),
                    s.step_sb[:, jp * 32:(jp + 1) * 32],
                    start=(jp == 0), stop=(jp == NT - 1))
            n = s.mm.inc(_i)
            s.b_va.did_rd("mm", n)
            if h == H - 1:
                s.b_pg.did_wr("mm", n)

    @block.vector
    def _(v):
        s.b_pg.rd(v, s)
        s.b_sfx.wr(v, s)
        v.tensor_copy(s.sfx[0:65, 0:256], s.p_g[0:65, 0:256])
        _i = v.tensor_copy(s.sfx[0:65, 256:512], s.p_g[0:65, 512:768])
        n = s.dv.inc(_i)
        s.b_sfx.did_wr("dv", n)
        s.b_pg.did_rd("dv", n)


def emit_B_attn_qchunk(s, block, qc):
    SC = 1.0 / math.sqrt(C)
    emit_B_loadw(s, block, "l0fqT", "cBw")

    emit_B_loadx(s, block, qc)

    for ec in range(NP):
        psb = s.b_ps0 if ec % 2 == 0 else s.b_ps1
        ps = s.p_s0 if ec % 2 == 0 else s.p_s1

        @block.tensor
        def _(t, ec=ec, ps=ps, psb=psb):
            s.b_xcb.rd(t, s)
            s.b_w.rd(t, s)
            psb.wr(t, s)
            for dc in range(ND):
                _i = t.matmul(ps[:, 0:512],
                              s.w_sb[:, dc * 1024 + ec * 128:
                                     dc * 1024 + (ec + 1) * 128],
                              s.xcb[:, dc * 512:(dc + 1) * 512],
                              start=(dc == 0), stop=(dc == ND - 1))
            n = s.mm.inc(_i)
            psb.did_wr("mm", n)
            s.b_xcb.did_rd("mm", n)
            s.b_w.did_rd("mm", n)

        @block.vector
        def _(v, ec=ec, ps=ps, psb=psb):
            psb.rd(v, s)
            s.b_setup.rd(v, s)
            s.b_qt.wr(v, s)
            _i = v.tensor_scalar(s.qt[:, ec * 512:(ec + 1) * 512],
                                 ps[:, 0:512], SC, s.bq_sb[:, ec:ec + 1],
                                 ALU.mult, ALU.add)
            n = s.dv.inc(_i)
            s.b_qt.did_wr("dv", n)
            psb.did_rd("dv", n)
            s.b_setup.did_rd("dv", n)

    jmax = qc * 4 + 3
    pend = [None]
    for ec in range(NP):
        def premask(j, ps, psb, qc=qc):
            if j < qc * 4:
                return

            @block.vector
            def _(v, j=j, ps=ps, psb=psb, qc=qc):
                jd = j - qc * 4
                psb.rd(v, s)
                s.b_setup.rd(v, s)
                psb.wr(v, s)
                _i = v.tensor_mul(ps[:, 0:512], ps[:, 0:512],
                                  s.maskB_sb[:, jd * 512:(jd + 1) * 512])
                n = s.dv.inc(_i)
                v.wait_ge(s.dv.h, n)
                _i = v.tensor_mul(ps[:, 512:1024], ps[:, 512:1024],
                                  s.maskB_sb[:, jd * 512:(jd + 1) * 512])
                n = s.dv.inc(_i)
                psb.did_wr("dv", n)
                s.b_setup.did_rd("dv", n)

        emit_attn_pair(
            s, block, list(range(jmax + 1)),
            qtA=s.qt[0:64, ec * 512:(ec + 1) * 512],
            qtB=s.qt[64:128, ec * 512:(ec + 1) * 512],
            qt_buf=s.b_qt,
            kA_fn=lambda j, ec=ec: s.kt[0:64, ec * T + j * 128:
                                        ec * T + (j + 1) * 128],
            kB_fn=lambda j, ec=ec: s.kt[64:128, ec * T + j * 128:
                                        ec * T + (j + 1) * 128],
            ktbuf_fn=lambda j: s.b_kt,
            vaA_fn=lambda j, ec=ec: s.va_view(j, 2 * ec),
            vaB_fn=lambda j, ec=ec: s.va_view(j, 2 * ec + 1),
            vabuf_fn=lambda j: s.b_va,
            premask_fn=premask, parity=ec % 2)
        vw = av_views(s, ec % 2)

        def fin(ec=ec, vw=vw, qc=qc):
         if qc < NQ - 1:
             @block.vector
             def _(v, ec=ec, qc=qc, vw=vw):
                 s.b_sfx.rd(v, s)
                 for bb in vw["bufs"]:
                     bb.rd(v, s)
                     bb.wr(v, s)
                 j0 = qc * 4 + 4
                 _i = v.tensor_scalar_add(vw["avA"], vw["avA"],
                                          s.sfx[0:65, (2 * ec) * 32 + j0:
                                                (2 * ec) * 32 + j0 + 1])
                 n = s.dv.inc(_i)
                 _i = v.tensor_scalar_add(vw["avB"], vw["avB"],
                                          s.sfx[0:65, (2 * ec + 1) * 32 + j0:
                                                (2 * ec + 1) * 32 + j0 + 1])
                 n = s.dv.inc(_i)
                 for bb in vw["bufs"]:
                     bb.did_wr("dv", n)
                 s.b_sfx.did_rd("dv", n)

         emit_attn_norm_pre(s, block, vw)

         @block.vector
         def _(v, ec=ec, vw=vw):
             for bb in vw["bufs"]:
                 bb.rd(v, s)
             s.b_o1 = s.b_o12[0]
             s.b_o1.wr(v, s)
             s.b_aB.rd(v, s)
             _i = v.tensor_mul(s.a_o1[0:64, 0:512], vw["valA"],
                               s.a_aB[0:64, 0:512])
             n = s.dv.inc(_i)
             s.b_bB.rd(v, s)
             _i = v.tensor_mul(s.a_o1[64:128, 0:512], vw["valB"],
                               s.a_bB[0:64, 0:512])
             n = s.dv.inc(_i)
             for bb in vw["bufs"]:
                 bb.did_rd("dv", n)
             s.b_aB.did_rd("dv", n)
             s.b_bB.did_rd("dv", n)
             v.wait_ge(s.dv.h, n)
             s.b_setup.rd(v, s)
             _i = v.tensor_scalar_add(s.a_o1[:, 0:512], s.a_o1[:, 0:512],
                                      s.bv_sb[:, ec:ec + 1])
             n = s.dv.inc(_i)
             s.b_o1.did_wr("dv", n)
             s.b_setup.did_rd("dv", n)
             v.wait_ge(s.dv.h, n)
             s.b_xcb.wr(v, s)
             _i = v.tensor_copy(s.xcb[:, ec * 512:(ec + 1) * 512],
                                s.a_o1[:, 0:512])
             n = s.dv.inc(_i)
             s.b_xcb.did_wr("dv", n)

         @block.sync
         def _(sy, ec=ec, qc=qc):
             ch = s.sem("cBhf")
             s.b_o1.rd(sy, s)
             n = ch.inc(sy.dma_start(
                 out=s.hbm["hTf"][ec * 128:(ec + 1) * 128,
                                  qc * 512:(qc + 1) * 512],
                 in_=s.a_o1[:, 0:512]), 16)
             s.b_o1.did_rd("cBhf", n)

         @block.gpsimd
         def _(g, ec=ec, qc=qc):
             ch = s.sem("cBhb")
             s.b_o1.rd(g, s)
             n = ch.inc(g.dma_start(
                 out=s.hbm["hTb"][ec * 128:(ec + 1) * 128,
                                  qc * 512:(qc + 1) * 512],
                 in_=s.a_o1[:, 0:512]), 16)
             s.b_o1.did_rd("cBhb", n)

        if pend[0] is not None:
            pend[0]()
        pend[0] = fin

    pend[0]()

    for tt in range(4):
        j = qc * 4 + tt

        @block.tensor
        def _(t, tt=tt):
            s.b_xcb.rd(t, s)
            s.b_setup.rd(t, s)
            s.b_pc.wr(t, s)
            for dc in range(ND):
                _i = t.matmul(s.p_c[:, 0:8],
                              s.xcb[:, dc * 512 + tt * 128:
                                    dc * 512 + (tt + 1) * 128],
                              s.sw_sb[:, dc * 8:(dc + 1) * 8],
                              start=(dc == 0), stop=(dc == ND - 1))
            n = s.mm.inc(_i)
            s.b_pc.did_wr("mm", n)
            s.b_xcb.did_rd("mm", n)
            s.b_setup.did_rd("mm", n)

        @block.vector
        def _(v, j=j):
            s.b_pc.rd(v, s)
            s.b_bswB.rd(v, s)
            s.b_lg.wr(v, s)
            _i = v.tensor_add(s.lg_sb[:, 0:8], s.p_c[:, 0:8], s.bswB_sb[:, :])
            n = s.dv.inc(_i)
            s.b_lg.did_wr("dv", n)
            s.b_pc.did_rd("dv", n)
            v.wait_ge(s.dv.h, n)
            s.b_mx.wr(v, s)
            _i = v.max(s.mx_sb[:, 0:8], s.lg_sb[:, 0:8])
            n = s.dv.inc(_i)
            s.b_mx.did_wr("dv", n)
            v.wait_ge(s.dv.h, n)
            s.b_mi.wr(v, s)
            _i = v.max_index(s.mi_sb[:, 0:8], s.mx_sb[:, 0:8], s.lg_sb[:, 0:8])
            n = s.dv.inc(_i)
            s.b_mi.did_wr("dv", n)
            v.wait_ge(s.dv.h, n)
            s.b_ln.wr(v, s)
            _i = v.tensor_scalar_mul(s.ln_sb[:, 0:1], s.mx_sb[:, 0:1], -1.0)
            n = s.dv.inc(_i)
            s.b_ln.did_wr("dv", n)
            s.b_idxf.wr(v, s)
            _i = v.tensor_copy(s.idxf_sb[:, 0:1], s.mi_sb[:, 0:1])
            n = s.dv.inc(_i)
            s.b_idxf.did_wr("dv", n)

        @block.scalar
        def _(sc, j=j):
            s.b_lg.rd(sc, s)
            s.b_ln.rd(sc, s)
            s.b_pex.wr(sc, s)
            _i = sc.activation(s.pex_sb[:, 0:8], s.lg_sb[:, 0:8], AF.Exp,
                               bias=s.ln_sb[:, 0:1])
            n = s.act.inc(_i)
            s.b_pex.did_wr("act", n)
            s.b_lg.did_rd("act", n)
            s.b_ln.did_rd("act", n)

        @block.vector
        def _(v, j=j):
            s.b_pex.rd(v, s)
            s.b_zz.wr(v, s)
            _i = v.reduce_sum(s.zz_sb[:, 0:1], s.pex_sb[:, 0:8], axis=AX.X)
            n = s.dv.inc(_i)
            s.b_zz.did_wr("dv", n)
            s.b_pex.did_rd("dv", n)
            v.wait_ge(s.dv.h, n)
            s.b_rp.wr(v, s)
            _i = v.reciprocal(s.rp_sb[:, j:j + 1], s.zz_sb[:, 0:1])
            n = s.dv.inc(_i)
            s.b_rp.did_wr("dv", n)
            s.b_zz.did_rd("dv", n)
            s.b_idxf.rd(v, s)
            s.b_setup.rd(v, s)
            s.b_oh.wr(v, s)
            _i = v.tensor_single_scalar(s.oh_sb[:, j * 8:(j + 1) * 8],
                                        s.iota8_sb[:, :],
                                        s.idxf_sb[:, 0:1], ALU.is_equal)
            n = s.dv.inc(_i)
            s.b_oh.did_wr("dv", n)
            s.b_idxf.did_rd("dv", n)
            v.wait_ge(s.dv.h, n)
            s.b_ohb.wr(v, s)
            _i = v.tensor_copy(s.ohb_sb[:, j * 8:(j + 1) * 8],
                               s.oh_sb[:, j * 8:(j + 1) * 8])
            n = s.dv.inc(_i)
            s.b_ohb.did_wr("dv", n)
            s.b_oh.did_rd("dv", n)

        @block.tensor
        def _(t, j=j):
            s.b_oh.rd(t, s)
            s.b_pc.wr(t, s)
            _i = t.matmul(s.p_c[0:1, 0:8], s.ones128f[:, :],
                          s.oh_sb[:, j * 8:(j + 1) * 8],
                          start=True, stop=True)
            n = s.mm.inc(_i)
            s.b_oh.did_rd("mm", n)
            s.b_pc.did_wr("mm", n)

        @block.vector
        def _(v, j=j):
            s.b_pc.rd(v, s)
            s.b_tot.wr(v, s)
            _i = v.tensor_copy(s.tot_sb[0:1, j * 8:(j + 1) * 8],
                               s.p_c[0:1, 0:8])
            n = s.dv.inc(_i)
            s.b_tot.did_wr("dv", n)
            s.b_pc.did_rd("dv", n)


def emit_B_counts(s, block):
    @block.gpsimd
    def _(g):
        ch = s.sem("cBt")
        s.b_tot.rd(g, s)
        n = ch.inc(g.dma_start(out=s.hbm["scr_tot"], in_=s.tot_sb[:, :]), 16)
        g.wait_ge(ch.h, n)
        s.b_t32.wr(g, s)
        n = ch.inc(g.dma_start(
            out=s.t32_sb[:, :],
            in_=s.hbm["scr_tot"].rearrange("o (r c) -> (o r) c", c=8)), 16)
        s.b_t32.did_wr("cBt", n)

    @block.tensor
    def _(t):
        s.b_t32.rd(t, s)
        s.b_pc.wr(t, s)
        _i = t.matmul(s.p_c[0:1, 0:8], s.ones32f[:, :], s.t32_sb[:, :],
                      start=True, stop=True)
        n = s.mm.inc(_i)
        s.b_pc.did_wr("mm", n)
        s.b_t32.did_rd("mm", n)

    @block.vector
    def _(v):
        s.b_pc.rd(v, s)
        s.b_tall.wr(v, s)
        _i = v.tensor_copy(s.tall_sb[:, :], s.p_c[0:1, 0:8])
        n = s.dv.inc(_i)
        s.b_tall.did_wr("dv", n)
        s.b_pc.did_rd("dv", n)

    @block.tensor
    def _(t):
        s.b_tall.rd(t, s)
        s.b_setup.rd(t, s)
        s.b_pd.wr(t, s)
        _i = t.matmul(s.p_d[0:8, 0:8], s.rankc_sb[:, :], s.tall_sb[:, :],
                      start=True, stop=True)
        n = s.mm.inc(_i)
        s.b_pd.did_wr("mm", n)
        s.b_tall.did_rd("mm", n)

    @block.vector
    def _(v):
        s.b_pd.rd(v, s)
        s.b_slot.wr(v, s)
        _i = v.tensor_copy(s.slot_sb[:, :], s.p_d[0:8, 0:8])
        n = s.dv.inc(_i)
        s.b_slot.did_wr("dv", n)
        s.b_pd.did_rd("dv", n)

    @block.gpsimd
    def _(g):
        ch = s.sem("cBcc")
        s.b_slot.rd(g, s)
        n = ch.inc(g.dma_start(out=s.hbm["cc_in"], in_=s.slot_sb[:, :]), 16)
        g.wait_ge(ch.h, n)
        g.collective_compute(
            "AllReduce", mybir.AluOpType.add,
            replica_groups=[list(range(8))],
            ins=[s.hbm["cc_in"].opt()],
            outs=[s.hbm["cc_out"].opt()],
        ).then_inc(s.cc.h, 1)
        s.cc.n += 1
        g.wait_ge(s.cc.h, s.cc.n)
        s.b_gat.wr(g, s)
        n = ch.inc(g.dma_start(out=s.gat_sb[:, :], in_=s.hbm["cc_out"]), 16)
        s.b_gat.did_wr("cBcc", n)

    @block.tensor
    def _(t):
        s.b_gat.rd(t, s)
        s.b_setup.rd(t, s)
        s.b_pc.wr(t, s)
        _i = t.matmul(s.p_c[0:1, 0:8], s.rankm_sb[:, :], s.gat_sb[:, :],
                      start=True, stop=True)
        n = s.mm.inc(_i)
        s.b_pc.did_wr("mm", n)
        s.b_gat.did_rd("mm", n)

    @block.vector
    def _(v):
        s.b_pc.rd(v, s)
        s.b_coff.wr(v, s)
        _i = v.tensor_copy(s.coff_sb[:, :], s.p_c[0:1, 0:8])
        n = s.dv.inc(_i)
        s.b_coff.did_wr("dv", n)
        s.b_pc.did_rd("dv", n)

    @block.tensor
    def _(t):
        s.b_t32.rd(t, s)
        s.b_coff.rd(t, s)
        s.b_setup.rd(t, s)
        s.b_pd.wr(t, s)
        t.matmul(s.p_d[0:32, 0:8], s.ltri32_sb[:, :], s.t32_sb[:, :],
                 start=True, stop=False)
        _i = t.matmul(s.p_d[0:32, 0:8], s.ones1_32f[:, :], s.coff_sb[:, :],
                      start=False, stop=True)
        n = s.mm.inc(_i)
        s.b_pd.did_wr("mm", n)
        s.b_t32.did_rd("mm", n)
        s.b_coff.did_rd("mm", n)

    @block.vector
    def _(v):
        s.b_pd.rd(v, s)
        s.b_roff.wr(v, s)
        _i = v.tensor_copy(s.roff_sb[:, :], s.p_d[0:32, 0:8])
        n = s.dv.inc(_i)
        s.b_roff.did_wr("dv", n)
        s.b_pd.did_rd("dv", n)

    @block.gpsimd
    def _(g):
        ch = s.sem("cBt")
        s.b_roff.rd(g, s)
        n = ch.inc(g.dma_start(
            out=s.hbm["scr_row"].rearrange("o (r c) -> (o r) c", c=8),
            in_=s.roff_sb[:, :]), 16)
        g.wait_ge(ch.h, n)
        s.b_rrow.wr(g, s)
        n = ch.inc(g.dma_start(out=s.rrow_sb[:, :], in_=s.hbm["scr_row"]), 16)
        s.b_rrow.did_wr("cBt", n)


def emit_B_pos(s, block):
    @block.tensor
    def _(t):
        s.b_ohb.rd(t, s)
        s.b_setup.rd(t, s)
        s.b_pc.wr(t, s)
        _i = t.matmul(s.p_c[:, 0:256], s.ltri_sb[:, :], s.ohb_sb[:, 0:256],
                      start=True, stop=True)
        n = s.mm.inc(_i)
        s.b_pc.did_wr("mm", n)
        s.b_ohb.did_rd("mm", n)
        s.b_rrow.rd(t, s)
        s.b_pd.wr(t, s)
        _i = t.matmul(s.p_d[:, 0:256], s.ones1_128f[:, :],
                      s.rrow_sb[0:1, 0:256], start=True, stop=True)
        n = s.mm.inc(_i)
        s.b_pd.did_wr("mm", n)
        s.b_rrow.did_rd("mm", n)

    @block.vector
    def _(v):
        s.b_pd.rd(v, s)
        s.b_rb8.wr(v, s)
        _i = v.tensor_copy(s.rb8_sb[:, 0:256], s.p_d[:, 0:256])
        n = s.dv.inc(_i)
        s.b_rb8.did_wr("dv", n)
        s.b_pd.did_rd("dv", n)
        v.wait_ge(s.dv.h, n)
        s.b_pc.rd(v, s)
        s.b_pf8.wr(v, s)
        _i = v.tensor_add(s.pf8_sb[:, 0:256], s.p_c[:, 0:256],
                          s.rb8_sb[:, 0:256])
        n = s.dv.inc(_i)
        s.b_pf8.did_wr("dv", n)
        s.b_pc.did_rd("dv", n)
        s.b_rb8.did_rd("dv", n)
        v.wait_ge(s.dv.h, n)
        s.b_oh.rd(v, s)
        _i = v.tensor_mul(s.pf8_sb[:, 0:256], s.pf8_sb[:, 0:256],
                          s.oh_sb[:, 0:256])
        n = s.dv.inc(_i)
        s.b_pf8.did_wr("dv", n)
        s.b_oh.did_rd("dv", n)
        v.wait_ge(s.dv.h, n)
        s.b_pos.wr(v, s)
        _i = v.reduce_sum(
            s.pos_sb[:, 0:32],
            s.a_o1[:, 512:768].rearrange("p (t e) -> p t e", e=8),
            axis=AX.X)
        n = s.dv.inc(_i)
        s.b_pos.did_wr("dv", n)
        s.b_pf8.did_rd("dv", n)
        v.wait_ge(s.dv.h, n)
        s.b_kept.wr(v, s)
        _i = v.tensor_single_scalar(s.kept_sb[:, 0:32], s.pos_sb[:, 0:32],
                                    float(CAP), ALU.is_lt)
        n = s.dv.inc(_i)
        s.b_kept.did_wr("dv", n)
        s.b_pos.did_rd("dv", n)

    for j in range(NT):
        sl = j % 2

        @block.vector
        def _(v, j=j, sl=sl):
            s.b_oh.rd(v, s)
            s.b_kept.rd(v, s)
            s.b_ohkb2[sl].wr(v, s)
            _i = v.tensor_scalar_mul(s.ohkb_sb[:, sl * 8:(sl + 1) * 8],
                                     s.oh_sb[:, j * 8:(j + 1) * 8],
                                     s.kept_sb[:, j:j + 1])
            n = s.dv.inc(_i)
            s.b_ohkb2[sl].did_wr("dv", n)
            s.b_oh.did_rd("dv", n)
            s.b_kept.did_rd("dv", n)

        @block.tensor
        def _(t, j=j, sl=sl):
            s.b_ohkb2[sl].rd(t, s)
            s.b_setup.rd(t, s)
            pp = s.p_c if sl == 0 else s.p_d
            ppb = s.b_pc if sl == 0 else s.b_pd
            ppb.wr(t, s)
            _i = t.transpose(pp[0:8, 0:128],
                             s.ohkb_sb[:, sl * 8:(sl + 1) * 8],
                             s.identf_sb[:, :])
            n = s.mm.inc(_i)
            ppb.did_wr("mm", n)
            s.b_ohkb2[sl].did_rd("mm", n)

        @block.vector
        def _(v, j=j, sl=sl):
            pp = s.p_c if sl == 0 else s.p_d
            ppb = s.b_pc if sl == 0 else s.b_pd
            ppb.rd(v, s)
            s.b_ohkT.wr(v, s)
            _i = v.tensor_copy(s.ohkT_sb[0:8, j * 128:(j + 1) * 128],
                               pp[0:8, 0:128])
            n = s.dv.inc(_i)
            s.b_ohkT.did_wr("dv", n)
            ppb.did_rd("dv", n)


def eb_tc0_loads(s, block):
    @block.gpsimd
    def _(g):
        ch = s.sem("cBs")
        s.b_setup.wr(g, s)
        n = ch.inc(g.dma_start(out=s.be_sb[:, :], in_=s.hbm["l0fbe"]), 16)
        s.b_setup.did_wr("cBs", n)

    @block.scalar
    def _(sc):
        ch = s.sem("cBwes")
        for e_ in range(E):
            s.b_wes[e_].wr(sc, s)
            for dc in range(ND):
                ch.inc(sc.dma_start(
                    out=s.we_sb[:, (e_ * ND + dc) * 1024:
                                (e_ * ND + dc + 1) * 1024],
                    in_=s.hbm["l0fWeT"][e_, dc * 128:(dc + 1) * 128, :]),
                    16)
            s.b_wes[e_].did_wr("cBwes", ch.n)


def eb_mBprep(s, block, tc):
    for tt in range(4):
        j = tc * 4 + tt

        @block.vector
        def _(v, j=j, tt=tt):
            s.b_rp.rd(v, s)
            s.b_kept.rd(v, s)
            s.b_m1c.wr(v, s)
            _i = v.tensor_mul(s.m1c_sb[:, 0:1], s.rp_sb[:, j:j + 1],
                              s.kept_sb[:, j:j + 1])
            n = s.dv.inc(_i)
            s.b_m1c.did_wr("dv", n)
            v.wait_ge(s.dv.h, n)
            s.b_m2c.wr(v, s)
            _i = v.tensor_sub(s.m2c_sb[:, 0:1], s.rp_sb[:, j:j + 1],
                              s.m1c_sb[:, 0:1])
            n = s.dv.inc(_i)
            s.b_m2c.did_wr("dv", n)
            s.b_rp.did_rd("dv", n)
            s.b_kept.did_rd("dv", n)

        @block.tensor
        def _(t, tt=tt):
            s.b_m1c.rd(t, s)
            s.b_m2c.rd(t, s)
            s.b_setup.rd(t, s)
            if tt == 0:
                s.b_pc.wr(t, s)
                s.b_pd.wr(t, s)
            _i = t.matmul(s.p_c[0:1, tt * 128:(tt + 1) * 128],
                          s.m1c_sb[:, 0:1], s.identf_sb[:, :],
                          start=(tt == 0), stop=(tt == 3))
            n = s.mm.inc(_i)
            _i = t.matmul(s.p_d[0:1, tt * 128:(tt + 1) * 128],
                          s.m2c_sb[:, 0:1], s.identf_sb[:, :],
                          start=(tt == 0), stop=(tt == 3))
            n = s.mm.inc(_i)
            s.b_m1c.did_rd("mm", n)
            s.b_m2c.did_rd("mm", n)
            if tt == 3:
                s.b_pc.did_wr("mm", n)
                s.b_pd.did_wr("mm", n)

    @block.vector
    def _(v):
        s.b_pc.rd(v, s)
        s.b_m1r.wr(v, s)
        _i = v.tensor_copy(s.m1r_sb[:, :], s.p_c[0:1, 0:512])
        n = s.dv.inc(_i)
        s.b_m1r.did_wr("dv", n)
        s.b_pc.did_rd("dv", n)
        s.b_pd.rd(v, s)
        s.b_m2r.wr(v, s)
        _i = v.tensor_copy(s.m2r_sb[:, :], s.p_d[0:1, 0:512])
        n = s.dv.inc(_i)
        s.b_m2r.did_wr("dv", n)
        s.b_pd.did_rd("dv", n)

    @block.tensor
    def _(t):
        s.b_m1r.rd(t, s)
        s.b_m2r.rd(t, s)
        s.b_pc.wr(t, s)
        _i = t.matmul(s.p_c[:, 0:512], s.ones1_128f[:, :], s.m1r_sb[:, :],
                      start=True, stop=True)
        n = s.mm.inc(_i)
        s.b_pc.did_wr("mm", n)
        s.b_pd.wr(t, s)
        _i = t.matmul(s.p_d[:, 0:512], s.ones1_128f[:, :], s.m2r_sb[:, :],
                      start=True, stop=True)
        n = s.mm.inc(_i)
        s.b_pd.did_wr("mm", n)
        s.b_m1r.did_rd("mm", n)
        s.b_m2r.did_rd("mm", n)

    @block.vector
    def _(v):
        s.b_pc.rd(v, s)
        s.b_mB1.wr(v, s)
        _i = v.tensor_copy(s.mB1_sb[:, :], s.p_c[:, 0:512])
        n = s.dv.inc(_i)
        s.b_mB1.did_wr("dv", n)
        s.b_pc.did_rd("dv", n)
        s.b_pd.rd(v, s)
        s.b_mB2.wr(v, s)
        _i = v.tensor_copy(s.mB2_sb[:, :], s.p_d[:, 0:512])
        n = s.dv.inc(_i)
        s.b_mB2.did_wr("dv", n)
        s.b_pd.did_rd("dv", n)


def eb_prep_ok(s, block, tc):
    for e_ in range(E):
        @block.scalar
        def _(sc, e_=e_, tc=tc):
            ch = s.sem("cBok")
            s.b_ohkT.rd(sc, s)
            s.b_okr.wr(sc, s)
            n = ch.inc(sc.dma_start(
                out=s.okr_sb[0:1, 0:512],
                in_=s.ohkT_sb[e_:e_ + 1, tc * 512:(tc + 1) * 512]), 16)
            s.b_okr.did_wr("cBok", n)
            s.b_ohkT.did_rd("cBok", n)

        @block.tensor
        def _(t, e_=e_):
            s.b_okr.rd(t, s)
            s.b_pc.wr(t, s)
            _i = t.matmul(s.p_c[:, 0:512], s.ones1_128b[:, :],
                          s.okr_sb[0:1, 0:512],
                          start=True, stop=True)
            n = s.mm.inc(_i)
            s.b_pc.did_wr("mm", n)
            s.b_okr.did_rd("mm", n)

        @block.vector
        def _(v, e_=e_):
            s.b_pc.rd(v, s)
            s.b_okA.wr(v, s)
            _i = v.tensor_copy(s.okA_sb[:, e_ * 512:(e_ + 1) * 512],
                               s.p_c[:, 0:512])
            n = s.dv.inc(_i)
            s.b_okA.did_wr("dv", n)
            s.b_pc.did_rd("dv", n)

    @block.scalar
    def _(sc, tc=tc):
        ch = s.sem("cBht")
        s.b_htb.wr(sc, s)
        for dc in range(ND):
            ch.inc(sc.dma_start(
                out=s.htb_sb[:, dc * 512:(dc + 1) * 512],
                in_=s.hbm["hTb"][dc * 128:(dc + 1) * 128,
                                 tc * 512:(tc + 1) * 512]), 16)
        s.b_htb.did_wr("cBht", ch.n)


BANKS = None


def eb_banks(s):
    return [(s.p_s0, s.b_ps0, 0), (s.p_s0, s.b_ps0, 512),
            (s.p_s1, s.b_ps1, 0), (s.p_s1, s.b_ps1, 512)]


def eb_acc(s, block, tc, ph, es=None, bias=True):
    banks = eb_banks(s)
    for e_ in (es if es is not None else range(E)):
        for h in range(2):
            sl = (e_ * 2 + h) % 2

            if ph == 0:
                @block.vector
                def _(v, e_=e_, h=h, sl=sl):
                    s.b_htb.rd(v, s)
                    s.b_okA.rd(v, s)
                    s.b_hm2[sl].wr(v, s)
                    for i, dc in enumerate(range(h * 4, h * 4 + 4)):
                        _i = v.tensor_mul(
                            s.hm_sb2[sl][:, i * 512:(i + 1) * 512],
                            s.htb_sb[:, dc * 512:(dc + 1) * 512],
                            s.okA_sb[:, e_ * 512:(e_ + 1) * 512])
                    n = s.dv.inc(_i)
                    s.b_hm2[sl].did_wr("dv", n)
                    s.b_htb.did_rd("dv", n)
                    s.b_okA.did_rd("dv", n)
            else:
                @block.gpsimd
                def _(g, e_=e_, h=h, sl=sl):
                    s.b_htb.rd(g, s)
                    s.b_okA.rd(g, s)
                    s.b_hm2[sl].wr(g, s)
                    for i, dc in enumerate(range(h * 4, h * 4 + 4)):
                        _i = g.tensor_mul(
                            s.hm_sb2[sl][:, i * 512:(i + 1) * 512],
                            s.htb_sb[:, dc * 512:(dc + 1) * 512],
                            s.okA_sb[:, e_ * 512:(e_ + 1) * 512])
                    n = s.gp.inc(_i)
                    s.b_hm2[sl].did_wr("gp", n)
                    s.b_htb.did_rd("gp", n)
                    s.b_okA.did_rd("gp", n)

            @block.tensor
            def _(t, ph=ph, e_=e_, h=h, sl=sl):
                s.b_wes[e_].rd(t, s)
                s.b_hm2[sl].rd(t, s)
                if e_ == 0 and h == 0:
                    s.b_ps0.wr(t, s)
                    s.b_ps1.wr(t, s)
                for fc in range(ph * 4, ph * 4 + 4):
                    ps, psb, po = banks[fc - ph * 4]
                    for i, dc in enumerate(range(h * 4, h * 4 + 4)):
                        _i = t.matmul(
                            ps[:, po:po + 512],
                            s.we_sb[:, (e_ * ND + dc) * 1024 + fc * 128:
                                    (e_ * ND + dc) * 1024 +
                                    (fc + 1) * 128],
                            s.hm_sb2[sl][:, i * 512:(i + 1) * 512],
                            start=(e_ == 0 and h == 0 and i == 0),
                            stop=False)
                n = s.mm.inc(_i)
                s.b_wes[e_].did_rd("mm", n)
                s.b_hm2[sl].did_rd("mm", n)

    if not bias:
        return

    @block.tensor
    def _(t, ph=ph, tc=tc):
        s.b_setup.rd(t, s)
        s.b_ohkT.rd(t, s)
        for fc in range(ph * 4, ph * 4 + 4):
            ps, psb, po = banks[fc - ph * 4]
            _i = t.matmul(ps[:, po:po + 512],
                          s.be_sb[0:8, fc * 128:(fc + 1) * 128],
                          s.ohkT_sb[0:8, tc * 512:(tc + 1) * 512],
                          start=False, stop=True)
            n = s.mm.inc(_i)
            if fc == ph * 4 + 3:
                s.b_ps0.did_wr("mm", n)
                s.b_ps1.did_wr("mm", n)
        s.b_ohkT.did_rd("mm", n)


def eb_comb(s, block, tc, ph):
    banks = eb_banks(s)
    for fc in range(ph * 4, ph * 4 + 4):
        ps, psb, po = banks[fc - ph * 4]

        @block.sync
        def _(sy, fc=fc, tc=tc):
            ch = s.sem("cBhtf%d" % (fc % 2))
            s.b_htf[fc % 2].wr(sy, s)
            n = ch.inc(sy.dma_start(
                out=s.a_xf[:, (fc % 2) * 512:((fc % 2) + 1) * 512],
                in_=s.hbm["hTf"][fc * 128:(fc + 1) * 128,
                                 tc * 512:(tc + 1) * 512]), 16)
            s.b_htf[fc % 2].did_wr("cBhtf%d" % (fc % 2), n)
            ch = s.sem("cBx1f%d" % (fc % 2))
            s.b_x1f2[fc % 2].wr(sy, s)
            n = ch.inc(sy.dma_start(
                out=s.x1f_sb[:, (fc % 2) * 512:((fc % 2) + 1) * 512],
                in_=s.hbm["x1Tf"][fc * 128:(fc + 1) * 128,
                                  tc * 512:(tc + 1) * 512]), 16)
            s.b_x1f2[fc % 2].did_wr("cBx1f%d" % (fc % 2), n)

        @block.vector
        def _(v, fc=fc, ps=ps, psb=psb, po=po):
            psb.rd(v, s)
            s.b_mB1.rd(v, s)
            s.b_rb8.wr(v, s)
            _i = v.tensor_mul(s.a_o1[:, 0:512], ps[:, po:po + 512],
                              s.mB1_sb[:, :])
            n = s.dv.inc(_i)
            s.b_rb8.did_wr("dv", n)
            psb.did_rd("dv", n)
            s.b_mB1.did_rd("dv", n)
            s.b_htf[fc % 2].rd(v, s)
            s.b_mB2.rd(v, s)
            s.b_o2.wr(v, s)
            _i = v.tensor_mul(
                s.a_o1[:, 512:1024],
                s.a_xf[:, (fc % 2) * 512:((fc % 2) + 1) * 512],
                s.mB2_sb[:, :])
            n = s.dv.inc(_i)
            s.b_o2.did_wr("dv", n)
            s.b_htf[fc % 2].did_rd("dv", n)
            s.b_mB2.did_rd("dv", n)
            v.wait_ge(s.dv.h, n)
            s.b_rb8.rd(v, s)
            s.b_o2.rd(v, s)
            s.b_rb8.wr(v, s)
            _i = v.tensor_add(s.a_o1[:, 0:512], s.a_o1[:, 0:512],
                              s.a_o1[:, 512:1024])
            n = s.dv.inc(_i)
            s.b_rb8.did_wr("dv", n)
            s.b_o2.did_rd("dv", n)
            v.wait_ge(s.dv.h, n)
            s.b_x1f2[fc % 2].rd(v, s)
            s.b_r.wr(v, s)
            _i = v.tensor_add(
                s.a_r[:, fc * 512:(fc + 1) * 512], s.a_o1[:, 0:512],
                s.x1f_sb[:, (fc % 2) * 512:((fc % 2) + 1) * 512])
            n = s.dv.inc(_i)
            s.b_r.did_wr("dv", n)
            s.b_x1f2[fc % 2].did_rd("dv", n)
            s.b_rb8.did_rd("dv", n)


def eb_lns(s, block, tc, mid1=None, mid2=None):
    def post_cb(ec, off, ob):
        @block.sync
        def _(sy, ec=ec, tc=tc):
            ch = s.sem("cBx1f%d" % (ec % 2))
            s.b_x1f2[ec % 2].wr(sy, s)
            n = ch.inc(sy.dma_start(
                out=s.x1f_sb[:, (ec % 2) * 512:((ec % 2) + 1) * 512],
                in_=s.hbm["x1Tf"][ec * 128:(ec + 1) * 128,
                                  tc * 512:(tc + 1) * 512]), 16)
            s.b_x1f2[ec % 2].did_wr("cBx1f%d" % (ec % 2), n)

        @block.vector
        def _(v, ec=ec, off=off, ob=ob):
            ob.rd(v, s)
            s.b_x1f2[ec % 2].rd(v, s)
            s.b_r.wr(v, s)
            _i = v.tensor_add(s.a_r[:, ec * 512:(ec + 1) * 512],
                              s.a_o1[:, off:off + 512],
                              s.x1f_sb[:, (ec % 2) * 512:((ec % 2) + 1) * 512])
            n = s.dv.inc(_i)
            s.b_r.did_wr("dv", n)
            ob.did_rd("dv", n)
            s.b_x1f2[ec % 2].did_rd("dv", n)

    emit_ln_store(s, block, s.b_r, s.a_r, None, None, qc=tc,
                  gamma=s.gb_g, beta=s.gb_b, store=False, post_cb=post_cb,
                  mid_cb=mid1)

    emit_ln_store(s, block, s.b_r, s.a_r, s.hbm["x2Tf"], s.hbm["x2Tb"],
                  qc=tc, ch_of="cBof", ch_ob="cBob", mid_cb=mid2)



def emit_B_experts(s, block):
    eb_tc0_loads(s, block)
    eb_prep_ok(s, block, 0)
    eb_acc(s, block, 0, 0)
    for tc in range(NQ):
        eb_mBprep(s, block, tc)
        eb_comb(s, block, tc, 0)
        eb_acc(s, block, tc, 1)
        eb_comb(s, block, tc, 1)
        if tc + 1 < NQ:
            eb_prep_ok(s, block, tc + 1)
            mid1 = lambda tc=tc: eb_acc(s, block, tc + 1, 0,
                                        es=range(0, 4), bias=False)
            mid2 = lambda tc=tc: eb_acc(s, block, tc + 1, 0,
                                        es=range(4, 8), bias=True)
        else:
            mid1 = mid2 = None
        eb_lns(s, block, tc, mid1=mid1, mid2=mid2)


def emit_stage_B(s, block):
    emit_B_setup(s, block)
    emit_B_kv(s, block)
    for qc in range(NQ):
        emit_B_attn_qchunk(s, block, qc)
    emit_B_counts(s, block)
    emit_B_pos(s, block)


def emit_C_setup(s, block):
    @block.gpsimd
    def _(g):
        ch = s.sem("cCs")
        for ec in range(ND):
            ch.inc(g.dma_start(out=s.cqb_sb[:, ec:ec + 1],
                               in_=s.hbm["l1qb"][ec * 128:(ec + 1) * 128, :]),
                   16)
            ch.inc(g.dma_start(out=s.ckb_sb[:, ec:ec + 1],
                               in_=s.hbm["l1kb"][ec * 128:(ec + 1) * 128, :]),
                   16)
            ch.inc(g.dma_start(out=s.cvb_sb[:, ec:ec + 1],
                               in_=s.hbm["l1vb"][ec * 128:(ec + 1) * 128, :]),
                   16)
            ch.inc(g.dma_start(out=s.cob_sb[:, ec:ec + 1],
                               in_=s.hbm["l1ob"][ec * 128:(ec + 1) * 128, :]),
                   16)
        ch.inc(g.dma_start(out=s.p32_sb[:, :], in_=s.hbm["p32"]), 16)
        s.b_csetup.did_wr("cCs", ch.n)

    @block.vector
    def _(v):
        s.b_csetup.rd(v, s)
        _i = v.tensor_scalar_mul(s.cqb_sb[:, 0:8], s.cqb_sb[:, 0:8],
                                 1.0 / math.sqrt(HD))
        n = s.dv.inc(_i)
        s.b_csetup.did_rd("dv", n)
        s.b_csetup.w["dv"] = n


def emit_C_rope_tile(s, block, ps, psb, bias_col, scale, qc, out_t, out_buf,
                     out_col):
    @block.vector
    def _(v):
        psb.rd(v, s)
        s.b_csetup.rd(v, s)
        s.b_craw.wr(v, s)
        _i = v.tensor_scalar(s.craw_sb[:, 0:512], ps[:, 0:512], scale,
                             bias_col, ALU.mult, ALU.add)
        n = s.dv.inc(_i)
        s.b_craw.did_wr("dv", n)
        psb.did_rd("dv", n)
        s.b_csetup.did_rd("dv", n)

    @block.vector
    def _(v):
        s.b_craw.rd(v, s)
        s.b_crawb.wr(v, s)
        _i = v.tensor_copy(s.crawb_sb[:, 0:512], s.craw_sb[:, 0:512])
        n = s.dv.inc(_i)
        s.b_crawb.did_wr("dv", n)

    @block.tensor
    def _(t):
        s.b_crawb.rd(t, s)
        s.b_csetup.rd(t, s)
        s.b_pd.wr(t, s)
        _i = t.matmul(s.p_d[:, 0:512], s.p32_sb[:, :], s.crawb_sb[:, 0:512],
                      start=True, stop=True)
        n = s.mm.inc(_i)
        s.b_pd.did_wr("mm", n)
        s.b_crawb.did_rd("mm", n)

    @block.vector
    def _(v):
        s.b_craw.rd(v, s)
        s.b_ccs.rd(v, s)
        s.b_craw.wr(v, s)
        _i = v.tensor_mul(s.craw_sb[:, 0:512], s.craw_sb[:, 0:512],
                          s.cosb_sb[:, 0:512])
        n = s.dv.inc(_i)
        s.b_craw.did_wr("dv", n)
        s.b_pd.rd(v, s)
        s.b_crawb.wr(v, s)
        _i = v.tensor_mul(s.crawb_sb[:, 0:512], s.p_d[:, 0:512],
                          s.sinb_sb[:, 0:512])
        n = s.dv.inc(_i)
        s.b_crawb.did_wr("dv", n)
        s.b_pd.did_rd("dv", n)
        s.b_ccs.did_rd("dv", n)
        v.wait_ge(s.dv.h, n)
        out_buf.wr(v, s)
        _i = v.tensor_add(out_t[:, out_col:out_col + 512],
                          s.craw_sb[:, 0:512], s.crawb_sb[:, 0:512])
        n = s.dv.inc(_i)
        out_buf.did_wr("dv", n)
        s.b_craw.did_rd("dv", n)
        s.b_crawb.did_rd("dv", n)


def emit_stage_C(s, block):
    emit_C_setup(s, block)

    @block.vector
    def _(v):
        view = s.va.ap().rearrange("p (n x) -> p n x", x=65)
        _i = v.memset(view[:, :, 64:65], 1.0)
        n = s.dv.inc(_i)
        s.b_va.did_wr("dv", n)

    def load_cs(qc):
        @block.gpsimd
        def _(g, qc=qc):
            ch = s.sem("cCcs")
            s.b_ccs.wr(g, s)
            ch.inc(g.dma_start(out=s.cosb_sb[:, :],
                               in_=s.hbm["ropecos"][:, qc * 512:
                                                    (qc + 1) * 512]), 16)
            ch.inc(g.dma_start(out=s.sinb_sb[:, :],
                               in_=s.hbm["ropesin"][:, qc * 512:
                                                    (qc + 1) * 512]), 16)
            s.b_ccs.did_wr("cCcs", ch.n)

    emit_B_loadw(s, block, "l1kT", "cCw")
    for qc in range(NQ):
        emit_B_loadx(s, block, qc, hname="x2Tb")
        load_cs(qc)
        for ec in range(NP):
            psb = s.b_ps0 if ec % 2 == 0 else s.b_ps1
            ps = s.p_s0 if ec % 2 == 0 else s.p_s1

            @block.tensor
            def _(t, ec=ec, ps=ps, psb=psb):
                s.b_xcb.rd(t, s)
                s.b_w.rd(t, s)
                psb.wr(t, s)
                for dc in range(ND):
                    _i = t.matmul(ps[:, 0:512],
                                  s.w_sb[:, dc * 1024 + ec * 128:
                                         dc * 1024 + (ec + 1) * 128],
                                  s.xcb[:, dc * 512:(dc + 1) * 512],
                                  start=(dc == 0), stop=(dc == ND - 1))
                n = s.mm.inc(_i)
                psb.did_wr("mm", n)
                s.b_xcb.did_rd("mm", n)
                s.b_w.did_rd("mm", n)

            emit_C_rope_tile(s, block, ps, psb, s.ckb_sb[:, ec:ec + 1], 1.0,
                             qc, s.kt, s.b_kt,
                             ec * T + qc * 512)

    emit_B_loadw(s, block, "l1vT", "cCw")
    for qc in range(NQ):
        emit_B_loadx(s, block, qc, hname="x2Tb")
        for tt in range(4):
            for eh in range(2):
                @block.tensor
                def _(t, tt=tt, eh=eh):
                    s.b_xcb.rd(t, s)
                    s.b_w.rd(t, s)
                    s.b_pc.wr(t, s)
                    for dc in range(ND):
                        _i = t.matmul(
                            s.p_c[:, 0:512],
                            s.xcb[:, dc * 512 + tt * 128:
                                  dc * 512 + (tt + 1) * 128],
                            s.w_sb[:, dc * 1024 + eh * 512:
                                   dc * 1024 + (eh + 1) * 512],
                            start=(dc == 0), stop=(dc == ND - 1))
                    n = s.mm.inc(_i)
                    s.b_pc.did_wr("mm", n)
                    s.b_xcb.did_rd("mm", n)
                    s.b_w.did_rd("mm", n)

                @block.vector
                def _(v, tt=tt, eh=eh, qc=qc):
                    s.b_pc.rd(v, s)
                    s.b_va.wr(v, s)
                    view = s.va.ap().rearrange("p (n x) -> p n x", x=65)
                    src = s.p_c.ap().rearrange("p (h x) -> p h x", x=64)
                    til = qc * 4 + tt
                    _i = v.tensor_copy(
                        view[:, til * 16 + eh * 8:til * 16 + (eh + 1) * 8,
                             0:64], src[:, 0:8, :])
                    n = s.dv.inc(_i)
                    s.b_va.did_wr("dv", n)
                    s.b_pc.did_rd("dv", n)

    def q_pack(qc):
        emit_B_loadw(s, block, "l1qT", "cCw")
        emit_B_loadx(s, block, qc, hname="x2Tb")
        load_cs(qc)
        for ec in range(NP):
            psb = s.b_ps0 if ec % 2 == 0 else s.b_ps1
            ps = s.p_s0 if ec % 2 == 0 else s.p_s1

            @block.tensor
            def _(t, ec=ec, ps=ps, psb=psb):
                s.b_xcb.rd(t, s)
                s.b_w.rd(t, s)
                psb.wr(t, s)
                for dc in range(ND):
                    _i = t.matmul(ps[:, 0:512],
                                  s.w_sb[:, dc * 1024 + ec * 128:
                                         dc * 1024 + (ec + 1) * 128],
                                  s.xcb[:, dc * 512:(dc + 1) * 512],
                                  start=(dc == 0), stop=(dc == ND - 1))
                n = s.mm.inc(_i)
                psb.did_wr("mm", n)
                s.b_xcb.did_rd("mm", n)
                s.b_w.did_rd("mm", n)

            emit_C_rope_tile(s, block, ps, psb, s.cqb_sb[:, ec:ec + 1],
                             1.0 / math.sqrt(HD), qc, s.qt, s.b_qt, ec * 512)

    q_pack(0)
    for qc in range(NQ):
        pend = [None]
        for ec in range(NP):
            emit_attn_pair(
                s, block, list(range(NT)),
                qtA=s.qt[0:64, ec * 512:(ec + 1) * 512],
                qtB=s.qt[64:128, ec * 512:(ec + 1) * 512],
                qt_buf=s.b_qt,
                kA_fn=lambda j, ec=ec: s.kt[0:64, ec * T + j * 128:
                                            ec * T + (j + 1) * 128],
                kB_fn=lambda j, ec=ec: s.kt[64:128, ec * T + j * 128:
                                            ec * T + (j + 1) * 128],
                ktbuf_fn=lambda j: s.b_kt,
                vaA_fn=lambda j, ec=ec: s.va_view(j, 2 * ec),
                vaB_fn=lambda j, ec=ec: s.va_view(j, 2 * ec + 1),
                vabuf_fn=lambda j: s.b_va, parity=ec % 2)
            vw = av_views(s, ec % 2)

            def fin(ec=ec, vw=vw):
                emit_attn_norm_pre(s, block, vw)

                @block.vector
                def _(v, ec=ec, vw=vw):
                    for bb in vw["bufs"]:
                        bb.rd(v, s)
                    s.b_ct1.wr(v, s)
                    s.b_aB.rd(v, s)
                    _i = v.tensor_mul(s.ct1_sb[0:64, 0:512], vw["valA"],
                                      s.a_aB[0:64, 0:512])
                    n = s.dv.inc(_i)
                    s.b_bB.rd(v, s)
                    _i = v.tensor_mul(s.ct1_sb[64:128, 0:512], vw["valB"],
                                      s.a_bB[0:64, 0:512])
                    n = s.dv.inc(_i)
                    for bb in vw["bufs"]:
                        bb.did_rd("dv", n)
                    s.b_aB.did_rd("dv", n)
                    s.b_bB.did_rd("dv", n)
                    s.b_ct1.did_wr("dv", n)
                    v.wait_ge(s.dv.h, n)
                    s.b_csetup.rd(v, s)
                    s.b_xcb.wr(v, s)
                    _i = v.tensor_scalar_add(
                        s.xcb[:, ec * 512:(ec + 1) * 512],
                        s.ct1_sb[:, 0:512], s.cvb_sb[:, ec:ec + 1])
                    n = s.dv.inc(_i)
                    s.b_xcb.did_wr("dv", n)
                    s.b_ct1.did_rd("dv", n)
                    s.b_csetup.did_rd("dv", n)

            if pend[0] is not None:
                pend[0]()
            pend[0] = fin

        pend[0]()

        emit_B_loadw(s, block, "l1oT", "cCw")
        for ec in range(ND):
            psb = s.b_ps0 if ec % 2 == 0 else s.b_ps1
            ps = s.p_s0 if ec % 2 == 0 else s.p_s1

            @block.sync
            def _(sy, ec=ec, qc=qc):
                ch = s.sem("cCxf0")
                s.b_xf[0].wr(sy, s)
                n = ch.inc(sy.dma_start(
                    out=s.a_xf[:, 0:512],
                    in_=s.hbm["x2Tf"][ec * 128:(ec + 1) * 128,
                                      qc * 512:(qc + 1) * 512]), 16)
                s.b_xf[0].did_wr("cCxf0", n)

            @block.tensor
            def _(t, ec=ec, ps=ps, psb=psb):
                s.b_xcb.rd(t, s)
                s.b_w.rd(t, s)
                psb.wr(t, s)
                for dc in range(ND):
                    _i = t.matmul(ps[:, 0:512],
                                  s.w_sb[:, dc * 1024 + ec * 128:
                                         dc * 1024 + (ec + 1) * 128],
                                  s.xcb[:, dc * 512:(dc + 1) * 512],
                                  start=(dc == 0), stop=(dc == ND - 1))
                n = s.mm.inc(_i)
                psb.did_wr("mm", n)
                s.b_xcb.did_rd("mm", n)
                s.b_w.did_rd("mm", n)

            @block.vector
            def _(v, ec=ec, ps=ps, psb=psb):
                psb.rd(v, s)
                s.b_xf[0].rd(v, s)
                s.b_csetup.rd(v, s)
                s.b_r.wr(v, s)
                _i = v.scalar_tensor_tensor(
                    s.a_r[:, ec * 512:(ec + 1) * 512], ps[:, 0:512],
                    s.cob_sb[:, ec:ec + 1],
                    s.a_xf[:, 0:512],
                    ALU.add, ALU.add)
                n = s.dv.inc(_i)
                psb.did_rd("dv", n)
                s.b_xf[0].did_rd("dv", n)
                s.b_csetup.did_rd("dv", n)
                s.b_r.did_wr("dv", n)

        emit_ln_store(s, block, s.b_r, s.a_r, s.hbm["x3Tf"], s.hbm["x3Tb"],
                      qc=qc, ch_of="cCof", ch_ob="cCob",
                      mid_cb=(lambda qc=qc: q_pack(qc + 1))
                      if qc + 1 < NQ else None)


def emit_stage_D(s, block):
    @block.gpsimd
    def _(g):
        ch = s.sem("cDx")
        s.b_x3b.wr(g, s)
        for dc in range(ND):
            ch.inc(g.dma_start(
                out=s.x3b_sb[:, dc * T:(dc + 1) * T],
                in_=s.hbm["x3Tb"][dc * 128:(dc + 1) * 128, :]), 16)
        s.b_x3b.did_wr("cDx", ch.n)

    @block.gpsimd
    def _(g):
        ch = s.sem("cDs")
        ch.inc(g.dma_start(out=s.cent_sb[:, :], in_=s.hbm["l1cent"]), 16)
        ch.inc(g.dma_start(out=s.temp_sb[:, :], in_=s.hbm["l1temp"]), 16)
        ch.inc(g.dma_start(out=s.iota16_sb[:, :], in_=s.hbm["iota16"]), 16)
        ch.inc(g.dma_start(out=s.didf_sb[:, :], in_=s.hbm["identf"]), 16)
        ch.inc(g.dma_start(out=s.didb_sb[:, :], in_=s.hbm["identb"]), 16)
        for dc in range(ND):
            ch.inc(g.dma_start(
                out=s.wr_sb[:, dc * 128:(dc + 1) * 128],
                in_=s.hbm["l1wrT"][dc * 128:(dc + 1) * 128, :]), 16)
        s.b_dsetup.did_wr("cDs", ch.n)

    @block.scalar
    def _(sc):
        s.b_dsetup.rd(sc, s)
        s.b_csq.wr(sc, s)
        _i = sc.activation(s.csq_sb[:, :], s.cent_sb[:, :], AF.Square)
        n = s.act.inc(_i)
        s.b_csq.did_wr("act", n)
        s.b_dsetup.did_rd("act", n)

    @block.vector
    def _(v):
        s.b_csq.rd(v, s)
        s.b_cn.wr(v, s)
        _i = v.reduce_sum(s.cw_sb[0:16, 0:1], s.csq_sb[:, :], axis=AX.X)
        n = s.dv.inc(_i)
        s.b_cn.did_wr("dv", n)
        s.b_csq.did_rd("dv", n)

    @block.scalar
    def _(sc):
        s.b_cn.rd(sc, s)
        s.b_cn.wr(sc, s)
        _i = sc.activation(s.cw_sb[0:16, 0:1], s.cw_sb[0:16, 0:1], AF.Sqrt)
        n = s.act.inc(_i)
        s.b_cn.did_wr("act", n)

    @block.vector
    def _(v):
        s.b_cn.rd(v, s)
        s.b_cn.wr(v, s)
        _i = v.tensor_scalar_max(s.cw_sb[0:16, 0:1], s.cw_sb[0:16, 0:1],
                                 1e-12)
        n = s.dv.inc(_i)
        v.wait_ge(s.dv.h, n)
        _i = v.reciprocal(s.cw_sb[0:16, 1:2], s.cw_sb[0:16, 0:1])
        n = s.dv.inc(_i)
        s.b_cn.did_wr("dv", n)
        v.wait_ge(s.dv.h, n)
        s.b_dsetup.rd(v, s)
        s.b_cnm.wr(v, s)
        _i = v.tensor_scalar_mul(s.cnm_sb[:, :], s.cent_sb[:, :],
                                 s.cw_sb[0:16, 1:2])
        n = s.dv.inc(_i)
        s.b_cnm.did_wr("dv", n)
        s.b_cn.did_rd("dv", n)
        s.b_dsetup.did_rd("dv", n)


    @block.tensor
    def _(t):
        s.b_cnm.rd(t, s)
        s.b_dsetup.rd(t, s)
        s.b_pc.wr(t, s)
        _i = t.transpose(s.p_c[0:128, 0:16], s.cnm_sb[:, :],
                         s.didf_sb[0:16, 0:16])
        n = s.mm.inc(_i)
        s.b_pc.did_wr("mm", n)
        s.b_cnm.did_rd("mm", n)

    @block.vector
    def _(v):
        s.b_pc.rd(v, s)
        s.b_cnT.wr(v, s)
        _i = v.tensor_copy(s.cnT_sb[:, :], s.p_c[0:128, 0:16])
        n = s.dv.inc(_i)
        s.b_cnT.did_wr("dv", n)
        s.b_pc.did_rd("dv", n)

    @block.scalar
    def _(sc):
        s.b_dsetup.rd(sc, s)
        s.b_stemp.wr(sc, s)
        _i = sc.activation(s.stemp_sb[:, :], s.temp_sb[:, :], AF.Exp)
        n = s.act.inc(_i)
        s.b_stemp.did_wr("act", n)

    @block.vector
    def _(v):
        s.b_stemp.rd(v, s)
        s.b_stemp.wr(v, s)
        _i = v.tensor_scalar_mul(s.stemp_sb[:, :], s.stemp_sb[:, :],
                                 1.0 / math.sqrt(GS))
        n = s.dv.inc(_i)
        s.b_stemp.did_wr("dv", n)

    for tc in range(NQ):
        @block.tensor
        def _(t, tc=tc):
            s.b_x3b.rd(t, s)
            s.b_dsetup.rd(t, s)
            s.b_pc.wr(t, s)
            for dc in range(ND):
                _i = t.matmul(s.p_c[:, 0:512],
                              s.wr_sb[:, dc * 128:(dc + 1) * 128],
                              s.x3b_sb[:, dc * T + tc * 512:dc * T +
                                       (tc + 1) * 512],
                              start=(dc == 0), stop=(dc == ND - 1))
            n = s.mm.inc(_i)
            s.b_pc.did_wr("mm", n)
            s.b_x3b.did_rd("mm", n)
            s.b_dsetup.did_rd("mm", n)

        @block.vector
        def _(v, tc=tc):
            s.b_pc.rd(v, s)
            s.b_zf.wr(v, s)
            _i = v.tensor_copy(s.zf_sb[:, 0:512], s.p_c[:, 0:512])
            n = s.dv.inc(_i)
            s.b_zf.did_wr("dv", n)
            s.b_pc.did_rd("dv", n)
            v.wait_ge(s.dv.h, n)
            s.b_zb.wr(v, s)
            _i = v.tensor_copy(s.zb_sb[:, 0:512], s.zf_sb[:, 0:512])
            n = s.dv.inc(_i)
            s.b_zb.did_wr("dv", n)
            v.wait_ge(s.dv.h, n)
            s.b_zsq.wr(v, s)
            _i = v.tensor_mul(s.zsq_sb[:, 0:512], s.zb_sb[:, 0:512],
                              s.zb_sb[:, 0:512])
            n = s.dv.inc(_i)
            s.b_zsq.did_wr("dv", n)

        @block.tensor
        def _(t, tc=tc):
            s.b_zsq.rd(t, s)
            s.b_pd.wr(t, s)
            _i = t.matmul(s.p_d[0:1, 0:512], s.ones128b[:, :],
                          s.zsq_sb[:, 0:512], start=True, stop=True)
            n = s.mm.inc(_i)
            s.b_pd.did_wr("mm", n)
            s.b_zsq.did_rd("mm", n)

        @block.scalar
        def _(sc, tc=tc):
            s.b_pd.rd(sc, s)
            s.b_rowm.wr(sc, s)
            _i = sc.activation(s.row_m[:, 0:512], s.p_d[0:1, 0:512], AF.Sqrt)
            n = s.act.inc(_i)
            s.b_rowm.did_wr("act", n)
            s.b_pd.did_rd("act", n)

        @block.vector
        def _(v, tc=tc):
            s.b_rowm.rd(v, s)
            s.b_rowm.wr(v, s)
            _i = v.tensor_scalar_max(s.row_m[:, 0:512], s.row_m[:, 0:512],
                                     1e-12)
            n = s.dv.inc(_i)
            v.wait_ge(s.dv.h, n)
            s.b_rowa.wr(v, s)
            _i = emit_recip_row(s, v, s.row_a[:, 0:512], s.row_m[:, 0:512])
            n = s.dv.inc(_i)
            s.b_rowa.did_wr("dv", n)
            s.b_rowm.did_rd("dv", n)
            v.wait_ge(s.dv.h, n)
            s.b_stemp.rd(v, s)
            _i = v.tensor_scalar_mul(s.row_a[:, 0:512], s.row_a[:, 0:512],
                                     s.stemp_sb[0:1, 0:1])
            n = s.dv.inc(_i)
            s.b_rowa.did_wr("dv", n)
            s.b_stemp.did_rd("dv", n)

        @block.tensor
        def _(t, tc=tc):
            s.b_rowa.rd(t, s)
            s.b_pd.wr(t, s)
            _i = t.matmul(s.p_d[:, 0:512], s.ones1_128f[:, :],
                          s.row_a[:, 0:512], start=True, stop=True)
            n = s.mm.inc(_i)
            s.b_pd.did_wr("mm", n)
            s.b_rowa.did_rd("mm", n)

        @block.vector
        def _(v, tc=tc):
            s.b_pd.rd(v, s)
            s.b_zf.rd(v, s)
            s.b_znb.wr(v, s)
            _i = v.tensor_mul(s.znb_sb[:, 0:512], s.zf_sb[:, 0:512],
                              s.p_d[:, 0:512])
            n = s.dv.inc(_i)
            s.b_znb.did_wr("dv", n)
            s.b_pd.did_rd("dv", n)
            s.b_zf.did_rd("dv", n)

        for tt in range(4):
            j = tc * 4 + tt

            @block.tensor
            def _(t, tt=tt):
                s.b_znb.rd(t, s)
                s.b_cnT.rd(t, s)
                s.b_pc.wr(t, s)
                _i = t.matmul(s.p_c[:, 0:16],
                              s.znb_sb[:, tt * 128:(tt + 1) * 128],
                              s.cnT_sb[:, :], start=True, stop=True)
                n = s.mm.inc(_i)
                s.b_pc.did_wr("mm", n)
                s.b_znb.did_rd("mm", n)
                s.b_cnT.did_rd("mm", n)

            @block.vector
            def _(v, j=j):
                s.b_pc.rd(v, s)
                s.b_lg16.wr(v, s)
                _i = v.tensor_copy(s.lg16_sb[:, :], s.p_c[:, 0:16])
                n = s.dv.inc(_i)
                s.b_lg16.did_wr("dv", n)
                s.b_pc.did_rd("dv", n)
                v.wait_ge(s.dv.h, n)
                s.b_mx16.wr(v, s)
                _i = v.max(s.mx16_sb[:, :], s.lg16_sb[:, :])
                n = s.dv.inc(_i)
                s.b_mx16.did_wr("dv", n)
                v.wait_ge(s.dv.h, n)
                s.b_ln.wr(v, s)
                _i = v.tensor_scalar_mul(s.ln_sb[:, 0:1], s.mx16_sb[:, 0:1],
                                         -1.0)
                n = s.dv.inc(_i)
                s.b_ln.did_wr("dv", n)

            @block.scalar
            def _(sc, j=j):
                s.b_lg16.rd(sc, s)
                s.b_ln.rd(sc, s)
                s.b_pex16.wr(sc, s)
                _i = sc.activation(s.pex16_sb[:, :], s.lg16_sb[:, :], AF.Exp,
                                   bias=s.ln_sb[:, 0:1])
                n = s.act.inc(_i)
                s.b_pex16.did_wr("act", n)
                s.b_lg16.did_rd("act", n)
                s.b_ln.did_rd("act", n)

            @block.vector
            def _(v, j=j):
                s.b_pex16.rd(v, s)
                s.b_zz.wr(v, s)
                _i = v.reduce_sum(s.zz_sb[:, 0:1], s.pex16_sb[:, :],
                                  axis=AX.X)
                n = s.dv.inc(_i)
                s.b_zz.did_wr("dv", n)
                v.wait_ge(s.dv.h, n)
                _i = v.reciprocal(s.zz_sb[:, 1:2], s.zz_sb[:, 0:1])
                n = s.dv.inc(_i)
                s.b_zz.did_wr("dv", n)
                v.wait_ge(s.dv.h, n)
                s.b_pr16.wr(v, s)
                _i = v.tensor_scalar_mul(s.pr16_sb[:, :], s.pex16_sb[:, :],
                                         s.zz_sb[:, 1:2])
                n = s.dv.inc(_i)
                s.b_pr16.did_wr("dv", n)
                s.b_pex16.did_rd("dv", n)
                s.b_zz.did_rd("dv", n)
                v.wait_ge(s.dv.h, n)
                s.b_mx16.rd(v, s)
                s.b_lg16.rd(v, s)
                s.b_mi16.wr(v, s)
                _i = v.max_index(s.mi16_sb[:, :], s.mx16_sb[:, :],
                                 s.lg16_sb[:, :])
                n = s.dv.inc(_i)
                s.b_mi16.did_wr("dv", n)
                s.b_mx16.did_rd("dv", n)
                s.b_lg16.did_rd("dv", n)
                v.wait_ge(s.dv.h, n)
                s.b_i0.wr(v, s)
                _i = v.tensor_copy(s.i0_sb[:, 0:1], s.mi16_sb[:, 0:1])
                n = s.dv.inc(_i)
                _i = v.tensor_copy(s.i0_sb[:, 1:2], s.mi16_sb[:, 1:2])
                n = s.dv.inc(_i)
                s.b_i0.did_wr("dv", n)
                s.b_mi16.did_rd("dv", n)
                v.wait_ge(s.dv.h, n)
                s.b_dsetup.rd(v, s)
                s.b_eq.wr(v, s)
                _i = v.tensor_single_scalar(s.eq_sb[:, 0:16],
                                            s.iota16_sb[:, :],
                                            s.i0_sb[:, 0:1], ALU.is_equal)
                n = s.dv.inc(_i)
                _i = v.tensor_single_scalar(s.eq_sb[:, 16:32],
                                            s.iota16_sb[:, :],
                                            s.i0_sb[:, 1:2], ALU.is_equal)
                n = s.dv.inc(_i)
                s.b_eq.did_wr("dv", n)
                s.b_i0.did_rd("dv", n)
                v.wait_ge(s.dv.h, n)
                _i = v.tensor_add(s.eq_sb[:, 0:16], s.eq_sb[:, 0:16],
                                  s.eq_sb[:, 16:32])
                n = s.dv.inc(_i)
                s.b_eq.did_wr("dv", n)
                v.wait_ge(s.dv.h, n)
                s.b_pr16.rd(v, s)
                s.b_gt.wr(v, s)
                _i = v.tensor_mul(s.gt_sb[:, 0:16], s.pr16_sb[:, :],
                                  s.eq_sb[:, 0:16])
                n = s.dv.inc(_i)
                s.b_gt.did_wr("dv", n)
                s.b_pr16.did_rd("dv", n)
                s.b_eq.did_rd("dv", n)

            @block.tensor
            def _(t, j=j):
                s.b_gt.rd(t, s)
                s.b_dsetup.rd(t, s)
                s.b_pd.wr(t, s)
                _i = t.transpose(s.p_d[0:16, 0:128], s.gt_sb[:, 0:16],
                                 s.didf_sb[:, :])
                n = s.mm.inc(_i)
                s.b_pd.did_wr("mm", n)
                s.b_gt.did_rd("mm", n)

            @block.vector
            def _(v, j=j):
                s.b_pd.rd(v, s)
                s.b_gT.wr(v, s)
                _i = v.tensor_copy(s.gT_sb[0:16, j * 128:(j + 1) * 128],
                                   s.p_d[0:16, 0:128])
                n = s.dv.inc(_i)
                s.b_gT.did_wr("dv", n)
                s.b_pd.did_rd("dv", n)

    for nh in range(H):
        islot = nh % 2

        @block.gpsimd
        def _(g, nh=nh, islot=islot):
            ch = s.sem("cDie%d" % islot)
            s.b_ie[islot].wr(g, s)
            for dc in range(ND):
                ch.inc(g.dma_start(
                    out=s.ie_sb[islot][:, dc * 512:(dc + 1) * 512],
                    in_=s.hbm["l1IE"][nh, dc * 128:(dc + 1) * 128, :]), 16)
            s.b_ie[islot].did_wr("cDie%d" % islot, ch.n)
            s.b_oe[islot].wr(g, s)
            for hc in range(4):
                ch.inc(g.dma_start(
                    out=s.oe_sb[islot][:, hc * 64:(hc + 1) * 64],
                    in_=s.hbm["l1OE"][nh, hc * 128:(hc + 1) * 128, :]), 16)
            s.b_oe[islot].did_wr("cDie%d" % islot, ch.n)

        for tc in range(NQ):
            @block.gpsimd
            def _(g, nh=nh, tc=tc):
                ch = s.sem("cDgr")
                s.b_gT.rd(g, s)
                s.b_gr.wr(g, s)
                n = ch.inc(g.dma_start(
                    out=s.gr_sb[0:1, 0:512],
                    in_=s.gT_sb[nh:nh + 1, tc * 512:(tc + 1) * 512]), 16)
                s.b_gr.did_wr("cDgr", n)
                s.b_gT.did_rd("cDgr", n)

            for hc in range(4):
                ps = s.p_s0 if hc % 2 == 0 else s.p_s1
                psb = s.b_ps0 if hc % 2 == 0 else s.b_ps1

                @block.tensor
                def _(t, nh=nh, tc=tc, hc=hc, ps=ps, psb=psb, islot=islot):
                    s.b_ie[islot].rd(t, s)
                    s.b_x3b.rd(t, s)
                    psb.wr(t, s)
                    for dc in range(ND):
                        _i = t.matmul(
                            ps[:, 0:512],
                            s.ie_sb[islot][:, dc * 512 + hc * 128:
                                           dc * 512 + (hc + 1) * 128],
                            s.x3b_sb[:, dc * T + tc * 512:dc * T +
                                     (tc + 1) * 512],
                            start=(dc == 0), stop=(dc == ND - 1))
                    n = s.mm.inc(_i)
                    psb.did_wr("mm", n)
                    s.b_ie[islot].did_rd("mm", n)
                    s.b_x3b.did_rd("mm", n)

                @block.scalar
                def _(sc, hc=hc, ps=ps, psb=psb):
                    psb.rd(sc, s)
                    s.b_hb.wr(sc, s)
                    _i = sc.activation(s.hb_sb[:, hc * 512:(hc + 1) * 512],
                                       ps[:, 0:512], AF.Gelu)
                    n = s.act.inc(_i)
                    s.b_hb.did_wr("act", n)
                    psb.did_rd("act", n)

            @block.tensor
            def _(t, nh=nh, tc=tc, islot=islot):
                s.b_hb.rd(t, s)
                s.b_oe[islot].rd(t, s)
                s.b_pg.wr(t, s)
                for hc in range(4):
                    _i = t.matmul(s.p_g[0:64, 0:512],
                                  s.oe_sb[islot][:, hc * 64:(hc + 1) * 64],
                                  s.hb_sb[:, hc * 512:(hc + 1) * 512],
                                  start=(hc == 0), stop=(hc == 3))
                n = s.mm.inc(_i)
                s.b_pg.did_wr("mm", n)
                s.b_hb.did_rd("mm", n)
                s.b_oe[islot].did_rd("mm", n)
                s.b_gr.rd(t, s)
                s.b_pc.wr(t, s)
                _i = t.matmul(s.p_c[0:64, 0:512], s.ones1_64f[:, :],
                              s.gr_sb[0:1, 0:512], start=True, stop=True)
                n = s.mm.inc(_i)
                s.b_pc.did_wr("mm", n)
                s.b_gr.did_rd("mm", n)

            @block.vector
            def _(v, nh=nh, tc=tc):
                s.b_pc.rd(v, s)
                s.b_gB.wr(v, s)
                _i = v.tensor_copy(s.gB_sb[0:64, 0:512], s.p_c[0:64, 0:512])
                n = s.dv.inc(_i)
                s.b_gB.did_wr("dv", n)
                s.b_pc.did_rd("dv", n)
                v.wait_ge(s.dv.h, n)
                s.b_pg.rd(v, s)
                s.b_ct1.wr(v, s)
                _i = v.tensor_mul(s.ct1_sb[0:64, 0:512], s.p_g[0:64, 0:512],
                                  s.gB_sb[0:64, 0:512])
                n = s.dv.inc(_i)
                s.b_ct1.did_wr("dv", n)
                s.b_pg.did_rd("dv", n)
                s.b_gB.did_rd("dv", n)

            @block.sync
            def _(sy, nh=nh, tc=tc):
                ch = s.sem("cDy4")
                s.b_ct1.rd(sy, s)
                n = ch.inc(sy.dma_start(
                    out=s.hbm["scr_y4"][nh * 64:(nh + 1) * 64,
                                        tc * 512:(tc + 1) * 512],
                    in_=s.ct1_sb[0:64, 0:512]), 16)
                s.b_ct1.did_rd("cDy4", n)

    for tc in range(NQ):
        @block.gpsimd
        def _(g, tc=tc):
            ch = s.sem("cDy4r")
            s.b_r.wr(g, s)
            for ec in range(ND):
                ch.inc(g.dma_start(
                    out=s.a_r[:, ec * 512:(ec + 1) * 512],
                    in_=s.hbm["scr_y4"][ec * 128:(ec + 1) * 128,
                                        tc * 512:(tc + 1) * 512]), 16)
            s.b_r.did_wr("cDy4r", ch.n)

        for ec in range(ND):
            @block.sync
            def _(sy, ec=ec, tc=tc):
                ch = s.sem("cDxf%d" % (ec % 2))
                s.b_xf[ec % 2].wr(sy, s)
                n = ch.inc(sy.dma_start(
                    out=s.a_xf[:, (ec % 2) * 512:((ec % 2) + 1) * 512],
                    in_=s.hbm["x3Tf"][ec * 128:(ec + 1) * 128,
                                      tc * 512:(tc + 1) * 512]), 16)
                s.b_xf[ec % 2].did_wr("cDxf%d" % (ec % 2), n)

            @block.vector
            def _(v, ec=ec):
                s.b_xf[ec % 2].rd(v, s)
                s.b_r.rd(v, s)
                s.b_r.wr(v, s)
                _i = v.tensor_add(s.a_r[:, ec * 512:(ec + 1) * 512],
                                  s.a_r[:, ec * 512:(ec + 1) * 512],
                                  s.a_xf[:, (ec % 2) * 512:
                                         ((ec % 2) + 1) * 512])
                n = s.dv.inc(_i)
                s.b_r.did_wr("dv", n)
                s.b_xf[ec % 2].did_rd("dv", n)

        emit_ln_store(s, block, s.b_r, s.a_r, s.hbm["xoutT"], None,
                      qc=tc, ch_of="cDof")


def build_program(stages):
    from contextlib import ExitStack
    cst = _consts()
    nc = bass.Bass("TRN2", target_bir_lowering=False)
    s = S()
    s.nc = nc
    s.cst = cst
    s.hbm = {}
    s.innames = set()

    def din(name, shape, dt=F32):
        s.hbm[name] = nc.dram_tensor(name, list(shape), dt,
                                     kind="ExternalInput").ap()
        s.innames.add(name)

    def dout(name, shape, dt=F32):
        s.hbm[name] = nc.dram_tensor(name, list(shape), dt,
                                     kind="ExternalOutput").ap()

    def dscr(name, shape, dt, produced, consumed):
        if produced in stages and consumed in stages:
            s.hbm[name] = nc.dram_tensor("scr_" + name, list(shape), dt).ap()
        elif produced in stages:
            dout(name, shape, dt)
        elif consumed in stages:
            din(name, shape, dt)

    if "A" in stages:
        din("x0T", (C, T))
    dscr("x1Tf", (C, T), F32, "A", "B")
    dscr("x1Tb", (C, T), BF16, "A", "B")
    dscr("hTf", (C, T), F32, "B", "B")
    dscr("hTb", (C, T), BF16, "B", "B")
    dscr("x2Tf", (C, T), F32, "B", "C")
    dscr("x2Tb", (C, T), BF16, "B", "C")
    dscr("x3Tf", (C, T), F32, "C", "D")
    dscr("x3Tb", (C, T), BF16, "C", "D")
    if "D" in stages:
        dout("xoutT", (C, T))
    if "D" in stages:
        s.hbm["scr_y4"] = nc.dram_tensor("scr_y4", [C, T], F32).ap()
    if "B" in stages:
        s.hbm["cc_in"] = nc.dram_tensor("cc_in", [8, 8], F32).ap()
        s.hbm["cc_out"] = nc.dram_tensor("cc_out", [8, 8], F32).ap()
        s.hbm["scr_tot"] = nc.dram_tensor("scr_tot", [1, 256], F32).ap()
        s.hbm["scr_row"] = nc.dram_tensor("scr_row", [1, 256], F32).ap()

    for name, arr in cst.items():
        dt = BF16 if arr.dtype == _BF else F32
        din(name, arr.shape, dt)
    din("rankcol", (1, 8))
    din("rankmask", (8, 1))

    if "A" in stages:
        for n in ["l0aqT", "l0akT", "l0avT", "l0aoT"]:
            din(n, (C, C))
    if "B" in stages:
        for n in ["l0fqT", "l0fkT", "l0fvT"]:
            din(n, (C, C))
        din("l0fbq", (C, 1)); din("l0fbk", (C, 1)); din("l0fbv", (C, 1))
        din("l0fWeT", (E, C, C), BF16); din("l0fbe", (E, C))
        din("l0fswT", (C, E)); din("l0fbsw", (1, E))
        din("l0fg", (C, 1)); din("l0fb", (C, 1))
    if "C" in stages:
        for n in ["l1qT", "l1kT", "l1vT", "l1oT"]:
            din(n, (C, C))
        for n in ["l1qb", "l1kb", "l1ob"]:
            din(n, (C, 1))
        din("l1vb", (C, 1))
    if "D" in stages:
        din("l1wrT", (C, GS)); din("l1cent", (H, GS)); din("l1temp", (1, 1))
        din("l1IE", (H, C, HS)); din("l1OE", (H, HS, HD))

    with ExitStack() as ctx:
        e = ctx.enter_context
        s.p_s0 = e(nc.psum_tensor("p_s0", [128, 1024], F32))
        s.p_s1 = e(nc.psum_tensor("p_s1", [128, 1024], F32))
        s.p_g = e(nc.psum_tensor("p_g", [128, 1024], F32))
        s.p_c = e(nc.psum_tensor("p_c", [128, 512], F32))
        s.p_d = e(nc.psum_tensor("p_d", [128, 512], F32))

        s.sems = {}
        s._ctx = ctx
        s.mm = s.sem("mm")
        s.act = s.sem("act")
        s.dv = s.sem("dv")
        s.gp = s.sem("gp")
        s.cc = s.sem("cc")
        s.bar = s.sem("bar")
        s.barn = 0

        s.ones1_64f = e(nc.sbuf_tensor("ones1_64f", [1, 64], F32))
        s.ones1_128f = e(nc.sbuf_tensor("ones1_128f", [1, 128], F32))
        s.ones1_128b = e(nc.sbuf_tensor("ones1_128b", [1, 128], BF16))
        s.ones128b = e(nc.sbuf_tensor("ones128b", [128, 1], BF16))
        s.ones128f = e(nc.sbuf_tensor("ones128f", [128, 1], F32))
        s.row_a = e(nc.sbuf_tensor("row_a", [1, 512], F32))
        s.row_b = e(nc.sbuf_tensor("row_b", [1, 512], F32))
        s.row_m = e(nc.sbuf_tensor("row_m", [1, 512], F32))
        s.row_q = e(nc.sbuf_tensor("row_q", [1, 512], F32))
        s.epsT = e(nc.sbuf_tensor("epsT", [1, 1], F32))
        s.row_s = s.row_q
        s.a_rsq2 = e(nc.sbuf_tensor("a_rsq2", [128, 1024], BF16))
        s.a_aB = e(nc.sbuf_tensor("a_aB", [128, 512], F32))
        s.a_bB = e(nc.sbuf_tensor("a_bB", [128, 512], F32))
        s.a_o1 = e(nc.sbuf_tensor("a_o1", [128, 1024], F32))

        for nm in ("b_ps0 b_ps1 b_pc b_pd b_pg b_rowa b_rowb "
                   "b_rowm b_rowq b_rowv b_rowsd b_aB b_bB b_o1 "
                   "b_rdb b_ones").split():
            setattr(s, nm, Buf())
        s.b_rsq2 = [Buf(), Buf()]
        s.b_o12 = [Buf(), Buf()]

        block = FakeBlock()
        if True:
            @block.vector
            def _(v):
                v.memset(s.ones1_64f[:, :], 1.0)
                v.memset(s.ones1_128f[:, :], 1.0)
                v.memset(s.ones1_128b[:, :], 1.0)
                v.memset(s.ones128b[:, :], 1.0)
                v.memset(s.ones128f[:, :], 1.0)
                _i = v.memset(s.epsT[:, :], EPS)
                n = s.dv.inc(_i)
                s.b_ones.did_wr("dv", n)

            @block.tensor
            def _(t):
                t.wait_ge(s.dv.h, s.dv.n)

            @block.scalar
            def _(sc):
                sc.wait_ge(s.dv.h, s.dv.n)

            if "A" in stages:
                with ExitStack() as actx:
                    ea = actx.enter_context
                    s.a_w = [ea(nc.sbuf_tensor(f"a_w{i}", [128, ND * 1024],
                                               BF16)) for i in range(4)]
                    s.a_xbf = ea(nc.sbuf_tensor("a_xbf", [128, ND * 512],
                                                BF16))
                    s.a_qt = ea(nc.sbuf_tensor("a_qt", [128, NP * 512], BF16))
                    s.a_ktw = ea(nc.sbuf_tensor("a_ktw",
                                                [128, 3 * NP * 512], BF16))
                    s.a_vaw = ea(nc.sbuf_tensor("a_vaw",
                                                [128, 3 * 4 * 1040], BF16))
                    s.pb_sb = ea(nc.sbuf_tensor("a_pb", [128, 2 * 1024], BF16))
                    s.a_ot = ea(nc.sbuf_tensor("a_ot", [128, ND * 512], BF16))
                    s.a_xf = ea(nc.sbuf_tensor("a_xf", [128, 2 * 512], F32))
                    s.a_r = ea(nc.sbuf_tensor("a_r", [128, ND * 512], F32))
                    s.a_maskA = ea(nc.sbuf_tensor("a_maskA", [128, 8 * 512],
                                                  BF16))
                    for nm in ("b_aw b_r b_maskA").split():
                        setattr(s, nm, Buf())
                    _bx, _bq, _bo = Buf(), Buf(), Buf()
                    s.b_xbf2 = [_bx, _bx]
                    s.b_qt2 = [_bq, _bq]
                    s.b_ot2 = [_bo, _bo]
                    s.b_ktw = [Buf(), Buf(), Buf()]
                    s.b_vaw = [Buf(), Buf(), Buf()]
                    s.b_pbuf = [Buf(), Buf()]
                    s.b_xf = [Buf(), Buf()]
                    s.b_pb_half = [[0, 0], [0, 0]]

                    emit_stage_A(s, block)
                _barrier(s, block)

            if "B" in stages:
                s.fence = {k: sem.n for k, sem in s.sems.items()}

                def fbuf():
                    b = Buf()
                    b.w = dict(s.fence)
                    return b

                from contextlib import ExitStack as _ES
                with _ES() as bctx:
                    eb = bctx.enter_context
                    s.bq_sb = eb(nc.sbuf_tensor("bbq", [128, 8], F32))
                    s.bk_sb = eb(nc.sbuf_tensor("bbk", [128, 8], F32))
                    s.bv_sb = eb(nc.sbuf_tensor("bbv", [128, 8], F32))
                    s.gb_g = eb(nc.sbuf_tensor("bgg", [128, 8], F32))
                    s.gb_b = eb(nc.sbuf_tensor("bgb", [128, 8], F32))
                    s.sw_sb = eb(nc.sbuf_tensor("bsw", [128, 64], BF16))
                    s.bsw_sb = eb(nc.sbuf_tensor("bbsw", [1, 8], F32))
                    s.bswB_sb = eb(nc.sbuf_tensor("bbswB", [128, 8], F32))
                    s.ltri_sb = eb(nc.sbuf_tensor("bltri", [128, 128], BF16))
                    s.ltri32_sb = eb(nc.sbuf_tensor("bltri32", [32, 32], F32))
                    s.identf_sb = eb(nc.sbuf_tensor("bidf", [128, 128], F32))
                    s.rp_sb = eb(nc.sbuf_tensor("brp", [128, 32], F32))
                    s.kept_sb = eb(nc.sbuf_tensor("bkept", [128, 32], F32))
                    s.ohkT_sb = eb(nc.sbuf_tensor("bohkT", [8, T], BF16))
                    s.m1c_sb = eb(nc.sbuf_tensor("bm1c", [128, 1], F32))
                    s.m2c_sb = eb(nc.sbuf_tensor("bm2c", [128, 1], F32))
                    for nm in ("b_setup b_w b_bswB b_kt b_va b_xcb b_qt "
                               "b_sfx b_oh b_ohb b_rp b_kept b_ohkT b_ohkb "
                               "b_tot b_t32 b_tall b_slot b_gat b_coff "
                               "b_roff b_rrow b_lg b_mx b_pex b_mi b_ln "
                               "b_idxf b_zz b_pos b_rb8 b_pf8 b_m1c b_m2c "
                               "b_m1r b_m2r b_mB1 b_mB2 b_okr b_htb "
                               "b_o2 b_r").split():
                        setattr(s, nm, fbuf())
                    s.b_pbuf = [fbuf(), fbuf()]
                    s.b_htf = [fbuf(), fbuf()]
                    s.b_ohkb2 = [fbuf(), fbuf()]
                    s.rb8_sb = ColView(s.a_o1, 0)
                    s.b_rb8 = s.b_o12[0]
                    s.pf8_sb = ColView(s.a_o1, 512)
                    s.b_pf8 = s.b_o12[1]

                    @block.vector
                    def _(v):
                        _wait(v, s, s.fence)
                        _i = v.memset(s.row_m[0:1, 0:1], 0.0)
                        n = s.dv.inc(_i)

                    emit_B_setup(s, block)

                    with _ES() as b2ctx:
                        e2 = b2ctx.enter_context
                        s.kt = e2(nc.sbuf_tensor("bkt", [128, NP * T], BF16))
                        s.w_sb = e2(nc.sbuf_tensor("bw", [128, ND * 1024],
                                                   BF16))
                        s.maskB_sb = e2(nc.sbuf_tensor("bmaskB", [128, 2048],
                                                       BF16))
                        s.step_sb = e2(nc.sbuf_tensor("bstep", [128, 1024],
                                                      BF16))
                        s.rankc_sb = e2(nc.sbuf_tensor("brkc", [1, 8], F32))
                        s.rankm_sb = e2(nc.sbuf_tensor("brkm", [8, 1], F32))
                        s.iota8_sb = e2(nc.sbuf_tensor("bio8", [128, 8], F32))
                        s.oh_sb = e2(nc.sbuf_tensor("boh", [128, 256], F32))
                        s.ohb_sb = e2(nc.sbuf_tensor("bohb", [128, 256], BF16))
                        s.ohkb_sb = e2(nc.sbuf_tensor("bohkb", [128, 16], F32))
                        s.tot_sb = e2(nc.sbuf_tensor("btot", [1, 256], F32))
                        s.t32_sb = e2(nc.sbuf_tensor("bt32", [32, 8], F32))
                        s.tall_sb = e2(nc.sbuf_tensor("btall", [1, 8], F32))
                        s.slot_sb = e2(nc.sbuf_tensor("bslot", [8, 8], F32))
                        s.gat_sb = e2(nc.sbuf_tensor("bgat", [8, 8], F32))
                        s.coff_sb = e2(nc.sbuf_tensor("bcoff", [1, 8], F32))
                        s.roff_sb = e2(nc.sbuf_tensor("broff", [32, 8], F32))
                        s.rrow_sb = e2(nc.sbuf_tensor("brrow", [1, 256], F32))
                        s.lg_sb = e2(nc.sbuf_tensor("blg", [128, 8], F32))
                        s.mx_sb = e2(nc.sbuf_tensor("bmx", [128, 8], F32))
                        s.pex_sb = e2(nc.sbuf_tensor("bpex", [128, 8], F32))
                        s.mi_sb = e2(nc.sbuf_tensor("bmi", [128, 8], U32))
                        s.ln_sb = e2(nc.sbuf_tensor("bln", [128, 1], F32))
                        s.idxf_sb = e2(nc.sbuf_tensor("bidxf", [128, 1], F32))
                        s.zz_sb = e2(nc.sbuf_tensor("bzz", [128, 1], F32))
                        s.pos_sb = e2(nc.sbuf_tensor("bpos", [128, 32], F32))
                        s.ones1_32f = e2(nc.sbuf_tensor("bones1_32", [1, 32], F32))
                        s.ones32f = e2(nc.sbuf_tensor("bones32", [32, 1], F32))

                        s.va = e2(nc.sbuf_tensor("bva", [128, NT * 16 * 65],
                                                 BF16))
                        s.qt = e2(nc.sbuf_tensor("bqt", [128, NP * 512], BF16))
                        s.xcb = e2(nc.sbuf_tensor("bxcb", [128, ND * 512],
                                                  BF16))
                        s.pb_sb = e2(nc.sbuf_tensor("bpb", [128, 2048], BF16))
                        s.sfx = e2(nc.sbuf_tensor("bsfx", [128, 512], F32))
                        @block.vector
                        def _(v):
                            v.memset(s.ones1_32f[:, :], 1.0)
                            _i = v.memset(s.ones32f[:, :], 1.0)
                            n = s.dv.inc(_i)
                            s.b_setup.did_wr("dv", n)

                        emit_B_mst(s, block)
                        emit_B_kv(s, block)
                        for qc in range(NQ):
                            emit_B_attn_qchunk(s, block, qc)
                        emit_B_counts(s, block)
                        emit_B_pos(s, block)

                    s.fence = {k: sem.n for k, sem in s.sems.items()}
                    with _ES() as b6ctx:
                        e6 = b6ctx.enter_context
                        s.we_sb = e6(nc.sbuf_tensor("bwe",
                                                    [128, E * ND * 1024],
                                                    BF16))
                        s.hm_sb2 = [e6(nc.sbuf_tensor("bhm%d" % i,
                                                      [128, 4 * 512], BF16))
                                    for i in range(2)]
                        s.okA_sb = e6(nc.sbuf_tensor("bokA", [128, E * 512],
                                                     BF16))
                        s.x1f_sb = e6(nc.sbuf_tensor("bx1f", [128, 2 * 512],
                                                     F32))
                        s.htb_sb = e6(nc.sbuf_tensor("bhtb", [128, ND * 512],
                                                     BF16))
                        s.okr_sb = e6(nc.sbuf_tensor("bokr", [1, 512], BF16))
                        s.be_sb = e6(nc.sbuf_tensor("bbe", [8, 1024], BF16))

                        s.a_r = e6(nc.sbuf_tensor("br", [128, ND * 512], F32))
                        s.a_xf = e6(nc.sbuf_tensor("bxf", [128, 2 * 512],
                                                   F32))
                        for nm in ("b_okA b_htb b_okr b_r").split():
                            setattr(s, nm, fbuf())
                        s.m1r_sb = s.row_a
                        s.b_m1r = s.b_rowa
                        s.m2r_sb = s.row_b
                        s.b_m2r = s.b_rowb
                        s.b_hm2 = [fbuf(), fbuf()]
                        s.b_o2 = s.b_o12[1]
                        s.b_rb8 = s.b_o12[0]
                        s.b_wes = [fbuf() for _ in range(E)]
                        s.b_x1f2 = [fbuf(), fbuf()]
                        s.b_htf = [fbuf(), fbuf()]
                        s.mB1_sb = s.a_aB
                        s.b_mB1 = s.b_aB
                        s.mB2_sb = s.a_bB
                        s.b_mB2 = s.b_bB
                        emit_B_experts(s, block)
                _barrier(s, block)

            if "C" in stages:
                s.fence = {k: sem.n for k, sem in s.sems.items()}

                def fbuf():
                    b = Buf()
                    b.w = dict(s.fence)
                    return b

                from contextlib import ExitStack as _ES
                with _ES() as cctx:
                    ec_ = cctx.enter_context
                    s.kt = ec_(nc.sbuf_tensor("ckt", [128, NP * T], BF16))
                    s.va = ec_(nc.sbuf_tensor("cva", [128, NT * 16 * 65],
                                              BF16))
                    s.qt = ec_(nc.sbuf_tensor("cqt", [128, NP * 512], BF16))
                    s.xcb = ec_(nc.sbuf_tensor("cxcb", [128, ND * 512], BF16))
                    s.pb_sb = ec_(nc.sbuf_tensor("cpb", [128, 2048], BF16))
                    s.w_sb = ec_(nc.sbuf_tensor("cw", [128, ND * 1024], BF16))
                    s.cosb_sb = ec_(nc.sbuf_tensor("ccos", [128, 512], BF16))
                    s.sinb_sb = ec_(nc.sbuf_tensor("csin", [128, 512], BF16))
                    s.crawb_sb = ec_(nc.sbuf_tensor("ccrawb", [128, 512],
                                                    BF16))
                    s.cqb_sb = ec_(nc.sbuf_tensor("ccqb", [128, 8], F32))
                    s.ckb_sb = ec_(nc.sbuf_tensor("cckb", [128, 8], F32))
                    s.cvb_sb = ec_(nc.sbuf_tensor("ccvb", [128, 8], F32))
                    s.cob_sb = ec_(nc.sbuf_tensor("ccob", [128, 8], F32))
                    s.p32_sb = ec_(nc.sbuf_tensor("cp32", [128, 128], BF16))
                    s.a_r = ec_(nc.sbuf_tensor("cr", [128, ND * 512], F32))
                    s.a_xf = ec_(nc.sbuf_tensor("cxf", [128, 512], F32))
                    s.c_rope = ec_(nc.sbuf_tensor("crope", [128, 512],
                                                  F32))
                    s.b_rope = fbuf()
                    s.ct1_sb = ColView(s.c_rope, 0)
                    s.craw_sb = ColView(s.c_rope, 0)
                    for nm in ("b_csetup b_ccs b_crawb "
                               "b_kt b_va b_qt b_xcb b_w b_r").split():
                        setattr(s, nm, fbuf())
                    s.b_ct1 = s.b_rope
                    s.b_craw = s.b_rope
                    s.b_pbuf = [fbuf(), fbuf()]
                    s.b_xf = [fbuf(), fbuf()]
                    emit_stage_C(s, block)
                _barrier(s, block)

            if "D" in stages:
                s.fence = {k: sem.n for k, sem in s.sems.items()}

                def fbuf():
                    b = Buf()
                    b.w = dict(s.fence)
                    return b

                from contextlib import ExitStack as _ES
                with _ES() as dctx:
                    ed = dctx.enter_context
                    s.x3b_sb = ed(nc.sbuf_tensor("dx3b", [128, ND * T], BF16))
                    s.wr_sb = ed(nc.sbuf_tensor("dwr", [128, ND * 128], BF16))
                    s.cent_sb = ed(nc.sbuf_tensor("dcent", [16, 128], F32))
                    s.csq_sb = ed(nc.sbuf_tensor("dcsq", [16, 128], F32))
                    s.cw_sb = ed(nc.sbuf_tensor("dcw", [16, 2], F32))
                    s.cnm_sb = ed(nc.sbuf_tensor("dcnm", [16, 128], F32))
                    s.cnT_sb = ed(nc.sbuf_tensor("dcnT", [128, 16], BF16))
                    s.temp_sb = ed(nc.sbuf_tensor("dtemp", [1, 1], F32))
                    s.stemp_sb = ed(nc.sbuf_tensor("dstemp", [1, 1], F32))
                    s.iota16_sb = ed(nc.sbuf_tensor("dio16", [128, 16], F32))
                    s.didf_sb = ed(nc.sbuf_tensor("didf", [128, 128], F32))
                    s.didb_sb = ed(nc.sbuf_tensor("didb", [128, 128], BF16))
                    s.zf_sb = ed(nc.sbuf_tensor("dzf", [128, 512], F32))
                    s.zb_sb = ed(nc.sbuf_tensor("dzb", [128, 512], BF16))
                    s.zsq_sb = ed(nc.sbuf_tensor("dzsq", [128, 512], BF16))
                    s.znb_sb = ed(nc.sbuf_tensor("dznb", [128, 512], BF16))
                    s.lg16_sb = ed(nc.sbuf_tensor("dlg16", [128, 16], F32))
                    s.mx16_sb = ed(nc.sbuf_tensor("dmx16", [128, 8], F32))
                    s.mi16_sb = ed(nc.sbuf_tensor("dmi16", [128, 8], U32))
                    s.pex16_sb = ed(nc.sbuf_tensor("dpex16", [128, 16], F32))
                    s.pr16_sb = ed(nc.sbuf_tensor("dpr16", [128, 16], F32))
                    s.i0_sb = ed(nc.sbuf_tensor("di0", [128, 2], F32))
                    s.eq_sb = ed(nc.sbuf_tensor("deq", [128, 32], F32))
                    s.gt_sb = ed(nc.sbuf_tensor("dgt", [128, 16], F32))
                    s.gT_sb = ed(nc.sbuf_tensor("dgT", [16, T], F32))
                    s.gr_sb = ed(nc.sbuf_tensor("dgr", [1, 512], F32))
                    s.gB_sb = ed(nc.sbuf_tensor("dgB", [128, 512], F32))
                    s.ie_sb = [ed(nc.sbuf_tensor("die%d" % i,
                                                 [128, ND * 512], BF16))
                               for i in range(2)]
                    s.oe_sb = [ed(nc.sbuf_tensor("doe%d" % i,
                                                 [128, 4 * 64], BF16))
                               for i in range(2)]
                    s.hb_sb = ed(nc.sbuf_tensor("dhb", [128, 4 * 512], BF16))
                    s.ct1_sb = ed(nc.sbuf_tensor("dct1", [128, 512], F32))
                    s.ln_sb = ed(nc.sbuf_tensor("dln", [128, 1], F32))
                    s.zz_sb = ed(nc.sbuf_tensor("dzz", [128, 2], F32))
                    s.a_r = ed(nc.sbuf_tensor("dr", [128, ND * 512], F32))
                    s.a_xf = ed(nc.sbuf_tensor("dxf", [128, 2 * 512], F32))
                    for nm in ("b_x3b b_dsetup b_csq b_cn b_cnm b_cnT "
                               "b_stemp b_zf b_zb b_zsq b_znb b_lg16 b_mx16 "
                               "b_mi16 b_pex16 b_pr16 b_i0 b_eq b_gt b_gT "
                               "b_gr b_gB b_hb b_ct1 b_r b_ln "
                               "b_zz").split():
                        setattr(s, nm, fbuf())
                    s.b_ie = [fbuf(), fbuf()]
                    s.b_oe = [fbuf(), fbuf()]
                    s.b_xf = [fbuf(), fbuf()]
                    emit_stage_D(s, block)
                _barrier(s, block)

        with nc.Block() as rblock:
            block.flush(rblock)

    return nc, s


def _core_inputs(inputs, w, core, stages):
    cst = _consts()
    d = {}
    for name, arr in cst.items():
        d[name] = np.asarray(arr)
    d["rankcol"] = (np.arange(8) == core).astype(np.float32).reshape(1, 8)
    d["rankmask"] = (np.arange(8) < core).astype(np.float32).reshape(8, 1)
    if "A" in stages:
        x = np.asarray(inputs["L0_attn_x"][core])
        d["x0T"] = np.ascontiguousarray(x.T)
        for n in ["l0aqT", "l0akT", "l0avT", "l0aoT"]:
            d[n] = w[n]
    if "B" in stages:
        for n in ["l0fqT", "l0fkT", "l0fvT", "l0fbq", "l0fbk", "l0fbv",
                  "l0fWeT", "l0fbe", "l0fswT", "l0fbsw", "l0fg", "l0fb"]:
            d[n] = w[n]
    if "C" in stages:
        for n in ["l1qT", "l1kT", "l1vT", "l1oT", "l1qb", "l1kb", "l1vb",
                  "l1ob"]:
            d[n] = w[n]
    if "D" in stages:
        for n in ["l1wrT", "l1cent", "l1temp", "l1IE", "l1OE"]:
            d[n] = w[n]
    return d


def kernel(**inputs):
    w = _prep_weights(inputs)
    stage_sets = [STAGES] if FUSED else list(STAGES)

    carry = [dict() for _ in range(8)]
    out = np.zeros((8, T, C), np.float32)
    for stages in stage_sets:
        nc, s = build_program(stages)
        in_maps = []
        for core in range(8):
            m = _core_inputs(inputs, w, core, stages)
            m.update(carry[core])
            m = {k: v for k, v in m.items() if k in s.innames}
            missing = s.innames - set(m)
            assert not missing, f"missing inputs {missing}"
            in_maps.append(m)
        res = run_bass_kernel_spmd(nc, in_maps, core_ids=list(range(8)))
        for core in range(8):
            carry[core].update(res.results[core])
        del nc, s
    for core in range(8):
        if "xoutT" in carry[core]:
            out[core] = np.asarray(carry[core]["xoutT"], np.float32).T
    return out
